# version 04
# baseline (speedup 1.0000x reference)
import contextlib
import numpy as np
import concourse.bass as bass
import concourse.mybir as mybir
from concourse.bass_utils import run_bass_kernel_spmd

F32 = mybir.dt.float32
BF16 = mybir.dt.bfloat16
OP = mybir.AluOpType
AF = mybir.ActivationFunctionType
AX = mybir.AxisListType

B, S, D = 4, 4096, 1024
T = 2048
HALO = 3
TH = T + HALO
NCORE = 8
DC = 8
DIN = 2048
NH = 32
HP = 64
DST = 128
Q = 128
NCH = 16
INTER0P = 2816
DHID = 4096
NEG = -1e30
GROUPS = [[0, 1], [2, 3], [4, 5], [6, 7]]

NS_OFF = [0, 512, 1024, 1536, 1539]
NS_W = [512, 512, 512, 512, 512]


class Chan:
    def __init__(self, sem):
        self.sem = sem
        self.n = 0

    def tok(self):
        return (self, self.n)


class Emitter:
    ENGS = ("t", "v", "s", "g", "y")

    def __init__(self):
        self.ops = {e: [] for e in self.ENGS}

    def emit(self, eng, fn, waits=(), incs=()):
        ws = []
        for w in waits:
            if w is None:
                continue
            ch, v = w
            if v > 0:
                ws.append((ch, v))
        for ch, v in incs:
            ch.n += v
        self.ops[eng].append((fn, tuple(ws), tuple(incs)))

    def realize(self, block):
        def mk(key):
            def body(eng):
                last = {}
                for fn, waits, incs in self.ops[key]:
                    for ch, v in waits:
                        if last.get(id(ch), -1) >= v:
                            continue
                        eng.wait_ge(ch.sem, v)
                        last[id(ch)] = v
                    ins = fn(eng)
                    for ch, v in incs:
                        ins = ins.then_inc(ch.sem, v)
            return body

        block.sync(mk("y"))
        block.tensor(mk("t"))
        block.vector(mk("v"))
        block.scalar(mk("s"))
        block.gpsimd(mk("g"))


def _tile_k(w, kc):
    K, M = w.shape
    assert K == kc * 128
    return np.ascontiguousarray(w.reshape(kc, 128, M))


def _pad_cols(w, m_to):
    out = np.zeros((w.shape[0], m_to), dtype=w.dtype)
    out[:, : w.shape[1]] = w
    return out


def _pad_rows(w, k_to):
    out = np.zeros((k_to, w.shape[1]), dtype=w.dtype)
    out[: w.shape[0], :] = w
    return out


def _oslab(wT, kc, oc):
    K, M = wT.shape
    assert K == kc * 128 and M == oc * 128
    a = wT.reshape(kc, 128, oc, 128).transpose(2, 1, 0, 3)
    return np.ascontiguousarray(a.reshape(oc, 128, kc * 128))


def _h2mv(H2B, k, no):
    return H2B[:, k * T + no: k * T + no + 512]


def host_prepare(inputs):
    f32 = np.float32
    x0 = np.asarray(inputs["L0_attn_hidden_states"], f32)

    ipT = np.ascontiguousarray(np.asarray(inputs["L0_attn_in_proj_w"], f32).T)
    ipTp = _pad_cols(ipT, 35 * 128)
    W_ip = np.ascontiguousarray(ipTp.reshape(8, 128, 35, 128).transpose(2, 0, 1, 3))

    convw = np.asarray(inputs["L0_attn_conv_w"], f32)
    conv_diag = np.zeros((18, 4, 128, 128), f32)
    for c in range(18):
        for t in range(4):
            np.fill_diagonal(conv_diag[c, t], convw[c * 128:(c + 1) * 128, t])
    conv_b = np.ascontiguousarray(np.asarray(inputs["L0_attn_conv_b"], f32).reshape(18, 128).T)

    dt_bias = np.tile(np.asarray(inputs["L0_attn_dt_bias"], f32).reshape(NH, 1), (4, 1))
    A_log = np.tile(np.asarray(inputs["L0_attn_A_log"], f32).reshape(NH, 1), (4, 1))
    Dvec = np.asarray(inputs["L0_attn_D"], f32)
    D_diag = np.zeros((NH, 128, 128), f32)
    for h in range(NH):
        np.fill_diagonal(D_diag[h], Dvec[h])

    ssm_w = np.ascontiguousarray(np.asarray(inputs["L0_attn_ssm_norm_w"], f32).reshape(16, 128).T)
    norm_w = np.ascontiguousarray(np.asarray(inputs["L0_attn_norm_w"], f32).reshape(8, 128).T)
    post_w = np.ascontiguousarray(np.asarray(inputs["L0_attn_post_norm_w"], f32).reshape(8, 128).T)

    W_op = _tile_k(np.ascontiguousarray(np.asarray(inputs["L0_attn_out_proj_w"], f32).T), 16)
    W_g = _tile_k(_pad_cols(np.ascontiguousarray(np.asarray(inputs["L0_attn_gate_w"], f32).T), INTER0P), 8)
    W_u = _tile_k(_pad_cols(np.ascontiguousarray(np.asarray(inputs["L0_attn_up_w"], f32).T), INTER0P), 8)
    W_d = _tile_k(_pad_rows(np.ascontiguousarray(np.asarray(inputs["L0_attn_down_w"], f32).T), INTER0P), 22)

    Win = _tile_k(np.asarray(inputs["L0_ffn_Win"], f32), 8)
    bin_ = np.asarray(inputs["L0_ffn_bin_"], f32).reshape(64, 1)
    W1 = np.asarray(inputs["L0_ffn_W1"], f32)
    b1 = np.asarray(inputs["L0_ffn_b1"], f32)
    W2 = np.asarray(inputs["L0_ffn_W2"], f32)
    b2 = np.asarray(inputs["L0_ffn_b2"], f32).reshape(64, 1)
    Wout0 = np.ascontiguousarray(np.asarray(inputs["L0_ffn_Wout"], f32))
    bout0 = np.ascontiguousarray(np.asarray(inputs["L0_ffn_bout"], f32).reshape(8, 128).T)

    wqkv = np.asarray(inputs["L1_attn_w_qkv"], f32)
    W_qk, W_v = {}, {}
    for hg in range(2):
        blocks = []
        for j in range(4):
            h0, h1 = hg * 8 + 2 * j, hg * 8 + 2 * j + 1
            blocks.append(np.concatenate([wqkv[:, h0, 0:64], wqkv[:, h1, 0:64]], axis=1))
            blocks.append(np.concatenate([wqkv[:, h0, 64:128], wqkv[:, h1, 64:128]], axis=1))
        W_qk[hg] = _tile_k(np.concatenate(blocks, axis=1), 8)
        v = np.concatenate([wqkv[:, h, 128:192] for h in range(hg * 8, hg * 8 + 8)], axis=1)
        W_v[hg] = _tile_k(v, 8)
    W_wo = _tile_k(np.asarray(inputs["L1_attn_w_out"], f32).reshape(1024, 1024), 8)

    W_t1 = _oslab(np.ascontiguousarray(np.asarray(inputs["L1_ffn_w1_w"], f32).T), 8, 32)
    W_t2 = _oslab(np.ascontiguousarray(np.asarray(inputs["L1_ffn_w2_w"], f32).T), 8, 32)
    W_h1 = _oslab(np.ascontiguousarray(np.asarray(inputs["L1_ffn_h1_w"], f32).T), 32, 32)
    W_h2 = _oslab(np.ascontiguousarray(np.asarray(inputs["L1_ffn_h2_w"], f32).T), 32, 32)
    W_t3 = _oslab(np.ascontiguousarray(np.asarray(inputs["L1_ffn_w3_w"], f32).T), 32, 8)
    s_t1 = np.ascontiguousarray(np.asarray(inputs["L1_ffn_w1_s"], f32).reshape(32, 128).T)
    s_t2 = np.ascontiguousarray(np.asarray(inputs["L1_ffn_w2_s"], f32).reshape(32, 128).T)
    s_h1 = np.ascontiguousarray(np.asarray(inputs["L1_ffn_h1_s"], f32).reshape(32, 128).T)
    s_h2 = np.ascontiguousarray(np.asarray(inputs["L1_ffn_h2_s"], f32).reshape(32, 128).T)
    s_t3 = np.ascontiguousarray(np.asarray(inputs["L1_ffn_w3_s"], f32).reshape(8, 128).T)

    ident = np.eye(128, dtype=f32)
    ltri01 = np.triu(np.ones((128, 128), f32))
    attn_mask = np.zeros((4, 128, 512), f32)
    for d in range(4):
        for k in range(128):
            attn_mask[d, k, : min(d * 128 + k, 512)] = NEG

    shared = dict(
        W_ip=W_ip, conv_diag=conv_diag, conv_b=conv_b, dt_bias=dt_bias,
        A_log=A_log, D_diag=D_diag, ssm_w=ssm_w, norm_w=norm_w, post_w=post_w,
        W_op=W_op, W_g=W_g, W_u=W_u, W_d=W_d,
        Win=Win, bin_=bin_, W1=W1, b1=b1, W2=W2, b2=b2, Wout0=Wout0, bout0=bout0,
        W_wo=W_wo, W_t1=W_t1, W_t2=W_t2, W_h1=W_h1, W_h2=W_h2, W_t3=W_t3,
        s_t1=s_t1, s_t2=s_t2, s_h1=s_h1, s_h2=s_h2, s_t3=s_t3,
        ident=ident, ltri01=ltri01, attn_mask=attn_mask,
    )

    in_maps = []
    for c in range(NCORE):
        b, hf = c // 2, c % 2
        if hf == 0:
            xs = np.zeros((TH, D), np.float32)
            xs[HALO:] = x0[b, 0:T]
        else:
            xs = x0[b, T - HALO: T - HALO + TH]
        m = dict(shared)
        m["x"] = np.ascontiguousarray(xs.T)
        m["W_qkv_qk"] = W_qk[hf]
        m["W_qkv_v"] = W_v[hf]
        m["uinit"] = np.full((128, 1), float(hf), np.float32)
        m["uinv"] = np.full((128, 1), float(1 - hf), np.float32)
        in_maps.append(m)
    return in_maps


def build_program(dbg=None, stop_after=None):
    nc = bass.Bass("TRN2", target_bir_lowering=False, debug=False, detect_race_conditions=False)

    def din(name, shape):
        return nc.dram_tensor(name, list(shape), F32, kind="ExternalInput").ap()

    x_in = din("x", (D, TH))
    W_ip = din("W_ip", (35, 8, 128, 128))
    conv_diag = din("conv_diag", (18, 4, 128, 128))
    conv_b = din("conv_b", (128, 18))
    dt_bias = din("dt_bias", (128, 1))
    A_log = din("A_log", (128, 1))
    D_diag_in = din("D_diag", (NH, 128, 128))
    ssm_w = din("ssm_w", (128, 16))
    norm_w = din("norm_w", (128, 8))
    post_w = din("post_w", (128, 8))
    W_op = din("W_op", (16, 128, 1024))
    W_g = din("W_g", (8, 128, INTER0P))
    W_u = din("W_u", (8, 128, INTER0P))
    W_d = din("W_d", (22, 128, 1024))
    Win = din("Win", (8, 128, 64))
    bin_in = din("bin_", (64, 1))
    W1_in = din("W1", (64, 16))
    b1_in = din("b1", (64, 16))
    W2_in = din("W2", (64, 16))
    b2_in = din("b2", (64, 1))
    Wout0 = din("Wout0", (64, 1024))
    bout0 = din("bout0", (128, 8))
    W_qkv_qk = din("W_qkv_qk", (8, 128, 1024))
    W_qkv_v = din("W_qkv_v", (8, 128, 512))
    W_wo = din("W_wo", (8, 128, 1024))
    W_t1 = din("W_t1", (32, 128, 1024))
    W_t2 = din("W_t2", (32, 128, 1024))
    W_h1 = din("W_h1", (32, 128, 4096))
    W_h2 = din("W_h2", (32, 128, 4096))
    W_t3 = din("W_t3", (8, 128, 4096))
    s_t1 = din("s_t1", (128, 32))
    s_t2 = din("s_t2", (128, 32))
    s_h1 = din("s_h1", (128, 32))
    s_h2 = din("s_h2", (128, 32))
    s_t3 = din("s_t3", (128, 8))
    ident_in = din("ident", (128, 128))
    ltri_in = din("ltri01", (128, 128))
    amask_in = din("attn_mask", (4, 128, 512))
    uinit_in = din("uinit", (128, 1))
    uinv_in = din("uinv", (128, 1))

    y_out = nc.dram_tensor("y", [D, T], F32, kind="ExternalOutput").ap()
    dbg_out = None
    if dbg is not None:
        dbg_out = nc.dram_tensor("dbg", [dbg[1], dbg[2]], F32, kind="ExternalOutput").ap()

    xtm_d = nc.dram_tensor("xtm_d", [16, 128, 2048], BF16).ap()
    schunk_d = nc.dram_tensor("schunk_d", [NCH, 128, DIN], F32).ap()
    sAG_in = nc.dram_tensor("sAG_in", [128, DIN], F32).ap()
    sAG_out = nc.dram_tensor("sAG_out", [256, DIN], F32).ap()
    x1u_d = nc.dram_tensor("x1u_d", [D, T], F32).ap()
    t1AG_in = nc.dram_tensor("t1AG_in", [D, T], BF16).ap()
    t1AG_out = nc.dram_tensor("t1AG_out", [2 * D, T], BF16).ap()
    oAG_in = nc.dram_tensor("oAG_in", [512, S], BF16).ap()
    oAG_out = nc.dram_tensor("oAG_out", [1024, S], BF16).ap()
    x1p_d = nc.dram_tensor("x1p_d", [D, T], F32).ap()
    x2_d = nc.dram_tensor("x2_d", [D, T], F32).ap()

    em = Emitter()
    stack = contextlib.ExitStack()

    def sbuf(name, shape, dtype, st=None):
        return (st or stack).enter_context(nc.sbuf_tensor(name, list(shape), dtype))

    def chan(name):
        return Chan(stack.enter_context(nc.semaphore(name)))

    ch_t = chan("c_t")
    ch_v = chan("c_v")
    ch_s = chan("c_s")
    ldy = chan("ldy")
    cc = chan("cc")
    _dch = {}

    def dch(key):
        if key not in _dch:
            _dch[key] = chan(f"d_{key}")
        return _dch[key]

    def gdma(out, in_, key, waits=(), accum=False):
        ch = dch(key) if isinstance(key, str) else key
        if accum:
            em.emit("g", lambda e, o=out, i=in_: e.dma_start(out=o, in_=i, accum_op=OP.add),
                    waits, ((ch, 16),))
        else:
            em.emit("g", lambda e, o=out, i=in_: e.dma_start(out=o, in_=i),
                    waits, ((ch, 16),))
        return ch.tok()

    def ydma(out, in_, waits=()):
        em.emit("y", lambda e, o=out, i=in_: e.dma_start(out=o, in_=i),
                waits, ((ldy, 16),))
        return ldy.tok()

    def mm(out, lhsT, rhs, start, stop, waits=(), inc=False, tp=None):
        em.emit("t", lambda e, o=out, l=lhsT, r=rhs, a=start, b=stop, t=tp:
                e.matmul(o, l, r, start=a, stop=b, tile_position=t),
                waits, ((ch_t, 1),) if inc else ())
        return ch_t.tok() if inc else None

    def tr(out, in_, idn, waits=(), inc=True):
        em.emit("t", lambda e, o=out, i=in_, d=idn: e.transpose(o, i, d),
                waits, ((ch_t, 1),) if inc else ())
        return ch_t.tok() if inc else None

    def vop(fn, waits=(), inc=True):
        em.emit("v", fn, waits, ((ch_v, 1),) if inc else ())
        return ch_v.tok() if inc else None

    def sop(fn, waits=(), inc=True):
        em.emit("s", fn, waits, ((ch_s, 1),) if inc else ())
        return ch_s.tok() if inc else None

    identb = sbuf("identb", [128, 128], BF16)
    identF = sbuf("identF", [64, 64], F32)
    ltrib = sbuf("ltrib", [128, 128], F32)

    onesF = sbuf("onesF", [1, 128], F32)
    onescolb = sbuf("onescolb", [128, 1], BF16)
    onescolF = sbuf("onescolF", [128, 1], F32)
    nwcol = sbuf("nwcol", [128, 8], F32)
    pwcol = sbuf("pwcol", [128, 8], F32)
    swcol = sbuf("swcol", [128, 16], F32)
    cbcol = sbuf("cbcol", [128, 18], F32)
    dtbcol = sbuf("dtbcol", [128, 1], F32)
    nAcol = sbuf("nAcol", [128, 1], F32)
    uinitcol = sbuf("uinitcol", [128, 1], F32)
    uinvcol = sbuf("uinvcol", [128, 1], F32)
    mean_r = sbuf("mean_r", [1, TH], F32)
    rc_r = sbuf("rc_r", [1, TH], F32)
    us_r = sbuf("us_r", [1, TH], F32)
    pL0 = contextlib.ExitStack()
    UALL = sbuf("UALL", [128, DC * TH], BF16, pL0)
    CFM = sbuf("CFM", [128, T], BF16, pL0)
    hrA = sbuf("hrA", [32, T], F32, pL0)
    hrB = sbuf("hrB", [32, T], F32, pL0)
    hrD = sbuf("hrD", [32, T], F32, pL0)
    cgT = sbuf("cgT", [32, 16], F32, pL0)
    cgrep = sbuf("cgrep", [1, 512], F32, pL0)
    decCols = sbuf("decCols", [128, 512], F32, pL0)
    dT = sbuf("dT", [128, NCH * 32], F32, pL0)
    wAll = sbuf("wAll", [128, NCH * 32], F32, pL0)
    P1B = sbuf("P1B", [128, NCH * 128], F32, pL0)
    Scur = sbuf("Scur", [128, 128], F32, pL0)
    Sinit = sbuf("Sinit", [128, DIN], F32, pL0)
    WIPR = sbuf("WIPR", [128, 8 * 128], BF16, pL0)
    p1s2 = contextlib.ExitStack()
    var_r = sbuf("var_r", [1, TH], F32, p1s2)
    scr_r = sbuf("scr_r", [1, TH], F32, p1s2)
    BFM = sbuf("BFM", [128, T], BF16, p1s2)
    BTM = sbuf("BTM", [128, NCH * 128], BF16, p1s2)
    hrC = sbuf("hrC", [32, T], F32, p1s2)
    XBCR = sbuf("XBCR", [128, 2 * TH], BF16, p1s2)
    CDG = sbuf("CDG", [128, 2 * 512], BF16, p1s2)
    decrow = sbuf("decrow", [1, 512], F32, p1s2)
    cgeCols = sbuf("cgeCols", [128, 512], F32, p1s2)
    XFR = sbuf("XFR", [128, TH], BF16, p1s2)
    XCV = sbuf("XCV", [128, T], BF16, p1s2)
    XTMp = sbuf("XTMp", [128, 2048], BF16, p1s2)
    BDR = sbuf("BDR", [128, 2 * 128], BF16, p1s2)
    SEV = sbuf("SEV", [128, 2 * 128], F32, p1s2)

    psA = stack.enter_context(nc.psum_tensor("psA", [128, 2048], F32))
    psB = stack.enter_context(nc.psum_tensor("psB", [128, 2048], F32))

    def bank(i, rows_=128, cols=512, roff=0):
        src = psA if i < 4 else psB
        return src[roff:roff + rows_, (i % 4) * 512:(i % 4) * 512 + cols]

    def bankb(i, rows_=128, cols=128):
        src = psA if i < 4 else psB
        return src[0:rows_, (i % 4) * 512:(i % 4) * 512 + 256].bitcast(BF16)[:, 0:cols]

    bank_rd = {}
    bank_wr = {}

    gdma(identF[:, :], ident_in[0:64, 0:64], "consts")
    for dst, src in ((identb, ident_in), (ltrib, ltri_in), (nwcol, norm_w),
                     (pwcol, post_w), (swcol, ssm_w), (cbcol, conv_b),
                     (dtbcol, dt_bias), (nAcol, A_log), (uinitcol, uinit_in),
                     (uinvcol, uinv_in)):
        gdma(dst[:, :], src[:, :], "consts")
    consts = dch("consts").tok()
    vop(lambda e: e.memset(onesF[:, :], 1.0))
    vop(lambda e: e.memset(onescolb[:, :], 1.0))
    vop(lambda e: e.memset(onescolF[:, :], 1.0))
    ones_tok = ch_v.tok()
    sop(lambda e: e.activation(nAcol[:, :], nAcol[:, :], AF.Exp), waits=(consts,))
    nA_tok = vop(lambda e: e.tensor_scalar(nAcol[:, :], nAcol[:, :], -1.0, None, OP.mult),
                 waits=(ch_s.tok(),))

    p0s = contextlib.ExitStack()
    X0F = sbuf("X0F", [128, DC * 512], F32, p0s)
    X0B = sbuf("X0B", [128, DC * 512], BF16, p0s)
    SQB = sbuf("SQB", [128, DC * 512], BF16, p0s)
    UTMP = sbuf("UTMP", [128, 512], F32, p0s)
    x0_free = [None, None]
    u_tok = [None] * 5
    prev_stats = None
    for g in range(5):
        go, gw = NS_OFF[g], NS_W[g]
        sl = 0
        x0tok = gdma(X0F[:, sl:sl + DC * gw],
                     x_in.rearrange("(c p) t -> p c t", p=128)[:, :, go:go + gw],
                     "x0f", waits=(x0_free[0], x0_free[1]))
        xbtok = vop(lambda e, gw=gw, sl=sl: e.tensor_copy(
            X0B[:, :DC * gw], X0F[:, sl:sl + DC * gw]), waits=(x0tok, prev_stats))
        sqtok = sop(lambda e, gw=gw, sl=sl: e.activation(
            SQB[:, :DC * gw], X0F[:, sl:sl + DC * gw], AF.Square),
            waits=(x0tok, prev_stats))
        b0, b1v = bank(0, 1, gw), bank(1, 1, gw)
        for c in range(DC):
            w = [xbtok, ones_tok, bank_rd.get(0)] if c == 0 else []
            mm(b0, onescolb[:, :], X0B[:, c * gw:(c + 1) * gw],
               start=(c == 0), stop=(c == DC - 1), waits=tuple(w))
        for c in range(DC):
            w = [sqtok, bank_rd.get(1)] if c == 0 else []
            stats_tok = mm(b1v, onescolb[:, :], SQB[:, c * gw:(c + 1) * gw],
                           start=(c == 0), stop=(c == DC - 1), waits=tuple(w),
                           inc=(c == DC - 1))
        vop(lambda e, go=go, gw=gw, b0=b0: e.tensor_scalar(
            mean_r[0:1, go:go + gw], b0, 1.0 / D, None, OP.mult), waits=(stats_tok,))
        vop(lambda e, go=go, gw=gw, b1v=b1v: e.tensor_scalar(
            scr_r[0:1, go:go + gw], b1v, 1.0 / D, None, OP.mult))
        bank_rd[0] = bank_rd[1] = ch_v.tok()
        stats_rows_tok = ch_v.tok()
        vop(lambda e, go=go, gw=gw: e.tensor_mul(
            var_r[0:1, go:go + gw], mean_r[0:1, go:go + gw], mean_r[0:1, go:go + gw]))
        vop(lambda e, go=go, gw=gw: e.tensor_sub(
            var_r[0:1, go:go + gw], scr_r[0:1, go:go + gw], var_r[0:1, go:go + gw]))
        var_tok = vop(lambda e, go=go, gw=gw: e.tensor_scalar(
            var_r[0:1, go:go + gw], var_r[0:1, go:go + gw], 0.0, None, OP.max))
        vop(lambda e, go=go, gw=gw: e.tensor_scalar(
            rc_r[0:1, go:go + gw], var_r[0:1, go:go + gw], 0.0, 1e-5, OP.max, OP.add),
            waits=(var_tok,))
        sop(lambda e, go=go, gw=gw: e.activation(
            rc_r[0:1, go:go + gw], rc_r[0:1, go:go + gw], AF.Ln),
            waits=(ch_v.tok(),))
        rc_tok = sop(lambda e, go=go, gw=gw: e.activation(
            rc_r[0:1, go:go + gw], rc_r[0:1, go:go + gw], AF.Exp, scale=-0.5))
        vop(lambda e, go=go, gw=gw: e.tensor_mul(
            scr_r[0:1, go:go + gw], var_r[0:1, go:go + gw], rc_r[0:1, go:go + gw]),
            waits=(rc_tok,))
        q_tok = vop(lambda e, go=go, gw=gw: e.tensor_mul(
            scr_r[0:1, go:go + gw], scr_r[0:1, go:go + gw], rc_r[0:1, go:go + gw]))
        vop(lambda e, go=go, gw=gw: e.tensor_scalar(
            scr_r[0:1, go:go + gw], scr_r[0:1, go:go + gw], 1e-6, None, OP.add),
            waits=(q_tok,))
        sop(lambda e, go=go, gw=gw: e.activation(
            scr_r[0:1, go:go + gw], scr_r[0:1, go:go + gw], AF.Ln),
            waits=(ch_v.tok(),))
        e_tok = sop(lambda e, go=go, gw=gw: e.activation(
            scr_r[0:1, go:go + gw], scr_r[0:1, go:go + gw], AF.Exp, scale=-0.5))
        us_tok = vop(lambda e, go=go, gw=gw: e.tensor_mul(
            us_r[0:1, go:go + gw], scr_r[0:1, go:go + gw], rc_r[0:1, go:go + gw]),
            waits=(e_tok,))
        b2v, b3v = bank(2, 128, gw), bank(3, 128, gw)
        w = [ones_tok, bank_rd.get(2), stats_rows_tok]
        mm(b2v, onesF[0:1, :], mean_r[0:1, go:go + gw], start=True, stop=True,
           waits=tuple(w))
        w = [us_tok, bank_rd.get(3)]
        bc_tok = mm(b3v, onesF[0:1, :], us_r[0:1, go:go + gw], start=True, stop=True,
                    waits=tuple(w), inc=True)
        for c in range(DC):
            vop(lambda e, c=c, gw=gw, sl=sl, b2v=b2v: e.tensor_sub(
                UTMP[:, 0:gw], X0F[:, sl + c * gw: sl + (c + 1) * gw], b2v),
                waits=(bc_tok,) if c == 0 else ())
            vop(lambda e, c=c, go=go, gw=gw, b3v=b3v: e.scalar_tensor_tensor(
                UALL[:, c * TH + go: c * TH + go + gw],
                UTMP[:, 0:gw], nwcol[:, c:c + 1], b3v, OP.mult, OP.mult),
                waits=(consts,) if c == 0 else ())
        u_tok[g] = ch_v.tok()
        prev_stats = stats_tok
        bank_rd[2] = bank_rd[3] = ch_v.tok()
        x0_free[0] = ch_v.tok()
        x0_free[1] = ch_s.tok()
    u_all = ch_v.tok()
    p0s.close()
    for i in range(8):
        bank_rd[i] = None
    bank_rd[0] = bank_rd[1] = bank_rd[2] = bank_rd[3] = u_all

    if stop_after == "P0":
        DUMPS = sbuf("DUMPS", [128, 64], F32, p1s2)
        vop(lambda e: e.tensor_copy(DUMPS[0:1, 0:3], psA[0:1, 0:3]), waits=(u_all,))
        vop(lambda e: e.tensor_copy(DUMPS[0:1, 4:7], psA[0:1, 512:515]))
        vop(lambda e: e.tensor_copy(DUMPS[:, 8:32], X0B[:, 0:24]))
        vop(lambda e: e.tensor_copy(DUMPS[:, 32:56], SQB[:, 0:24]))
        dtk = vop(lambda e: e.tensor_copy(DUMPS[:, 56:57], onescolb[:, :]))
        gdma(y_out[0:1, :], mean_r[0:1, 3:2051], "dump", waits=(u_all,))
        gdma(y_out[1:2, :], us_r[0:1, 3:2051], "dump")
        gdma(y_out[2:3, :], rc_r[0:1, 3:2051], "dump")
        gdma(y_out[3:4, :], var_r[0:1, 3:2051], "dump")
        gdma(y_out[4:5, :], scr_r[0:1, 3:2051], "dump")
        gdma(y_out[576:704, :], UALL[:, 3:2051], "dump")
        gdma(y_out[704:832, :], UALL[:, 7 * TH + 3:7 * TH + 2051], "dump")
        d9 = gdma(y_out[320:448, 0:64], DUMPS[:, :], "dump", waits=(dtk,))
        p1s2.close()
        pL0.close()
        _finish(em, nc, stack, y_out, dbg, dbg_out, locals(), d9, ydma, ch_v, ch_s, ch_t)
        return nc

    wip_state = {"free": [u_all] * 8, "idx": 0}

    def inproj_mtile(mt, evac, slices=range(5), bksel=(4, 5, 6, 7, 0), dmaslot=1):
        toks = []
        for k in range(8):
            sl_i = wip_state["idx"] % 8
            wip_state["idx"] += 1
            wtok = gdma(WIPR[:, sl_i * 128:(sl_i + 1) * 128], W_ip[mt, k, :, :],
                        f"wipr{sl_i}", waits=(wip_state["free"][sl_i],))
            toks.append((sl_i, wtok))
        for ki, (sl_i, wtok) in enumerate(toks):
            for ng in slices:
                no, nw = NS_OFF[ng], NS_W[ng]
                bi = bksel[ng if len(bksel) == 5 else list(slices).index(ng)]
                w = [wtok]
                if ki == 0:
                    w.append(u_all)
                    w.append(bank_rd.get(bi))
                t = mm(bank(bi, 128, nw), WIPR[:, sl_i * 128:(sl_i + 1) * 128],
                       UALL[:, ki * TH + no: ki * TH + no + nw],
                       start=(ki == 0), stop=(ki == 7),
                       waits=tuple(w), inc=(ki == 7 or ng == list(slices)[-1]))
                if ki == 7:
                    bank_wr[bi] = t
            wip_state["free"][sl_i] = ch_t.tok()
        for ng in slices:
            bi = bksel[ng if len(bksel) == 5 else list(slices).index(ng)]
            bank_rd[bi] = evac(ng, bank(bi, 128, NS_W[ng]), bank_wr[bi])

    cdg_free = [u_all, u_all]
    xbc_free = [None, None]

    def evac_to_ring(ring_slot):
        def evac(ng, bv, wr):
            no, nw = NS_OFF[ng], NS_W[ng]
            return vop(lambda e, no=no, nw=nw, bv=bv, rs=ring_slot: e.tensor_copy(
                XBCR[:, rs * TH + no: rs * TH + no + nw], bv),
                waits=(wr, xbc_free[ring_slot]) if ng == 0 else (wr,))
        return evac

    def emit_conv(cchunk, ring_slot, out_fn, bks=(1, 2)):
        dslot = cchunk % 2
        dtok = gdma(CDG[:, dslot * 512:(dslot + 1) * 512],
                    conv_diag.rearrange("c t p m -> c p t m")[cchunk, :, :, :],
                    f"cdg{dslot}", waits=(cdg_free[dslot],))
        xtok = ch_v.tok()
        last = None
        for ng in range(4):
            no = ng * 512
            bi = bks[ng % 2]
            cb = bank(bi, 128, 512)
            for tap in range(4):
                w = [dtok, xtok, bank_rd.get(bi)] if tap == 0 else []
                t = mm(cb, CDG[:, dslot * 512 + tap * 128: dslot * 512 + (tap + 1) * 128],
                       XBCR[:, ring_slot * TH + no + tap: ring_slot * TH + no + tap + 512],
                       start=(tap == 0), stop=(tap == 3), waits=tuple(w), inc=(tap == 3))
            bank_wr[bi] = t
            last = out_fn(ng, cb, no, 512, t)
            bank_rd[bi] = last
        cdg_free[dslot] = last
        xbc_free[ring_slot] = last
        return last

    for bc in range(2):
        mt = 32 + bc
        inproj_mtile(mt, evac_to_ring(bc % 2))
        xbc_tok = ch_v.tok()

        def bc_out(ng, cb, no, nw, wr, bc=bc):
            dst = BFM if bc == 0 else CFM
            return sop(lambda e, cb=cb, no=no, nw=nw, dst=dst, bc=bc: e.activation(
                dst[:, no: no + nw], cb, AF.Silu,
                bias=cbcol[:, 16 + bc:17 + bc]), waits=(wr,))
        emit_conv(16 + bc, bc % 2, bc_out)
    bc_done = ch_s.tok()
    def evac_dt(ng, bv, wr):
        no, nw = NS_OFF[ng], NS_W[ng]
        if ng == 0:
            return vop(lambda e, bv=bv: e.tensor_copy(hrA[0:32, 0:509], bv[0:32, 3:512]),
                       waits=(wr,))
        return vop(lambda e, bv=bv, no=no, nw=nw: e.tensor_copy(
            hrA[0:32, no - 3: no - 3 + nw], bv[0:32, 0:nw]), waits=(wr,))
    inproj_mtile(34, evac_dt)
    dtraw_tok = ch_v.tok()
    for tc in range(NCH):
        tb = bankb(7, 128, 128)
        w = [bc_done, consts, bank_rd.get(7)]
        t = tr(tb, BFM[:, tc * 128:(tc + 1) * 128], identb[:, :], waits=tuple(w))
        bank_rd[7] = vop(lambda e, tc=tc, tb=tb: e.tensor_copy(
            BTM[:, tc * 128:(tc + 1) * 128], tb), waits=(t,))
    btm_tok = ch_v.tok()

    vop(lambda e: e.tensor_scalar(hrA[:, :], hrA[:, :], dtbcol[0:32, :], None, OP.add),
        waits=(dtraw_tok, consts))
    dtb_tok = ch_v.tok()
    sop(lambda e: e.activation(hrC[:, :], hrA[:, :], AF.Abs), waits=(dtb_tok,))
    sop(lambda e: e.activation(hrC[:, :], hrC[:, :], AF.Exp, scale=-1.0))
    vop(lambda e: e.tensor_scalar(hrC[:, :], hrC[:, :], 1.0, None, OP.add),
        waits=(ch_s.tok(),))
    ln_tok = sop(lambda e: e.activation(hrC[:, :], hrC[:, :], AF.Ln),
                 waits=(ch_v.tok(),))
    vop(lambda e: e.tensor_relu(hrB[:, :], hrA[:, :]))
    dt_tok = vop(lambda e: e.tensor_add(hrB[:, :], hrB[:, :], hrC[:, :]), waits=(ln_tok,))
    dtA_tok = vop(lambda e: e.tensor_scalar(hrC[:, :], hrB[:, :], nAcol[0:32, :],
                                            None, OP.mult), waits=(dt_tok, nA_tok))
    lndt_tok = sop(lambda e: e.activation(hrB[:, :], hrB[:, :], AF.Ln), waits=(dtA_tok,))
    zr_tok = vop(lambda e: e.memset(hrA[:, :], 0.0))
    cg_tok = vop(lambda e: e.tensor_tensor_scan(
        hrD[:, :], hrC[:, :], hrA[:, :], 0.0, OP.add, OP.add), waits=(zr_tok,))
    d_tok = vop(lambda e: e.tensor_sub(hrB[:, :], hrB[:, :], hrD[:, :]),
                waits=(cg_tok, lndt_tok))

    cgT_tok = vop(lambda e: e.tensor_copy(
        cgT[:, :], hrD.rearrange("h (k q) -> h k q", q=Q)[:, :, Q - 1]), waits=(cg_tok,))
    cgrep_tok = gdma(cgrep[:, :], cgT[:, :], "rows0", waits=(cgT_tok,))
    vop(lambda e: e.memset(decrow[:, :], 0.0))
    z_tok = ch_v.tok()
    vop(lambda e: e.tensor_copy(
        decrow.rearrange("o (h k) -> o h k", k=16)[:, :, 1:16],
        cgrep.rearrange("o (h k) -> o h k", k=16)[:, :, 0:15]),
        waits=(cgrep_tok, z_tok))
    ddiff_tok = vop(lambda e: e.tensor_sub(decrow[:, :], cgrep[:, :], decrow[:, :]))
    decrow_tok = sop(lambda e: e.activation(decrow[:, :], decrow[:, :], AF.Exp),
                     waits=(ddiff_tok,))
    w = [decrow_tok, ones_tok, bank_rd.get(0)]
    t = mm(bank(0, 128, 512), onesF[:, :], decrow[:, :], start=True, stop=True,
           waits=tuple(w), inc=True)
    dec_tok = vop(lambda e: e.tensor_copy(decCols[:, :], bank(0, 128, 512)), waits=(t,))
    bank_rd[0] = dec_tok
    w = [cgrep_tok, ones_tok, bank_rd.get(1)]
    t = mm(bank(1, 128, 512), onesF[:, :], cgrep[:, :], start=True, stop=True,
           waits=tuple(w), inc=True)
    cge_tok = vop(lambda e: e.tensor_copy(cgeCols[:, :], bank(1, 128, 512)), waits=(t,))
    bank_rd[1] = cge_tok
    for k in range(NCH):
        tb = bank(2, 128, 32)
        w = [d_tok, consts, bank_rd.get(2)]
        t = tr(tb, hrB[0:32, k * Q:(k + 1) * Q], identF[0:32, 0:32], waits=tuple(w))
        bank_rd[2] = vop(lambda e, k=k, tb=tb: e.tensor_copy(
            dT[:, k * 32:(k + 1) * 32], tb), waits=(t,))
    dT_tok = ch_v.tok()
    for k in range(NCH):
        vop(lambda e, k=k: e.tensor_add(
            wAll[:, k * 32:(k + 1) * 32], dT[:, k * 32:(k + 1) * 32],
            cgeCols[:, k::16]), waits=(dT_tok, cge_tok) if k == 0 else ())
        sop(lambda e, k=k: e.activation(
            wAll[:, k * 32:(k + 1) * 32], wAll[:, k * 32:(k + 1) * 32], AF.Exp),
            waits=(ch_v.tok(),))
    wAll_tok = ch_s.tok()
    for k in range(NCH):
        if k == 0:
            vop(lambda e: e.tensor_copy(hrA[:, 0:Q], hrD[:, 0:Q]), waits=(cg_tok,))
        else:
            vop(lambda e, k=k: e.tensor_scalar(
                hrA[:, k * Q:(k + 1) * Q], hrD[:, k * Q:(k + 1) * Q],
                cgT[:, k - 1:k], None, OP.subtract),
                waits=(cgT_tok,) if k == 1 else ())
    edec_tok = sop(lambda e: e.activation(hrA[:, :], hrA[:, :], AF.Exp),
                   waits=(ch_v.tok(),))
    for k in range(NCH):
        bv = bank(3, 128, 128)
        w = [bc_done, bank_rd.get(3)]
        t = mm(bv, BFM[:, k * 128:(k + 1) * 128], CFM[:, k * 128:(k + 1) * 128],
               start=True, stop=True, waits=tuple(w), inc=True)
        bank_rd[3] = vop(lambda e, k=k, bv=bv: e.tensor_mul(
            P1B[:, k * 128:(k + 1) * 128], bv, ltrib[:, :]), waits=(t, consts))
    p1b_tok = ch_v.tok()

    if stop_after == "P2":
        p1s2.close()
        pL0.close()
        _finish(em, nc, stack, y_out, dbg, dbg_out, locals(),
                p1b_tok, ydma, ch_v, ch_s, ch_t)
        return nc

    xfr_free = None
    xcv_free = None
    xtmp_free = None
    sc_zero = None
    for pair in range(16):
        def evac_x(ng, bv, wr, _ff=None):
            no, nw = NS_OFF[ng], NS_W[ng]
            w = [wr] if ng else [wr, xfr_free]
            return sop(lambda e, no=no, nw=nw, bv=bv: e.activation(
                XFR[:, no:no + nw], bv, AF.Identity), waits=tuple(x for x in w if x))
        inproj_mtile(16 + pair, evac_x)
        xf_tok = ch_s.tok()
        def x_out(ng, cb, no, nw, wr, pair=pair):
            w = [wr] if ng else [wr, xcv_free]
            return sop(lambda e, cb=cb, no=no, nw=nw, pair=pair: e.activation(
                XCV[:, no:no + nw], cb, AF.Silu, bias=cbcol[:, pair:pair + 1]),
                waits=tuple(x for x in w if x))
        dslot = pair % 2
        dtok = gdma(CDG[:, dslot * 512:(dslot + 1) * 512],
                    conv_diag.rearrange("c t p m -> c p t m")[pair, :, :, :],
                    f"cdg{dslot}", waits=(cdg_free[dslot],))
        last = None
        for ng in range(4):
            no = ng * 512
            bi = (1, 2)[ng % 2]
            cb = bank(bi, 128, 512)
            for tap in range(4):
                w = [dtok, xf_tok, bank_rd.get(bi)] if tap == 0 else []
                t = mm(cb, CDG[:, dslot * 512 + tap * 128: dslot * 512 + (tap + 1) * 128],
                       XFR[:, no + tap: no + tap + 512],
                       start=(tap == 0), stop=(tap == 3), waits=tuple(w), inc=(tap == 3))
            last = x_out(ng, cb, no, 512, t)
            bank_rd[bi] = last
        cdg_free[dslot] = last
        xfr_free = last
        silu_tok = ch_s.tok()
        for tc in range(NCH):
            for hh in range(2):
                tb = bankb(3, 128, 64)
                w = [silu_tok, consts, bank_rd.get(3)]
                if tc == 0 and hh == 0 and xtmp_free is not None:
                    w.extend(xtmp_free)
                t = tr(tb, XCV[hh * 64:(hh + 1) * 64, tc * 128:(tc + 1) * 128],
                       identb[hh * 64:(hh + 1) * 64, hh * 64:(hh + 1) * 64],
                       waits=tuple(x for x in w if x))
                bank_rd[3] = vop(lambda e, tc=tc, hh=hh, tb=tb: e.tensor_copy(
                    XTMp[:, (tc * 2 + hh) * 64:(tc * 2 + hh) * 64 + 64], tb), waits=(t,))
        xtm_tok = ch_v.tok()
        xcv_free = xtm_tok
        ydma(xtm_d[pair, :, :], XTMp[:, :], waits=(xtm_tok,))
        xtm_st = ldy.tok()
        vop(lambda e: e.memset(Scur[:, :], 0.0), waits=(sc_zero,))
        sc_tok = ch_v.tok()
        sev_free = [None, None]
        for k in range(NCH):
            bd_toks = []
            for hh in range(2):
                h = pair * 2 + hh
                w = [btm_tok, wAll_tok]
                if bank_rd.get(90 + hh):
                    w.append(bank_rd[90 + hh])
                bd_toks.append(vop(lambda e, k=k, hh=hh, h=h: e.tensor_scalar(
                    BDR[:, hh * 128:(hh + 1) * 128], BTM[:, k * 128:(k + 1) * 128],
                    wAll[:, k * 32 + h: k * 32 + h + 1], None, OP.mult),
                    waits=tuple(w)))
            bi = 4 + (k % 2)
            bv = bank(bi, 128, 128)
            for hh in range(2):
                w = [bd_toks[hh], xtm_tok]
                if hh == 0 and bank_rd.get(bi):
                    w.append(bank_rd[bi])
                t = mm(bv[:, hh * 64:(hh + 1) * 64], BDR[:, hh * 128:(hh + 1) * 128],
                       XTMp[:, (k * 2 + hh) * 64:(k * 2 + hh) * 64 + 64],
                       start=True, stop=True, waits=tuple(w), inc=(hh == 1))
            bank_wr[bi] = t
            bank_rd[90] = bank_rd[91] = t
            es = k % 2
            ev = vop(lambda e, bv=bv, es=es: e.tensor_copy(
                SEV[:, es * 128:(es + 1) * 128], bv), waits=(t, sev_free[es]))
            bank_rd[bi] = ev
            ydma(schunk_d[k, :, pair * 128:(pair + 1) * 128],
                 SEV[:, es * 128:(es + 1) * 128], waits=(ev,))
            sev_free[es] = ldy.tok()
            for hh in range(2):
                h = pair * 2 + hh
                vop(lambda e, hh=hh, h=h, k=k: e.tensor_scalar(
                    Scur[:, hh * 64:(hh + 1) * 64], Scur[:, hh * 64:(hh + 1) * 64],
                    decCols[:, h * 16 + k: h * 16 + k + 1], None, OP.mult),
                    waits=(sc_tok, dec_tok) if k == 0 and hh == 0 else ())
            vop(lambda e, es=es: e.tensor_add(
                Scur[:, :], Scur[:, :], SEV[:, es * 128:(es + 1) * 128]), waits=(ev,))
            sc_tok = ch_v.tok()
        ydma(sAG_in[:, pair * 128:(pair + 1) * 128], Scur[:, :], waits=(sc_tok,))
        sc_zero = ldy.tok()
        xtmp_free = (xtm_st, ch_t.tok())
    sA_done = ldy.tok()

    em.emit("g", lambda e: e.collective_compute(
        "AllGather", OP.bypass, replica_groups=GROUPS,
        ins=[sAG_in[:, :]], outs=[sAG_out[:, :]]),
        (sA_done,), ((cc, 1),))
    ag1 = cc.tok()
    si_ld = gdma(Sinit[:, :], sAG_out[0:128, :], "sinit", waits=(ag1,))
    si_tok = vop(lambda e: e.tensor_scalar(Sinit[:, :], Sinit[:, :], uinitcol[:, :],
                                           None, OP.mult), waits=(si_ld, consts))

    if stop_after == "A":
        d1 = ydma(y_out[0:128, :], Sinit[:, :], waits=(si_tok,))
        d2 = ydma(y_out[128:160, :], hrD[:, :], waits=(cg_tok,))
        d3 = ydma(y_out[160:192, :], hrB[:, :], waits=(d_tok,))
        d4 = ydma(y_out[192:320, :], schunk_d[5, :, :], waits=(sA_done,))
        d5 = gdma(y_out[320:448, :], xtm_d[3, :, :], "dump", waits=(sA_done,))
        d6 = ydma(y_out[448:576, :], sAG_in[:, :], waits=(sA_done,))
        d7 = gdma(y_out[576:704, :], UALL[:, 3:2051], "dump", waits=(u_all,))
        d8 = gdma(y_out[704:832, :], BFM[:, :], "dump", waits=(bc_done,))
        d9 = ydma(y_out[832:864, :], hrA[:, :], waits=(dtraw_tok,))
        p1s2.close()
        pL0.close()
        _finish(em, nc, stack, y_out, dbg, dbg_out, locals(), d6, ydma, ch_v, ch_s, ch_t)
        return nc

    p1s2.close()
    p3s = contextlib.ExitStack()
    SLOTS = sbuf("SLOTS", [2, 6 * 1024], F32, p3s)
    MHALF = sbuf("MHALF", [128, 16 * 1024], BF16, p3s)
    SZR = sbuf("SZR", [128, 2 * 1024], BF16, p3s)
    XTMR = sbuf("XTMR", [128, 2 * 2048], BF16, p3s)
    SCH = sbuf("SCH", [128, 2 * 128], F32, p3s)
    SCARRY = sbuf("SCARRY", [128, 16 * 128], F32, p3s)
    E2R = sbuf("E2R", [128, 2 * 256], F32, p3s)
    SCB = sbuf("SCB", [128, 2 * 128], BF16, p3s)
    GTR = sbuf("GTR", [128, 2 * 256], BF16, p3s)
    CHR = sbuf("CHR", [128, 2 * 128], BF16, p3s)
    DDG = sbuf("DDG", [128, NH * 128], BF16, p3s)
    ddg_toks = []
    for h in range(NH):
        ddg_toks.append(gdma(DDG[:, h * 128:(h + 1) * 128], D_diag_in[h, :, :], "ddg",
                             waits=(sA_done,) if h == 0 else ()))
    ddg_tok = dch("ddg").tok()
    for r in range(2):
        vop(lambda e, r=r: e.memset(SLOTS[0:2, r * 3072: r * 3072 + 2048], 1.0))
    slots_ones = ch_v.tok()
    slot_free = [sA_done] * 2
    szr_free = [None, None]
    e2_free = [None, None]
    xtmr_free = [None, None]
    sch_free = [None, None]
    scb_free = [None, None]
    gt_free = [None, None]
    chr_free = [None, None]
    mh_guard = None

    SQH = sbuf("SQH", [128, 512], BF16, p3s)
    X0R = sbuf("X0R", [128, 2 * 512], F32, p3s)
    WSC = sbuf("WSC", [128, 2 * 128], F32, p3s)
    for half in range(2):
        k0 = half * 8
        for pair in range(16):
            sl_toks = []
            rr = [0, 0]
            for hh in range(2):
                h = pair * 2 + hh
                r = hh
                rr[hh] = r
                hc = slice(k0 * Q, k0 * Q + 1024)
                w = (slot_free[r], slots_ones, d_tok, edec_tok)
                gdma(SLOTS[1:2, r * 3072: r * 3072 + 1024], hrB[h:h + 1, hc], f"slr{r}", waits=w)
                gdma(SLOTS[0:1, r * 3072 + 1024: r * 3072 + 2048], hrD[h:h + 1, hc], f"slr{r}")
                sl_toks.append(gdma(SLOTS[0:1, r * 3072 + 2048: r * 3072 + 3072],
                                    hrA[h:h + 1, hc], f"slr{r}"))
            xr = pair % 2
            xt_tok = gdma(XTMR[:, xr * 2048:(xr + 1) * 2048], xtm_d[pair, :, :], f"xtmr{xr}",
                          waits=(sA_done, xtmr_free[xr]))
            zoff = k0 * Q + HALO
            szr = pair % 2
            sz_tok = _emit_z(em, mm, bank, sop, vop, gdma, W_ip, WIPR, wip_state, UALL,
                             u_all, bank_rd, bank_wr, SZR, szr, pair, zoff,
                             szr_free[szr], ch_t, ch_s)
            scur_init = None
            if half == 0:
                scur_init = vop(lambda e, pair=pair: e.tensor_copy(
                    SCARRY[:, pair * 128:(pair + 1) * 128], Sinit[:, pair * 128:(pair + 1) * 128]),
                    waits=(si_tok,))
            sc_tok2 = scur_init or ch_v.tok()
            for kk in range(8):
                k = k0 + kk
                sr = kk % 2
                sch_tok = gdma(SCH[:, sr * 128:(sr + 1) * 128],
                               schunk_d[k, :, pair * 128:(pair + 1) * 128], f"sch{sr}",
                               waits=(sA_done, sch_free[sr]))
                psT = bank(0, 128, 256)
                for hh in range(2):
                    h = pair * 2 + hh
                    r = rr[hh]
                    w = [sl_toks[hh], bank_rd.get(0)] if hh == 0 else [sl_toks[hh]]
                    t = mm(psT[:, hh * 128:(hh + 1) * 128],
                           SLOTS[0:2, r * 3072 + kk * Q: r * 3072 + (kk + 1) * Q],
                           SLOTS[0:2, r * 3072 + 1024 + kk * Q: r * 3072 + 1024 + (kk + 1) * Q],
                           start=True, stop=True, waits=tuple(w), inc=(hh == 1))
                bank_wr[0] = t
                er = kk % 2
                clamp_tok = vop(lambda e, psT=psT: e.tensor_scalar(
                    psT, psT, 80.0, None, OP.min), waits=(t,))
                e2_tok = sop(lambda e, er=er, psT=psT: e.activation(
                    E2R[:, er * 256:(er + 1) * 256], psT, AF.Exp),
                    waits=(clamp_tok, e2_free[er]))
                bank_rd[0] = e2_tok
                g_toks = []
                for hh in range(2):
                    g_toks.append(vop(lambda e, er=er, hh=hh, k=k: e.scalar_tensor_tensor(
                        GTR[:, er * 256 + hh * 128: er * 256 + (hh + 1) * 128],
                        E2R[:, er * 256 + hh * 128: er * 256 + (hh + 1) * 128],
                        1e30, P1B[:, k * 128:(k + 1) * 128], OP.min, OP.mult),
                        waits=(e2_tok, p1b_tok, gt_free[er])))
                e2_free[er] = g_toks[1]
                ebv = bank(1, 128, 128)
                w = [sl_toks[0], sl_toks[1], ones_tok, bank_rd.get(1), chr_free[0]]
                t = mm(ebv, onesF[:, :],
                       SLOTS[0:1, rr[0] * 3072 + 2048 + kk * Q: rr[0] * 3072 + 2048 + (kk + 1) * Q],
                       start=True, stop=True, waits=tuple(x for x in w if x), inc=True)
                ch_tokA = vop(lambda e, ebv=ebv, k=k: e.tensor_mul(
                    CHR[:, 0:128], CFM[:, k * 128:(k + 1) * 128], ebv),
                    waits=(t, bc_done))
                w2 = [ones_tok]
                t2 = mm(ebv, onesF[:, :],
                        SLOTS[0:1, rr[1] * 3072 + 2048 + kk * Q: rr[1] * 3072 + 2048 + (kk + 1) * Q],
                        start=True, stop=True, waits=(ch_tokA, chr_free[1]), inc=True)
                ch_tokB = vop(lambda e, ebv=ebv, k=k: e.tensor_mul(
                    CHR[:, 128:256], CFM[:, k * 128:(k + 1) * 128], ebv),
                    waits=(t2, bc_done))
                bank_rd[1] = ch_tokB
                ch_toks = [ch_tokA, ch_tokB]
                scb_tok = vop(lambda e, pair=pair, sr=sr: e.tensor_copy(
                    SCB[:, sr * 128:(sr + 1) * 128], SCARRY[:, pair * 128:(pair + 1) * 128]),
                    waits=(sc_tok2, scb_free[sr]))
                ybi = 2 + (kk // 4)
                yb = bank(ybi, 128, 512)
                for hh in range(2):
                    h = pair * 2 + hh
                    tp = (0, 64 * hh)
                    w = [g_toks[hh], xt_tok]
                    if kk % 4 == 0 and hh == 0:
                        w.append(bank_rd.get(ybi))
                    mm(yb[64 * hh: 64 * hh + 64, (kk % 4) * 128:(kk % 4) * 128 + 128],
                       XTMR[:, xr * 2048 + (k - 0) * 128 + hh * 64: xr * 2048 + k * 128 + hh * 64 + 64]
                       if False else
                       XTMR[:, xr * 2048 + (k * 2 + hh) * 64: xr * 2048 + (k * 2 + hh) * 64 + 64],
                       GTR[:, er * 256 + hh * 128: er * 256 + (hh + 1) * 128],
                       start=True, stop=False, waits=tuple(w), tp=tp)
                    mm(yb[64 * hh: 64 * hh + 64, (kk % 4) * 128:(kk % 4) * 128 + 128],
                       SCB[:, sr * 128 + hh * 64: sr * 128 + hh * 64 + 64],
                       CHR[:, hh * 128:(hh + 1) * 128],
                       start=False, stop=False, waits=(ch_toks[hh], scb_tok), tp=tp)
                    t = mm(yb[64 * hh: 64 * hh + 64, (kk % 4) * 128:(kk % 4) * 128 + 128],
                           XTMR[:, xr * 2048 + (k * 2 + hh) * 64: xr * 2048 + (k * 2 + hh) * 64 + 64],
                           DDG[:, (pair * 2 + hh) * 128:(pair * 2 + hh) * 128 + 128],
                           start=False, stop=True, waits=(ddg_tok,), tp=tp,
                           inc=(hh == 1))
                bank_wr[ybi] = t
                gt_free[er] = t
                chr_free[0] = chr_free[1] = t
                xtmr_free[xr] = t
                scb_free[sr] = t
                for hh in range(2):
                    h = pair * 2 + hh
                    vop(lambda e, pair=pair, hh=hh, h=h, k=k: e.tensor_scalar(
                        SCARRY[:, pair * 128 + hh * 64: pair * 128 + hh * 64 + 64],
                        SCARRY[:, pair * 128 + hh * 64: pair * 128 + hh * 64 + 64],
                        decCols[:, h * 16 + k: h * 16 + k + 1], None, OP.mult),
                        waits=(t, dec_tok))
                vop(lambda e, pair=pair, sr=sr: e.tensor_add(
                    SCARRY[:, pair * 128:(pair + 1) * 128],
                    SCARRY[:, pair * 128:(pair + 1) * 128],
                    SCH[:, sr * 128:(sr + 1) * 128]), waits=(sch_tok,))
                sc_tok2 = ch_v.tok()
                sch_free[sr] = sc_tok2
                if kk % 4 == 3:
                    w = [bank_wr[ybi], sz_tok, mh_guard]
                    gt = vop(lambda e, ybi=ybi, pair=pair, kk=kk, szr=szr: e.tensor_mul(
                        MHALF[:, pair * 1024 + (kk // 4) * 512: pair * 1024 + (kk // 4) * 512 + 512],
                        bank(ybi, 128, 512),
                        SZR[:, szr * 1024 + (kk // 4) * 512: szr * 1024 + (kk // 4) * 512 + 512]),
                        waits=tuple(w))
                    bank_rd[ybi] = gt
                    szr_free[szr] = gt
            for hh in range(2):
                slot_free[rr[hh]] = ch_t.tok()
        m_tok = ch_v.tok()

        if stop_after == "Bg":
            p3s.close()
            pL0.close()
            _finish(em, nc, stack, y_out, dbg, dbg_out, locals(), m_tok, ydma,
                    ch_v, ch_s, ch_t)
            return nc

        for sl2 in range(2):
            so2 = sl2 * 512
            bsum = bank(4, 1, 512)
            for pair in range(16):
                sq = sop(lambda e, pair=pair, sl2=sl2: e.activation(
                    SQH[:, 0:512], MHALF[:, pair * 1024 + sl2 * 512: pair * 1024 + sl2 * 512 + 512],
                    AF.Square), waits=(m_tok,) if pair == 0 else (ch_t.tok(),))
                w = [sq]
                if pair == 0:
                    w.append(bank_rd.get(4))
                t = mm(bsum, onescolb[:, :], SQH[:, 0:512], start=(pair == 0),
                       stop=(pair == 15), waits=tuple(w), inc=True)
            vop(lambda e, bsum=bsum, so2=so2: e.tensor_scalar(
                us_r[0:1, so2:so2 + 512], bsum, 1.0 / DIN, None, OP.mult), waits=(t,))
            bank_rd[4] = ch_v.tok()
            vop(lambda e, so2=so2: e.tensor_scalar(
                us_r[0:1, so2:so2 + 512], us_r[0:1, so2:so2 + 512], 1e-6, None, OP.add))
            sop(lambda e, so2=so2: e.activation(
                us_r[0:1, so2:so2 + 512], us_r[0:1, so2:so2 + 512], AF.Ln),
                waits=(ch_v.tok(),))
            sop(lambda e, so2=so2: e.activation(
                us_r[0:1, so2:so2 + 512], us_r[0:1, so2:so2 + 512], AF.Exp, scale=-0.5))
        yn_tok = ch_s.tok()
        wsc_free = [None, None]
        for e_i in range(8):
            for sl2 in range(2):
                bi = 6 + (sl2 % 2)
                bv = bank(bi, 128, 512)
                for kp in range(16):
                    sl_i = wip_state["idx"] % 8
                    wip_state["idx"] += 1
                    ws = kp % 2
                    wtok0 = gdma(WSC[:, ws * 128:(ws + 1) * 128],
                                 W_op[kp, :, e_i * 128:(e_i + 1) * 128], f"wsc{ws}",
                                 waits=(wsc_free[ws],))
                    wtok = vop(lambda e, sl_i=sl_i, ws=ws, kp=kp: e.tensor_scalar(
                        WIPR[:, sl_i * 128:(sl_i + 1) * 128], WSC[:, ws * 128:(ws + 1) * 128],
                        swcol[:, kp:kp + 1], None, OP.mult),
                        waits=(wtok0, consts, wip_state["free"][sl_i]))
                    wsc_free[ws] = wtok
                    w = [wtok, m_tok]
                    if kp == 0:
                        w.append(bank_rd.get(bi))
                    t = mm(bv, WIPR[:, sl_i * 128:(sl_i + 1) * 128],
                           MHALF[:, kp * 1024 + sl2 * 512: kp * 1024 + sl2 * 512 + 512],
                           start=(kp == 0), stop=(kp == 15), waits=tuple(w),
                           inc=True)
                    wip_state["free"][sl_i] = ch_t.tok()
                bank_wr[bi] = t
                tok0 = half * 1024 + sl2 * 512
                xr2 = sl2 % 2
                x0t = gdma(X0R[:, xr2 * 512:(xr2 + 1) * 512],
                           x_in.rearrange("(c p) t -> p c t", p=128)[:, e_i, HALO + tok0: HALO + tok0 + 512],
                           f"x0r{xr2}", waits=(bank_rd.get(200 + xr2), yn_tok))
                w = [ones_tok, bank_rd.get(1), u_all]
                t1b = mm(bank(1, 128, 512), onesF[:, :], mean_r[0:1, HALO + tok0: HALO + tok0 + 512],
                         start=True, stop=True, waits=tuple(x for x in w if x), inc=True)
                w = [ones_tok, bank_rd.get(0), u_all]
                t2b = mm(bank(0, 128, 512), onesF[:, :], rc_r[0:1, HALO + tok0: HALO + tok0 + 512],
                         start=True, stop=True, waits=tuple(x for x in w if x), inc=True)
                vop(lambda e, xr2=xr2, x0t=x0t, t1b=t1b: e.tensor_sub(
                    X0R[:, xr2 * 512:(xr2 + 1) * 512], X0R[:, xr2 * 512:(xr2 + 1) * 512],
                    bank(1, 128, 512)), waits=(x0t, t1b))
                bank_rd[1] = ch_v.tok()
                vop(lambda e, xr2=xr2: e.tensor_mul(
                    X0R[:, xr2 * 512:(xr2 + 1) * 512], X0R[:, xr2 * 512:(xr2 + 1) * 512],
                    bank(0, 128, 512)), waits=(t2b,))
                bank_rd[0] = ch_v.tok()
                w = [yn_tok, ones_tok, bank_rd.get(2)]
                t3b = mm(bank(2, 128, 512), onesF[:, :], us_r[0:1, sl2 * 512: sl2 * 512 + 512],
                         start=True, stop=True, waits=tuple(x for x in w if x), inc=True)
                cpR = vop(lambda e: e.tensor_copy(
                    XTMR[:, 0:1024].bitcast(F32), bank(2, 128, 512)),
                    waits=(t3b, xtmr_free[0]))
                sc_t = vop(lambda e, bv=bv: e.tensor_mul(
                    bv, bv, XTMR[:, 0:1024].bitcast(F32)), waits=(t, cpR))
                bank_rd[2] = sc_t
                xtmr_free[0] = sc_t
                u1t = vop(lambda e, xr2=xr2, bv=bv: e.tensor_add(
                    X0R[:, xr2 * 512:(xr2 + 1) * 512], X0R[:, xr2 * 512:(xr2 + 1) * 512],
                    bv), waits=())
                bank_rd[bi] = u1t
                ydma(x1u_d.rearrange("(c p) t -> p c t", p=128)[:, e_i, tok0: tok0 + 512],
                     X0R[:, xr2 * 512:(xr2 + 1) * 512], waits=(u1t,))
                bank_rd[200 + xr2] = ldy.tok()
                if stop_after == "Bu" and half == 0 and e_i == 0 and sl2 == 0:
                    p3s.close()
                    pL0.close()
                    _finish(em, nc, stack, y_out, dbg, dbg_out, locals(),
                            bank_rd[200 + xr2], ydma, ch_v, ch_s, ch_t)
                    return nc
        mh_guard = ch_t.tok()
    u1_done = ldy.tok()

    if stop_after == "B":
        p3s.close()
        pL0.close()
        dtok = ydma(y_out[:, :], x1u_d[:, :], waits=(u1_done,))
        _finish(em, nc, stack, y_out, dbg, dbg_out, locals(), dtok, ydma, ch_v, ch_s, ch_t)
        return nc

    p3s.close()
    pL0.close()
    SQ2 = sbuf("SQ2", [128, DC * 512], BF16)
    p45 = contextlib.ExitStack()
    WIP2 = sbuf("WIP2", [128, 16 * 128], BF16, p45)
    wip2 = {"free": [u1_done] * 16, "idx": 0}

    def w2tile(dsrc, waits=()):
        sl_i = wip2["idx"] % 16
        wip2["idx"] += 1
        tok = gdma(WIP2[:, sl_i * 128:(sl_i + 1) * 128], dsrc, f"wip2_{sl_i}",
                   waits=(wip2["free"][sl_i],) + tuple(waits))
        return sl_i, tok

    RES = sbuf("RES", [128, DC * T], F32, p45)
    H2B = sbuf("H2B", [128, DC * T], BF16, p45)
    U1B = sbuf("U1B", [128, 2 * DC * 512], BF16, p45)
    p4u = contextlib.ExitStack()
    U1F = sbuf("U1F", [128, 2 * DC * 512], F32, p4u)
    r1 = gdma(RES[:, :], x_in.rearrange("(c p) t -> p c t", p=128)[:, :, HALO:],
              "res", waits=(u1_done,))
    r2 = gdma(RES[:, :], x1u_d.rearrange("(c p) t -> p c t", p=128),
              "res", waits=(r1,), accum=True)
    u1b_free = [u1_done, u1_done]
    u1f_free = [u1_done, u1_done]
    h2toks = []
    for sl in range(4):
        us = sl % 2
        u1t = gdma(U1F[:, us * DC * 512:(us + 1) * DC * 512],
                   x1u_d.rearrange("(c p) t -> p c t", p=128)[:, :, sl * 512:(sl + 1) * 512],
                   f"u1f{us}", waits=(u1f_free[us],))
        sqt = sop(lambda e, us=us: e.activation(
            SQ2[:, :], U1F[:, us * DC * 512:(us + 1) * DC * 512], AF.Square),
            waits=(u1t,))
        bs = bank(6, 1, 512)
        for c in range(DC):
            w = [sqt, bank_rd.get(6)] if c == 0 else []
            t = mm(bs, onescolb[:, :], SQ2[:, c * 512:(c + 1) * 512],
                   start=(c == 0), stop=(c == DC - 1), waits=tuple(w), inc=(c == DC - 1))
        vop(lambda e, bs=bs, sl=sl: e.tensor_scalar(
            us_r[0:1, sl * 512: sl * 512 + 512], bs, 1.0 / D, None, OP.mult),
            waits=(t,))
        bank_rd[6] = ch_v.tok()
        vop(lambda e, sl=sl: e.tensor_scalar(
            us_r[0:1, sl * 512: sl * 512 + 512], us_r[0:1, sl * 512: sl * 512 + 512],
            1e-6, None, OP.add))
        sop(lambda e, sl=sl: e.activation(
            us_r[0:1, sl * 512: sl * 512 + 512], us_r[0:1, sl * 512: sl * 512 + 512],
            AF.Ln), waits=(ch_v.tok(),))
        rt = sop(lambda e, sl=sl: e.activation(
            us_r[0:1, sl * 512: sl * 512 + 512], us_r[0:1, sl * 512: sl * 512 + 512],
            AF.Exp, scale=-0.5))
        bb = bank(7, 128, 512)
        w = [rt, ones_tok, bank_rd.get(7)]
        t = mm(bb, onesF[:, :], us_r[0:1, sl * 512: sl * 512 + 512], start=True,
               stop=True, waits=tuple(x for x in w if x), inc=True)
        for c in range(DC):
            vop(lambda e, c=c, sl=sl, us=us, bb=bb: e.scalar_tensor_tensor(
                H2B[:, c * T + sl * 512: c * T + sl * 512 + 512],
                U1F[:, us * DC * 512 + c * 512: us * DC * 512 + (c + 1) * 512],
                pwcol[:, c:c + 1], bb, OP.mult, OP.mult),
                waits=(t,) if c == 0 else ())
        bank_rd[7] = ch_v.tok()
        u1f_free[us] = ch_v.tok()
        h2toks.append(ch_v.tok())
    h2_tok = ch_v.tok()
    p4u.close()
    p4h = contextlib.ExitStack()
    SGT = sbuf("SGT", [128, 2 * 512], F32, p4h)
    HH = sbuf("HH", [128, 22 * 1024], BF16, p4h)

    for half in range(2):
        for mt in range(22):
            toks = []
            for k in range(8):
                sg, gtk = w2tile(W_g[k, :, mt * 128:(mt + 1) * 128])
                su, utk = w2tile(W_u[k, :, mt * 128:(mt + 1) * 128])
                toks.append((sg, gtk, su, utk))
            for k, (sg, gtk, su, utk) in enumerate(toks):
                for ss in range(2):
                    no = half * 1024 + ss * 512
                    w = [gtk, h2_tok] if k == 0 else [gtk]
                    if k == 0:
                        w.append(bank_rd.get(ss))
                    mm(bank(ss, 128, 512), WIP2[:, sg * 128:(sg + 1) * 128],
                       _h2mv(H2B, k, no), start=(k == 0), stop=(k == 7),
                       waits=tuple(x for x in w if x), inc=(k == 7))
                    w = [utk] if k else [utk, bank_rd.get(2 + ss)]
                    t = mm(bank(2 + ss, 128, 512), WIP2[:, su * 128:(su + 1) * 128],
                           _h2mv(H2B, k, no), start=(k == 0), stop=(k == 7),
                           waits=tuple(x for x in w if x), inc=True)
                wip2["free"][sg] = ch_t.tok()
                wip2["free"][su] = ch_t.tok()
            gu_done = ch_t.tok()
            for ss in range(2):
                st = sop(lambda e, ss=ss: e.activation(
                    SGT[:, ss * 512:(ss + 1) * 512], bank(ss, 128, 512), AF.Silu),
                    waits=(gu_done,))
                bank_rd[ss] = st
                ht = vop(lambda e, mt=mt, half=half, ss=ss: e.tensor_mul(
                    HH[:, mt * 1024 + ss * 512: mt * 1024 + ss * 512 + 512],
                    SGT[:, ss * 512:(ss + 1) * 512], bank(2 + ss, 128, 512)),
                    waits=(st,))
                bank_rd[2 + ss] = ht
        hh_tok = ch_v.tok()
        for e_i in range(8):
            for k2 in range(22):
                sd, dtk2 = w2tile(W_d[k2, :, e_i * 128:(e_i + 1) * 128])
                for ss in range(2):
                    w = [dtk2, hh_tok] if k2 == 0 else [dtk2]
                    if k2 == 0:
                        w.append(bank_rd.get(4 + ss))
                    t = mm(bank(4 + ss, 128, 512), WIP2[:, sd * 128:(sd + 1) * 128],
                           HH[:, k2 * 1024 + ss * 512: k2 * 1024 + ss * 512 + 512],
                           start=(k2 == 0), stop=(k2 == 21),
                           waits=tuple(x for x in w if x), inc=True)
                wip2["free"][sd] = ch_t.tok()
            for ss in range(2):
                no = half * 1024 + ss * 512
                w = [t if ss == 1 else ch_t.tok(), r2]
                rt = vop(lambda e, e_i=e_i, no=no, ss=ss: e.tensor_add(
                    RES[:, e_i * T + no: e_i * T + no + 512],
                    RES[:, e_i * T + no: e_i * T + no + 512], bank(4 + ss, 128, 512)),
                    waits=tuple(w))
                bank_rd[4 + ss] = rt
    x1_tok = ch_v.tok()

    if stop_after == "P4":
        st = ydma(x1p_d[:, :].rearrange("(c p) t -> p c t", p=128),
                  RES.rearrange("p (c t) -> p c t", c=DC), waits=(x1_tok,))
        st = ydma(y_out[:, :], x1p_d[:, :], waits=(st,))
        p4h.close()
        p45.close()
        _finish(em, nc, stack, y_out, dbg, dbg_out, locals(), st, ydma, ch_v, ch_s, ch_t)
        return nc

    p4h.close()
    p5s = contextlib.ExitStack()
    WINS = sbuf("WINS", [128, 8 * 64], BF16, p5s)
    WOS = sbuf("WOS", [64, 1024], BF16, p5s)
    W1S = sbuf("W1S", [64, 16], F32, p5s)
    B1S = sbuf("B1S", [64, 16], F32, p5s)
    W2S = sbuf("W2S", [64, 16], F32, p5s)
    BINS = sbuf("BINS", [64, 1], F32, p5s)
    B2S = sbuf("B2S", [64, 1], F32, p5s)
    BOUTS = sbuf("BOUTS", [128, 8], F32, p5s)
    WSR = sbuf("WSR", [1, 64], F32, p5s)
    S0 = sbuf("S0", [64, T], F32, p5s)
    GAC = sbuf("GAC", [64, T], F32, p5s)
    TMP5 = sbuf("TMP5", [64, T], F32, p5s)
    GB = sbuf("GB", [64, T], BF16, p5s)
    for dst, srcd in ((WINS, Win.rearrange("k p m -> p k m")),
                      (WOS, Wout0), (W1S, W1_in), (B1S, b1_in), (W2S, W2_in),
                      (BINS, bin_in), (B2S, b2_in), (BOUTS, bout0)):
        gdma(dst[:, :], srcd[:, :, :] if dst is WINS else srcd[:, :], "sc5",
             waits=(x1_tok,) if dst is WINS else ())
    p5w = dch("sc5").tok()
    bw = bank(6, 64, 1)
    for k in range(8):
        w = [p5w, ones_tok, bank_rd.get(6)] if k == 0 else []
        t = mm(bw, WINS[:, k * 64:(k + 1) * 64], onescolb[:, :],
               start=(k == 0), stop=(k == 7), waits=tuple(w), inc=(k == 7))
    wst = vop(lambda e: e.tensor_copy(TMP5[:, 0:1], bw), waits=(t,))
    bank_rd[6] = wst
    tt = tr(bank(7, 1, 64), TMP5[0:64, 0:1], identF[:, :], waits=(wst, bank_rd.get(7)))
    wsr_tok = vop(lambda e: e.tensor_copy(WSR[:, :], bank(7, 1, 64)), waits=(tt,))
    bank_rd[7] = wsr_tok
    for sl in range(4):
        us = sl % 2
        xbt = vop(lambda e, sl=sl, us=us: e.tensor_copy(
            U1B[:, us * DC * 512:(us + 1) * DC * 512],
            RES.rearrange("p (c t) -> p c t", c=DC)[:, :, sl * 512:(sl + 1) * 512]),
            waits=(x1_tok, u1b_free[us]))
        sqt = sop(lambda e, us=us: e.activation(
            SQ2[:, :], U1B[:, us * DC * 512:(us + 1) * DC * 512], AF.Square),
            waits=(xbt,))
        b0, b1v = bank(0, 1, 512), bank(1, 1, 512)
        for c in range(DC):
            w = [xbt, ones_tok, bank_rd.get(0)] if c == 0 else []
            mm(b0, onescolb[:, :], U1B[:, us * DC * 512 + c * 512: us * DC * 512 + (c + 1) * 512],
               start=(c == 0), stop=(c == DC - 1), waits=tuple(w))
        for c in range(DC):
            w = [sqt, bank_rd.get(1)] if c == 0 else []
            t = mm(b1v, onescolb[:, :], SQ2[:, c * 512:(c + 1) * 512],
                   start=(c == 0), stop=(c == DC - 1), waits=tuple(w), inc=(c == DC - 1))
        so = sl * 512
        vop(lambda e, so=so, b0=b0: e.tensor_scalar(
            mean_r[0:1, so:so + 512], b0, 1.0 / D, None, OP.mult), waits=(t,))
        vop(lambda e, so=so, b1v=b1v: e.tensor_scalar(
            us_r[0:1, so:so + 512], b1v, 1.0 / D, None, OP.mult))
        bank_rd[0] = bank_rd[1] = ch_v.tok()
        vop(lambda e, so=so: e.tensor_mul(
            rc_r[0:1, so:so + 512], mean_r[0:1, so:so + 512], mean_r[0:1, so:so + 512]))
        vop(lambda e, so=so: e.tensor_sub(
            rc_r[0:1, so:so + 512], us_r[0:1, so:so + 512], rc_r[0:1, so:so + 512]))
        vop(lambda e, so=so: e.tensor_scalar(
            rc_r[0:1, so:so + 512], rc_r[0:1, so:so + 512], 0.0, 1e-5, OP.max, OP.add))
        lt = ch_v.tok()
        sop(lambda e, so=so: e.activation(
            rc_r[0:1, so:so + 512], rc_r[0:1, so:so + 512], AF.Ln), waits=(lt,))
        rct = sop(lambda e, so=so: e.activation(
            rc_r[0:1, so:so + 512], rc_r[0:1, so:so + 512], AF.Exp, scale=-0.5))
        nmt = vop(lambda e, so=so: e.tensor_scalar(
            us_r[0:1, so:so + 512], mean_r[0:1, so:so + 512], -1.0, None, OP.mult))
        bs = bank(2, 64, 512)
        for k in range(8):
            w = [p5w, xbt, bank_rd.get(2)] if k == 0 else []
            mm(bs, WINS[:, k * 64:(k + 1) * 64],
               U1B[:, us * DC * 512 + k * 512: us * DC * 512 + (k + 1) * 512],
               start=(k == 0), stop=False, waits=tuple(w))
        t = mm(bs, WSR[:, :], us_r[0:1, so:so + 512], start=False, stop=True,
               waits=(wsr_tok, nmt), inc=True)
        bbc = bank(3, 64, 512)
        w = [rct, ones_tok, bank_rd.get(3)]
        t2 = mm(bbc, onesF[:, 0:64], rc_r[0:1, so:so + 512], start=True, stop=True,
                waits=tuple(x for x in w if x), inc=True)
        vop(lambda e, bbc=bbc: e.tensor_copy(
            SQ2[:, 0:1024].bitcast(F32)[0:64, :], bbc), waits=(t2,))
        vop(lambda e, so=so, bs=bs: e.tensor_mul(
            S0[:, so:so + 512], bs, SQ2[:, 0:1024].bitcast(F32)[0:64, :]),
            waits=(t,))
        bank_rd[2] = bank_rd[3] = ch_v.tok()
        vop(lambda e, so=so: e.tensor_scalar(
            S0[:, so:so + 512], S0[:, so:so + 512], BINS[:, :], None, OP.add),
            waits=(p5w,))
        u1b_free[us] = ch_v.tok()
    s_tok = ch_v.tok()
    for i in range(16):
        rt = sop(lambda e, i=i: e.activation(
            TMP5[:, :], S0[0:64, :], AF.Relu, bias=B1S[:, i:i + 1],
            scale=W1S[:, i:i + 1]), waits=(s_tok, p5w) if i == 0 else (ch_v.tok(),))
        if i == 0:
            vop(lambda e: e.tensor_scalar(GAC[0:64, :], TMP5[0:64, :], W2S[:, 0:1],
                                          None, OP.mult), waits=(rt,))
        else:
            vop(lambda e, i=i: e.scalar_tensor_tensor(
                GAC[0:64, :], TMP5[0:64, :], W2S[:, i:i + 1], GAC[0:64, :],
                OP.mult, OP.add), waits=(rt,))
    vop(lambda e: e.tensor_scalar(GAC[0:64, :], GAC[0:64, :], B2S[:, :], None, OP.add))
    gb_tok = vop(lambda e: e.tensor_copy(GB[0:64, :], GAC[0:64, :]))
    for e_i in range(8):
        for sl in range(4):
            bo = bank(4 + sl % 2, 128, 512)
            w = [gb_tok, p5w, bank_rd.get(4 + sl % 2)]
            t = mm(bo, WOS[:, e_i * 128:(e_i + 1) * 128], GB[0:64, sl * 512:(sl + 1) * 512],
                   start=True, stop=True, waits=tuple(x for x in w if x), inc=True)
            rt = vop(lambda e, e_i=e_i, sl=sl, bo=bo: e.scalar_tensor_tensor(
                RES[:, e_i * T + sl * 512: e_i * T + sl * 512 + 512],
                bo, BOUTS[:, e_i:e_i + 1],
                RES[:, e_i * T + sl * 512: e_i * T + sl * 512 + 512], OP.add, OP.add),
                waits=(t,))
            bank_rd[4 + sl % 2] = rt
    x1p_tok = ch_v.tok()
    st = ydma(x1p_d.rearrange("(c p) t -> p c t", p=128),
              RES.rearrange("p (c t) -> p c t", c=DC), waits=(x1p_tok,))
    for sl in range(4):
        us = sl % 2
        xbt = vop(lambda e, sl=sl, us=us: e.tensor_copy(
            U1B[:, us * DC * 512:(us + 1) * DC * 512],
            RES.rearrange("p (c t) -> p c t", c=DC)[:, :, sl * 512:(sl + 1) * 512]),
            waits=(x1p_tok, u1b_free[us]))
        sqt = sop(lambda e, us=us: e.activation(
            SQ2[:, :], U1B[:, us * DC * 512:(us + 1) * DC * 512], AF.Square),
            waits=(xbt,))
        b0, b1v = bank(0, 1, 512), bank(1, 1, 512)
        for c in range(DC):
            w = [xbt, ones_tok, bank_rd.get(0)] if c == 0 else []
            mm(b0, onescolb[:, :], U1B[:, us * DC * 512 + c * 512: us * DC * 512 + (c + 1) * 512],
               start=(c == 0), stop=(c == DC - 1), waits=tuple(w))
        for c in range(DC):
            w = [sqt, bank_rd.get(1)] if c == 0 else []
            t = mm(b1v, onescolb[:, :], SQ2[:, c * 512:(c + 1) * 512],
                   start=(c == 0), stop=(c == DC - 1), waits=tuple(w), inc=(c == DC - 1))
        so = sl * 512
        vop(lambda e, so=so, b0=b0: e.tensor_scalar(
            mean_r[0:1, so:so + 512], b0, 1.0 / D, None, OP.mult), waits=(t,))
        vop(lambda e, so=so, b1v=b1v: e.tensor_scalar(
            us_r[0:1, so:so + 512], b1v, 1.0 / D, None, OP.mult))
        bank_rd[0] = bank_rd[1] = ch_v.tok()
        vop(lambda e, so=so: e.tensor_mul(
            rc_r[0:1, so:so + 512], mean_r[0:1, so:so + 512], mean_r[0:1, so:so + 512]))
        vop(lambda e, so=so: e.tensor_sub(
            rc_r[0:1, so:so + 512], us_r[0:1, so:so + 512], rc_r[0:1, so:so + 512]))
        vop(lambda e, so=so: e.tensor_scalar(
            rc_r[0:1, so:so + 512], rc_r[0:1, so:so + 512], 0.0, 1e-5, OP.max, OP.add))
        lt = ch_v.tok()
        sop(lambda e, so=so: e.activation(
            rc_r[0:1, so:so + 512], rc_r[0:1, so:so + 512], AF.Ln), waits=(lt,))
        rct = sop(lambda e, so=so: e.activation(
            rc_r[0:1, so:so + 512], rc_r[0:1, so:so + 512], AF.Exp, scale=-0.5))
        bm = bank(2, 128, 512)
        w = [ones_tok, bank_rd.get(2), ch_v.tok()]
        mm(bm, onesF[:, :], mean_r[0:1, so:so + 512], start=True, stop=True,
           waits=tuple(x for x in w if x))
        br = bank(3, 128, 512)
        w = [rct, bank_rd.get(3)]
        t2 = mm(br, onesF[:, :], rc_r[0:1, so:so + 512], start=True, stop=True,
                waits=tuple(x for x in w if x), inc=True)
        for c in range(DC):
            vop(lambda e, c=c, us=us, bm=bm: e.tensor_sub(
                SQ2[:, c * 512:(c + 1) * 512],
                U1B[:, us * DC * 512 + c * 512: us * DC * 512 + (c + 1) * 512], bm),
                waits=(t2,) if c == 0 else ())
            vop(lambda e, c=c, us=us, br=br: e.tensor_mul(
                U1B[:, us * DC * 512 + c * 512: us * DC * 512 + (c + 1) * 512],
                SQ2[:, c * 512:(c + 1) * 512], br))
        t1t = ch_v.tok()
        bank_rd[2] = bank_rd[3] = t1t
        gdma(t1AG_in.rearrange("(c p) t -> p c t", p=128)[:, :, sl * 512:(sl + 1) * 512],
             U1B[:, us * DC * 512:(us + 1) * DC * 512], "t1st", waits=(t1t,))
        u1b_free[us] = dch("t1st").tok()
    t1_in_done = dch("t1st").tok()
    em.emit("g", lambda e: e.collective_compute(
        "AllGather", OP.bypass, replica_groups=GROUPS,
        ins=[t1AG_in[:, :]], outs=[t1AG_out[:, :]]),
        (t1_in_done,), ((cc, 1),))
    ag2 = cc.tok()
    p5s.close()
    p45.close()

    if stop_after == "P5":
        st = ydma(y_out[:, :], x1p_d[:, :], waits=(st,))
        _finish(em, nc, stack, y_out, dbg, dbg_out, locals(), st, ydma, ch_v, ch_s, ch_t)
        return nc

    p6s = contextlib.ExitStack()
    QA = sbuf("QA", [128, 4 * S], BF16, p6s)
    KA = sbuf("KA", [128, 4 * S], BF16, p6s)
    V65 = sbuf("V65", [128, 32 * 8 * 65], BF16, p6s)
    PTB = sbuf("PTB", [128, 2 * 1024], BF16, p6s)
    T1T = sbuf("T1T", [128, 2 * DC * 512], BF16, p6s)
    OTB = sbuf("OTB", [64, 2 * 512], BF16, p6s)
    MHC = sbuf("MHC", [128, 8], F32, p6s)
    SRW = sbuf("SRW", [1, 2 * 512], F32, p6s)
    QNR = sbuf("QNR", [1, 8], F32, p6s)
    amaskb = sbuf("amaskb", [128, 4 * 512], BF16, p6s)
    amask_tok = gdma(amaskb[:, :], amask_in.rearrange("d p t -> p d t"), "amask", waits=(ag2,))
    p6w = contextlib.ExitStack()
    QKW = sbuf("QKW", [128, 8 * 1024], BF16, p6w)
    VW = sbuf("VW", [128, 8 * 512], BF16, p6w)
    for k in range(8):
        gdma(QKW[:, k * 1024:(k + 1) * 1024], W_qkv_qk[k, :, :], "qkw",
             waits=(ag2,) if k == 0 else ())
        gdma(VW[:, k * 512:(k + 1) * 512], W_qkv_v[k, :, :], "qkw")
    qkw_tok = dch("qkw").tok()
    vo_tok = vop(lambda e: e.memset(V65[:, :], 1.0))
    t1t_free = [ag2, ag2]
    qk_ev = None
    for ts in range(8):
        slot = ts // 4
        us = ts % 2
        t1tok = gdma(T1T[:, us * DC * 512:(us + 1) * DC * 512],
                     t1AG_out.rearrange("(s c p) t -> s p c t", s=2, p=128)[
                         slot, :, :, (ts % 4) * 512:(ts % 4) * 512 + 512],
                     f"t1t{us}", waits=(ag2, t1t_free[us]))
        for m2 in range(8):
            bi = m2 % 2
            bv = bank(bi, 128, 512)
            for k in range(8):
                w = [t1tok, qkw_tok] if k == 0 else []
                if k == 0:
                    w.append(bank_rd.get(bi))
                t = mm(bv, QKW[:, k * 1024 + m2 * 128: k * 1024 + (m2 + 1) * 128],
                       T1T[:, us * DC * 512 + k * 512: us * DC * 512 + (k + 1) * 512],
                       start=(k == 0), stop=(k == 7), waits=tuple(x for x in w if x),
                       inc=(k == 7))
            dst = QA if m2 % 2 == 0 else KA
            j = m2 // 2
            ev = vop(lambda e, dst=dst, j=j, ts=ts, bv=bv: e.tensor_copy(
                dst[:, j * S + ts * 512: j * S + ts * 512 + 512], bv), waits=(t,))
            bank_rd[bi] = ev
        for tt in range(4):
            bi = 2 + tt % 2
            bv = bank(bi, 128, 512)
            for k in range(8):
                w = [t1tok, qkw_tok] if k == 0 else []
                if k == 0:
                    w.append(bank_rd.get(bi))
                t = mm(bv, T1T[:, us * DC * 512 + k * 512 + tt * 128:
                               us * DC * 512 + k * 512 + tt * 128 + 128],
                       VW[:, k * 512:(k + 1) * 512],
                       start=(k == 0), stop=(k == 7), waits=tuple(x for x in w if x),
                       inc=(k == 7))
            gt = ts * 4 + tt
            for h in range(8):
                w = [t, vo_tok] if h == 0 else []
                ev = vop(lambda e, gt=gt, h=h, bv=bv: e.tensor_copy(
                    V65[:, (gt * 8 + h) * 65: (gt * 8 + h) * 65 + 64], bv[:, h * 64:(h + 1) * 64]),
                    waits=tuple(w))
            bank_rd[bi] = ev
        t1t_free[us] = ch_v.tok()
    qkv_tok = ch_v.tok()
    p6w.close()
    if stop_after == "P6v":
        p6s.close()
        _finish(em, nc, stack, y_out, dbg, dbg_out, locals(), qkv_tok, ydma, ch_v, ch_s, ch_t)
        return nc
    for h in range(8):
        j, hb = h // 2, 64 * (h % 2)
        for src_a, qn in ((QA, 0), (KA, 1)):
            sqt = sop(lambda e, src_a=src_a, j=j, hb=hb: e.activation(
                SQ2[hb:hb + 64, :], src_a[hb:hb + 64, j * S:(j + 1) * S], AF.Square),
                waits=(qkv_tok,))
            for g in range(8):
                w = [sqt, bank_rd.get(4)]
                t = mm(bank(4, 1, 512), onescolb[hb:hb + 64, :],
                       SQ2[hb:hb + 64, g * 512:(g + 1) * 512],
                       start=True, stop=True, waits=tuple(x for x in w if x), inc=True)
                if g == 0:
                    rt = vop(lambda e, qn=qn: e.reduce_max(
                        QNR[0:1, qn:qn + 1], bank(4, 1, 512), axis=AX.X), waits=(t,))
                else:
                    vop(lambda e: e.reduce_max(QNR[0:1, 2:3], bank(4, 1, 512), axis=AX.X),
                        waits=(t,))
                    rt = vop(lambda e, qn=qn: e.tensor_max(
                        QNR[0:1, qn:qn + 1], QNR[0:1, qn:qn + 1], QNR[0:1, 2:3]))
                bank_rd[4] = rt
        vop(lambda e: e.tensor_mul(QNR[0:1, 3:4], QNR[0:1, 0:1], QNR[0:1, 1:2]))
        lt = ch_v.tok()
        sop(lambda e: e.activation(QNR[0:1, 3:4], QNR[0:1, 3:4], AF.Ln), waits=(lt,))
        et = sop(lambda e: e.activation(QNR[0:1, 3:4], QNR[0:1, 3:4], AF.Exp, scale=0.5))
        nt = vop(lambda e: e.tensor_scalar(QNR[0:1, 3:4], QNR[0:1, 3:4], -0.125,
                                           None, OP.mult), waits=(et,))
        bb = bank(5, 128, 1)
        w = [nt, ones_tok, bank_rd.get(5)]
        t = mm(bb, onesF[:, :], QNR[0:1, 3:4], start=True, stop=True,
               waits=tuple(x for x in w if x), inc=True)
        mv = vop(lambda e, h=h, bb=bb: e.tensor_copy(MHC[:, h:h + 1], bb), waits=(t,))
        bank_rd[5] = mv
    mhc_tok = ch_v.tok()
    if stop_after == "P6q":
        p6s.close()
        _finish(em, nc, stack, y_out, dbg, dbg_out, locals(), mhc_tok, ydma, ch_v, ch_s, ch_t)
        return nc
    ptb_free = [mhc_tok, mhc_tok]
    ot_free = [None, None]
    for h in range(8):
        j, hb = h // 2, 64 * (h % 2)
        for qs in range(8):
            po = bank(6, 65, 512)
            nkv = qs * 4 + 4
            pv_t = None
            for tb in range((nkv + 1) // 2):
                tlist = [t for t in (2 * tb, 2 * tb + 1) if t < nkv]
                us = tb % 2
                stb = bank(2 * us, 128, 1024)
                for ti, t_kv in enumerate(tlist):
                    w = [qkv_tok]
                    if ti == 0:
                        w.append(bank_rd.get(2 * us))
                        w.append(bank_rd.get(2 * us + 1))
                    d = t_kv - 4 * qs
                    t = mm(stb[:, ti * 512:(ti + 1) * 512],
                           KA[hb:hb + 64, j * S + t_kv * 128: j * S + t_kv * 128 + 128],
                           QA[hb:hb + 64, j * S + qs * 512: j * S + qs * 512 + 512],
                           start=True, stop=(d < 0), waits=tuple(w), inc=(d < 0))
                    if d >= 0:
                        t = mm(stb[:, ti * 512:(ti + 1) * 512], identb[:, :],
                               amaskb[:, d * 512:(d + 1) * 512],
                               start=False, stop=True, waits=(amask_tok,),
                               inc=True)
                bank_wr[2 * us] = t
                et = sop(lambda e, us=us, stb=stb, h=h, n=len(tlist): e.activation(
                    PTB[:, us * 1024: us * 1024 + n * 512], stb[:, 0:n * 512],
                    AF.Exp, bias=MHC[:, h:h + 1], scale=0.125),
                    waits=(t, ptb_free[us], mhc_tok))
                bank_rd[2 * us] = bank_rd[2 * us + 1] = et
                for ti, t_kv in enumerate(tlist):
                    w = [et]
                    if tb == 0 and ti == 0:
                        w.append(bank_rd.get(6))
                    pv_t = mm(po, V65[:, (t_kv * 8 + h) * 65: (t_kv * 8 + h) * 65 + 65],
                              PTB[:, us * 1024 + ti * 512: us * 1024 + (ti + 1) * 512],
                              start=(tb == 0 and ti == 0), stop=(t_kv == nkv - 1),
                              waits=tuple(w), inc=(t_kv == nkv - 1))
                ptb_free[us] = pv_t
            su = qs % 2
            sv = vop(lambda e, su=su, po=po: e.tensor_copy(
                SRW[0:1, su * 512:(su + 1) * 512], po[64:65, :]), waits=(pv_t,))
            rv = vop(lambda e, su=su: e.reciprocal(
                SRW[0:1, su * 512:(su + 1) * 512], SRW[0:1, su * 512:(su + 1) * 512]))
            bb = bank(7, 64, 512)
            w = [rv, ones_tok, bank_rd.get(7)]
            t = mm(bb, onesF[:, 0:64], SRW[0:1, su * 512:(su + 1) * 512], start=True,
                   stop=True, waits=tuple(x for x in w if x), inc=True)
            cpb = vop(lambda e, bb=bb: e.tensor_copy(
                SQ2[:, 0:1024].bitcast(F32)[0:64, :], bb), waits=(t,))
            ov = vop(lambda e, su=su, po=po: e.tensor_mul(
                OTB[:, su * 512:(su + 1) * 512], po[0:64, :],
                SQ2[:, 0:1024].bitcast(F32)[0:64, :]),
                waits=(cpb, ot_free[su]))
            bank_rd[6] = ov
            bank_rd[7] = ov
            gdma(oAG_in[h * 64:(h + 1) * 64, qs * 512:(qs + 1) * 512],
                 OTB[:, su * 512:(su + 1) * 512], f"ost{su}", waits=(ov,))
            ot_free[su] = dch(f"ost{su}").tok()
    o_done = (dch("ost0").tok(), dch("ost1").tok())
    em.emit("g", lambda e: e.collective_compute(
        "AllGather", OP.bypass, replica_groups=GROUPS,
        ins=[oAG_in[:, :]], outs=[oAG_out[:, :]]),
        o_done, ((cc, 1),))
    ag3 = cc.tok()
    if stop_after == "P6a":
        p6s.close()
        _finish(em, nc, stack, y_out, dbg, dbg_out, locals(), ag3, ydma, ch_v, ch_s, ch_t)
        return nc
    OSEL = sbuf("OSEL", [128, 8 * 512], BF16, p6s)
    OAB = sbuf("OAB", [128, 2 * 512], BF16, p6s)
    X2T = sbuf("X2T", [128, 2 * 512], F32, p6s)
    X2B = sbuf("X2B", [128, 512], BF16, p6s)
    oab_free = [ag3, ag3]
    x2t_free = [None, None]
    osel_free = None
    for g in range(4):
        osel_toks = []
        for k in range(8):
            ua = k % 2
            ta = gdma(OAB[:, ua * 512:(ua + 1) * 512],
                      oAG_out[k * 128:(k + 1) * 128, g * 512:(g + 1) * 512], f"oab{ua}",
                      waits=(ag3, oab_free[ua]))
            sel1 = vop(lambda e, k=k, ua=ua: e.tensor_scalar(
                OSEL[:, k * 512:(k + 1) * 512], OAB[:, ua * 512:(ua + 1) * 512],
                uinvcol[:, :], None, OP.mult), waits=(ta, consts, osel_free))
            tb2 = gdma(OAB[:, ua * 512:(ua + 1) * 512],
                       oAG_out[k * 128:(k + 1) * 128, 2048 + g * 512: 2048 + (g + 1) * 512],
                       f"oab{ua}", waits=(sel1,))
            sel2 = vop(lambda e, k=k, ua=ua: e.scalar_tensor_tensor(
                OSEL[:, k * 512:(k + 1) * 512], OAB[:, ua * 512:(ua + 1) * 512],
                uinitcol[:, :], OSEL[:, k * 512:(k + 1) * 512], OP.mult, OP.add),
                waits=(tb2,))
            oab_free[ua] = sel2
            osel_toks.append(sel2)
        osel_tok = ch_v.tok()
        for e_i in range(8):
            bi = 4 + e_i % 2
            bv = bank(bi, 128, 512)
            toks = []
            for k in range(8):
                swo, wtk = w2tile(W_wo[k, :, e_i * 128:(e_i + 1) * 128])
                toks.append((swo, wtk))
            for k, (swo, wtk) in enumerate(toks):
                w = [wtk, osel_tok] if k == 0 else [wtk]
                if k == 0:
                    w.append(bank_rd.get(bi))
                t = mm(bv, WIP2[:, swo * 128:(swo + 1) * 128],
                       OSEL[:, k * 512:(k + 1) * 512],
                       start=(k == 0), stop=(k == 7), waits=tuple(w), inc=True)
                wip2["free"][swo] = ch_t.tok()
            ux = e_i % 2
            x1pt = gdma(X2T[:, ux * 512:(ux + 1) * 512],
                        x1p_d.rearrange("(c p) t -> p c t", p=128)[:, e_i, g * 512:(g + 1) * 512],
                        f"x2t{ux}", waits=(x2t_free[ux],))
            xt = vop(lambda e, ux=ux, bv=bv: e.tensor_add(
                X2T[:, ux * 512:(ux + 1) * 512], X2T[:, ux * 512:(ux + 1) * 512], bv),
                waits=(x1pt, t))
            bank_rd[bi] = xt
            ydma(x2_d.rearrange("(c p) t -> p c t", p=128)[:, e_i, g * 512:(g + 1) * 512],
                 X2T[:, ux * 512:(ux + 1) * 512], waits=(xt,))
            xbt = vop(lambda e, ux=ux: e.tensor_copy(
                X2B[:, :], X2T[:, ux * 512:(ux + 1) * 512]), waits=(xt,))
            sqt = sop(lambda e: e.activation(SQ2[:, 0:512], X2B[:, :], AF.Square),
                      waits=(xbt,))
            w = [xbt, ones_tok, bank_rd.get(0)] if e_i == 0 else [xbt]
            mm(bank(0, 1, 512), onescolb[:, :], X2B[:, :],
               start=(e_i == 0), stop=(e_i == 7), waits=tuple(x for x in w if x))
            w = [sqt] if e_i else [sqt, bank_rd.get(1)]
            t = mm(bank(1, 1, 512), onescolb[:, :], SQ2[:, 0:512],
                   start=(e_i == 0), stop=(e_i == 7), waits=tuple(x for x in w if x),
                   inc=(e_i == 7))
            x2t_free[ux] = ch_v.tok()
        osel_free = ch_t.tok()
        so = g * 512
        vop(lambda e, so=so: e.tensor_scalar(
            mean_r[0:1, so:so + 512], bank(0, 1, 512), 1.0 / D, None, OP.mult),
            waits=(t,))
        vop(lambda e, so=so: e.tensor_scalar(
            us_r[0:1, so:so + 512], bank(1, 1, 512), 1.0 / D, None, OP.mult))
        bank_rd[0] = bank_rd[1] = ch_v.tok()
        vop(lambda e, so=so: e.tensor_mul(
            rc_r[0:1, so:so + 512], mean_r[0:1, so:so + 512], mean_r[0:1, so:so + 512]))
        vop(lambda e, so=so: e.tensor_sub(
            rc_r[0:1, so:so + 512], us_r[0:1, so:so + 512], rc_r[0:1, so:so + 512]))
        vop(lambda e, so=so: e.tensor_scalar(
            rc_r[0:1, so:so + 512], rc_r[0:1, so:so + 512], 0.0, 1e-5, OP.max, OP.add))
        lt = ch_v.tok()
        sop(lambda e, so=so: e.activation(
            rc_r[0:1, so:so + 512], rc_r[0:1, so:so + 512], AF.Ln), waits=(lt,))
        sop(lambda e, so=so: e.activation(
            rc_r[0:1, so:so + 512], rc_r[0:1, so:so + 512], AF.Exp, scale=-0.5))
    x2_done = ldy.tok()
    x2s_tok = ch_s.tok()
    p6s.close()

    if stop_after == "P6":
        dtok = ydma(y_out[:, :], x2_d[:, :], waits=(x2_done,))
        _finish(em, nc, stack, y_out, dbg, dbg_out, locals(), dtok, ydma, ch_v, ch_s, ch_t)
        return nc

    p7s = contextlib.ExitStack()
    VNB = sbuf("VNB", [128, DC * 1024], BF16, p7s)
    HP = sbuf("HP", [128, 32 * 1024], BF16, p7s)
    GP = sbuf("GP", [128, 32 * 1024], BF16, p7s)
    OSL = sbuf("OSL", [128, 2 * 32 * 128], BF16, p7s)
    X2R = sbuf("X2R", [128, DC * 512], BF16, p7s)
    X3T = sbuf("X3T", [128, 512], F32, p7s)
    SGT2 = sbuf("SGT2", [128, 512], F32, p7s)
    SC1 = sbuf("SC1", [128, 32], F32, p7s)
    SC2 = sbuf("SC2", [128, 32], F32, p7s)
    SCH1 = sbuf("SCH1", [128, 32], F32, p7s)
    SCH2 = sbuf("SCH2", [128, 32], F32, p7s)
    SC3 = sbuf("SC3", [128, 8], F32, p7s)
    for dst, srcd in ((SC1, s_t1), (SC2, s_t2), (SCH1, s_h1), (SCH2, s_h2), (SC3, s_t3)):
        gdma(dst[:, :], srcd[:, :], "sc7", waits=(x2_done,) if dst is SC1 else ())
    p7w = dch("sc7").tok()
    x2r_free = [None, None]

    def emit_v(ps_i):
        vtoks = []
        for sl in (2 * ps_i, 2 * ps_i + 1):
            x2t = gdma(X2R[:, 0: DC * 512],
                       x2_d.rearrange("(c p) t -> p c t", p=128)[:, :, sl * 512:(sl + 1) * 512],
                       "x2rv", waits=(x2_done, x2r_free[0], x2r_free[1]))
            so = sl * 512
            lo = (sl - 2 * ps_i) * 512
            bm = bank(2, 128, 512)
            w = [ones_tok, bank_rd.get(2), x2s_tok]
            mm(bm, onesF[:, :], mean_r[0:1, so:so + 512], start=True, stop=True,
               waits=tuple(x for x in w if x))
            br = bank(3, 128, 512)
            w = [bank_rd.get(3), x2s_tok]
            t2 = mm(br, onesF[:, :], rc_r[0:1, so:so + 512], start=True, stop=True,
                    waits=tuple(x for x in w if x), inc=True)
            for c in range(DC):
                vop(lambda e, c=c, bm=bm: e.tensor_sub(
                    SQ2[:, c * 512:(c + 1) * 512],
                    X2R[:, c * 512:(c + 1) * 512], bm),
                    waits=(t2, x2t, vnb_guard) if c == 0 else ())
                vop(lambda e, c=c, lo=lo, br=br: e.tensor_mul(
                    VNB[:, c * 1024 + lo: c * 1024 + lo + 512],
                    SQ2[:, c * 512:(c + 1) * 512], br))
            bank_rd[2] = bank_rd[3] = ch_v.tok()
            x2r_free[0] = ch_v.tok()
        return ch_v.tok()

    osl_free = [p7w, p7w]
    x3_out_tok = None
    hp_guard = None
    vnb_guard = None
    for ps_i in range(2):
        tok0 = ps_i * 1024
        v_tok = emit_v(ps_i)
        x2r_free[0] = x2r_free[1] = v_tok
        for o in range(32):
            uo = o % 2
            t1w = gdma(OSL[:, uo * 4096: uo * 4096 + 1024], W_t1[o, :, :], f"osl{uo}",
                       waits=(osl_free[uo],))
            t2w = gdma(OSL[:, uo * 4096 + 1024: uo * 4096 + 2048], W_t2[o, :, :], f"osl{uo}")
            wt = dch(f"osl{uo}").tok()
            for ss in range(2):
                no = ss * 512
                for k in range(8):
                    w = [wt, v_tok] if k == 0 else []
                    if k == 0:
                        w.append(bank_rd.get(ss))
                    mm(bank(ss, 128, 512), OSL[:, uo * 4096 + k * 128: uo * 4096 + (k + 1) * 128],
                       VNB[:, k * 1024 + no: k * 1024 + no + 512],
                       start=(k == 0), stop=(k == 7), waits=tuple(x for x in w if x))
                    w = [wt] if k else [wt, bank_rd.get(2 + ss)]
                    t = mm(bank(2 + ss, 128, 512),
                           OSL[:, uo * 4096 + 1024 + k * 128: uo * 4096 + 1024 + (k + 1) * 128],
                           VNB[:, k * 1024 + no: k * 1024 + no + 512],
                           start=(k == 0), stop=(k == 7), waits=tuple(x for x in w if x),
                           inc=(k == 7))
            gu = ch_t.tok()
            osl_free[uo] = gu
            vnb_guard = gu
            ht = None
            for ss in range(2):
                st2 = sop(lambda e, ss=ss, o=o: e.activation(
                    SGT2[:, :], bank(ss, 128, 512), AF.Silu,
                    scale=SC1[:, o:o + 1]), waits=(gu, p7w, ht))
                bank_rd[ss] = st2
                ht = vop(lambda e, o=o, ss=ss: e.scalar_tensor_tensor(
                    HP[:, o * 1024 + ss * 512: o * 1024 + ss * 512 + 512],
                    bank(2 + ss, 128, 512), SC2[:, o:o + 1],
                    SGT2[:, :], OP.mult, OP.mult),
                    waits=(st2, hp_guard))
                bank_rd[2 + ss] = ht
        hp_tok = ch_v.tok()
        for o in range(32):
            uo = o % 2
            hw = gdma(OSL[:, uo * 4096: uo * 4096 + 4096], W_h1[o, :, :], f"osl{uo}",
                      waits=(osl_free[uo],))
            for ss in range(2):
                bi = 4 + ss
                for k in range(32):
                    w = [hw, hp_tok] if k == 0 else []
                    if k == 0:
                        w.append(bank_rd.get(bi))
                    t = mm(bank(bi, 128, 512), OSL[:, uo * 4096 + k * 128: uo * 4096 + (k + 1) * 128],
                           HP[:, k * 1024 + ss * 512: k * 1024 + ss * 512 + 512],
                           start=(k == 0), stop=(k == 31), waits=tuple(x for x in w if x),
                           inc=(k == 31))
                ge = sop(lambda e, o=o, ss=ss, bi=bi: e.activation(
                    GP[:, o * 1024 + ss * 512: o * 1024 + ss * 512 + 512],
                    bank(bi, 128, 512), AF.Gelu, scale=SCH1[:, o:o + 1]),
                    waits=(t, p7w))
                bank_rd[bi] = ge
            osl_free[uo] = ch_t.tok()
        gp_tok = ch_s.tok()
        for o in range(32):
            uo = o % 2
            hw = gdma(OSL[:, uo * 4096: uo * 4096 + 4096], W_h2[o, :, :], f"osl{uo}",
                      waits=(osl_free[uo],))
            for ss in range(2):
                bi = 4 + ss
                for k in range(32):
                    w = [hw, gp_tok] if k == 0 else []
                    if k == 0:
                        w.append(bank_rd.get(bi))
                    t = mm(bank(bi, 128, 512), OSL[:, uo * 4096 + k * 128: uo * 4096 + (k + 1) * 128],
                           GP[:, k * 1024 + ss * 512: k * 1024 + ss * 512 + 512],
                           start=(k == 0), stop=(k == 31), waits=tuple(x for x in w if x),
                           inc=(k == 31))
                ge = sop(lambda e, o=o, ss=ss, bi=bi: e.activation(
                    HP[:, o * 1024 + ss * 512: o * 1024 + ss * 512 + 512],
                    bank(bi, 128, 512), AF.Gelu, scale=SCH2[:, o:o + 1]),
                    waits=(t, p7w))
                bank_rd[bi] = ge
            osl_free[uo] = ch_t.tok()
        h2p_tok = ch_s.tok()
        hp_guard = None
        for e_i in range(8):
            uo = e_i % 2
            hw = gdma(OSL[:, uo * 4096: uo * 4096 + 4096], W_t3[e_i, :, :], f"osl{uo}",
                      waits=(osl_free[uo],))
            for ss in range(2):
                bi = 6 + ss
                no = tok0 + ss * 512
                for k in range(32):
                    w = [hw, h2p_tok] if k == 0 else []
                    if k == 0:
                        w.append(bank_rd.get(bi))
                    t = mm(bank(bi, 128, 512), OSL[:, uo * 4096 + k * 128: uo * 4096 + (k + 1) * 128],
                           HP[:, k * 1024 + ss * 512: k * 1024 + ss * 512 + 512],
                           start=(k == 0), stop=(k == 31), waits=tuple(x for x in w if x),
                           inc=(k == 31))
                ux = ss
                x2t = gdma(X2R[:, ux * 512: ux * 512 + 512],
                           x2_d.rearrange("(c p) t -> p c t", p=128)[:, e_i, no:no + 512],
                           f"x2w{ux}", waits=(x2r_free[ux],))
                xo = vop(lambda e, bi=bi, e_i=e_i, ux=ux: e.scalar_tensor_tensor(
                    X3T[:, :],
                    bank(bi, 128, 512), SC3[:, e_i:e_i + 1],
                    X2R[:, ux * 512: ux * 512 + 512], OP.mult, OP.add),
                    waits=(t, x2t, p7w, x3_out_tok))
                bank_rd[bi] = xo
                x3_out_tok = ydma(
                    y_out.rearrange("(c p) t -> p c t", p=128)[:, e_i, no:no + 512],
                    X3T[:, :], waits=(xo,))
                x2r_free[ux] = x3_out_tok
                if stop_after is not None and stop_after.startswith("P7w") and (
                        (ps_i, e_i, ss) == tuple(int(x) for x in stop_after[3:].split("_"))):
                    p7s.close()
                    _finish(em, nc, stack, y_out, dbg, dbg_out, locals(),
                            x3_out_tok, ydma, ch_v, ch_s, ch_t)
                    return nc
            osl_free[uo] = ch_t.tok()
        hp_guard = ch_t.tok()
    p7s.close()
    _finish(em, nc, stack, y_out, dbg, dbg_out, locals(), x3_out_tok, ydma, ch_v, ch_s, ch_t)
    return nc


def _emit_z(em, mm, bank, sop, vop, gdma, W_ip, WIPR, wip_state, UALL, u_all,
            bank_rd, bank_wr, SZR, szr, pair, zoff, szfree, ch_t, ch_s):
    toks = []
    for k in range(8):
        sl_i = wip_state["idx"] % 8
        wip_state["idx"] += 1
        wtok = gdma(WIPR[:, sl_i * 128:(sl_i + 1) * 128], W_ip[pair, k, :, :],
                    f"wipr{sl_i}", waits=(wip_state["free"][sl_i],))
        for ss in range(2):
            bi = 6 + ss
            w = [wtok]
            if k == 0:
                w += [u_all, bank_rd.get(bi)]
            t = mm(bank(bi, 128, 512), WIPR[:, sl_i * 128:(sl_i + 1) * 128],
                   UALL[:, k * 2051 + zoff + ss * 512: k * 2051 + zoff + ss * 512 + 512],
                   start=(k == 0), stop=(k == 7), waits=tuple(w),
                   inc=(k == 7 or ss == 1))
            if k == 7:
                bank_wr[bi] = t
        wip_state["free"][sl_i] = ch_t.tok()
    for ss in range(2):
        bi = 6 + ss
        tok = sop(lambda e, bi=bi, ss=ss, szr=szr: e.activation(
            SZR[:, szr * 1024 + ss * 512: szr * 1024 + ss * 512 + 512],
            bank(bi, 128, 512), AF.Silu), waits=(bank_wr[bi], szfree))
        bank_rd[bi] = tok
    return ch_s.tok()


def _finish(em, nc, stack, y_out, dbg, dbg_out, lcl, final_tok, ydma, ch_v, ch_s, ch_t):
    ldy = lcl["ldy"]
    if dbg is not None:
        name, rrows, rcols = dbg
        tsrc = lcl[name]
        em.emit("g", lambda e: e.dma_start(out=dbg_out[:, :], in_=tsrc[0:rrows, 0:rcols]),
                (final_tok, ch_v.tok(), ch_s.tok(), ch_t.tok()),
                ((ldy, 16),))
    em.emit("y", lambda e: e.wait_ge(ldy.sem, ldy.n), (final_tok,), ())
    with nc.Block() as block:
        em.realize(block)
    stack.close()


def kernel(**inputs):
    in_maps = host_prepare(inputs)
    nc = build_program()
    res = run_bass_kernel_spmd(nc, in_maps, core_ids=list(range(NCORE)))
    out = np.zeros((B, S, D), np.float32)
    for c in range(NCORE):
        b, hf = c // 2, c % 2
        out[b, hf * T:(hf + 1) * T, :] = res.results[c]["y"].T
    return out


# version 05
# speedup vs baseline: 1.0099x; 1.0099x over previous
import contextlib
import numpy as np
import concourse.bass as bass
import concourse.mybir as mybir
from concourse.bass_utils import run_bass_kernel_spmd

F32 = mybir.dt.float32
BF16 = mybir.dt.bfloat16
OP = mybir.AluOpType
AF = mybir.ActivationFunctionType
AX = mybir.AxisListType

B, S, D = 4, 4096, 1024
T = 2048
HALO = 3
TH = T + HALO
NCORE = 8
DC = 8
DIN = 2048
NH = 32
HP = 64
DST = 128
Q = 128
NCH = 16
INTER0P = 2816
DHID = 4096
NEG = -1e30
GROUPS = [[0, 1], [2, 3], [4, 5], [6, 7]]

NS_OFF = [0, 512, 1024, 1536, 1539]
NS_W = [512, 512, 512, 512, 512]


class Chan:
    def __init__(self, sem):
        self.sem = sem
        self.n = 0

    def tok(self):
        return (self, self.n)


class Emitter:
    ENGS = ("t", "v", "s", "g", "y")

    def __init__(self):
        self.ops = {e: [] for e in self.ENGS}

    def emit(self, eng, fn, waits=(), incs=()):
        ws = []
        for w in waits:
            if w is None:
                continue
            ch, v = w
            if v > 0:
                ws.append((ch, v))
        for ch, v in incs:
            ch.n += v
        self.ops[eng].append((fn, tuple(ws), tuple(incs)))

    def realize(self, block):
        def mk(key):
            def body(eng):
                last = {}
                for fn, waits, incs in self.ops[key]:
                    for ch, v in waits:
                        if last.get(id(ch), -1) >= v:
                            continue
                        eng.wait_ge(ch.sem, v)
                        last[id(ch)] = v
                    ins = fn(eng)
                    for ch, v in incs:
                        ins = ins.then_inc(ch.sem, v)
            return body

        block.sync(mk("y"))
        block.tensor(mk("t"))
        block.vector(mk("v"))
        block.scalar(mk("s"))
        block.gpsimd(mk("g"))


def _tile_k(w, kc):
    K, M = w.shape
    assert K == kc * 128
    return np.ascontiguousarray(w.reshape(kc, 128, M))


def _pad_cols(w, m_to):
    out = np.zeros((w.shape[0], m_to), dtype=w.dtype)
    out[:, : w.shape[1]] = w
    return out


def _pad_rows(w, k_to):
    out = np.zeros((k_to, w.shape[1]), dtype=w.dtype)
    out[: w.shape[0], :] = w
    return out


def _oslab(wT, kc, oc):
    K, M = wT.shape
    assert K == kc * 128 and M == oc * 128
    a = wT.reshape(kc, 128, oc, 128).transpose(2, 1, 0, 3)
    return np.ascontiguousarray(a.reshape(oc, 128, kc * 128))


def _h2mv(H2B, k, no):
    return H2B[:, k * T + no: k * T + no + 512]


def host_prepare(inputs):
    f32 = np.float32
    x0 = np.asarray(inputs["L0_attn_hidden_states"], f32)

    ipT = np.ascontiguousarray(np.asarray(inputs["L0_attn_in_proj_w"], f32).T)
    ipTp = _pad_cols(ipT, 35 * 128)
    W_ip = np.ascontiguousarray(ipTp.reshape(8, 128, 35, 128).transpose(2, 0, 1, 3))

    convw = np.asarray(inputs["L0_attn_conv_w"], f32)
    conv_diag = np.zeros((18, 4, 128, 128), f32)
    for c in range(18):
        for t in range(4):
            np.fill_diagonal(conv_diag[c, t], convw[c * 128:(c + 1) * 128, t])
    conv_b = np.ascontiguousarray(np.asarray(inputs["L0_attn_conv_b"], f32).reshape(18, 128).T)

    dt_bias = np.tile(np.asarray(inputs["L0_attn_dt_bias"], f32).reshape(NH, 1), (4, 1))
    A_log = np.tile(np.asarray(inputs["L0_attn_A_log"], f32).reshape(NH, 1), (4, 1))
    Dvec = np.asarray(inputs["L0_attn_D"], f32)
    D_diag = np.zeros((NH, 128, 128), f32)
    for h in range(NH):
        np.fill_diagonal(D_diag[h], Dvec[h])

    ssm_w = np.ascontiguousarray(np.asarray(inputs["L0_attn_ssm_norm_w"], f32).reshape(16, 128).T)
    norm_w = np.ascontiguousarray(np.asarray(inputs["L0_attn_norm_w"], f32).reshape(8, 128).T)
    post_w = np.ascontiguousarray(np.asarray(inputs["L0_attn_post_norm_w"], f32).reshape(8, 128).T)

    W_op = _tile_k(np.ascontiguousarray(np.asarray(inputs["L0_attn_out_proj_w"], f32).T), 16)
    W_g = _tile_k(_pad_cols(np.ascontiguousarray(np.asarray(inputs["L0_attn_gate_w"], f32).T), INTER0P), 8)
    W_u = _tile_k(_pad_cols(np.ascontiguousarray(np.asarray(inputs["L0_attn_up_w"], f32).T), INTER0P), 8)
    W_d = _tile_k(_pad_rows(np.ascontiguousarray(np.asarray(inputs["L0_attn_down_w"], f32).T), INTER0P), 22)

    Win = _tile_k(np.asarray(inputs["L0_ffn_Win"], f32), 8)
    bin_ = np.asarray(inputs["L0_ffn_bin_"], f32).reshape(64, 1)
    W1 = np.asarray(inputs["L0_ffn_W1"], f32)
    b1 = np.asarray(inputs["L0_ffn_b1"], f32)
    W2 = np.asarray(inputs["L0_ffn_W2"], f32)
    b2 = np.asarray(inputs["L0_ffn_b2"], f32).reshape(64, 1)
    Wout0 = np.ascontiguousarray(np.asarray(inputs["L0_ffn_Wout"], f32))
    bout0 = np.ascontiguousarray(np.asarray(inputs["L0_ffn_bout"], f32).reshape(8, 128).T)

    wqkv = np.asarray(inputs["L1_attn_w_qkv"], f32)
    W_qk, W_v = {}, {}
    for hg in range(2):
        blocks = []
        for j in range(4):
            h0, h1 = hg * 8 + 2 * j, hg * 8 + 2 * j + 1
            blocks.append(np.concatenate([wqkv[:, h0, 0:64], wqkv[:, h1, 0:64]], axis=1))
            blocks.append(np.concatenate([wqkv[:, h0, 64:128], wqkv[:, h1, 64:128]], axis=1))
        W_qk[hg] = _tile_k(np.concatenate(blocks, axis=1), 8)
        v = np.concatenate([wqkv[:, h, 128:192] for h in range(hg * 8, hg * 8 + 8)], axis=1)
        W_v[hg] = _tile_k(v, 8)
    W_wo = _tile_k(np.asarray(inputs["L1_attn_w_out"], f32).reshape(1024, 1024), 8)

    W_t1 = _oslab(np.ascontiguousarray(np.asarray(inputs["L1_ffn_w1_w"], f32).T), 8, 32)
    W_t2 = _oslab(np.ascontiguousarray(np.asarray(inputs["L1_ffn_w2_w"], f32).T), 8, 32)
    W_h1 = _oslab(np.ascontiguousarray(np.asarray(inputs["L1_ffn_h1_w"], f32).T), 32, 32)
    W_h2 = _oslab(np.ascontiguousarray(np.asarray(inputs["L1_ffn_h2_w"], f32).T), 32, 32)
    W_t3 = _oslab(np.ascontiguousarray(np.asarray(inputs["L1_ffn_w3_w"], f32).T), 32, 8)
    s_t1 = np.ascontiguousarray(np.asarray(inputs["L1_ffn_w1_s"], f32).reshape(32, 128).T)
    s_t2 = np.ascontiguousarray(np.asarray(inputs["L1_ffn_w2_s"], f32).reshape(32, 128).T)
    s_h1 = np.ascontiguousarray(np.asarray(inputs["L1_ffn_h1_s"], f32).reshape(32, 128).T)
    s_h2 = np.ascontiguousarray(np.asarray(inputs["L1_ffn_h2_s"], f32).reshape(32, 128).T)
    s_t3 = np.ascontiguousarray(np.asarray(inputs["L1_ffn_w3_s"], f32).reshape(8, 128).T)

    ident = np.eye(128, dtype=f32)
    ltri01 = np.triu(np.ones((128, 128), f32))
    attn_mask = np.zeros((4, 128, 512), f32)
    for d in range(4):
        for k in range(128):
            attn_mask[d, k, : min(d * 128 + k, 512)] = NEG

    shared = dict(
        W_ip=W_ip, conv_diag=conv_diag, conv_b=conv_b, dt_bias=dt_bias,
        A_log=A_log, D_diag=D_diag, ssm_w=ssm_w, norm_w=norm_w, post_w=post_w,
        W_op=W_op, W_g=W_g, W_u=W_u, W_d=W_d,
        Win=Win, bin_=bin_, W1=W1, b1=b1, W2=W2, b2=b2, Wout0=Wout0, bout0=bout0,
        W_wo=W_wo, W_t1=W_t1, W_t2=W_t2, W_h1=W_h1, W_h2=W_h2, W_t3=W_t3,
        s_t1=s_t1, s_t2=s_t2, s_h1=s_h1, s_h2=s_h2, s_t3=s_t3,
        ident=ident, ltri01=ltri01, attn_mask=attn_mask,
    )

    in_maps = []
    for c in range(NCORE):
        b, hf = c // 2, c % 2
        if hf == 0:
            xs = np.zeros((TH, D), np.float32)
            xs[HALO:] = x0[b, 0:T]
        else:
            xs = x0[b, T - HALO: T - HALO + TH]
        m = dict(shared)
        m["x"] = np.ascontiguousarray(xs.T)
        m["W_qkv_qk"] = W_qk[hf]
        m["W_qkv_v"] = W_v[hf]
        m["uinit"] = np.full((128, 1), float(hf), np.float32)
        m["uinv"] = np.full((128, 1), float(1 - hf), np.float32)
        in_maps.append(m)
    return in_maps


def build_program(dbg=None, stop_after=None):
    nc = bass.Bass("TRN2", target_bir_lowering=False, debug=False, detect_race_conditions=False)

    def din(name, shape):
        return nc.dram_tensor(name, list(shape), F32, kind="ExternalInput").ap()

    x_in = din("x", (D, TH))
    W_ip = din("W_ip", (35, 8, 128, 128))
    conv_diag = din("conv_diag", (18, 4, 128, 128))
    conv_b = din("conv_b", (128, 18))
    dt_bias = din("dt_bias", (128, 1))
    A_log = din("A_log", (128, 1))
    D_diag_in = din("D_diag", (NH, 128, 128))
    ssm_w = din("ssm_w", (128, 16))
    norm_w = din("norm_w", (128, 8))
    post_w = din("post_w", (128, 8))
    W_op = din("W_op", (16, 128, 1024))
    W_g = din("W_g", (8, 128, INTER0P))
    W_u = din("W_u", (8, 128, INTER0P))
    W_d = din("W_d", (22, 128, 1024))
    Win = din("Win", (8, 128, 64))
    bin_in = din("bin_", (64, 1))
    W1_in = din("W1", (64, 16))
    b1_in = din("b1", (64, 16))
    W2_in = din("W2", (64, 16))
    b2_in = din("b2", (64, 1))
    Wout0 = din("Wout0", (64, 1024))
    bout0 = din("bout0", (128, 8))
    W_qkv_qk = din("W_qkv_qk", (8, 128, 1024))
    W_qkv_v = din("W_qkv_v", (8, 128, 512))
    W_wo = din("W_wo", (8, 128, 1024))
    W_t1 = din("W_t1", (32, 128, 1024))
    W_t2 = din("W_t2", (32, 128, 1024))
    W_h1 = din("W_h1", (32, 128, 4096))
    W_h2 = din("W_h2", (32, 128, 4096))
    W_t3 = din("W_t3", (8, 128, 4096))
    s_t1 = din("s_t1", (128, 32))
    s_t2 = din("s_t2", (128, 32))
    s_h1 = din("s_h1", (128, 32))
    s_h2 = din("s_h2", (128, 32))
    s_t3 = din("s_t3", (128, 8))
    ident_in = din("ident", (128, 128))
    ltri_in = din("ltri01", (128, 128))
    amask_in = din("attn_mask", (4, 128, 512))
    uinit_in = din("uinit", (128, 1))
    uinv_in = din("uinv", (128, 1))

    y_out = nc.dram_tensor("y", [D, T], F32, kind="ExternalOutput").ap()
    dbg_out = None
    if dbg is not None:
        dbg_out = nc.dram_tensor("dbg", [dbg[1], dbg[2]], F32, kind="ExternalOutput").ap()

    xtm_d = nc.dram_tensor("xtm_d", [16, 128, 2048], BF16).ap()
    schunk_d = nc.dram_tensor("schunk_d", [NCH, 128, DIN], F32).ap()
    sAG_in = nc.dram_tensor("sAG_in", [128, DIN], F32).ap()
    sAG_out = nc.dram_tensor("sAG_out", [256, DIN], F32).ap()
    x1u_d = nc.dram_tensor("x1u_d", [D, T], F32).ap()
    t1AG_in = nc.dram_tensor("t1AG_in", [D, T], BF16).ap()
    t1AG_out = nc.dram_tensor("t1AG_out", [2 * D, T], BF16).ap()
    oAG_in = nc.dram_tensor("oAG_in", [512, S], BF16).ap()
    oAG_out = nc.dram_tensor("oAG_out", [1024, S], BF16).ap()
    x1p_d = nc.dram_tensor("x1p_d", [D, T], F32).ap()
    x2_d = nc.dram_tensor("x2_d", [D, T], F32).ap()

    em = Emitter()
    stack = contextlib.ExitStack()

    def sbuf(name, shape, dtype, st=None):
        return (st or stack).enter_context(nc.sbuf_tensor(name, list(shape), dtype))

    def chan(name):
        return Chan(stack.enter_context(nc.semaphore(name)))

    ch_t = chan("c_t")
    ch_v = chan("c_v")
    ch_s = chan("c_s")
    ldy = chan("ldy")
    cc = chan("cc")
    _dch = {}

    def dch(key):
        if key not in _dch:
            _dch[key] = chan(f"d_{key}")
        return _dch[key]

    def gdma(out, in_, key, waits=(), accum=False):
        ch = dch(key) if isinstance(key, str) else key
        if accum:
            em.emit("g", lambda e, o=out, i=in_: e.dma_start(out=o, in_=i, accum_op=OP.add),
                    waits, ((ch, 16),))
        else:
            em.emit("g", lambda e, o=out, i=in_: e.dma_start(out=o, in_=i),
                    waits, ((ch, 16),))
        return ch.tok()

    def ydma(out, in_, waits=()):
        em.emit("y", lambda e, o=out, i=in_: e.dma_start(out=o, in_=i),
                waits, ((ldy, 16),))
        return ldy.tok()

    def mm(out, lhsT, rhs, start, stop, waits=(), inc=False, tp=None):
        em.emit("t", lambda e, o=out, l=lhsT, r=rhs, a=start, b=stop, t=tp:
                e.matmul(o, l, r, start=a, stop=b, tile_position=t),
                waits, ((ch_t, 1),) if inc else ())
        return ch_t.tok() if inc else None

    def tr(out, in_, idn, waits=(), inc=True):
        em.emit("t", lambda e, o=out, i=in_, d=idn: e.transpose(o, i, d),
                waits, ((ch_t, 1),) if inc else ())
        return ch_t.tok() if inc else None

    def vop(fn, waits=(), inc=True):
        em.emit("v", fn, waits, ((ch_v, 1),) if inc else ())
        return ch_v.tok() if inc else None

    def sop(fn, waits=(), inc=True):
        em.emit("s", fn, waits, ((ch_s, 1),) if inc else ())
        return ch_s.tok() if inc else None

    identb = sbuf("identb", [128, 128], BF16)
    identF = sbuf("identF", [64, 64], F32)
    ltrib = sbuf("ltrib", [128, 128], F32)

    onesF = sbuf("onesF", [1, 128], F32)
    onescolb = sbuf("onescolb", [128, 1], BF16)
    onescolF = sbuf("onescolF", [128, 1], F32)
    nwcol = sbuf("nwcol", [128, 8], F32)
    pwcol = sbuf("pwcol", [128, 8], F32)
    swcol = sbuf("swcol", [128, 16], F32)
    cbcol = sbuf("cbcol", [128, 18], F32)
    dtbcol = sbuf("dtbcol", [128, 1], F32)
    nAcol = sbuf("nAcol", [128, 1], F32)
    uinitcol = sbuf("uinitcol", [128, 1], F32)
    uinvcol = sbuf("uinvcol", [128, 1], F32)
    mean_r = sbuf("mean_r", [1, TH], F32)
    rc_r = sbuf("rc_r", [1, TH], F32)
    us_r = sbuf("us_r", [1, TH], F32)
    pL0 = contextlib.ExitStack()
    UALL = sbuf("UALL", [128, DC * TH], BF16, pL0)
    CFM = sbuf("CFM", [128, T], BF16, pL0)
    hrA = sbuf("hrA", [32, T], F32, pL0)
    hrB = sbuf("hrB", [32, T], F32, pL0)
    hrD = sbuf("hrD", [32, T], F32, pL0)
    cgT = sbuf("cgT", [32, 16], F32, pL0)
    cgrep = sbuf("cgrep", [1, 512], F32, pL0)
    decCols = sbuf("decCols", [128, 512], F32, pL0)
    dT = sbuf("dT", [128, NCH * 32], F32, pL0)
    wAll = sbuf("wAll", [128, NCH * 32], F32, pL0)
    P1B = sbuf("P1B", [128, NCH * 128], F32, pL0)
    Scur = sbuf("Scur", [128, 128], F32, pL0)
    Sinit = sbuf("Sinit", [128, DIN], F32, pL0)
    WIPR = sbuf("WIPR", [128, 8 * 128], BF16, pL0)
    p1s2 = contextlib.ExitStack()
    var_r = sbuf("var_r", [1, TH], F32, p1s2)
    scr_r = sbuf("scr_r", [1, TH], F32, p1s2)
    BFM = sbuf("BFM", [128, T], BF16, p1s2)
    BTM = sbuf("BTM", [128, NCH * 128], BF16, p1s2)
    hrC = sbuf("hrC", [32, T], F32, p1s2)
    XBCR = sbuf("XBCR", [128, 2 * TH], BF16, p1s2)
    CDG = sbuf("CDG", [128, 2 * 512], BF16, p1s2)
    decrow = sbuf("decrow", [1, 512], F32, p1s2)
    cgeCols = sbuf("cgeCols", [128, 512], F32, p1s2)
    XFR = sbuf("XFR", [128, TH], BF16, p1s2)
    XCV = sbuf("XCV", [128, T], BF16, p1s2)
    XTMp = sbuf("XTMp", [128, 2048], BF16, p1s2)
    BDR = sbuf("BDR", [128, 2 * 128], BF16, p1s2)
    SEV = sbuf("SEV", [128, 2 * 128], F32, p1s2)

    psA = stack.enter_context(nc.psum_tensor("psA", [128, 2048], F32))
    psB = stack.enter_context(nc.psum_tensor("psB", [128, 2048], F32))

    def bank(i, rows_=128, cols=512, roff=0):
        src = psA if i < 4 else psB
        return src[roff:roff + rows_, (i % 4) * 512:(i % 4) * 512 + cols]

    def bankb(i, rows_=128, cols=128):
        src = psA if i < 4 else psB
        return src[0:rows_, (i % 4) * 512:(i % 4) * 512 + 256].bitcast(BF16)[:, 0:cols]

    bank_rd = {}
    bank_wr = {}

    gdma(identF[:, :], ident_in[0:64, 0:64], "consts")
    for dst, src in ((identb, ident_in), (ltrib, ltri_in), (nwcol, norm_w),
                     (pwcol, post_w), (swcol, ssm_w), (cbcol, conv_b),
                     (dtbcol, dt_bias), (nAcol, A_log), (uinitcol, uinit_in),
                     (uinvcol, uinv_in)):
        gdma(dst[:, :], src[:, :], "consts")
    consts = dch("consts").tok()
    vop(lambda e: e.memset(onesF[:, :], 1.0))
    vop(lambda e: e.memset(onescolb[:, :], 1.0))
    vop(lambda e: e.memset(onescolF[:, :], 1.0))
    ones_tok = ch_v.tok()
    sop(lambda e: e.activation(nAcol[:, :], nAcol[:, :], AF.Exp), waits=(consts,))
    nA_tok = vop(lambda e: e.tensor_scalar(nAcol[:, :], nAcol[:, :], -1.0, None, OP.mult),
                 waits=(ch_s.tok(),))

    p0s = contextlib.ExitStack()
    X0F = sbuf("X0F", [128, DC * 512], F32, p0s)
    X0B = sbuf("X0B", [128, DC * 512], BF16, p0s)
    SQB = sbuf("SQB", [128, DC * 512], BF16, p0s)
    UTMP = sbuf("UTMP", [128, 512], F32, p0s)
    x0_free = [None, None]
    u_tok = [None] * 5
    prev_stats = None
    for g in range(5):
        go, gw = NS_OFF[g], NS_W[g]
        sl = 0
        x0tok = gdma(X0F[:, sl:sl + DC * gw],
                     x_in.rearrange("(c p) t -> p c t", p=128)[:, :, go:go + gw],
                     "x0f", waits=(x0_free[0], x0_free[1]))
        xbtok = vop(lambda e, gw=gw, sl=sl: e.tensor_copy(
            X0B[:, :DC * gw], X0F[:, sl:sl + DC * gw]), waits=(x0tok, prev_stats))
        sqtok = sop(lambda e, gw=gw, sl=sl: e.activation(
            SQB[:, :DC * gw], X0F[:, sl:sl + DC * gw], AF.Square),
            waits=(x0tok, prev_stats))
        b0, b1v = bank(0, 1, gw), bank(1, 1, gw)
        for c in range(DC):
            w = [xbtok, ones_tok, bank_rd.get(0)] if c == 0 else []
            mm(b0, onescolb[:, :], X0B[:, c * gw:(c + 1) * gw],
               start=(c == 0), stop=(c == DC - 1), waits=tuple(w))
        for c in range(DC):
            w = [sqtok, bank_rd.get(1)] if c == 0 else []
            stats_tok = mm(b1v, onescolb[:, :], SQB[:, c * gw:(c + 1) * gw],
                           start=(c == 0), stop=(c == DC - 1), waits=tuple(w),
                           inc=(c == DC - 1))
        vop(lambda e, go=go, gw=gw, b0=b0: e.tensor_scalar(
            mean_r[0:1, go:go + gw], b0, 1.0 / D, None, OP.mult), waits=(stats_tok,))
        vop(lambda e, go=go, gw=gw, b1v=b1v: e.tensor_scalar(
            scr_r[0:1, go:go + gw], b1v, 1.0 / D, None, OP.mult))
        bank_rd[0] = bank_rd[1] = ch_v.tok()
        stats_rows_tok = ch_v.tok()
        vop(lambda e, go=go, gw=gw: e.tensor_mul(
            var_r[0:1, go:go + gw], mean_r[0:1, go:go + gw], mean_r[0:1, go:go + gw]))
        vop(lambda e, go=go, gw=gw: e.tensor_sub(
            var_r[0:1, go:go + gw], scr_r[0:1, go:go + gw], var_r[0:1, go:go + gw]))
        var_tok = vop(lambda e, go=go, gw=gw: e.tensor_scalar(
            var_r[0:1, go:go + gw], var_r[0:1, go:go + gw], 0.0, None, OP.max))
        vop(lambda e, go=go, gw=gw: e.tensor_scalar(
            rc_r[0:1, go:go + gw], var_r[0:1, go:go + gw], 0.0, 1e-5, OP.max, OP.add),
            waits=(var_tok,))
        sop(lambda e, go=go, gw=gw: e.activation(
            rc_r[0:1, go:go + gw], rc_r[0:1, go:go + gw], AF.Ln),
            waits=(ch_v.tok(),))
        rc_tok = sop(lambda e, go=go, gw=gw: e.activation(
            rc_r[0:1, go:go + gw], rc_r[0:1, go:go + gw], AF.Exp, scale=-0.5))
        vop(lambda e, go=go, gw=gw: e.tensor_mul(
            scr_r[0:1, go:go + gw], var_r[0:1, go:go + gw], rc_r[0:1, go:go + gw]),
            waits=(rc_tok,))
        q_tok = vop(lambda e, go=go, gw=gw: e.tensor_mul(
            scr_r[0:1, go:go + gw], scr_r[0:1, go:go + gw], rc_r[0:1, go:go + gw]))
        vop(lambda e, go=go, gw=gw: e.tensor_scalar(
            scr_r[0:1, go:go + gw], scr_r[0:1, go:go + gw], 1e-6, None, OP.add),
            waits=(q_tok,))
        sop(lambda e, go=go, gw=gw: e.activation(
            scr_r[0:1, go:go + gw], scr_r[0:1, go:go + gw], AF.Ln),
            waits=(ch_v.tok(),))
        e_tok = sop(lambda e, go=go, gw=gw: e.activation(
            scr_r[0:1, go:go + gw], scr_r[0:1, go:go + gw], AF.Exp, scale=-0.5))
        us_tok = vop(lambda e, go=go, gw=gw: e.tensor_mul(
            us_r[0:1, go:go + gw], scr_r[0:1, go:go + gw], rc_r[0:1, go:go + gw]),
            waits=(e_tok,))
        b2v, b3v = bank(2, 128, gw), bank(3, 128, gw)
        w = [ones_tok, bank_rd.get(2), stats_rows_tok]
        mm(b2v, onesF[0:1, :], mean_r[0:1, go:go + gw], start=True, stop=True,
           waits=tuple(w))
        w = [us_tok, bank_rd.get(3)]
        bc_tok = mm(b3v, onesF[0:1, :], us_r[0:1, go:go + gw], start=True, stop=True,
                    waits=tuple(w), inc=True)
        for c in range(DC):
            vop(lambda e, c=c, gw=gw, sl=sl, b2v=b2v: e.tensor_sub(
                UTMP[:, 0:gw], X0F[:, sl + c * gw: sl + (c + 1) * gw], b2v),
                waits=(bc_tok,) if c == 0 else ())
            vop(lambda e, c=c, go=go, gw=gw, b3v=b3v: e.scalar_tensor_tensor(
                UALL[:, c * TH + go: c * TH + go + gw],
                UTMP[:, 0:gw], nwcol[:, c:c + 1], b3v, OP.mult, OP.mult),
                waits=(consts,) if c == 0 else ())
        u_tok[g] = ch_v.tok()
        prev_stats = stats_tok
        bank_rd[2] = bank_rd[3] = ch_v.tok()
        x0_free[0] = ch_v.tok()
        x0_free[1] = ch_s.tok()
    u_all = ch_v.tok()
    p0s.close()
    for i in range(8):
        bank_rd[i] = None
    bank_rd[0] = bank_rd[1] = bank_rd[2] = bank_rd[3] = u_all

    if stop_after == "P0":
        DUMPS = sbuf("DUMPS", [128, 64], F32, p1s2)
        vop(lambda e: e.tensor_copy(DUMPS[0:1, 0:3], psA[0:1, 0:3]), waits=(u_all,))
        vop(lambda e: e.tensor_copy(DUMPS[0:1, 4:7], psA[0:1, 512:515]))
        vop(lambda e: e.tensor_copy(DUMPS[:, 8:32], X0B[:, 0:24]))
        vop(lambda e: e.tensor_copy(DUMPS[:, 32:56], SQB[:, 0:24]))
        dtk = vop(lambda e: e.tensor_copy(DUMPS[:, 56:57], onescolb[:, :]))
        gdma(y_out[0:1, :], mean_r[0:1, 3:2051], "dump", waits=(u_all,))
        gdma(y_out[1:2, :], us_r[0:1, 3:2051], "dump")
        gdma(y_out[2:3, :], rc_r[0:1, 3:2051], "dump")
        gdma(y_out[3:4, :], var_r[0:1, 3:2051], "dump")
        gdma(y_out[4:5, :], scr_r[0:1, 3:2051], "dump")
        gdma(y_out[576:704, :], UALL[:, 3:2051], "dump")
        gdma(y_out[704:832, :], UALL[:, 7 * TH + 3:7 * TH + 2051], "dump")
        d9 = gdma(y_out[320:448, 0:64], DUMPS[:, :], "dump", waits=(dtk,))
        p1s2.close()
        pL0.close()
        _finish(em, nc, stack, y_out, dbg, dbg_out, locals(), d9, ydma, ch_v, ch_s, ch_t)
        return nc

    wip_state = {"free": [u_all] * 8, "idx": 0}

    def inproj_mtile(mt, evac, slices=range(5), bksel=(4, 5, 6, 7, 0), dmaslot=1):
        toks = []
        for k in range(8):
            sl_i = wip_state["idx"] % 8
            wip_state["idx"] += 1
            wtok = gdma(WIPR[:, sl_i * 128:(sl_i + 1) * 128], W_ip[mt, k, :, :],
                        f"wipr{sl_i}", waits=(wip_state["free"][sl_i],))
            toks.append((sl_i, wtok))
        for ki, (sl_i, wtok) in enumerate(toks):
            for ng in slices:
                no, nw = NS_OFF[ng], NS_W[ng]
                bi = bksel[ng if len(bksel) == 5 else list(slices).index(ng)]
                w = [wtok]
                if ki == 0:
                    w.append(u_all)
                    w.append(bank_rd.get(bi))
                t = mm(bank(bi, 128, nw), WIPR[:, sl_i * 128:(sl_i + 1) * 128],
                       UALL[:, ki * TH + no: ki * TH + no + nw],
                       start=(ki == 0), stop=(ki == 7),
                       waits=tuple(w), inc=(ki == 7 or ng == list(slices)[-1]))
                if ki == 7:
                    bank_wr[bi] = t
            wip_state["free"][sl_i] = ch_t.tok()
        for ng in slices:
            bi = bksel[ng if len(bksel) == 5 else list(slices).index(ng)]
            bank_rd[bi] = evac(ng, bank(bi, 128, NS_W[ng]), bank_wr[bi])

    cdg_free = [u_all, u_all]
    xbc_free = [None, None]

    def evac_to_ring(ring_slot):
        def evac(ng, bv, wr):
            no, nw = NS_OFF[ng], NS_W[ng]
            return vop(lambda e, no=no, nw=nw, bv=bv, rs=ring_slot: e.tensor_copy(
                XBCR[:, rs * TH + no: rs * TH + no + nw], bv),
                waits=(wr, xbc_free[ring_slot]) if ng == 0 else (wr,))
        return evac

    def emit_conv(cchunk, ring_slot, out_fn, bks=(1, 2)):
        dslot = cchunk % 2
        dtok = gdma(CDG[:, dslot * 512:(dslot + 1) * 512],
                    conv_diag.rearrange("c t p m -> c p t m")[cchunk, :, :, :],
                    f"cdg{dslot}", waits=(cdg_free[dslot],))
        xtok = ch_v.tok()
        last = None
        for ng in range(4):
            no = ng * 512
            bi = bks[ng % 2]
            cb = bank(bi, 128, 512)
            for tap in range(4):
                w = [dtok, xtok, bank_rd.get(bi)] if tap == 0 else []
                t = mm(cb, CDG[:, dslot * 512 + tap * 128: dslot * 512 + (tap + 1) * 128],
                       XBCR[:, ring_slot * TH + no + tap: ring_slot * TH + no + tap + 512],
                       start=(tap == 0), stop=(tap == 3), waits=tuple(w), inc=(tap == 3))
            bank_wr[bi] = t
            last = out_fn(ng, cb, no, 512, t)
            bank_rd[bi] = last
        cdg_free[dslot] = last
        xbc_free[ring_slot] = last
        return last

    for bc in range(2):
        mt = 32 + bc
        inproj_mtile(mt, evac_to_ring(bc % 2))
        xbc_tok = ch_v.tok()

        def bc_out(ng, cb, no, nw, wr, bc=bc):
            dst = BFM if bc == 0 else CFM
            return sop(lambda e, cb=cb, no=no, nw=nw, dst=dst, bc=bc: e.activation(
                dst[:, no: no + nw], cb, AF.Silu,
                bias=cbcol[:, 16 + bc:17 + bc]), waits=(wr,))
        emit_conv(16 + bc, bc % 2, bc_out)
    bc_done = ch_s.tok()
    def evac_dt(ng, bv, wr):
        no, nw = NS_OFF[ng], NS_W[ng]
        if ng == 0:
            return vop(lambda e, bv=bv: e.tensor_copy(hrA[0:32, 0:509], bv[0:32, 3:512]),
                       waits=(wr,))
        return vop(lambda e, bv=bv, no=no, nw=nw: e.tensor_copy(
            hrA[0:32, no - 3: no - 3 + nw], bv[0:32, 0:nw]), waits=(wr,))
    inproj_mtile(34, evac_dt)
    dtraw_tok = ch_v.tok()
    for tc in range(NCH):
        tb = bankb(7, 128, 128)
        w = [bc_done, consts, bank_rd.get(7)]
        t = tr(tb, BFM[:, tc * 128:(tc + 1) * 128], identb[:, :], waits=tuple(w))
        bank_rd[7] = vop(lambda e, tc=tc, tb=tb: e.tensor_copy(
            BTM[:, tc * 128:(tc + 1) * 128], tb), waits=(t,))
    btm_tok = ch_v.tok()

    vop(lambda e: e.tensor_scalar(hrA[:, :], hrA[:, :], dtbcol[0:32, :], None, OP.add),
        waits=(dtraw_tok, consts))
    dtb_tok = ch_v.tok()
    sop(lambda e: e.activation(hrC[:, :], hrA[:, :], AF.Abs), waits=(dtb_tok,))
    sop(lambda e: e.activation(hrC[:, :], hrC[:, :], AF.Exp, scale=-1.0))
    vop(lambda e: e.tensor_scalar(hrC[:, :], hrC[:, :], 1.0, None, OP.add),
        waits=(ch_s.tok(),))
    ln_tok = sop(lambda e: e.activation(hrC[:, :], hrC[:, :], AF.Ln),
                 waits=(ch_v.tok(),))
    vop(lambda e: e.tensor_relu(hrB[:, :], hrA[:, :]))
    dt_tok = vop(lambda e: e.tensor_add(hrB[:, :], hrB[:, :], hrC[:, :]), waits=(ln_tok,))
    dtA_tok = vop(lambda e: e.tensor_scalar(hrC[:, :], hrB[:, :], nAcol[0:32, :],
                                            None, OP.mult), waits=(dt_tok, nA_tok))
    lndt_tok = sop(lambda e: e.activation(hrB[:, :], hrB[:, :], AF.Ln), waits=(dtA_tok,))
    zr_tok = vop(lambda e: e.memset(hrA[:, :], 0.0))
    cg_tok = vop(lambda e: e.tensor_tensor_scan(
        hrD[:, :], hrC[:, :], hrA[:, :], 0.0, OP.add, OP.add), waits=(zr_tok,))
    d_tok = vop(lambda e: e.tensor_sub(hrB[:, :], hrB[:, :], hrD[:, :]),
                waits=(cg_tok, lndt_tok))

    cgT_tok = vop(lambda e: e.tensor_copy(
        cgT[:, :], hrD.rearrange("h (k q) -> h k q", q=Q)[:, :, Q - 1]), waits=(cg_tok,))
    cgrep_tok = gdma(cgrep[:, :], cgT[:, :], "rows0", waits=(cgT_tok,))
    vop(lambda e: e.memset(decrow[:, :], 0.0))
    z_tok = ch_v.tok()
    vop(lambda e: e.tensor_copy(
        decrow.rearrange("o (h k) -> o h k", k=16)[:, :, 1:16],
        cgrep.rearrange("o (h k) -> o h k", k=16)[:, :, 0:15]),
        waits=(cgrep_tok, z_tok))
    ddiff_tok = vop(lambda e: e.tensor_sub(decrow[:, :], cgrep[:, :], decrow[:, :]))
    decrow_tok = sop(lambda e: e.activation(decrow[:, :], decrow[:, :], AF.Exp),
                     waits=(ddiff_tok,))
    w = [decrow_tok, ones_tok, bank_rd.get(0)]
    t = mm(bank(0, 128, 512), onesF[:, :], decrow[:, :], start=True, stop=True,
           waits=tuple(w), inc=True)
    dec_tok = vop(lambda e: e.tensor_copy(decCols[:, :], bank(0, 128, 512)), waits=(t,))
    bank_rd[0] = dec_tok
    w = [cgrep_tok, ones_tok, bank_rd.get(1)]
    t = mm(bank(1, 128, 512), onesF[:, :], cgrep[:, :], start=True, stop=True,
           waits=tuple(w), inc=True)
    cge_tok = vop(lambda e: e.tensor_copy(cgeCols[:, :], bank(1, 128, 512)), waits=(t,))
    bank_rd[1] = cge_tok
    for k in range(NCH):
        tb = bank(2, 128, 32)
        w = [d_tok, consts, bank_rd.get(2)]
        t = tr(tb, hrB[0:32, k * Q:(k + 1) * Q], identF[0:32, 0:32], waits=tuple(w))
        bank_rd[2] = vop(lambda e, k=k, tb=tb: e.tensor_copy(
            dT[:, k * 32:(k + 1) * 32], tb), waits=(t,))
    dT_tok = ch_v.tok()
    for k in range(NCH):
        vop(lambda e, k=k: e.tensor_add(
            wAll[:, k * 32:(k + 1) * 32], dT[:, k * 32:(k + 1) * 32],
            cgeCols[:, k::16]), waits=(dT_tok, cge_tok) if k == 0 else ())
        sop(lambda e, k=k: e.activation(
            wAll[:, k * 32:(k + 1) * 32], wAll[:, k * 32:(k + 1) * 32], AF.Exp),
            waits=(ch_v.tok(),))
    wAll_tok = ch_s.tok()
    for k in range(NCH):
        if k == 0:
            vop(lambda e: e.tensor_copy(hrA[:, 0:Q], hrD[:, 0:Q]), waits=(cg_tok,))
        else:
            vop(lambda e, k=k: e.tensor_scalar(
                hrA[:, k * Q:(k + 1) * Q], hrD[:, k * Q:(k + 1) * Q],
                cgT[:, k - 1:k], None, OP.subtract),
                waits=(cgT_tok,) if k == 1 else ())
    edec_tok = sop(lambda e: e.activation(hrA[:, :], hrA[:, :], AF.Exp),
                   waits=(ch_v.tok(),))
    for k in range(NCH):
        bv = bank(3, 128, 128)
        w = [bc_done, bank_rd.get(3)]
        t = mm(bv, BFM[:, k * 128:(k + 1) * 128], CFM[:, k * 128:(k + 1) * 128],
               start=True, stop=True, waits=tuple(w), inc=True)
        bank_rd[3] = vop(lambda e, k=k, bv=bv: e.tensor_mul(
            P1B[:, k * 128:(k + 1) * 128], bv, ltrib[:, :]), waits=(t, consts))
    p1b_tok = ch_v.tok()

    if stop_after == "P2":
        p1s2.close()
        pL0.close()
        _finish(em, nc, stack, y_out, dbg, dbg_out, locals(),
                p1b_tok, ydma, ch_v, ch_s, ch_t)
        return nc

    xfr_free = None
    xcv_free = None
    xtmp_free = None
    sc_zero = None
    for pair in range(16):
        def evac_x(ng, bv, wr, _ff=None):
            no, nw = NS_OFF[ng], NS_W[ng]
            w = [wr] if ng else [wr, xfr_free]
            return sop(lambda e, no=no, nw=nw, bv=bv: e.activation(
                XFR[:, no:no + nw], bv, AF.Identity), waits=tuple(x for x in w if x))
        inproj_mtile(16 + pair, evac_x)
        xf_tok = ch_s.tok()
        def x_out(ng, cb, no, nw, wr, pair=pair):
            w = [wr] if ng else [wr, xcv_free]
            return sop(lambda e, cb=cb, no=no, nw=nw, pair=pair: e.activation(
                XCV[:, no:no + nw], cb, AF.Silu, bias=cbcol[:, pair:pair + 1]),
                waits=tuple(x for x in w if x))
        dslot = pair % 2
        dtok = gdma(CDG[:, dslot * 512:(dslot + 1) * 512],
                    conv_diag.rearrange("c t p m -> c p t m")[pair, :, :, :],
                    f"cdg{dslot}", waits=(cdg_free[dslot],))
        last = None
        for ng in range(4):
            no = ng * 512
            bi = (1, 2)[ng % 2]
            cb = bank(bi, 128, 512)
            for tap in range(4):
                w = [dtok, xf_tok, bank_rd.get(bi)] if tap == 0 else []
                t = mm(cb, CDG[:, dslot * 512 + tap * 128: dslot * 512 + (tap + 1) * 128],
                       XFR[:, no + tap: no + tap + 512],
                       start=(tap == 0), stop=(tap == 3), waits=tuple(w), inc=(tap == 3))
            last = x_out(ng, cb, no, 512, t)
            bank_rd[bi] = last
        cdg_free[dslot] = last
        xfr_free = last
        silu_tok = ch_s.tok()
        for tc in range(NCH):
            for hh in range(2):
                tb = bankb(3, 128, 64)
                w = [silu_tok, consts, bank_rd.get(3)]
                if tc == 0 and hh == 0 and xtmp_free is not None:
                    w.extend(xtmp_free)
                t = tr(tb, XCV[hh * 64:(hh + 1) * 64, tc * 128:(tc + 1) * 128],
                       identb[hh * 64:(hh + 1) * 64, hh * 64:(hh + 1) * 64],
                       waits=tuple(x for x in w if x))
                bank_rd[3] = vop(lambda e, tc=tc, hh=hh, tb=tb: e.tensor_copy(
                    XTMp[:, (tc * 2 + hh) * 64:(tc * 2 + hh) * 64 + 64], tb), waits=(t,))
        xtm_tok = ch_v.tok()
        xcv_free = xtm_tok
        ydma(xtm_d[pair, :, :], XTMp[:, :], waits=(xtm_tok,))
        xtm_st = ldy.tok()
        vop(lambda e: e.memset(Scur[:, :], 0.0), waits=(sc_zero,))
        sc_tok = ch_v.tok()
        sev_free = [None, None]
        for k in range(NCH):
            bd_toks = []
            for hh in range(2):
                h = pair * 2 + hh
                w = [btm_tok, wAll_tok]
                if bank_rd.get(90 + hh):
                    w.append(bank_rd[90 + hh])
                bd_toks.append(vop(lambda e, k=k, hh=hh, h=h: e.tensor_scalar(
                    BDR[:, hh * 128:(hh + 1) * 128], BTM[:, k * 128:(k + 1) * 128],
                    wAll[:, k * 32 + h: k * 32 + h + 1], None, OP.mult),
                    waits=tuple(w)))
            bi = 4 + (k % 2)
            bv = bank(bi, 128, 128)
            for hh in range(2):
                w = [bd_toks[hh], xtm_tok]
                if hh == 0 and bank_rd.get(bi):
                    w.append(bank_rd[bi])
                t = mm(bv[:, hh * 64:(hh + 1) * 64], BDR[:, hh * 128:(hh + 1) * 128],
                       XTMp[:, (k * 2 + hh) * 64:(k * 2 + hh) * 64 + 64],
                       start=True, stop=True, waits=tuple(w), inc=(hh == 1))
            bank_wr[bi] = t
            bank_rd[90] = bank_rd[91] = t
            es = k % 2
            ev = vop(lambda e, bv=bv, es=es: e.tensor_copy(
                SEV[:, es * 128:(es + 1) * 128], bv), waits=(t, sev_free[es]))
            bank_rd[bi] = ev
            ydma(schunk_d[k, :, pair * 128:(pair + 1) * 128],
                 SEV[:, es * 128:(es + 1) * 128], waits=(ev,))
            sev_free[es] = ldy.tok()
            for hh in range(2):
                h = pair * 2 + hh
                vop(lambda e, hh=hh, h=h, k=k: e.tensor_scalar(
                    Scur[:, hh * 64:(hh + 1) * 64], Scur[:, hh * 64:(hh + 1) * 64],
                    decCols[:, h * 16 + k: h * 16 + k + 1], None, OP.mult),
                    waits=(sc_tok, dec_tok) if k == 0 and hh == 0 else ())
            vop(lambda e, es=es: e.tensor_add(
                Scur[:, :], Scur[:, :], SEV[:, es * 128:(es + 1) * 128]), waits=(ev,))
            sc_tok = ch_v.tok()
        ydma(sAG_in[:, pair * 128:(pair + 1) * 128], Scur[:, :], waits=(sc_tok,))
        sc_zero = ldy.tok()
        xtmp_free = (xtm_st, ch_t.tok())
    sA_done = ldy.tok()

    em.emit("g", lambda e: e.collective_compute(
        "AllGather", OP.bypass, replica_groups=GROUPS,
        ins=[sAG_in[:, :]], outs=[sAG_out[:, :]]),
        (sA_done,), ((cc, 1),))
    ag1 = cc.tok()
    si_ld = gdma(Sinit[:, :], sAG_out[0:128, :], "sinit", waits=(ag1,))
    si_tok = vop(lambda e: e.tensor_scalar(Sinit[:, :], Sinit[:, :], uinitcol[:, :],
                                           None, OP.mult), waits=(si_ld, consts))

    if stop_after == "A":
        d1 = ydma(y_out[0:128, :], Sinit[:, :], waits=(si_tok,))
        d2 = ydma(y_out[128:160, :], hrD[:, :], waits=(cg_tok,))
        d3 = ydma(y_out[160:192, :], hrB[:, :], waits=(d_tok,))
        d4 = ydma(y_out[192:320, :], schunk_d[5, :, :], waits=(sA_done,))
        d5 = gdma(y_out[320:448, :], xtm_d[3, :, :], "dump", waits=(sA_done,))
        d6 = ydma(y_out[448:576, :], sAG_in[:, :], waits=(sA_done,))
        d7 = gdma(y_out[576:704, :], UALL[:, 3:2051], "dump", waits=(u_all,))
        d8 = gdma(y_out[704:832, :], BFM[:, :], "dump", waits=(bc_done,))
        d9 = ydma(y_out[832:864, :], hrA[:, :], waits=(dtraw_tok,))
        p1s2.close()
        pL0.close()
        _finish(em, nc, stack, y_out, dbg, dbg_out, locals(), d6, ydma, ch_v, ch_s, ch_t)
        return nc

    p1s2.close()
    p3s = contextlib.ExitStack()
    SLOTS = sbuf("SLOTS", [2, 6 * 1024], F32, p3s)
    MHALF = sbuf("MHALF", [128, 16 * 1024], BF16, p3s)
    SZR = sbuf("SZR", [128, 2 * 1024], BF16, p3s)
    XTMR = sbuf("XTMR", [128, 2 * 2048], BF16, p3s)
    SCH = sbuf("SCH", [128, 2 * 128], F32, p3s)
    SCARRY = sbuf("SCARRY", [128, 16 * 128], F32, p3s)
    E2R = sbuf("E2R", [128, 2 * 256], F32, p3s)
    SCB = sbuf("SCB", [128, 2 * 128], BF16, p3s)
    GTR = sbuf("GTR", [128, 2 * 256], BF16, p3s)
    CHR = sbuf("CHR", [128, 2 * 128], BF16, p3s)
    DDG = sbuf("DDG", [128, NH * 128], BF16, p3s)
    ddg_toks = []
    for h in range(NH):
        ddg_toks.append(gdma(DDG[:, h * 128:(h + 1) * 128], D_diag_in[h, :, :], "ddg",
                             waits=(sA_done,) if h == 0 else ()))
    ddg_tok = dch("ddg").tok()
    for r in range(2):
        vop(lambda e, r=r: e.memset(SLOTS[0:2, r * 3072: r * 3072 + 2048], 1.0))
    slots_ones = ch_v.tok()
    slot_free = [sA_done] * 2
    szr_free = [None, None]
    e2_free = [None, None]
    xtmr_free = [None, None]
    sch_free = [None, None]
    scb_free = [None, None]
    gt_free = [None, None]
    chr_free = [None, None]
    mh_guard = None

    SQH = sbuf("SQH", [128, 512], BF16, p3s)
    X0R = sbuf("X0R", [128, 2 * 512], F32, p3s)
    WSC = sbuf("WSC", [128, 2 * 128], F32, p3s)
    for half in range(2):
        k0 = half * 8
        for pair in range(16):
            sl_toks = []
            rr = [0, 0]
            for hh in range(2):
                h = pair * 2 + hh
                r = hh
                rr[hh] = r
                hc = slice(k0 * Q, k0 * Q + 1024)
                w = (slot_free[r], slots_ones, d_tok, edec_tok)
                gdma(SLOTS[1:2, r * 3072: r * 3072 + 1024], hrB[h:h + 1, hc], f"slr{r}", waits=w)
                gdma(SLOTS[0:1, r * 3072 + 1024: r * 3072 + 2048], hrD[h:h + 1, hc], f"slr{r}")
                sl_toks.append(gdma(SLOTS[0:1, r * 3072 + 2048: r * 3072 + 3072],
                                    hrA[h:h + 1, hc], f"slr{r}"))
            xr = pair % 2
            xt_tok = gdma(XTMR[:, xr * 2048:(xr + 1) * 2048], xtm_d[pair, :, :], f"xtmr{xr}",
                          waits=(sA_done, xtmr_free[xr]))
            zoff = k0 * Q + HALO
            szr = pair % 2
            sz_tok = _emit_z(em, mm, bank, sop, vop, gdma, W_ip, WIPR, wip_state, UALL,
                             u_all, bank_rd, bank_wr, SZR, szr, pair, zoff,
                             szr_free[szr], ch_t, ch_s)
            scur_init = None
            if half == 0:
                scur_init = vop(lambda e, pair=pair: e.tensor_copy(
                    SCARRY[:, pair * 128:(pair + 1) * 128], Sinit[:, pair * 128:(pair + 1) * 128]),
                    waits=(si_tok,))
            sc_tok2 = scur_init or ch_v.tok()
            for kk in range(8):
                k = k0 + kk
                sr = kk % 2
                sch_tok = gdma(SCH[:, sr * 128:(sr + 1) * 128],
                               schunk_d[k, :, pair * 128:(pair + 1) * 128], f"sch{sr}",
                               waits=(sA_done, sch_free[sr]))
                psT = bank(0, 128, 256)
                for hh in range(2):
                    h = pair * 2 + hh
                    r = rr[hh]
                    w = [sl_toks[hh], bank_rd.get(0)] if hh == 0 else [sl_toks[hh]]
                    t = mm(psT[:, hh * 128:(hh + 1) * 128],
                           SLOTS[0:2, r * 3072 + kk * Q: r * 3072 + (kk + 1) * Q],
                           SLOTS[0:2, r * 3072 + 1024 + kk * Q: r * 3072 + 1024 + (kk + 1) * Q],
                           start=True, stop=True, waits=tuple(w), inc=(hh == 1))
                bank_wr[0] = t
                er = kk % 2
                clamp_tok = vop(lambda e, psT=psT: e.tensor_scalar(
                    psT, psT, 80.0, None, OP.min), waits=(t,))
                e2_tok = sop(lambda e, er=er, psT=psT: e.activation(
                    E2R[:, er * 256:(er + 1) * 256], psT, AF.Exp),
                    waits=(clamp_tok, e2_free[er]))
                bank_rd[0] = e2_tok
                g_toks = []
                for hh in range(2):
                    g_toks.append(vop(lambda e, er=er, hh=hh, k=k: e.scalar_tensor_tensor(
                        GTR[:, er * 256 + hh * 128: er * 256 + (hh + 1) * 128],
                        E2R[:, er * 256 + hh * 128: er * 256 + (hh + 1) * 128],
                        1e30, P1B[:, k * 128:(k + 1) * 128], OP.min, OP.mult),
                        waits=(e2_tok, p1b_tok, gt_free[er])))
                e2_free[er] = g_toks[1]
                ebv = bank(1, 128, 128)
                w = [sl_toks[0], sl_toks[1], ones_tok, bank_rd.get(1), chr_free[0]]
                t = mm(ebv, onesF[:, :],
                       SLOTS[0:1, rr[0] * 3072 + 2048 + kk * Q: rr[0] * 3072 + 2048 + (kk + 1) * Q],
                       start=True, stop=True, waits=tuple(x for x in w if x), inc=True)
                ch_tokA = vop(lambda e, ebv=ebv, k=k: e.tensor_mul(
                    CHR[:, 0:128], CFM[:, k * 128:(k + 1) * 128], ebv),
                    waits=(t, bc_done))
                w2 = [ones_tok]
                t2 = mm(ebv, onesF[:, :],
                        SLOTS[0:1, rr[1] * 3072 + 2048 + kk * Q: rr[1] * 3072 + 2048 + (kk + 1) * Q],
                        start=True, stop=True, waits=(ch_tokA, chr_free[1]), inc=True)
                ch_tokB = vop(lambda e, ebv=ebv, k=k: e.tensor_mul(
                    CHR[:, 128:256], CFM[:, k * 128:(k + 1) * 128], ebv),
                    waits=(t2, bc_done))
                bank_rd[1] = ch_tokB
                ch_toks = [ch_tokA, ch_tokB]
                scb_tok = vop(lambda e, pair=pair, sr=sr: e.tensor_copy(
                    SCB[:, sr * 128:(sr + 1) * 128], SCARRY[:, pair * 128:(pair + 1) * 128]),
                    waits=(sc_tok2, scb_free[sr]))
                ybi = 2 + (kk // 4)
                yb = bank(ybi, 128, 512)
                for hh in range(2):
                    h = pair * 2 + hh
                    tp = (0, 64 * hh)
                    w = [g_toks[hh], xt_tok]
                    if kk % 4 == 0 and hh == 0:
                        w.append(bank_rd.get(ybi))
                    mm(yb[64 * hh: 64 * hh + 64, (kk % 4) * 128:(kk % 4) * 128 + 128],
                       XTMR[:, xr * 2048 + (k - 0) * 128 + hh * 64: xr * 2048 + k * 128 + hh * 64 + 64]
                       if False else
                       XTMR[:, xr * 2048 + (k * 2 + hh) * 64: xr * 2048 + (k * 2 + hh) * 64 + 64],
                       GTR[:, er * 256 + hh * 128: er * 256 + (hh + 1) * 128],
                       start=True, stop=False, waits=tuple(w), tp=tp)
                    mm(yb[64 * hh: 64 * hh + 64, (kk % 4) * 128:(kk % 4) * 128 + 128],
                       SCB[:, sr * 128 + hh * 64: sr * 128 + hh * 64 + 64],
                       CHR[:, hh * 128:(hh + 1) * 128],
                       start=False, stop=False, waits=(ch_toks[hh], scb_tok), tp=tp)
                    t = mm(yb[64 * hh: 64 * hh + 64, (kk % 4) * 128:(kk % 4) * 128 + 128],
                           XTMR[:, xr * 2048 + (k * 2 + hh) * 64: xr * 2048 + (k * 2 + hh) * 64 + 64],
                           DDG[:, (pair * 2 + hh) * 128:(pair * 2 + hh) * 128 + 128],
                           start=False, stop=True, waits=(ddg_tok,), tp=tp,
                           inc=(hh == 1))
                bank_wr[ybi] = t
                gt_free[er] = t
                chr_free[0] = chr_free[1] = t
                xtmr_free[xr] = t
                scb_free[sr] = t
                for hh in range(2):
                    h = pair * 2 + hh
                    vop(lambda e, pair=pair, hh=hh, h=h, k=k: e.tensor_scalar(
                        SCARRY[:, pair * 128 + hh * 64: pair * 128 + hh * 64 + 64],
                        SCARRY[:, pair * 128 + hh * 64: pair * 128 + hh * 64 + 64],
                        decCols[:, h * 16 + k: h * 16 + k + 1], None, OP.mult),
                        waits=(t, dec_tok))
                vop(lambda e, pair=pair, sr=sr: e.tensor_add(
                    SCARRY[:, pair * 128:(pair + 1) * 128],
                    SCARRY[:, pair * 128:(pair + 1) * 128],
                    SCH[:, sr * 128:(sr + 1) * 128]), waits=(sch_tok,))
                sc_tok2 = ch_v.tok()
                sch_free[sr] = sc_tok2
                if kk % 4 == 3:
                    w = [bank_wr[ybi], sz_tok, mh_guard]
                    gt = vop(lambda e, ybi=ybi, pair=pair, kk=kk, szr=szr: e.tensor_mul(
                        MHALF[:, pair * 1024 + (kk // 4) * 512: pair * 1024 + (kk // 4) * 512 + 512],
                        bank(ybi, 128, 512),
                        SZR[:, szr * 1024 + (kk // 4) * 512: szr * 1024 + (kk // 4) * 512 + 512]),
                        waits=tuple(w))
                    bank_rd[ybi] = gt
                    szr_free[szr] = gt
            for hh in range(2):
                slot_free[rr[hh]] = ch_t.tok()
        m_tok = ch_v.tok()

        if stop_after == "Bg":
            p3s.close()
            pL0.close()
            _finish(em, nc, stack, y_out, dbg, dbg_out, locals(), m_tok, ydma,
                    ch_v, ch_s, ch_t)
            return nc

        for sl2 in range(2):
            so2 = sl2 * 512
            bsum = bank(4, 1, 512)
            for pair in range(16):
                sq = sop(lambda e, pair=pair, sl2=sl2: e.activation(
                    SQH[:, 0:512], MHALF[:, pair * 1024 + sl2 * 512: pair * 1024 + sl2 * 512 + 512],
                    AF.Square), waits=(m_tok,) if pair == 0 else (ch_t.tok(),))
                w = [sq]
                if pair == 0:
                    w.append(bank_rd.get(4))
                t = mm(bsum, onescolb[:, :], SQH[:, 0:512], start=(pair == 0),
                       stop=(pair == 15), waits=tuple(w), inc=True)
            vop(lambda e, bsum=bsum, so2=so2: e.tensor_scalar(
                us_r[0:1, so2:so2 + 512], bsum, 1.0 / DIN, None, OP.mult), waits=(t,))
            bank_rd[4] = ch_v.tok()
            vop(lambda e, so2=so2: e.tensor_scalar(
                us_r[0:1, so2:so2 + 512], us_r[0:1, so2:so2 + 512], 1e-6, None, OP.add))
            sop(lambda e, so2=so2: e.activation(
                us_r[0:1, so2:so2 + 512], us_r[0:1, so2:so2 + 512], AF.Ln),
                waits=(ch_v.tok(),))
            sop(lambda e, so2=so2: e.activation(
                us_r[0:1, so2:so2 + 512], us_r[0:1, so2:so2 + 512], AF.Exp, scale=-0.5))
        yn_tok = ch_s.tok()
        wsc_free = [None, None]
        for e_i in range(8):
            for sl2 in range(2):
                bi = 6 + (sl2 % 2)
                bv = bank(bi, 128, 512)
                for kp in range(16):
                    sl_i = wip_state["idx"] % 8
                    wip_state["idx"] += 1
                    ws = kp % 2
                    wtok0 = gdma(WSC[:, ws * 128:(ws + 1) * 128],
                                 W_op[kp, :, e_i * 128:(e_i + 1) * 128], f"wsc{ws}",
                                 waits=(wsc_free[ws],))
                    wtok = vop(lambda e, sl_i=sl_i, ws=ws, kp=kp: e.tensor_scalar(
                        WIPR[:, sl_i * 128:(sl_i + 1) * 128], WSC[:, ws * 128:(ws + 1) * 128],
                        swcol[:, kp:kp + 1], None, OP.mult),
                        waits=(wtok0, consts, wip_state["free"][sl_i]))
                    wsc_free[ws] = wtok
                    w = [wtok, m_tok]
                    if kp == 0:
                        w.append(bank_rd.get(bi))
                    t = mm(bv, WIPR[:, sl_i * 128:(sl_i + 1) * 128],
                           MHALF[:, kp * 1024 + sl2 * 512: kp * 1024 + sl2 * 512 + 512],
                           start=(kp == 0), stop=(kp == 15), waits=tuple(w),
                           inc=True)
                    wip_state["free"][sl_i] = ch_t.tok()
                bank_wr[bi] = t
                tok0 = half * 1024 + sl2 * 512
                xr2 = sl2 % 2
                x0t = gdma(X0R[:, xr2 * 512:(xr2 + 1) * 512],
                           x_in.rearrange("(c p) t -> p c t", p=128)[:, e_i, HALO + tok0: HALO + tok0 + 512],
                           f"x0r{xr2}", waits=(bank_rd.get(200 + xr2), yn_tok))
                w = [ones_tok, bank_rd.get(1), u_all]
                t1b = mm(bank(1, 128, 512), onesF[:, :], mean_r[0:1, HALO + tok0: HALO + tok0 + 512],
                         start=True, stop=True, waits=tuple(x for x in w if x), inc=True)
                w = [ones_tok, bank_rd.get(0), u_all]
                t2b = mm(bank(0, 128, 512), onesF[:, :], rc_r[0:1, HALO + tok0: HALO + tok0 + 512],
                         start=True, stop=True, waits=tuple(x for x in w if x), inc=True)
                vop(lambda e, xr2=xr2, x0t=x0t, t1b=t1b: e.tensor_sub(
                    X0R[:, xr2 * 512:(xr2 + 1) * 512], X0R[:, xr2 * 512:(xr2 + 1) * 512],
                    bank(1, 128, 512)), waits=(x0t, t1b))
                bank_rd[1] = ch_v.tok()
                vop(lambda e, xr2=xr2: e.tensor_mul(
                    X0R[:, xr2 * 512:(xr2 + 1) * 512], X0R[:, xr2 * 512:(xr2 + 1) * 512],
                    bank(0, 128, 512)), waits=(t2b,))
                bank_rd[0] = ch_v.tok()
                w = [yn_tok, ones_tok, bank_rd.get(2)]
                t3b = mm(bank(2, 128, 512), onesF[:, :], us_r[0:1, sl2 * 512: sl2 * 512 + 512],
                         start=True, stop=True, waits=tuple(x for x in w if x), inc=True)
                cpR = vop(lambda e: e.tensor_copy(
                    XTMR[:, 0:1024].bitcast(F32), bank(2, 128, 512)),
                    waits=(t3b, xtmr_free[0]))
                sc_t = vop(lambda e, bv=bv: e.tensor_mul(
                    bv, bv, XTMR[:, 0:1024].bitcast(F32)), waits=(t, cpR))
                bank_rd[2] = sc_t
                xtmr_free[0] = sc_t
                u1t = vop(lambda e, xr2=xr2, bv=bv: e.tensor_add(
                    X0R[:, xr2 * 512:(xr2 + 1) * 512], X0R[:, xr2 * 512:(xr2 + 1) * 512],
                    bv), waits=())
                bank_rd[bi] = u1t
                ydma(x1u_d.rearrange("(c p) t -> p c t", p=128)[:, e_i, tok0: tok0 + 512],
                     X0R[:, xr2 * 512:(xr2 + 1) * 512], waits=(u1t,))
                bank_rd[200 + xr2] = ldy.tok()
                if stop_after == "Bu" and half == 0 and e_i == 0 and sl2 == 0:
                    p3s.close()
                    pL0.close()
                    _finish(em, nc, stack, y_out, dbg, dbg_out, locals(),
                            bank_rd[200 + xr2], ydma, ch_v, ch_s, ch_t)
                    return nc
        mh_guard = ch_t.tok()
    u1_done = ldy.tok()

    if stop_after == "B":
        p3s.close()
        pL0.close()
        dtok = ydma(y_out[:, :], x1u_d[:, :], waits=(u1_done,))
        _finish(em, nc, stack, y_out, dbg, dbg_out, locals(), dtok, ydma, ch_v, ch_s, ch_t)
        return nc

    p3s.close()
    pL0.close()
    SQ2 = sbuf("SQ2", [128, DC * 512], BF16)
    p45 = contextlib.ExitStack()
    WIP2 = sbuf("WIP2", [128, 16 * 128], BF16, p45)
    wip2 = {"free": [u1_done] * 16, "idx": 0}

    def w2tile(dsrc, waits=()):
        sl_i = wip2["idx"] % 16
        wip2["idx"] += 1
        tok = gdma(WIP2[:, sl_i * 128:(sl_i + 1) * 128], dsrc, f"wip2_{sl_i}",
                   waits=(wip2["free"][sl_i],) + tuple(waits))
        return sl_i, tok

    RES = sbuf("RES", [128, DC * T], F32, p45)
    H2B = sbuf("H2B", [128, DC * T], BF16, p45)
    U1B = sbuf("U1B", [128, 2 * DC * 512], BF16, p45)
    p4u = contextlib.ExitStack()
    U1F = sbuf("U1F", [128, 2 * DC * 512], F32, p4u)
    r1 = gdma(RES[:, :], x_in.rearrange("(c p) t -> p c t", p=128)[:, :, HALO:],
              "res", waits=(u1_done,))
    r2 = gdma(RES[:, :], x1u_d.rearrange("(c p) t -> p c t", p=128),
              "res", waits=(r1,), accum=True)
    u1b_free = [u1_done, u1_done]
    u1f_free = [u1_done, u1_done]
    h2toks = []
    for sl in range(4):
        us = sl % 2
        u1t = gdma(U1F[:, us * DC * 512:(us + 1) * DC * 512],
                   x1u_d.rearrange("(c p) t -> p c t", p=128)[:, :, sl * 512:(sl + 1) * 512],
                   f"u1f{us}", waits=(u1f_free[us],))
        sqt = sop(lambda e, us=us: e.activation(
            SQ2[:, :], U1F[:, us * DC * 512:(us + 1) * DC * 512], AF.Square),
            waits=(u1t,))
        bs = bank(6, 1, 512)
        for c in range(DC):
            w = [sqt, bank_rd.get(6)] if c == 0 else []
            t = mm(bs, onescolb[:, :], SQ2[:, c * 512:(c + 1) * 512],
                   start=(c == 0), stop=(c == DC - 1), waits=tuple(w), inc=(c == DC - 1))
        vop(lambda e, bs=bs, sl=sl: e.tensor_scalar(
            us_r[0:1, sl * 512: sl * 512 + 512], bs, 1.0 / D, None, OP.mult),
            waits=(t,))
        bank_rd[6] = ch_v.tok()
        vop(lambda e, sl=sl: e.tensor_scalar(
            us_r[0:1, sl * 512: sl * 512 + 512], us_r[0:1, sl * 512: sl * 512 + 512],
            1e-6, None, OP.add))
        sop(lambda e, sl=sl: e.activation(
            us_r[0:1, sl * 512: sl * 512 + 512], us_r[0:1, sl * 512: sl * 512 + 512],
            AF.Ln), waits=(ch_v.tok(),))
        rt = sop(lambda e, sl=sl: e.activation(
            us_r[0:1, sl * 512: sl * 512 + 512], us_r[0:1, sl * 512: sl * 512 + 512],
            AF.Exp, scale=-0.5))
        bb = bank(7, 128, 512)
        w = [rt, ones_tok, bank_rd.get(7)]
        t = mm(bb, onesF[:, :], us_r[0:1, sl * 512: sl * 512 + 512], start=True,
               stop=True, waits=tuple(x for x in w if x), inc=True)
        for c in range(DC):
            vop(lambda e, c=c, sl=sl, us=us, bb=bb: e.scalar_tensor_tensor(
                H2B[:, c * T + sl * 512: c * T + sl * 512 + 512],
                U1F[:, us * DC * 512 + c * 512: us * DC * 512 + (c + 1) * 512],
                pwcol[:, c:c + 1], bb, OP.mult, OP.mult),
                waits=(t,) if c == 0 else ())
        bank_rd[7] = ch_v.tok()
        u1f_free[us] = ch_v.tok()
        h2toks.append(ch_v.tok())
    h2_tok = ch_v.tok()
    p4u.close()
    p4h = contextlib.ExitStack()
    SGT = sbuf("SGT", [128, 2 * 512], F32, p4h)
    HH = sbuf("HH", [128, 22 * 1024], BF16, p4h)

    for half in range(2):
        for mt in range(22):
            toks = []
            for k in range(8):
                sg, gtk = w2tile(W_g[k, :, mt * 128:(mt + 1) * 128])
                su, utk = w2tile(W_u[k, :, mt * 128:(mt + 1) * 128])
                toks.append((sg, gtk, su, utk))
            for k, (sg, gtk, su, utk) in enumerate(toks):
                for ss in range(2):
                    no = half * 1024 + ss * 512
                    w = [gtk, h2_tok] if k == 0 else [gtk]
                    if k == 0:
                        w.append(bank_rd.get(ss))
                    mm(bank(ss, 128, 512), WIP2[:, sg * 128:(sg + 1) * 128],
                       _h2mv(H2B, k, no), start=(k == 0), stop=(k == 7),
                       waits=tuple(x for x in w if x), inc=(k == 7))
                    w = [utk] if k else [utk, bank_rd.get(2 + ss)]
                    t = mm(bank(2 + ss, 128, 512), WIP2[:, su * 128:(su + 1) * 128],
                           _h2mv(H2B, k, no), start=(k == 0), stop=(k == 7),
                           waits=tuple(x for x in w if x), inc=True)
                wip2["free"][sg] = ch_t.tok()
                wip2["free"][su] = ch_t.tok()
            gu_done = ch_t.tok()
            for ss in range(2):
                st = sop(lambda e, ss=ss: e.activation(
                    SGT[:, ss * 512:(ss + 1) * 512], bank(ss, 128, 512), AF.Silu),
                    waits=(gu_done,))
                bank_rd[ss] = st
                ht = vop(lambda e, mt=mt, half=half, ss=ss: e.tensor_mul(
                    HH[:, mt * 1024 + ss * 512: mt * 1024 + ss * 512 + 512],
                    SGT[:, ss * 512:(ss + 1) * 512], bank(2 + ss, 128, 512)),
                    waits=(st,))
                bank_rd[2 + ss] = ht
        hh_tok = ch_v.tok()
        for e_i in range(8):
            for k2 in range(22):
                sd, dtk2 = w2tile(W_d[k2, :, e_i * 128:(e_i + 1) * 128])
                for ss in range(2):
                    w = [dtk2, hh_tok] if k2 == 0 else [dtk2]
                    if k2 == 0:
                        w.append(bank_rd.get(4 + ss))
                    t = mm(bank(4 + ss, 128, 512), WIP2[:, sd * 128:(sd + 1) * 128],
                           HH[:, k2 * 1024 + ss * 512: k2 * 1024 + ss * 512 + 512],
                           start=(k2 == 0), stop=(k2 == 21),
                           waits=tuple(x for x in w if x), inc=True)
                wip2["free"][sd] = ch_t.tok()
            for ss in range(2):
                no = half * 1024 + ss * 512
                w = [t if ss == 1 else ch_t.tok(), r2]
                rt = vop(lambda e, e_i=e_i, no=no, ss=ss: e.tensor_add(
                    RES[:, e_i * T + no: e_i * T + no + 512],
                    RES[:, e_i * T + no: e_i * T + no + 512], bank(4 + ss, 128, 512)),
                    waits=tuple(w))
                bank_rd[4 + ss] = rt
    x1_tok = ch_v.tok()

    if stop_after == "P4":
        st = ydma(x1p_d[:, :].rearrange("(c p) t -> p c t", p=128),
                  RES.rearrange("p (c t) -> p c t", c=DC), waits=(x1_tok,))
        st = ydma(y_out[:, :], x1p_d[:, :], waits=(st,))
        p4h.close()
        p45.close()
        _finish(em, nc, stack, y_out, dbg, dbg_out, locals(), st, ydma, ch_v, ch_s, ch_t)
        return nc

    p4h.close()
    p5s = contextlib.ExitStack()
    WINS = sbuf("WINS", [128, 8 * 64], BF16, p5s)
    WOS = sbuf("WOS", [64, 1024], BF16, p5s)
    W1S = sbuf("W1S", [64, 16], F32, p5s)
    B1S = sbuf("B1S", [64, 16], F32, p5s)
    W2S = sbuf("W2S", [64, 16], F32, p5s)
    BINS = sbuf("BINS", [64, 1], F32, p5s)
    B2S = sbuf("B2S", [64, 1], F32, p5s)
    BOUTS = sbuf("BOUTS", [128, 8], F32, p5s)
    WSR = sbuf("WSR", [1, 64], F32, p5s)
    S0 = sbuf("S0", [64, T], F32, p5s)
    GAC = sbuf("GAC", [64, T], F32, p5s)
    TMP5 = sbuf("TMP5", [64, T], F32, p5s)
    GB = sbuf("GB", [64, T], BF16, p5s)
    for dst, srcd in ((WINS, Win.rearrange("k p m -> p k m")),
                      (WOS, Wout0), (W1S, W1_in), (B1S, b1_in), (W2S, W2_in),
                      (BINS, bin_in), (B2S, b2_in), (BOUTS, bout0)):
        gdma(dst[:, :], srcd[:, :, :] if dst is WINS else srcd[:, :], "sc5",
             waits=(x1_tok,) if dst is WINS else ())
    p5w = dch("sc5").tok()
    bw = bank(6, 64, 1)
    for k in range(8):
        w = [p5w, ones_tok, bank_rd.get(6)] if k == 0 else []
        t = mm(bw, WINS[:, k * 64:(k + 1) * 64], onescolb[:, :],
               start=(k == 0), stop=(k == 7), waits=tuple(w), inc=(k == 7))
    wst = vop(lambda e: e.tensor_copy(TMP5[:, 0:1], bw), waits=(t,))
    bank_rd[6] = wst
    tt = tr(bank(7, 1, 64), TMP5[0:64, 0:1], identF[:, :], waits=(wst, bank_rd.get(7)))
    wsr_tok = vop(lambda e: e.tensor_copy(WSR[:, :], bank(7, 1, 64)), waits=(tt,))
    bank_rd[7] = wsr_tok
    for sl in range(4):
        us = sl % 2
        xbt = vop(lambda e, sl=sl, us=us: e.tensor_copy(
            U1B[:, us * DC * 512:(us + 1) * DC * 512],
            RES.rearrange("p (c t) -> p c t", c=DC)[:, :, sl * 512:(sl + 1) * 512]),
            waits=(x1_tok, u1b_free[us]))
        sqt = sop(lambda e, us=us: e.activation(
            SQ2[:, :], U1B[:, us * DC * 512:(us + 1) * DC * 512], AF.Square),
            waits=(xbt,))
        b0, b1v = bank(0, 1, 512), bank(1, 1, 512)
        for c in range(DC):
            w = [xbt, ones_tok, bank_rd.get(0)] if c == 0 else []
            mm(b0, onescolb[:, :], U1B[:, us * DC * 512 + c * 512: us * DC * 512 + (c + 1) * 512],
               start=(c == 0), stop=(c == DC - 1), waits=tuple(w))
        for c in range(DC):
            w = [sqt, bank_rd.get(1)] if c == 0 else []
            t = mm(b1v, onescolb[:, :], SQ2[:, c * 512:(c + 1) * 512],
                   start=(c == 0), stop=(c == DC - 1), waits=tuple(w), inc=(c == DC - 1))
        so = sl * 512
        vop(lambda e, so=so, b0=b0: e.tensor_scalar(
            mean_r[0:1, so:so + 512], b0, 1.0 / D, None, OP.mult), waits=(t,))
        vop(lambda e, so=so, b1v=b1v: e.tensor_scalar(
            us_r[0:1, so:so + 512], b1v, 1.0 / D, None, OP.mult))
        bank_rd[0] = bank_rd[1] = ch_v.tok()
        vop(lambda e, so=so: e.tensor_mul(
            rc_r[0:1, so:so + 512], mean_r[0:1, so:so + 512], mean_r[0:1, so:so + 512]))
        vop(lambda e, so=so: e.tensor_sub(
            rc_r[0:1, so:so + 512], us_r[0:1, so:so + 512], rc_r[0:1, so:so + 512]))
        vop(lambda e, so=so: e.tensor_scalar(
            rc_r[0:1, so:so + 512], rc_r[0:1, so:so + 512], 0.0, 1e-5, OP.max, OP.add))
        lt = ch_v.tok()
        sop(lambda e, so=so: e.activation(
            rc_r[0:1, so:so + 512], rc_r[0:1, so:so + 512], AF.Ln), waits=(lt,))
        rct = sop(lambda e, so=so: e.activation(
            rc_r[0:1, so:so + 512], rc_r[0:1, so:so + 512], AF.Exp, scale=-0.5))
        nmt = vop(lambda e, so=so: e.tensor_scalar(
            us_r[0:1, so:so + 512], mean_r[0:1, so:so + 512], -1.0, None, OP.mult))
        bs = bank(2, 64, 512)
        for k in range(8):
            w = [p5w, xbt, bank_rd.get(2)] if k == 0 else []
            mm(bs, WINS[:, k * 64:(k + 1) * 64],
               U1B[:, us * DC * 512 + k * 512: us * DC * 512 + (k + 1) * 512],
               start=(k == 0), stop=False, waits=tuple(w))
        t = mm(bs, WSR[:, :], us_r[0:1, so:so + 512], start=False, stop=True,
               waits=(wsr_tok, nmt), inc=True)
        bbc = bank(3, 64, 512)
        w = [rct, ones_tok, bank_rd.get(3)]
        t2 = mm(bbc, onesF[:, 0:64], rc_r[0:1, so:so + 512], start=True, stop=True,
                waits=tuple(x for x in w if x), inc=True)
        vop(lambda e, bbc=bbc: e.tensor_copy(
            SQ2[:, 0:1024].bitcast(F32)[0:64, :], bbc), waits=(t2,))
        vop(lambda e, so=so, bs=bs: e.tensor_mul(
            S0[:, so:so + 512], bs, SQ2[:, 0:1024].bitcast(F32)[0:64, :]),
            waits=(t,))
        bank_rd[2] = bank_rd[3] = ch_v.tok()
        vop(lambda e, so=so: e.tensor_scalar(
            S0[:, so:so + 512], S0[:, so:so + 512], BINS[:, :], None, OP.add),
            waits=(p5w,))
        u1b_free[us] = ch_v.tok()
    s_tok = ch_v.tok()
    for i in range(16):
        rt = sop(lambda e, i=i: e.activation(
            TMP5[:, :], S0[0:64, :], AF.Relu, bias=B1S[:, i:i + 1],
            scale=W1S[:, i:i + 1]), waits=(s_tok, p5w) if i == 0 else (ch_v.tok(),))
        if i == 0:
            vop(lambda e: e.tensor_scalar(GAC[0:64, :], TMP5[0:64, :], W2S[:, 0:1],
                                          None, OP.mult), waits=(rt,))
        else:
            vop(lambda e, i=i: e.scalar_tensor_tensor(
                GAC[0:64, :], TMP5[0:64, :], W2S[:, i:i + 1], GAC[0:64, :],
                OP.mult, OP.add), waits=(rt,))
    vop(lambda e: e.tensor_scalar(GAC[0:64, :], GAC[0:64, :], B2S[:, :], None, OP.add))
    gb_tok = vop(lambda e: e.tensor_copy(GB[0:64, :], GAC[0:64, :]))
    for e_i in range(8):
        for sl in range(4):
            bo = bank(4 + sl % 2, 128, 512)
            w = [gb_tok, p5w, bank_rd.get(4 + sl % 2)]
            t = mm(bo, WOS[:, e_i * 128:(e_i + 1) * 128], GB[0:64, sl * 512:(sl + 1) * 512],
                   start=True, stop=True, waits=tuple(x for x in w if x), inc=True)
            rt = vop(lambda e, e_i=e_i, sl=sl, bo=bo: e.scalar_tensor_tensor(
                RES[:, e_i * T + sl * 512: e_i * T + sl * 512 + 512],
                bo, BOUTS[:, e_i:e_i + 1],
                RES[:, e_i * T + sl * 512: e_i * T + sl * 512 + 512], OP.add, OP.add),
                waits=(t,))
            bank_rd[4 + sl % 2] = rt
    x1p_tok = ch_v.tok()
    st = ydma(x1p_d.rearrange("(c p) t -> p c t", p=128),
              RES.rearrange("p (c t) -> p c t", c=DC), waits=(x1p_tok,))
    for sl in range(4):
        us = sl % 2
        xbt = vop(lambda e, sl=sl, us=us: e.tensor_copy(
            U1B[:, us * DC * 512:(us + 1) * DC * 512],
            RES.rearrange("p (c t) -> p c t", c=DC)[:, :, sl * 512:(sl + 1) * 512]),
            waits=(x1p_tok, u1b_free[us]))
        sqt = sop(lambda e, us=us: e.activation(
            SQ2[:, :], U1B[:, us * DC * 512:(us + 1) * DC * 512], AF.Square),
            waits=(xbt,))
        b0, b1v = bank(0, 1, 512), bank(1, 1, 512)
        for c in range(DC):
            w = [xbt, ones_tok, bank_rd.get(0)] if c == 0 else []
            mm(b0, onescolb[:, :], U1B[:, us * DC * 512 + c * 512: us * DC * 512 + (c + 1) * 512],
               start=(c == 0), stop=(c == DC - 1), waits=tuple(w))
        for c in range(DC):
            w = [sqt, bank_rd.get(1)] if c == 0 else []
            t = mm(b1v, onescolb[:, :], SQ2[:, c * 512:(c + 1) * 512],
                   start=(c == 0), stop=(c == DC - 1), waits=tuple(w), inc=(c == DC - 1))
        so = sl * 512
        vop(lambda e, so=so, b0=b0: e.tensor_scalar(
            mean_r[0:1, so:so + 512], b0, 1.0 / D, None, OP.mult), waits=(t,))
        vop(lambda e, so=so, b1v=b1v: e.tensor_scalar(
            us_r[0:1, so:so + 512], b1v, 1.0 / D, None, OP.mult))
        bank_rd[0] = bank_rd[1] = ch_v.tok()
        vop(lambda e, so=so: e.tensor_mul(
            rc_r[0:1, so:so + 512], mean_r[0:1, so:so + 512], mean_r[0:1, so:so + 512]))
        vop(lambda e, so=so: e.tensor_sub(
            rc_r[0:1, so:so + 512], us_r[0:1, so:so + 512], rc_r[0:1, so:so + 512]))
        vop(lambda e, so=so: e.tensor_scalar(
            rc_r[0:1, so:so + 512], rc_r[0:1, so:so + 512], 0.0, 1e-5, OP.max, OP.add))
        lt = ch_v.tok()
        sop(lambda e, so=so: e.activation(
            rc_r[0:1, so:so + 512], rc_r[0:1, so:so + 512], AF.Ln), waits=(lt,))
        rct = sop(lambda e, so=so: e.activation(
            rc_r[0:1, so:so + 512], rc_r[0:1, so:so + 512], AF.Exp, scale=-0.5))
        bm = bank(2, 128, 512)
        w = [ones_tok, bank_rd.get(2), ch_v.tok()]
        mm(bm, onesF[:, :], mean_r[0:1, so:so + 512], start=True, stop=True,
           waits=tuple(x for x in w if x))
        br = bank(3, 128, 512)
        w = [rct, bank_rd.get(3)]
        t2 = mm(br, onesF[:, :], rc_r[0:1, so:so + 512], start=True, stop=True,
                waits=tuple(x for x in w if x), inc=True)
        for c in range(DC):
            vop(lambda e, c=c, us=us, bm=bm: e.tensor_sub(
                SQ2[:, c * 512:(c + 1) * 512],
                U1B[:, us * DC * 512 + c * 512: us * DC * 512 + (c + 1) * 512], bm),
                waits=(t2,) if c == 0 else ())
            vop(lambda e, c=c, us=us, br=br: e.tensor_mul(
                U1B[:, us * DC * 512 + c * 512: us * DC * 512 + (c + 1) * 512],
                SQ2[:, c * 512:(c + 1) * 512], br))
        t1t = ch_v.tok()
        bank_rd[2] = bank_rd[3] = t1t
        gdma(t1AG_in.rearrange("(c p) t -> p c t", p=128)[:, :, sl * 512:(sl + 1) * 512],
             U1B[:, us * DC * 512:(us + 1) * DC * 512], "t1st", waits=(t1t,))
        u1b_free[us] = dch("t1st").tok()
    t1_in_done = dch("t1st").tok()
    em.emit("g", lambda e: e.collective_compute(
        "AllGather", OP.bypass, replica_groups=GROUPS,
        ins=[t1AG_in[:, :]], outs=[t1AG_out[:, :]]),
        (t1_in_done,), ((cc, 1),))
    ag2 = cc.tok()
    p5s.close()
    p45.close()

    if stop_after == "P5":
        st = ydma(y_out[:, :], x1p_d[:, :], waits=(st,))
        _finish(em, nc, stack, y_out, dbg, dbg_out, locals(), st, ydma, ch_v, ch_s, ch_t)
        return nc

    p6s = contextlib.ExitStack()
    QA = sbuf("QA", [128, 4 * S], BF16, p6s)
    KA = sbuf("KA", [128, 4 * S], BF16, p6s)
    V65 = sbuf("V65", [128, 32 * 8 * 65], BF16, p6s)
    PTB = sbuf("PTB", [128, 2 * 1024], BF16, p6s)
    T1T = sbuf("T1T", [128, 2 * DC * 512], BF16, p6s)
    OTB = sbuf("OTB", [64, 2 * 512], BF16, p6s)
    MHC = sbuf("MHC", [128, 8], F32, p6s)
    SRW = sbuf("SRW", [1, 2 * 512], F32, p6s)
    QNR = sbuf("QNR", [1, 160], F32, p6s)
    amaskb = sbuf("amaskb", [128, 4 * 512], BF16, p6s)
    amask_tok = gdma(amaskb[:, :], amask_in.rearrange("d p t -> p d t"), "amask", waits=(ag2,))
    p6w = contextlib.ExitStack()
    QKW = sbuf("QKW", [128, 8 * 1024], BF16, p6w)
    VW = sbuf("VW", [128, 8 * 512], BF16, p6w)
    for k in range(8):
        gdma(QKW[:, k * 1024:(k + 1) * 1024], W_qkv_qk[k, :, :], "qkw",
             waits=(ag2,) if k == 0 else ())
        gdma(VW[:, k * 512:(k + 1) * 512], W_qkv_v[k, :, :], "qkw")
    qkw_tok = dch("qkw").tok()
    vo_tok = vop(lambda e: e.memset(V65[:, :], 1.0))
    t1t_free = [ag2, ag2]
    qk_ev = None
    for ts in range(8):
        slot = ts // 4
        us = ts % 2
        t1tok = gdma(T1T[:, us * DC * 512:(us + 1) * DC * 512],
                     t1AG_out.rearrange("(s c p) t -> s p c t", s=2, p=128)[
                         slot, :, :, (ts % 4) * 512:(ts % 4) * 512 + 512],
                     f"t1t{us}", waits=(ag2, t1t_free[us]))
        for m2 in range(8):
            bi = m2 % 2
            bv = bank(bi, 128, 512)
            for k in range(8):
                w = [t1tok, qkw_tok] if k == 0 else []
                if k == 0:
                    w.append(bank_rd.get(bi))
                t = mm(bv, QKW[:, k * 1024 + m2 * 128: k * 1024 + (m2 + 1) * 128],
                       T1T[:, us * DC * 512 + k * 512: us * DC * 512 + (k + 1) * 512],
                       start=(k == 0), stop=(k == 7), waits=tuple(x for x in w if x),
                       inc=(k == 7))
            dst = QA if m2 % 2 == 0 else KA
            j = m2 // 2
            ev = vop(lambda e, dst=dst, j=j, ts=ts, bv=bv: e.tensor_copy(
                dst[:, j * S + ts * 512: j * S + ts * 512 + 512], bv), waits=(t,))
            bank_rd[bi] = ev
        for tt in range(4):
            bi = 2 + tt % 2
            bv = bank(bi, 128, 512)
            for k in range(8):
                w = [t1tok, qkw_tok] if k == 0 else []
                if k == 0:
                    w.append(bank_rd.get(bi))
                t = mm(bv, T1T[:, us * DC * 512 + k * 512 + tt * 128:
                               us * DC * 512 + k * 512 + tt * 128 + 128],
                       VW[:, k * 512:(k + 1) * 512],
                       start=(k == 0), stop=(k == 7), waits=tuple(x for x in w if x),
                       inc=(k == 7))
            gt = ts * 4 + tt
            for h in range(8):
                w = [t, vo_tok] if h == 0 else []
                ev = vop(lambda e, gt=gt, h=h, bv=bv: e.tensor_copy(
                    V65[:, (gt * 8 + h) * 65: (gt * 8 + h) * 65 + 64], bv[:, h * 64:(h + 1) * 64]),
                    waits=tuple(w))
            bank_rd[bi] = ev
        t1t_free[us] = ch_v.tok()
    qkv_tok = ch_v.tok()
    p6w.close()
    if stop_after == "P6v":
        p6s.close()
        _finish(em, nc, stack, y_out, dbg, dbg_out, locals(), qkv_tok, ydma, ch_v, ch_s, ch_t)
        return nc
    sq_last = {0: None, 64: None}
    for h in range(8):
        j, hb = h // 2, 64 * (h % 2)
        for src_a, qn in ((QA, 0), (KA, 1)):
            sqt = sop(lambda e, src_a=src_a, j=j, hb=hb: e.activation(
                SQ2[hb:hb + 64, :], src_a[hb:hb + 64, j * S:(j + 1) * S], AF.Square),
                waits=(qkv_tok, sq_last[hb]))
            for g in range(8):
                slot = 4 + g % 2
                w = [sqt, bank_rd.get(slot)] if g < 2 else [bank_rd.get(slot)]
                t = mm(bank(slot, 1, 512), onescolb[hb:hb + 64, :],
                       SQ2[hb:hb + 64, g * 512:(g + 1) * 512],
                       start=True, stop=True, waits=tuple(x for x in w if x),
                       inc=True)
                bank_rd[slot] = vop(lambda e, h=h, qn=qn, g=g, slot=slot: e.reduce_max(
                    QNR[0:1, h * 16 + qn * 8 + g: h * 16 + qn * 8 + g + 1],
                    bank(slot, 1, 512), axis=AX.X), waits=(t,))
            sq_last[hb] = t
    for h in range(8):
        vop(lambda e, h=h: e.reduce_max(
            QNR[0:1, 128 + h:129 + h], QNR[0:1, h * 16:h * 16 + 8], axis=AX.X))
        vop(lambda e, h=h: e.reduce_max(
            QNR[0:1, 136 + h:137 + h], QNR[0:1, h * 16 + 8:h * 16 + 16], axis=AX.X))
    for h in range(8):
        vop(lambda e, h=h: e.tensor_mul(
            QNR[0:1, 144 + h:145 + h], QNR[0:1, 128 + h:129 + h],
            QNR[0:1, 136 + h:137 + h]))
    vop(lambda e: e.memset(SRW[0:1, 0:512], 0.0))
    lt = ch_v.tok()
    sop(lambda e: e.activation(QNR[0:1, 144:152], QNR[0:1, 144:152], AF.Ln),
        waits=(lt,))
    et = sop(lambda e: e.activation(QNR[0:1, 144:152], QNR[0:1, 144:152],
                                    AF.Exp, scale=0.5))
    nt = vop(lambda e: e.tensor_scalar(QNR[0:1, 144:152], QNR[0:1, 144:152],
                                       -0.125, None, OP.mult), waits=(et,))
    for h in range(8):
        bb = bank(6, 128, 1)
        w = [nt, ones_tok, bank_rd.get(6)]
        t = mm(bb, onesF[:, :], QNR[0:1, 144 + h:145 + h], start=True, stop=True,
               waits=tuple(x for x in w if x), inc=True)
        mv = vop(lambda e, h=h, bb=bb: e.tensor_copy(MHC[:, h:h + 1], bb), waits=(t,))
        bank_rd[6] = mv
    mhc_tok = ch_v.tok()
    if stop_after == "P6q":
        p6s.close()
        _finish(em, nc, stack, y_out, dbg, dbg_out, locals(), mhc_tok, ydma, ch_v, ch_s, ch_t)
        return nc
    ptb_free = [mhc_tok, mhc_tok]
    ot_free = [None, None]
    for h in range(8):
        j, hb = h // 2, 64 * (h % 2)
        for qs in range(8):
            po = bank(6, 65, 512)
            nkv = qs * 4 + 4
            pv_t = None
            for tb in range((nkv + 1) // 2):
                tlist = [t for t in (2 * tb, 2 * tb + 1) if t < nkv]
                us = tb % 2
                stb = bank(2 * us, 128, 1024)
                for ti, t_kv in enumerate(tlist):
                    w = [qkv_tok]
                    if ti == 0:
                        w.append(bank_rd.get(2 * us))
                        w.append(bank_rd.get(2 * us + 1))
                    d = t_kv - 4 * qs
                    t = mm(stb[:, ti * 512:(ti + 1) * 512],
                           KA[hb:hb + 64, j * S + t_kv * 128: j * S + t_kv * 128 + 128],
                           QA[hb:hb + 64, j * S + qs * 512: j * S + qs * 512 + 512],
                           start=True, stop=(d < 0), waits=tuple(w), inc=(d < 0))
                    if d >= 0:
                        t = mm(stb[:, ti * 512:(ti + 1) * 512], identb[:, :],
                               amaskb[:, d * 512:(d + 1) * 512],
                               start=False, stop=True, waits=(amask_tok,),
                               inc=True)
                bank_wr[2 * us] = t
                et = sop(lambda e, us=us, stb=stb, h=h, n=len(tlist): e.activation(
                    PTB[:, us * 1024: us * 1024 + n * 512], stb[:, 0:n * 512],
                    AF.Exp, bias=MHC[:, h:h + 1], scale=0.125),
                    waits=(t, ptb_free[us], mhc_tok))
                bank_rd[2 * us] = bank_rd[2 * us + 1] = et
                for ti, t_kv in enumerate(tlist):
                    w = [et]
                    if tb == 0 and ti == 0:
                        w.append(bank_rd.get(6))
                    pv_t = mm(po, V65[:, (t_kv * 8 + h) * 65: (t_kv * 8 + h) * 65 + 65],
                              PTB[:, us * 1024 + ti * 512: us * 1024 + (ti + 1) * 512],
                              start=(tb == 0 and ti == 0), stop=(t_kv == nkv - 1),
                              waits=tuple(w), inc=(t_kv == nkv - 1))
                ptb_free[us] = pv_t
            su = qs % 2
            sv = vop(lambda e, su=su, po=po: e.tensor_copy(
                SRW[0:1, su * 512:(su + 1) * 512], po[64:65, :]), waits=(pv_t,))
            rv = vop(lambda e, su=su: e.reciprocal(
                SRW[0:1, su * 512:(su + 1) * 512], SRW[0:1, su * 512:(su + 1) * 512]))
            bb = bank(7, 64, 512)
            w = [rv, ones_tok, bank_rd.get(7)]
            t = mm(bb, onesF[:, 0:64], SRW[0:1, su * 512:(su + 1) * 512], start=True,
                   stop=True, waits=tuple(x for x in w if x), inc=True)
            cpb = vop(lambda e, bb=bb: e.tensor_copy(
                SQ2[:, 0:1024].bitcast(F32)[0:64, :], bb), waits=(t,))
            ov = vop(lambda e, su=su, po=po: e.tensor_mul(
                OTB[:, su * 512:(su + 1) * 512], po[0:64, :],
                SQ2[:, 0:1024].bitcast(F32)[0:64, :]),
                waits=(cpb, ot_free[su]))
            bank_rd[6] = ov
            bank_rd[7] = ov
            gdma(oAG_in[h * 64:(h + 1) * 64, qs * 512:(qs + 1) * 512],
                 OTB[:, su * 512:(su + 1) * 512], f"ost{su}", waits=(ov,))
            ot_free[su] = dch(f"ost{su}").tok()
    o_done = (dch("ost0").tok(), dch("ost1").tok())
    em.emit("g", lambda e: e.collective_compute(
        "AllGather", OP.bypass, replica_groups=GROUPS,
        ins=[oAG_in[:, :]], outs=[oAG_out[:, :]]),
        o_done, ((cc, 1),))
    ag3 = cc.tok()
    if stop_after == "P6a":
        p6s.close()
        _finish(em, nc, stack, y_out, dbg, dbg_out, locals(), ag3, ydma, ch_v, ch_s, ch_t)
        return nc
    OSEL = sbuf("OSEL", [128, 8 * 512], BF16, p6s)
    OAB = sbuf("OAB", [128, 2 * 512], BF16, p6s)
    X2T = sbuf("X2T", [128, 2 * 512], F32, p6s)
    X2B = sbuf("X2B", [128, 512], BF16, p6s)
    oab_free = [ag3, ag3]
    x2t_free = [None, None]
    osel_free = None
    for g in range(4):
        osel_toks = []
        for k in range(8):
            ua = k % 2
            ta = gdma(OAB[:, ua * 512:(ua + 1) * 512],
                      oAG_out[k * 128:(k + 1) * 128, g * 512:(g + 1) * 512], f"oab{ua}",
                      waits=(ag3, oab_free[ua]))
            sel1 = vop(lambda e, k=k, ua=ua: e.tensor_scalar(
                OSEL[:, k * 512:(k + 1) * 512], OAB[:, ua * 512:(ua + 1) * 512],
                uinvcol[:, :], None, OP.mult), waits=(ta, consts, osel_free))
            tb2 = gdma(OAB[:, ua * 512:(ua + 1) * 512],
                       oAG_out[k * 128:(k + 1) * 128, 2048 + g * 512: 2048 + (g + 1) * 512],
                       f"oab{ua}", waits=(sel1,))
            sel2 = vop(lambda e, k=k, ua=ua: e.scalar_tensor_tensor(
                OSEL[:, k * 512:(k + 1) * 512], OAB[:, ua * 512:(ua + 1) * 512],
                uinitcol[:, :], OSEL[:, k * 512:(k + 1) * 512], OP.mult, OP.add),
                waits=(tb2,))
            oab_free[ua] = sel2
            osel_toks.append(sel2)
        osel_tok = ch_v.tok()
        for e_i in range(8):
            bi = 4 + e_i % 2
            bv = bank(bi, 128, 512)
            toks = []
            for k in range(8):
                swo, wtk = w2tile(W_wo[k, :, e_i * 128:(e_i + 1) * 128])
                toks.append((swo, wtk))
            for k, (swo, wtk) in enumerate(toks):
                w = [wtk, osel_tok] if k == 0 else [wtk]
                if k == 0:
                    w.append(bank_rd.get(bi))
                t = mm(bv, WIP2[:, swo * 128:(swo + 1) * 128],
                       OSEL[:, k * 512:(k + 1) * 512],
                       start=(k == 0), stop=(k == 7), waits=tuple(w), inc=True)
                wip2["free"][swo] = ch_t.tok()
            ux = e_i % 2
            x1pt = gdma(X2T[:, ux * 512:(ux + 1) * 512],
                        x1p_d.rearrange("(c p) t -> p c t", p=128)[:, e_i, g * 512:(g + 1) * 512],
                        f"x2t{ux}", waits=(x2t_free[ux],))
            xt = vop(lambda e, ux=ux, bv=bv: e.tensor_add(
                X2T[:, ux * 512:(ux + 1) * 512], X2T[:, ux * 512:(ux + 1) * 512], bv),
                waits=(x1pt, t))
            bank_rd[bi] = xt
            ydma(x2_d.rearrange("(c p) t -> p c t", p=128)[:, e_i, g * 512:(g + 1) * 512],
                 X2T[:, ux * 512:(ux + 1) * 512], waits=(xt,))
            xbt = vop(lambda e, ux=ux: e.tensor_copy(
                X2B[:, :], X2T[:, ux * 512:(ux + 1) * 512]), waits=(xt,))
            sqt = sop(lambda e: e.activation(SQ2[:, 0:512], X2B[:, :], AF.Square),
                      waits=(xbt,))
            w = [xbt, ones_tok, bank_rd.get(0)] if e_i == 0 else [xbt]
            mm(bank(0, 1, 512), onescolb[:, :], X2B[:, :],
               start=(e_i == 0), stop=(e_i == 7), waits=tuple(x for x in w if x))
            w = [sqt] if e_i else [sqt, bank_rd.get(1)]
            t = mm(bank(1, 1, 512), onescolb[:, :], SQ2[:, 0:512],
                   start=(e_i == 0), stop=(e_i == 7), waits=tuple(x for x in w if x),
                   inc=(e_i == 7))
            x2t_free[ux] = ch_v.tok()
        osel_free = ch_t.tok()
        so = g * 512
        vop(lambda e, so=so: e.tensor_scalar(
            mean_r[0:1, so:so + 512], bank(0, 1, 512), 1.0 / D, None, OP.mult),
            waits=(t,))
        vop(lambda e, so=so: e.tensor_scalar(
            us_r[0:1, so:so + 512], bank(1, 1, 512), 1.0 / D, None, OP.mult))
        bank_rd[0] = bank_rd[1] = ch_v.tok()
        vop(lambda e, so=so: e.tensor_mul(
            rc_r[0:1, so:so + 512], mean_r[0:1, so:so + 512], mean_r[0:1, so:so + 512]))
        vop(lambda e, so=so: e.tensor_sub(
            rc_r[0:1, so:so + 512], us_r[0:1, so:so + 512], rc_r[0:1, so:so + 512]))
        vop(lambda e, so=so: e.tensor_scalar(
            rc_r[0:1, so:so + 512], rc_r[0:1, so:so + 512], 0.0, 1e-5, OP.max, OP.add))
        lt = ch_v.tok()
        sop(lambda e, so=so: e.activation(
            rc_r[0:1, so:so + 512], rc_r[0:1, so:so + 512], AF.Ln), waits=(lt,))
        sop(lambda e, so=so: e.activation(
            rc_r[0:1, so:so + 512], rc_r[0:1, so:so + 512], AF.Exp, scale=-0.5))
    x2_done = ldy.tok()
    x2s_tok = ch_s.tok()
    p6s.close()

    if stop_after == "P6":
        dtok = ydma(y_out[:, :], x2_d[:, :], waits=(x2_done,))
        _finish(em, nc, stack, y_out, dbg, dbg_out, locals(), dtok, ydma, ch_v, ch_s, ch_t)
        return nc

    p7s = contextlib.ExitStack()
    VNB = sbuf("VNB", [128, DC * 1024], BF16, p7s)
    HP = sbuf("HP", [128, 32 * 1024], BF16, p7s)
    GP = sbuf("GP", [128, 32 * 1024], BF16, p7s)
    OSL = sbuf("OSL", [128, 2 * 32 * 128], BF16, p7s)
    X2R = sbuf("X2R", [128, DC * 512], BF16, p7s)
    X3T = sbuf("X3T", [128, 512], F32, p7s)
    SGT2 = sbuf("SGT2", [128, 512], F32, p7s)
    SC1 = sbuf("SC1", [128, 32], F32, p7s)
    SC2 = sbuf("SC2", [128, 32], F32, p7s)
    SCH1 = sbuf("SCH1", [128, 32], F32, p7s)
    SCH2 = sbuf("SCH2", [128, 32], F32, p7s)
    SC3 = sbuf("SC3", [128, 8], F32, p7s)
    for dst, srcd in ((SC1, s_t1), (SC2, s_t2), (SCH1, s_h1), (SCH2, s_h2), (SC3, s_t3)):
        gdma(dst[:, :], srcd[:, :], "sc7", waits=(x2_done,) if dst is SC1 else ())
    p7w = dch("sc7").tok()
    x2r_free = [None, None]

    def emit_v(ps_i):
        vtoks = []
        for sl in (2 * ps_i, 2 * ps_i + 1):
            x2t = gdma(X2R[:, 0: DC * 512],
                       x2_d.rearrange("(c p) t -> p c t", p=128)[:, :, sl * 512:(sl + 1) * 512],
                       "x2rv", waits=(x2_done, x2r_free[0], x2r_free[1]))
            so = sl * 512
            lo = (sl - 2 * ps_i) * 512
            bm = bank(2, 128, 512)
            w = [ones_tok, bank_rd.get(2), x2s_tok]
            mm(bm, onesF[:, :], mean_r[0:1, so:so + 512], start=True, stop=True,
               waits=tuple(x for x in w if x))
            br = bank(3, 128, 512)
            w = [bank_rd.get(3), x2s_tok]
            t2 = mm(br, onesF[:, :], rc_r[0:1, so:so + 512], start=True, stop=True,
                    waits=tuple(x for x in w if x), inc=True)
            for c in range(DC):
                vop(lambda e, c=c, bm=bm: e.tensor_sub(
                    SQ2[:, c * 512:(c + 1) * 512],
                    X2R[:, c * 512:(c + 1) * 512], bm),
                    waits=(t2, x2t, vnb_guard) if c == 0 else ())
                vop(lambda e, c=c, lo=lo, br=br: e.tensor_mul(
                    VNB[:, c * 1024 + lo: c * 1024 + lo + 512],
                    SQ2[:, c * 512:(c + 1) * 512], br))
            bank_rd[2] = bank_rd[3] = ch_v.tok()
            x2r_free[0] = ch_v.tok()
        return ch_v.tok()

    osl_free = [p7w, p7w]
    x3_out_tok = None
    hp_guard = None
    vnb_guard = None
    for ps_i in range(2):
        tok0 = ps_i * 1024
        v_tok = emit_v(ps_i)
        x2r_free[0] = x2r_free[1] = v_tok
        for o in range(32):
            uo = o % 2
            t1w = gdma(OSL[:, uo * 4096: uo * 4096 + 1024], W_t1[o, :, :], f"osl{uo}",
                       waits=(osl_free[uo],))
            t2w = gdma(OSL[:, uo * 4096 + 1024: uo * 4096 + 2048], W_t2[o, :, :], f"osl{uo}")
            wt = dch(f"osl{uo}").tok()
            for ss in range(2):
                no = ss * 512
                for k in range(8):
                    w = [wt, v_tok] if k == 0 else []
                    if k == 0:
                        w.append(bank_rd.get(ss))
                    mm(bank(ss, 128, 512), OSL[:, uo * 4096 + k * 128: uo * 4096 + (k + 1) * 128],
                       VNB[:, k * 1024 + no: k * 1024 + no + 512],
                       start=(k == 0), stop=(k == 7), waits=tuple(x for x in w if x))
                    w = [wt] if k else [wt, bank_rd.get(2 + ss)]
                    t = mm(bank(2 + ss, 128, 512),
                           OSL[:, uo * 4096 + 1024 + k * 128: uo * 4096 + 1024 + (k + 1) * 128],
                           VNB[:, k * 1024 + no: k * 1024 + no + 512],
                           start=(k == 0), stop=(k == 7), waits=tuple(x for x in w if x),
                           inc=(k == 7))
            gu = ch_t.tok()
            osl_free[uo] = gu
            vnb_guard = gu
            ht = None
            for ss in range(2):
                st2 = sop(lambda e, ss=ss, o=o: e.activation(
                    SGT2[:, :], bank(ss, 128, 512), AF.Silu,
                    scale=SC1[:, o:o + 1]), waits=(gu, p7w, ht))
                bank_rd[ss] = st2
                ht = vop(lambda e, o=o, ss=ss: e.scalar_tensor_tensor(
                    HP[:, o * 1024 + ss * 512: o * 1024 + ss * 512 + 512],
                    bank(2 + ss, 128, 512), SC2[:, o:o + 1],
                    SGT2[:, :], OP.mult, OP.mult),
                    waits=(st2, hp_guard))
                bank_rd[2 + ss] = ht
        hp_tok = ch_v.tok()
        for o in range(32):
            uo = o % 2
            hw = gdma(OSL[:, uo * 4096: uo * 4096 + 4096], W_h1[o, :, :], f"osl{uo}",
                      waits=(osl_free[uo],))
            for ss in range(2):
                bi = 4 + ss
                for k in range(32):
                    w = [hw, hp_tok] if k == 0 else []
                    if k == 0:
                        w.append(bank_rd.get(bi))
                    t = mm(bank(bi, 128, 512), OSL[:, uo * 4096 + k * 128: uo * 4096 + (k + 1) * 128],
                           HP[:, k * 1024 + ss * 512: k * 1024 + ss * 512 + 512],
                           start=(k == 0), stop=(k == 31), waits=tuple(x for x in w if x),
                           inc=(k == 31))
                ge = sop(lambda e, o=o, ss=ss, bi=bi: e.activation(
                    GP[:, o * 1024 + ss * 512: o * 1024 + ss * 512 + 512],
                    bank(bi, 128, 512), AF.Gelu, scale=SCH1[:, o:o + 1]),
                    waits=(t, p7w))
                bank_rd[bi] = ge
            osl_free[uo] = ch_t.tok()
        gp_tok = ch_s.tok()
        for o in range(32):
            uo = o % 2
            hw = gdma(OSL[:, uo * 4096: uo * 4096 + 4096], W_h2[o, :, :], f"osl{uo}",
                      waits=(osl_free[uo],))
            for ss in range(2):
                bi = 4 + ss
                for k in range(32):
                    w = [hw, gp_tok] if k == 0 else []
                    if k == 0:
                        w.append(bank_rd.get(bi))
                    t = mm(bank(bi, 128, 512), OSL[:, uo * 4096 + k * 128: uo * 4096 + (k + 1) * 128],
                           GP[:, k * 1024 + ss * 512: k * 1024 + ss * 512 + 512],
                           start=(k == 0), stop=(k == 31), waits=tuple(x for x in w if x),
                           inc=(k == 31))
                ge = sop(lambda e, o=o, ss=ss, bi=bi: e.activation(
                    HP[:, o * 1024 + ss * 512: o * 1024 + ss * 512 + 512],
                    bank(bi, 128, 512), AF.Gelu, scale=SCH2[:, o:o + 1]),
                    waits=(t, p7w))
                bank_rd[bi] = ge
            osl_free[uo] = ch_t.tok()
        h2p_tok = ch_s.tok()
        hp_guard = None
        for e_i in range(8):
            uo = e_i % 2
            hw = gdma(OSL[:, uo * 4096: uo * 4096 + 4096], W_t3[e_i, :, :], f"osl{uo}",
                      waits=(osl_free[uo],))
            for ss in range(2):
                bi = 6 + ss
                no = tok0 + ss * 512
                for k in range(32):
                    w = [hw, h2p_tok] if k == 0 else []
                    if k == 0:
                        w.append(bank_rd.get(bi))
                    t = mm(bank(bi, 128, 512), OSL[:, uo * 4096 + k * 128: uo * 4096 + (k + 1) * 128],
                           HP[:, k * 1024 + ss * 512: k * 1024 + ss * 512 + 512],
                           start=(k == 0), stop=(k == 31), waits=tuple(x for x in w if x),
                           inc=(k == 31))
                ux = ss
                x2t = gdma(X2R[:, ux * 512: ux * 512 + 512],
                           x2_d.rearrange("(c p) t -> p c t", p=128)[:, e_i, no:no + 512],
                           f"x2w{ux}", waits=(x2r_free[ux],))
                xo = vop(lambda e, bi=bi, e_i=e_i, ux=ux: e.scalar_tensor_tensor(
                    X3T[:, :],
                    bank(bi, 128, 512), SC3[:, e_i:e_i + 1],
                    X2R[:, ux * 512: ux * 512 + 512], OP.mult, OP.add),
                    waits=(t, x2t, p7w, x3_out_tok))
                bank_rd[bi] = xo
                x3_out_tok = ydma(
                    y_out.rearrange("(c p) t -> p c t", p=128)[:, e_i, no:no + 512],
                    X3T[:, :], waits=(xo,))
                x2r_free[ux] = x3_out_tok
                if stop_after is not None and stop_after.startswith("P7w") and (
                        (ps_i, e_i, ss) == tuple(int(x) for x in stop_after[3:].split("_"))):
                    p7s.close()
                    _finish(em, nc, stack, y_out, dbg, dbg_out, locals(),
                            x3_out_tok, ydma, ch_v, ch_s, ch_t)
                    return nc
            osl_free[uo] = ch_t.tok()
        hp_guard = ch_t.tok()
    p7s.close()
    _finish(em, nc, stack, y_out, dbg, dbg_out, locals(), x3_out_tok, ydma, ch_v, ch_s, ch_t)
    return nc


def _emit_z(em, mm, bank, sop, vop, gdma, W_ip, WIPR, wip_state, UALL, u_all,
            bank_rd, bank_wr, SZR, szr, pair, zoff, szfree, ch_t, ch_s):
    toks = []
    for k in range(8):
        sl_i = wip_state["idx"] % 8
        wip_state["idx"] += 1
        wtok = gdma(WIPR[:, sl_i * 128:(sl_i + 1) * 128], W_ip[pair, k, :, :],
                    f"wipr{sl_i}", waits=(wip_state["free"][sl_i],))
        for ss in range(2):
            bi = 6 + ss
            w = [wtok]
            if k == 0:
                w += [u_all, bank_rd.get(bi)]
            t = mm(bank(bi, 128, 512), WIPR[:, sl_i * 128:(sl_i + 1) * 128],
                   UALL[:, k * 2051 + zoff + ss * 512: k * 2051 + zoff + ss * 512 + 512],
                   start=(k == 0), stop=(k == 7), waits=tuple(w),
                   inc=(k == 7 or ss == 1))
            if k == 7:
                bank_wr[bi] = t
        wip_state["free"][sl_i] = ch_t.tok()
    for ss in range(2):
        bi = 6 + ss
        tok = sop(lambda e, bi=bi, ss=ss, szr=szr: e.activation(
            SZR[:, szr * 1024 + ss * 512: szr * 1024 + ss * 512 + 512],
            bank(bi, 128, 512), AF.Silu), waits=(bank_wr[bi], szfree))
        bank_rd[bi] = tok
    return ch_s.tok()


def _finish(em, nc, stack, y_out, dbg, dbg_out, lcl, final_tok, ydma, ch_v, ch_s, ch_t):
    ldy = lcl["ldy"]
    if dbg is not None:
        name, rrows, rcols = dbg
        tsrc = lcl[name]
        em.emit("g", lambda e: e.dma_start(out=dbg_out[:, :], in_=tsrc[0:rrows, 0:rcols]),
                (final_tok, ch_v.tok(), ch_s.tok(), ch_t.tok()),
                ((ldy, 16),))
    em.emit("y", lambda e: e.wait_ge(ldy.sem, ldy.n), (final_tok,), ())
    with nc.Block() as block:
        em.realize(block)
    stack.close()


def kernel(**inputs):
    in_maps = host_prepare(inputs)
    nc = build_program()
    res = run_bass_kernel_spmd(nc, in_maps, core_ids=list(range(NCORE)))
    out = np.zeros((B, S, D), np.float32)
    for c in range(NCORE):
        b, hf = c // 2, c % 2
        out[b, hf * T:(hf + 1) * T, :] = res.results[c]["y"].T
    return out


# version 06 (final)
# speedup vs baseline: 1.1222x; 1.1111x over previous
import contextlib
import numpy as np
import concourse.bass as bass
import concourse.mybir as mybir
from concourse.bass_utils import run_bass_kernel_spmd

F32 = mybir.dt.float32
BF16 = mybir.dt.bfloat16
OP = mybir.AluOpType
AF = mybir.ActivationFunctionType
AX = mybir.AxisListType

B, S, D = 4, 4096, 1024
T = 2048
HALO = 3
TH = T + HALO
NCORE = 8
DC = 8
DIN = 2048
NH = 32
HP = 64
DST = 128
Q = 128
NCH = 16
INTER0P = 2816
DHID = 4096
NEG = -1e30
GROUPS = [[0, 1], [2, 3], [4, 5], [6, 7]]

NS_OFF = [0, 512, 1024, 1536, 1539]
NS_W = [512, 512, 512, 512, 512]


class Chan:
    def __init__(self, sem):
        self.sem = sem
        self.n = 0

    def tok(self):
        return (self, self.n)


class Emitter:
    ENGS = ("t", "v", "s", "g", "y")

    def __init__(self):
        self.ops = {e: [] for e in self.ENGS}

    def emit(self, eng, fn, waits=(), incs=()):
        ws = []
        for w in waits:
            if w is None:
                continue
            ch, v = w
            if v > 0:
                ws.append((ch, v))
        for ch, v in incs:
            ch.n += v
        self.ops[eng].append((fn, tuple(ws), tuple(incs)))

    def realize(self, block):
        def mk(key):
            def body(eng):
                last = {}
                for fn, waits, incs in self.ops[key]:
                    for ch, v in waits:
                        if last.get(id(ch), -1) >= v:
                            continue
                        eng.wait_ge(ch.sem, v)
                        last[id(ch)] = v
                    ins = fn(eng)
                    for ch, v in incs:
                        ins = ins.then_inc(ch.sem, v)
            return body

        block.sync(mk("y"))
        block.tensor(mk("t"))
        block.vector(mk("v"))
        block.scalar(mk("s"))
        block.gpsimd(mk("g"))


def _tile_k(w, kc):
    K, M = w.shape
    assert K == kc * 128
    return np.ascontiguousarray(w.reshape(kc, 128, M))


def _pad_cols(w, m_to):
    out = np.zeros((w.shape[0], m_to), dtype=w.dtype)
    out[:, : w.shape[1]] = w
    return out


def _pad_rows(w, k_to):
    out = np.zeros((k_to, w.shape[1]), dtype=w.dtype)
    out[: w.shape[0], :] = w
    return out


def _oslab(wT, kc, oc):
    K, M = wT.shape
    assert K == kc * 128 and M == oc * 128
    a = wT.reshape(kc, 128, oc, 128).transpose(2, 1, 0, 3)
    return np.ascontiguousarray(a.reshape(oc, 128, kc * 128))


def _h2mv(H2B, k, no):
    return H2B[:, k * T + no: k * T + no + 512]


def host_prepare(inputs):
    f32 = np.float32
    x0 = np.asarray(inputs["L0_attn_hidden_states"], f32)

    ipT = np.ascontiguousarray(np.asarray(inputs["L0_attn_in_proj_w"], f32).T)
    ipTp = _pad_cols(ipT, 35 * 128)
    W_ip = np.ascontiguousarray(ipTp.reshape(8, 128, 35, 128).transpose(2, 0, 1, 3))

    convw = np.asarray(inputs["L0_attn_conv_w"], f32)
    conv_diag = np.zeros((18, 4, 128, 128), f32)
    for c in range(18):
        for t in range(4):
            np.fill_diagonal(conv_diag[c, t], convw[c * 128:(c + 1) * 128, t])
    conv_b = np.ascontiguousarray(np.asarray(inputs["L0_attn_conv_b"], f32).reshape(18, 128).T)

    dt_bias = np.tile(np.asarray(inputs["L0_attn_dt_bias"], f32).reshape(NH, 1), (4, 1))
    A_log = np.tile(np.asarray(inputs["L0_attn_A_log"], f32).reshape(NH, 1), (4, 1))
    Dvec = np.asarray(inputs["L0_attn_D"], f32)
    D_diag = np.zeros((NH, 128, 128), f32)
    for h in range(NH):
        np.fill_diagonal(D_diag[h], Dvec[h])

    ssm_w = np.ascontiguousarray(np.asarray(inputs["L0_attn_ssm_norm_w"], f32).reshape(16, 128).T)
    norm_w = np.ascontiguousarray(np.asarray(inputs["L0_attn_norm_w"], f32).reshape(8, 128).T)
    post_w = np.ascontiguousarray(np.asarray(inputs["L0_attn_post_norm_w"], f32).reshape(8, 128).T)

    W_op = _tile_k(np.ascontiguousarray(np.asarray(inputs["L0_attn_out_proj_w"], f32).T), 16)
    W_g = _tile_k(_pad_cols(np.ascontiguousarray(np.asarray(inputs["L0_attn_gate_w"], f32).T), INTER0P), 8)
    W_u = _tile_k(_pad_cols(np.ascontiguousarray(np.asarray(inputs["L0_attn_up_w"], f32).T), INTER0P), 8)
    W_d = _tile_k(_pad_rows(np.ascontiguousarray(np.asarray(inputs["L0_attn_down_w"], f32).T), INTER0P), 22)

    Win = _tile_k(np.asarray(inputs["L0_ffn_Win"], f32), 8)
    bin_ = np.asarray(inputs["L0_ffn_bin_"], f32).reshape(64, 1)
    W1 = np.asarray(inputs["L0_ffn_W1"], f32)
    b1 = np.asarray(inputs["L0_ffn_b1"], f32)
    W2 = np.asarray(inputs["L0_ffn_W2"], f32)
    b2 = np.asarray(inputs["L0_ffn_b2"], f32).reshape(64, 1)
    Wout0 = np.ascontiguousarray(np.asarray(inputs["L0_ffn_Wout"], f32))
    bout0 = np.ascontiguousarray(np.asarray(inputs["L0_ffn_bout"], f32).reshape(8, 128).T)

    wqkv = np.asarray(inputs["L1_attn_w_qkv"], f32)
    W_qk, W_v = {}, {}
    for hg in range(2):
        blocks = []
        for j in range(4):
            h0, h1 = hg * 8 + 2 * j, hg * 8 + 2 * j + 1
            blocks.append(np.concatenate([wqkv[:, h0, 0:64], wqkv[:, h1, 0:64]], axis=1))
            blocks.append(np.concatenate([wqkv[:, h0, 64:128], wqkv[:, h1, 64:128]], axis=1))
        W_qk[hg] = _tile_k(np.concatenate(blocks, axis=1), 8)
        v = np.concatenate([wqkv[:, h, 128:192] for h in range(hg * 8, hg * 8 + 8)], axis=1)
        W_v[hg] = _tile_k(v, 8)
    W_wo = _tile_k(np.asarray(inputs["L1_attn_w_out"], f32).reshape(1024, 1024), 8)

    W_t1 = _oslab(np.ascontiguousarray(np.asarray(inputs["L1_ffn_w1_w"], f32).T), 8, 32)
    W_t2 = _oslab(np.ascontiguousarray(np.asarray(inputs["L1_ffn_w2_w"], f32).T), 8, 32)
    W_h1 = _oslab(np.ascontiguousarray(np.asarray(inputs["L1_ffn_h1_w"], f32).T), 32, 32)
    W_h2 = _oslab(np.ascontiguousarray(np.asarray(inputs["L1_ffn_h2_w"], f32).T), 32, 32)
    W_t3 = _oslab(np.ascontiguousarray(np.asarray(inputs["L1_ffn_w3_w"], f32).T), 32, 8)
    s_t1 = np.ascontiguousarray(np.asarray(inputs["L1_ffn_w1_s"], f32).reshape(32, 128).T)
    s_t2 = np.ascontiguousarray(np.asarray(inputs["L1_ffn_w2_s"], f32).reshape(32, 128).T)
    s_h1 = np.ascontiguousarray(np.asarray(inputs["L1_ffn_h1_s"], f32).reshape(32, 128).T)
    s_h2 = np.ascontiguousarray(np.asarray(inputs["L1_ffn_h2_s"], f32).reshape(32, 128).T)
    s_t3 = np.ascontiguousarray(np.asarray(inputs["L1_ffn_w3_s"], f32).reshape(8, 128).T)

    ident = np.eye(128, dtype=f32)
    ltri01 = np.triu(np.ones((128, 128), f32))
    attn_mask = np.zeros((4, 128, 512), f32)
    for d in range(4):
        for k in range(128):
            attn_mask[d, k, : min(d * 128 + k, 512)] = NEG

    shared = dict(
        W_ip=W_ip, conv_diag=conv_diag, conv_b=conv_b, dt_bias=dt_bias,
        A_log=A_log, D_diag=D_diag, ssm_w=ssm_w, norm_w=norm_w, post_w=post_w,
        W_op=W_op, W_g=W_g, W_u=W_u, W_d=W_d,
        Win=Win, bin_=bin_, W1=W1, b1=b1, W2=W2, b2=b2, Wout0=Wout0, bout0=bout0,
        W_wo=W_wo, W_t1=W_t1, W_t2=W_t2, W_h1=W_h1, W_h2=W_h2, W_t3=W_t3,
        s_t1=s_t1, s_t2=s_t2, s_h1=s_h1, s_h2=s_h2, s_t3=s_t3,
        ident=ident, ltri01=ltri01, attn_mask=attn_mask,
    )

    in_maps = []
    for c in range(NCORE):
        b, hf = c // 2, c % 2
        if hf == 0:
            xs = np.zeros((TH, D), np.float32)
            xs[HALO:] = x0[b, 0:T]
        else:
            xs = x0[b, T - HALO: T - HALO + TH]
        m = dict(shared)
        m["x"] = np.ascontiguousarray(xs.T)
        m["W_qkv_qk"] = W_qk[hf]
        m["W_qkv_v"] = W_v[hf]
        m["uinit"] = np.full((128, 1), float(hf), np.float32)
        m["uinv"] = np.full((128, 1), float(1 - hf), np.float32)
        in_maps.append(m)
    return in_maps


def build_program(dbg=None, stop_after=None):
    nc = bass.Bass("TRN2", target_bir_lowering=False, debug=False, detect_race_conditions=False)

    def din(name, shape):
        return nc.dram_tensor(name, list(shape), F32, kind="ExternalInput").ap()

    x_in = din("x", (D, TH))
    W_ip = din("W_ip", (35, 8, 128, 128))
    conv_diag = din("conv_diag", (18, 4, 128, 128))
    conv_b = din("conv_b", (128, 18))
    dt_bias = din("dt_bias", (128, 1))
    A_log = din("A_log", (128, 1))
    D_diag_in = din("D_diag", (NH, 128, 128))
    ssm_w = din("ssm_w", (128, 16))
    norm_w = din("norm_w", (128, 8))
    post_w = din("post_w", (128, 8))
    W_op = din("W_op", (16, 128, 1024))
    W_g = din("W_g", (8, 128, INTER0P))
    W_u = din("W_u", (8, 128, INTER0P))
    W_d = din("W_d", (22, 128, 1024))
    Win = din("Win", (8, 128, 64))
    bin_in = din("bin_", (64, 1))
    W1_in = din("W1", (64, 16))
    b1_in = din("b1", (64, 16))
    W2_in = din("W2", (64, 16))
    b2_in = din("b2", (64, 1))
    Wout0 = din("Wout0", (64, 1024))
    bout0 = din("bout0", (128, 8))
    W_qkv_qk = din("W_qkv_qk", (8, 128, 1024))
    W_qkv_v = din("W_qkv_v", (8, 128, 512))
    W_wo = din("W_wo", (8, 128, 1024))
    W_t1 = din("W_t1", (32, 128, 1024))
    W_t2 = din("W_t2", (32, 128, 1024))
    W_h1 = din("W_h1", (32, 128, 4096))
    W_h2 = din("W_h2", (32, 128, 4096))
    W_t3 = din("W_t3", (8, 128, 4096))
    s_t1 = din("s_t1", (128, 32))
    s_t2 = din("s_t2", (128, 32))
    s_h1 = din("s_h1", (128, 32))
    s_h2 = din("s_h2", (128, 32))
    s_t3 = din("s_t3", (128, 8))
    ident_in = din("ident", (128, 128))
    ltri_in = din("ltri01", (128, 128))
    amask_in = din("attn_mask", (4, 128, 512))
    uinit_in = din("uinit", (128, 1))
    uinv_in = din("uinv", (128, 1))

    y_out = nc.dram_tensor("y", [D, T], F32, kind="ExternalOutput").ap()
    dbg_out = None
    if dbg is not None:
        dbg_out = nc.dram_tensor("dbg", [dbg[1], dbg[2]], F32, kind="ExternalOutput").ap()

    xtm_d = nc.dram_tensor("xtm_d", [16, 128, 2048], BF16).ap()
    schunk_d = nc.dram_tensor("schunk_d", [NCH, 128, DIN], F32).ap()
    sAG_in = nc.dram_tensor("sAG_in", [128, DIN], F32).ap()
    sAG_out = nc.dram_tensor("sAG_out", [256, DIN], F32).ap()
    x1u_d = nc.dram_tensor("x1u_d", [D, T], F32).ap()
    t1AG_in = nc.dram_tensor("t1AG_in", [D, T], BF16).ap()
    t1AG_out = nc.dram_tensor("t1AG_out", [2 * D, T], BF16).ap()
    oAG_in = nc.dram_tensor("oAG_in", [512, S], BF16).ap()
    oAG_out = nc.dram_tensor("oAG_out", [1024, S], BF16).ap()
    x1p_d = nc.dram_tensor("x1p_d", [D, T], F32).ap()
    x2_d = nc.dram_tensor("x2_d", [D, T], F32).ap()

    em = Emitter()
    stack = contextlib.ExitStack()

    def sbuf(name, shape, dtype, st=None):
        return (st or stack).enter_context(nc.sbuf_tensor(name, list(shape), dtype))

    def chan(name):
        return Chan(stack.enter_context(nc.semaphore(name)))

    ch_t = chan("c_t")
    ch_v = chan("c_v")
    ch_s = chan("c_s")
    ldy = chan("ldy")
    cc = chan("cc")
    _dch = {}

    def dch(key):
        if key not in _dch:
            _dch[key] = chan(f"d_{key}")
        return _dch[key]

    def gdma(out, in_, key, waits=(), accum=False):
        ch = dch(key) if isinstance(key, str) else key
        if accum:
            em.emit("g", lambda e, o=out, i=in_: e.dma_start(out=o, in_=i, accum_op=OP.add),
                    waits, ((ch, 16),))
        else:
            em.emit("g", lambda e, o=out, i=in_: e.dma_start(out=o, in_=i),
                    waits, ((ch, 16),))
        return ch.tok()

    def ydma(out, in_, waits=()):
        em.emit("y", lambda e, o=out, i=in_: e.dma_start(out=o, in_=i),
                waits, ((ldy, 16),))
        return ldy.tok()

    def mm(out, lhsT, rhs, start, stop, waits=(), inc=False, tp=None):
        em.emit("t", lambda e, o=out, l=lhsT, r=rhs, a=start, b=stop, t=tp:
                e.matmul(o, l, r, start=a, stop=b, tile_position=t),
                waits, ((ch_t, 1),) if inc else ())
        return ch_t.tok() if inc else None

    def tr(out, in_, idn, waits=(), inc=True):
        em.emit("t", lambda e, o=out, i=in_, d=idn: e.transpose(o, i, d),
                waits, ((ch_t, 1),) if inc else ())
        return ch_t.tok() if inc else None

    def vop(fn, waits=(), inc=True):
        em.emit("v", fn, waits, ((ch_v, 1),) if inc else ())
        return ch_v.tok() if inc else None

    def sop(fn, waits=(), inc=True):
        em.emit("s", fn, waits, ((ch_s, 1),) if inc else ())
        return ch_s.tok() if inc else None

    identb = sbuf("identb", [128, 128], BF16)
    identF = sbuf("identF", [64, 64], F32)
    ltrib = sbuf("ltrib", [128, 128], F32)

    onesF = sbuf("onesF", [1, 128], F32)
    onescolb = sbuf("onescolb", [128, 1], BF16)
    onescolF = sbuf("onescolF", [128, 1], F32)
    nwcol = sbuf("nwcol", [128, 8], F32)
    pwcol = sbuf("pwcol", [128, 8], F32)
    swcol = sbuf("swcol", [128, 16], F32)
    cbcol = sbuf("cbcol", [128, 18], F32)
    dtbcol = sbuf("dtbcol", [128, 1], F32)
    nAcol = sbuf("nAcol", [128, 1], F32)
    uinitcol = sbuf("uinitcol", [128, 1], F32)
    uinvcol = sbuf("uinvcol", [128, 1], F32)
    mean_r = sbuf("mean_r", [1, TH], F32)
    rc_r = sbuf("rc_r", [1, TH], F32)
    us_r = sbuf("us_r", [1, TH], F32)
    pL0 = contextlib.ExitStack()
    UALL = sbuf("UALL", [128, DC * TH], BF16, pL0)
    CFM = sbuf("CFM", [128, T], BF16, pL0)
    hrA = sbuf("hrA", [32, T], F32, pL0)
    hrB = sbuf("hrB", [32, T], F32, pL0)
    hrD = sbuf("hrD", [32, T], F32, pL0)
    cgT = sbuf("cgT", [32, 16], F32, pL0)
    cgrep = sbuf("cgrep", [1, 512], F32, pL0)
    decCols = sbuf("decCols", [128, 512], F32, pL0)
    dT = sbuf("dT", [128, NCH * 32], F32, pL0)
    wAll = sbuf("wAll", [128, NCH * 32], F32, pL0)
    P1B = sbuf("P1B", [128, NCH * 128], F32, pL0)
    Scur = sbuf("Scur", [128, 128], F32, pL0)
    Sinit = sbuf("Sinit", [128, DIN], F32, pL0)
    WIPR = sbuf("WIPR", [128, 8 * 128], BF16, pL0)
    p1s2 = contextlib.ExitStack()
    var_r = sbuf("var_r", [1, TH], F32, p1s2)
    scr_r = sbuf("scr_r", [1, TH], F32, p1s2)
    BFM = sbuf("BFM", [128, T], BF16, p1s2)
    BTM = sbuf("BTM", [128, NCH * 128], BF16, p1s2)
    hrC = sbuf("hrC", [32, T], F32, p1s2)
    XBCR = sbuf("XBCR", [128, 2 * TH], BF16, p1s2)
    CDG = sbuf("CDG", [128, 2 * 512], BF16, p1s2)
    decrow = sbuf("decrow", [1, 512], F32, p1s2)
    cgeCols = sbuf("cgeCols", [128, 512], F32, p1s2)
    XFR = sbuf("XFR", [128, TH], BF16, p1s2)
    XCV = sbuf("XCV", [128, T], BF16, p1s2)
    XTMp = sbuf("XTMp", [128, 2048], BF16, p1s2)
    BDR = sbuf("BDR", [128, 2 * 128], BF16, p1s2)
    SEV = sbuf("SEV", [128, 2 * 128], F32, p1s2)

    psA = stack.enter_context(nc.psum_tensor("psA", [128, 2048], F32))
    psB = stack.enter_context(nc.psum_tensor("psB", [128, 2048], F32))

    def bank(i, rows_=128, cols=512, roff=0):
        src = psA if i < 4 else psB
        return src[roff:roff + rows_, (i % 4) * 512:(i % 4) * 512 + cols]

    def bankb(i, rows_=128, cols=128):
        src = psA if i < 4 else psB
        return src[0:rows_, (i % 4) * 512:(i % 4) * 512 + 256].bitcast(BF16)[:, 0:cols]

    bank_rd = {}
    bank_wr = {}

    gdma(identF[:, :], ident_in[0:64, 0:64], "consts")
    for dst, src in ((identb, ident_in), (ltrib, ltri_in), (nwcol, norm_w),
                     (pwcol, post_w), (swcol, ssm_w), (cbcol, conv_b),
                     (dtbcol, dt_bias), (nAcol, A_log), (uinitcol, uinit_in),
                     (uinvcol, uinv_in)):
        gdma(dst[:, :], src[:, :], "consts")
    consts = dch("consts").tok()
    vop(lambda e: e.memset(onesF[:, :], 1.0))
    vop(lambda e: e.memset(onescolb[:, :], 1.0))
    vop(lambda e: e.memset(onescolF[:, :], 1.0))
    ones_tok = ch_v.tok()
    sop(lambda e: e.activation(nAcol[:, :], nAcol[:, :], AF.Exp), waits=(consts,))
    nA_tok = vop(lambda e: e.tensor_scalar(nAcol[:, :], nAcol[:, :], -1.0, None, OP.mult),
                 waits=(ch_s.tok(),))

    p0s = contextlib.ExitStack()
    X0F = sbuf("X0F", [128, DC * 512], F32, p0s)
    X0B = sbuf("X0B", [128, DC * 512], BF16, p0s)
    SQB = sbuf("SQB", [128, DC * 512], BF16, p0s)
    UTMP = sbuf("UTMP", [128, 512], F32, p0s)
    x0_free = [None, None]
    u_tok = [None] * 5
    prev_stats = None
    for g in range(5):
        go, gw = NS_OFF[g], NS_W[g]
        sl = 0
        x0tok = gdma(X0F[:, sl:sl + DC * gw],
                     x_in.rearrange("(c p) t -> p c t", p=128)[:, :, go:go + gw],
                     "x0f", waits=(x0_free[0], x0_free[1]))
        xbtok = vop(lambda e, gw=gw, sl=sl: e.tensor_copy(
            X0B[:, :DC * gw], X0F[:, sl:sl + DC * gw]), waits=(x0tok, prev_stats))
        sqtok = sop(lambda e, gw=gw, sl=sl: e.activation(
            SQB[:, :DC * gw], X0F[:, sl:sl + DC * gw], AF.Square),
            waits=(x0tok, prev_stats))
        b0, b1v = bank(0, 1, gw), bank(1, 1, gw)
        for c in range(DC):
            w = [xbtok, ones_tok, bank_rd.get(0)] if c == 0 else []
            mm(b0, onescolb[:, :], X0B[:, c * gw:(c + 1) * gw],
               start=(c == 0), stop=(c == DC - 1), waits=tuple(w))
        for c in range(DC):
            w = [sqtok, bank_rd.get(1)] if c == 0 else []
            stats_tok = mm(b1v, onescolb[:, :], SQB[:, c * gw:(c + 1) * gw],
                           start=(c == 0), stop=(c == DC - 1), waits=tuple(w),
                           inc=(c == DC - 1))
        vop(lambda e, go=go, gw=gw, b0=b0: e.tensor_scalar(
            mean_r[0:1, go:go + gw], b0, 1.0 / D, None, OP.mult), waits=(stats_tok,))
        vop(lambda e, go=go, gw=gw, b1v=b1v: e.tensor_scalar(
            scr_r[0:1, go:go + gw], b1v, 1.0 / D, None, OP.mult))
        bank_rd[0] = bank_rd[1] = ch_v.tok()
        stats_rows_tok = ch_v.tok()
        vop(lambda e, go=go, gw=gw: e.tensor_mul(
            var_r[0:1, go:go + gw], mean_r[0:1, go:go + gw], mean_r[0:1, go:go + gw]))
        vop(lambda e, go=go, gw=gw: e.tensor_sub(
            var_r[0:1, go:go + gw], scr_r[0:1, go:go + gw], var_r[0:1, go:go + gw]))
        var_tok = vop(lambda e, go=go, gw=gw: e.tensor_scalar(
            var_r[0:1, go:go + gw], var_r[0:1, go:go + gw], 0.0, None, OP.max))
        vop(lambda e, go=go, gw=gw: e.tensor_scalar(
            rc_r[0:1, go:go + gw], var_r[0:1, go:go + gw], 0.0, 1e-5, OP.max, OP.add),
            waits=(var_tok,))
        sop(lambda e, go=go, gw=gw: e.activation(
            rc_r[0:1, go:go + gw], rc_r[0:1, go:go + gw], AF.Ln),
            waits=(ch_v.tok(),))
        rc_tok = sop(lambda e, go=go, gw=gw: e.activation(
            rc_r[0:1, go:go + gw], rc_r[0:1, go:go + gw], AF.Exp, scale=-0.5))
        vop(lambda e, go=go, gw=gw: e.tensor_mul(
            scr_r[0:1, go:go + gw], var_r[0:1, go:go + gw], rc_r[0:1, go:go + gw]),
            waits=(rc_tok,))
        q_tok = vop(lambda e, go=go, gw=gw: e.tensor_mul(
            scr_r[0:1, go:go + gw], scr_r[0:1, go:go + gw], rc_r[0:1, go:go + gw]))
        vop(lambda e, go=go, gw=gw: e.tensor_scalar(
            scr_r[0:1, go:go + gw], scr_r[0:1, go:go + gw], 1e-6, None, OP.add),
            waits=(q_tok,))
        sop(lambda e, go=go, gw=gw: e.activation(
            scr_r[0:1, go:go + gw], scr_r[0:1, go:go + gw], AF.Ln),
            waits=(ch_v.tok(),))
        e_tok = sop(lambda e, go=go, gw=gw: e.activation(
            scr_r[0:1, go:go + gw], scr_r[0:1, go:go + gw], AF.Exp, scale=-0.5))
        us_tok = vop(lambda e, go=go, gw=gw: e.tensor_mul(
            us_r[0:1, go:go + gw], scr_r[0:1, go:go + gw], rc_r[0:1, go:go + gw]),
            waits=(e_tok,))
        b2v, b3v = bank(2, 128, gw), bank(3, 128, gw)
        w = [ones_tok, bank_rd.get(2), stats_rows_tok]
        mm(b2v, onesF[0:1, :], mean_r[0:1, go:go + gw], start=True, stop=True,
           waits=tuple(w))
        w = [us_tok, bank_rd.get(3)]
        bc_tok = mm(b3v, onesF[0:1, :], us_r[0:1, go:go + gw], start=True, stop=True,
                    waits=tuple(w), inc=True)
        for c in range(DC):
            vop(lambda e, c=c, gw=gw, sl=sl, b2v=b2v: e.tensor_sub(
                UTMP[:, 0:gw], X0F[:, sl + c * gw: sl + (c + 1) * gw], b2v),
                waits=(bc_tok,) if c == 0 else ())
            vop(lambda e, c=c, go=go, gw=gw, b3v=b3v: e.scalar_tensor_tensor(
                UALL[:, c * TH + go: c * TH + go + gw],
                UTMP[:, 0:gw], nwcol[:, c:c + 1], b3v, OP.mult, OP.mult),
                waits=(consts,) if c == 0 else ())
        u_tok[g] = ch_v.tok()
        prev_stats = stats_tok
        bank_rd[2] = bank_rd[3] = ch_v.tok()
        x0_free[0] = ch_v.tok()
        x0_free[1] = ch_s.tok()
    u_all = ch_v.tok()
    p0s.close()
    for i in range(8):
        bank_rd[i] = None
    bank_rd[0] = bank_rd[1] = bank_rd[2] = bank_rd[3] = u_all

    if stop_after == "P0":
        DUMPS = sbuf("DUMPS", [128, 64], F32, p1s2)
        vop(lambda e: e.tensor_copy(DUMPS[0:1, 0:3], psA[0:1, 0:3]), waits=(u_all,))
        vop(lambda e: e.tensor_copy(DUMPS[0:1, 4:7], psA[0:1, 512:515]))
        vop(lambda e: e.tensor_copy(DUMPS[:, 8:32], X0B[:, 0:24]))
        vop(lambda e: e.tensor_copy(DUMPS[:, 32:56], SQB[:, 0:24]))
        dtk = vop(lambda e: e.tensor_copy(DUMPS[:, 56:57], onescolb[:, :]))
        gdma(y_out[0:1, :], mean_r[0:1, 3:2051], "dump", waits=(u_all,))
        gdma(y_out[1:2, :], us_r[0:1, 3:2051], "dump")
        gdma(y_out[2:3, :], rc_r[0:1, 3:2051], "dump")
        gdma(y_out[3:4, :], var_r[0:1, 3:2051], "dump")
        gdma(y_out[4:5, :], scr_r[0:1, 3:2051], "dump")
        gdma(y_out[576:704, :], UALL[:, 3:2051], "dump")
        gdma(y_out[704:832, :], UALL[:, 7 * TH + 3:7 * TH + 2051], "dump")
        d9 = gdma(y_out[320:448, 0:64], DUMPS[:, :], "dump", waits=(dtk,))
        p1s2.close()
        pL0.close()
        _finish(em, nc, stack, y_out, dbg, dbg_out, locals(), d9, ydma, ch_v, ch_s, ch_t)
        return nc

    wip_state = {"free": [u_all] * 8, "idx": 0}

    def inproj_mtile(mt, evac, slices=range(5), bksel=(4, 5, 6, 7, 0), dmaslot=1):
        toks = []
        for k in range(8):
            sl_i = wip_state["idx"] % 8
            wip_state["idx"] += 1
            wtok = gdma(WIPR[:, sl_i * 128:(sl_i + 1) * 128], W_ip[mt, k, :, :],
                        f"wipr{sl_i}", waits=(wip_state["free"][sl_i],))
            toks.append((sl_i, wtok))
        for ki, (sl_i, wtok) in enumerate(toks):
            for ng in slices:
                no, nw = NS_OFF[ng], NS_W[ng]
                bi = bksel[ng if len(bksel) == 5 else list(slices).index(ng)]
                w = [wtok]
                if ki == 0:
                    w.append(u_all)
                    w.append(bank_rd.get(bi))
                t = mm(bank(bi, 128, nw), WIPR[:, sl_i * 128:(sl_i + 1) * 128],
                       UALL[:, ki * TH + no: ki * TH + no + nw],
                       start=(ki == 0), stop=(ki == 7),
                       waits=tuple(w), inc=(ki == 7 or ng == list(slices)[-1]))
                if ki == 7:
                    bank_wr[bi] = t
            wip_state["free"][sl_i] = ch_t.tok()
        for ng in slices:
            bi = bksel[ng if len(bksel) == 5 else list(slices).index(ng)]
            bank_rd[bi] = evac(ng, bank(bi, 128, NS_W[ng]), bank_wr[bi])

    cdg_free = [u_all, u_all]
    xbc_free = [None, None]

    def evac_to_ring(ring_slot):
        def evac(ng, bv, wr):
            no, nw = NS_OFF[ng], NS_W[ng]
            return vop(lambda e, no=no, nw=nw, bv=bv, rs=ring_slot: e.tensor_copy(
                XBCR[:, rs * TH + no: rs * TH + no + nw], bv),
                waits=(wr, xbc_free[ring_slot]) if ng == 0 else (wr,))
        return evac

    def emit_conv(cchunk, ring_slot, out_fn, bks=(1, 2)):
        dslot = cchunk % 2
        dtok = gdma(CDG[:, dslot * 512:(dslot + 1) * 512],
                    conv_diag.rearrange("c t p m -> c p t m")[cchunk, :, :, :],
                    f"cdg{dslot}", waits=(cdg_free[dslot],))
        xtok = ch_v.tok()
        last = None
        for ng in range(4):
            no = ng * 512
            bi = bks[ng % 2]
            cb = bank(bi, 128, 512)
            for tap in range(4):
                w = [dtok, xtok, bank_rd.get(bi)] if tap == 0 else []
                t = mm(cb, CDG[:, dslot * 512 + tap * 128: dslot * 512 + (tap + 1) * 128],
                       XBCR[:, ring_slot * TH + no + tap: ring_slot * TH + no + tap + 512],
                       start=(tap == 0), stop=(tap == 3), waits=tuple(w), inc=(tap == 3))
            bank_wr[bi] = t
            last = out_fn(ng, cb, no, 512, t)
            bank_rd[bi] = last
        cdg_free[dslot] = last
        xbc_free[ring_slot] = last
        return last

    for bc in range(2):
        mt = 32 + bc
        inproj_mtile(mt, evac_to_ring(bc % 2))
        xbc_tok = ch_v.tok()

        def bc_out(ng, cb, no, nw, wr, bc=bc):
            dst = BFM if bc == 0 else CFM
            return sop(lambda e, cb=cb, no=no, nw=nw, dst=dst, bc=bc: e.activation(
                dst[:, no: no + nw], cb, AF.Silu,
                bias=cbcol[:, 16 + bc:17 + bc]), waits=(wr,))
        emit_conv(16 + bc, bc % 2, bc_out)
    bc_done = ch_s.tok()
    def evac_dt(ng, bv, wr):
        no, nw = NS_OFF[ng], NS_W[ng]
        if ng == 0:
            return vop(lambda e, bv=bv: e.tensor_copy(hrA[0:32, 0:509], bv[0:32, 3:512]),
                       waits=(wr,))
        return vop(lambda e, bv=bv, no=no, nw=nw: e.tensor_copy(
            hrA[0:32, no - 3: no - 3 + nw], bv[0:32, 0:nw]), waits=(wr,))
    inproj_mtile(34, evac_dt)
    dtraw_tok = ch_v.tok()
    for tc in range(NCH):
        tb = bankb(7, 128, 128)
        w = [bc_done, consts, bank_rd.get(7)]
        t = tr(tb, BFM[:, tc * 128:(tc + 1) * 128], identb[:, :], waits=tuple(w))
        bank_rd[7] = vop(lambda e, tc=tc, tb=tb: e.tensor_copy(
            BTM[:, tc * 128:(tc + 1) * 128], tb), waits=(t,))
    btm_tok = ch_v.tok()

    vop(lambda e: e.tensor_scalar(hrA[:, :], hrA[:, :], dtbcol[0:32, :], None, OP.add),
        waits=(dtraw_tok, consts))
    dtb_tok = ch_v.tok()
    sop(lambda e: e.activation(hrC[:, :], hrA[:, :], AF.Abs), waits=(dtb_tok,))
    sop(lambda e: e.activation(hrC[:, :], hrC[:, :], AF.Exp, scale=-1.0))
    vop(lambda e: e.tensor_scalar(hrC[:, :], hrC[:, :], 1.0, None, OP.add),
        waits=(ch_s.tok(),))
    ln_tok = sop(lambda e: e.activation(hrC[:, :], hrC[:, :], AF.Ln),
                 waits=(ch_v.tok(),))
    vop(lambda e: e.tensor_relu(hrB[:, :], hrA[:, :]))
    dt_tok = vop(lambda e: e.tensor_add(hrB[:, :], hrB[:, :], hrC[:, :]), waits=(ln_tok,))
    dtA_tok = vop(lambda e: e.tensor_scalar(hrC[:, :], hrB[:, :], nAcol[0:32, :],
                                            None, OP.mult), waits=(dt_tok, nA_tok))
    lndt_tok = sop(lambda e: e.activation(hrB[:, :], hrB[:, :], AF.Ln), waits=(dtA_tok,))
    zr_tok = vop(lambda e: e.memset(hrA[:, :], 0.0))
    cg_tok = vop(lambda e: e.tensor_tensor_scan(
        hrD[:, :], hrC[:, :], hrA[:, :], 0.0, OP.add, OP.add), waits=(zr_tok,))
    d_tok = vop(lambda e: e.tensor_sub(hrB[:, :], hrB[:, :], hrD[:, :]),
                waits=(cg_tok, lndt_tok))

    cgT_tok = vop(lambda e: e.tensor_copy(
        cgT[:, :], hrD.rearrange("h (k q) -> h k q", q=Q)[:, :, Q - 1]), waits=(cg_tok,))
    cgrep_tok = gdma(cgrep[:, :], cgT[:, :], "rows0", waits=(cgT_tok,))
    vop(lambda e: e.memset(decrow[:, :], 0.0))
    z_tok = ch_v.tok()
    vop(lambda e: e.tensor_copy(
        decrow.rearrange("o (h k) -> o h k", k=16)[:, :, 1:16],
        cgrep.rearrange("o (h k) -> o h k", k=16)[:, :, 0:15]),
        waits=(cgrep_tok, z_tok))
    ddiff_tok = vop(lambda e: e.tensor_sub(decrow[:, :], cgrep[:, :], decrow[:, :]))
    decrow_tok = sop(lambda e: e.activation(decrow[:, :], decrow[:, :], AF.Exp),
                     waits=(ddiff_tok,))
    w = [decrow_tok, ones_tok, bank_rd.get(0)]
    t = mm(bank(0, 128, 512), onesF[:, :], decrow[:, :], start=True, stop=True,
           waits=tuple(w), inc=True)
    dec_tok = vop(lambda e: e.tensor_copy(decCols[:, :], bank(0, 128, 512)), waits=(t,))
    bank_rd[0] = dec_tok
    w = [cgrep_tok, ones_tok, bank_rd.get(1)]
    t = mm(bank(1, 128, 512), onesF[:, :], cgrep[:, :], start=True, stop=True,
           waits=tuple(w), inc=True)
    cge_tok = vop(lambda e: e.tensor_copy(cgeCols[:, :], bank(1, 128, 512)), waits=(t,))
    bank_rd[1] = cge_tok
    for k in range(NCH):
        tb = bank(2, 128, 32)
        w = [d_tok, consts, bank_rd.get(2)]
        t = tr(tb, hrB[0:32, k * Q:(k + 1) * Q], identF[0:32, 0:32], waits=tuple(w))
        bank_rd[2] = vop(lambda e, k=k, tb=tb: e.tensor_copy(
            dT[:, k * 32:(k + 1) * 32], tb), waits=(t,))
    dT_tok = ch_v.tok()
    for k in range(NCH):
        vop(lambda e, k=k: e.tensor_add(
            wAll[:, k * 32:(k + 1) * 32], dT[:, k * 32:(k + 1) * 32],
            cgeCols[:, k::16]), waits=(dT_tok, cge_tok) if k == 0 else ())
        sop(lambda e, k=k: e.activation(
            wAll[:, k * 32:(k + 1) * 32], wAll[:, k * 32:(k + 1) * 32], AF.Exp),
            waits=(ch_v.tok(),))
    wAll_tok = ch_s.tok()
    for k in range(NCH):
        if k == 0:
            vop(lambda e: e.tensor_copy(hrA[:, 0:Q], hrD[:, 0:Q]), waits=(cg_tok,))
        else:
            vop(lambda e, k=k: e.tensor_scalar(
                hrA[:, k * Q:(k + 1) * Q], hrD[:, k * Q:(k + 1) * Q],
                cgT[:, k - 1:k], None, OP.subtract),
                waits=(cgT_tok,) if k == 1 else ())
    edec_tok = sop(lambda e: e.activation(hrA[:, :], hrA[:, :], AF.Exp),
                   waits=(ch_v.tok(),))
    for k in range(NCH):
        bv = bank(3, 128, 128)
        w = [bc_done, bank_rd.get(3)]
        t = mm(bv, BFM[:, k * 128:(k + 1) * 128], CFM[:, k * 128:(k + 1) * 128],
               start=True, stop=True, waits=tuple(w), inc=True)
        bank_rd[3] = vop(lambda e, k=k, bv=bv: e.tensor_mul(
            P1B[:, k * 128:(k + 1) * 128], bv, ltrib[:, :]), waits=(t, consts))
    p1b_tok = ch_v.tok()

    if stop_after == "P2":
        p1s2.close()
        pL0.close()
        _finish(em, nc, stack, y_out, dbg, dbg_out, locals(),
                p1b_tok, ydma, ch_v, ch_s, ch_t)
        return nc

    xfr_free = None
    xcv_free = None
    xtmp_free = None
    sc_zero = None
    for pair in range(16):
        def evac_x(ng, bv, wr, _ff=None):
            no, nw = NS_OFF[ng], NS_W[ng]
            w = [wr] if ng else [wr, xfr_free]
            return sop(lambda e, no=no, nw=nw, bv=bv: e.activation(
                XFR[:, no:no + nw], bv, AF.Identity), waits=tuple(x for x in w if x))
        inproj_mtile(16 + pair, evac_x)
        xf_tok = ch_s.tok()
        def x_out(ng, cb, no, nw, wr, pair=pair):
            w = [wr] if ng else [wr, xcv_free]
            return sop(lambda e, cb=cb, no=no, nw=nw, pair=pair: e.activation(
                XCV[:, no:no + nw], cb, AF.Silu, bias=cbcol[:, pair:pair + 1]),
                waits=tuple(x for x in w if x))
        dslot = pair % 2
        dtok = gdma(CDG[:, dslot * 512:(dslot + 1) * 512],
                    conv_diag.rearrange("c t p m -> c p t m")[pair, :, :, :],
                    f"cdg{dslot}", waits=(cdg_free[dslot],))
        last = None
        for ng in range(4):
            no = ng * 512
            bi = (1, 2)[ng % 2]
            cb = bank(bi, 128, 512)
            for tap in range(4):
                w = [dtok, xf_tok, bank_rd.get(bi)] if tap == 0 else []
                t = mm(cb, CDG[:, dslot * 512 + tap * 128: dslot * 512 + (tap + 1) * 128],
                       XFR[:, no + tap: no + tap + 512],
                       start=(tap == 0), stop=(tap == 3), waits=tuple(w), inc=(tap == 3))
            last = x_out(ng, cb, no, 512, t)
            bank_rd[bi] = last
        cdg_free[dslot] = last
        xfr_free = last
        silu_tok = ch_s.tok()
        for tc in range(NCH):
            for hh in range(2):
                tb = bankb(3, 128, 64)
                w = [silu_tok, consts, bank_rd.get(3)]
                if tc == 0 and hh == 0 and xtmp_free is not None:
                    w.extend(xtmp_free)
                t = tr(tb, XCV[hh * 64:(hh + 1) * 64, tc * 128:(tc + 1) * 128],
                       identb[hh * 64:(hh + 1) * 64, hh * 64:(hh + 1) * 64],
                       waits=tuple(x for x in w if x))
                bank_rd[3] = vop(lambda e, tc=tc, hh=hh, tb=tb: e.tensor_copy(
                    XTMp[:, (tc * 2 + hh) * 64:(tc * 2 + hh) * 64 + 64], tb), waits=(t,))
        xtm_tok = ch_v.tok()
        xcv_free = xtm_tok
        ydma(xtm_d[pair, :, :], XTMp[:, :], waits=(xtm_tok,))
        xtm_st = ldy.tok()
        vop(lambda e: e.memset(Scur[:, :], 0.0), waits=(sc_zero,))
        sc_tok = ch_v.tok()
        sev_free = [None, None]
        for k in range(NCH):
            bd_toks = []
            for hh in range(2):
                h = pair * 2 + hh
                w = [btm_tok, wAll_tok]
                if bank_rd.get(90 + hh):
                    w.append(bank_rd[90 + hh])
                bd_toks.append(vop(lambda e, k=k, hh=hh, h=h: e.tensor_scalar(
                    BDR[:, hh * 128:(hh + 1) * 128], BTM[:, k * 128:(k + 1) * 128],
                    wAll[:, k * 32 + h: k * 32 + h + 1], None, OP.mult),
                    waits=tuple(w)))
            bi = 4 + (k % 2)
            bv = bank(bi, 128, 128)
            for hh in range(2):
                w = [bd_toks[hh], xtm_tok]
                if hh == 0 and bank_rd.get(bi):
                    w.append(bank_rd[bi])
                t = mm(bv[:, hh * 64:(hh + 1) * 64], BDR[:, hh * 128:(hh + 1) * 128],
                       XTMp[:, (k * 2 + hh) * 64:(k * 2 + hh) * 64 + 64],
                       start=True, stop=True, waits=tuple(w), inc=(hh == 1))
            bank_wr[bi] = t
            bank_rd[90] = bank_rd[91] = t
            es = k % 2
            ev = vop(lambda e, bv=bv, es=es: e.tensor_copy(
                SEV[:, es * 128:(es + 1) * 128], bv), waits=(t, sev_free[es]))
            bank_rd[bi] = ev
            ydma(schunk_d[k, :, pair * 128:(pair + 1) * 128],
                 SEV[:, es * 128:(es + 1) * 128], waits=(ev,))
            sev_free[es] = ldy.tok()
            for hh in range(2):
                h = pair * 2 + hh
                vop(lambda e, hh=hh, h=h, k=k: e.tensor_scalar(
                    Scur[:, hh * 64:(hh + 1) * 64], Scur[:, hh * 64:(hh + 1) * 64],
                    decCols[:, h * 16 + k: h * 16 + k + 1], None, OP.mult),
                    waits=(sc_tok, dec_tok) if k == 0 and hh == 0 else ())
            vop(lambda e, es=es: e.tensor_add(
                Scur[:, :], Scur[:, :], SEV[:, es * 128:(es + 1) * 128]), waits=(ev,))
            sc_tok = ch_v.tok()
        ydma(sAG_in[:, pair * 128:(pair + 1) * 128], Scur[:, :], waits=(sc_tok,))
        sc_zero = ldy.tok()
        xtmp_free = (xtm_st, ch_t.tok())
    sA_done = ldy.tok()

    em.emit("g", lambda e: e.collective_compute(
        "AllGather", OP.bypass, replica_groups=GROUPS,
        ins=[sAG_in[:, :]], outs=[sAG_out[:, :]]),
        (sA_done,), ((cc, 1),))
    ag1 = cc.tok()
    si_ld = gdma(Sinit[:, :], sAG_out[0:128, :], "sinit", waits=(ag1,))
    si_tok = vop(lambda e: e.tensor_scalar(Sinit[:, :], Sinit[:, :], uinitcol[:, :],
                                           None, OP.mult), waits=(si_ld, consts))

    if stop_after == "A":
        d1 = ydma(y_out[0:128, :], Sinit[:, :], waits=(si_tok,))
        d2 = ydma(y_out[128:160, :], hrD[:, :], waits=(cg_tok,))
        d3 = ydma(y_out[160:192, :], hrB[:, :], waits=(d_tok,))
        d4 = ydma(y_out[192:320, :], schunk_d[5, :, :], waits=(sA_done,))
        d5 = gdma(y_out[320:448, :], xtm_d[3, :, :], "dump", waits=(sA_done,))
        d6 = ydma(y_out[448:576, :], sAG_in[:, :], waits=(sA_done,))
        d7 = gdma(y_out[576:704, :], UALL[:, 3:2051], "dump", waits=(u_all,))
        d8 = gdma(y_out[704:832, :], BFM[:, :], "dump", waits=(bc_done,))
        d9 = ydma(y_out[832:864, :], hrA[:, :], waits=(dtraw_tok,))
        p1s2.close()
        pL0.close()
        _finish(em, nc, stack, y_out, dbg, dbg_out, locals(), d6, ydma, ch_v, ch_s, ch_t)
        return nc

    p1s2.close()
    p3s = contextlib.ExitStack()
    SLOTS = sbuf("SLOTS", [2, 6 * 1024], F32, p3s)
    MHALF = sbuf("MHALF", [128, 16 * 1024], BF16, p3s)
    SZR = sbuf("SZR", [128, 2 * 1024], BF16, p3s)
    XTMR = sbuf("XTMR", [128, 2 * 2048], BF16, p3s)
    SCH = sbuf("SCH", [128, 2 * 128], F32, p3s)
    SCARRY = sbuf("SCARRY", [128, 16 * 128], F32, p3s)
    E2R = sbuf("E2R", [128, 2 * 256], F32, p3s)
    SCB = sbuf("SCB", [128, 2 * 128], BF16, p3s)
    GTR = sbuf("GTR", [128, 2 * 256], BF16, p3s)
    CHR = sbuf("CHR", [128, 2 * 128], BF16, p3s)
    DDG = sbuf("DDG", [128, NH * 128], BF16, p3s)
    ddg_toks = []
    for h in range(NH):
        ddg_toks.append(gdma(DDG[:, h * 128:(h + 1) * 128], D_diag_in[h, :, :], "ddg",
                             waits=(sA_done,) if h == 0 else ()))
    ddg_tok = dch("ddg").tok()
    for r in range(2):
        vop(lambda e, r=r: e.memset(SLOTS[0:2, r * 3072: r * 3072 + 2048], 1.0))
    slots_ones = ch_v.tok()
    slot_free = [sA_done] * 2
    szr_free = [None, None]
    e2_free = [None, None]
    xtmr_free = [None, None]
    sch_free = [None, None]
    scb_free = [None, None]
    gt_free = [None, None]
    chr_free = [None, None]
    mh_guard = None

    SQH = sbuf("SQH", [128, 512], BF16, p3s)
    X0R = sbuf("X0R", [128, 2 * 512], F32, p3s)
    WSC = sbuf("WSC", [128, 2 * 128], F32, p3s)
    for half in range(2):
        k0 = half * 8
        for pair in range(16):
            sl_toks = []
            rr = [0, 0]
            for hh in range(2):
                h = pair * 2 + hh
                r = hh
                rr[hh] = r
                hc = slice(k0 * Q, k0 * Q + 1024)
                w = (slot_free[r], slots_ones, d_tok, edec_tok)
                gdma(SLOTS[1:2, r * 3072: r * 3072 + 1024], hrB[h:h + 1, hc], f"slr{r}", waits=w)
                gdma(SLOTS[0:1, r * 3072 + 1024: r * 3072 + 2048], hrD[h:h + 1, hc], f"slr{r}")
                sl_toks.append(gdma(SLOTS[0:1, r * 3072 + 2048: r * 3072 + 3072],
                                    hrA[h:h + 1, hc], f"slr{r}"))
            xr = pair % 2
            xt_tok = gdma(XTMR[:, xr * 2048:(xr + 1) * 2048], xtm_d[pair, :, :], f"xtmr{xr}",
                          waits=(sA_done, xtmr_free[xr]))
            zoff = k0 * Q + HALO
            szr = pair % 2
            sz_tok = _emit_z(em, mm, bank, sop, vop, gdma, W_ip, WIPR, wip_state, UALL,
                             u_all, bank_rd, bank_wr, SZR, szr, pair, zoff,
                             szr_free[szr], ch_t, ch_s)
            scur_init = None
            if half == 0:
                scur_init = vop(lambda e, pair=pair: e.tensor_copy(
                    SCARRY[:, pair * 128:(pair + 1) * 128], Sinit[:, pair * 128:(pair + 1) * 128]),
                    waits=(si_tok,))
            sc_tok2 = scur_init or ch_v.tok()
            for kk in range(8):
                k = k0 + kk
                sr = kk % 2
                sch_tok = gdma(SCH[:, sr * 128:(sr + 1) * 128],
                               schunk_d[k, :, pair * 128:(pair + 1) * 128], f"sch{sr}",
                               waits=(sA_done, sch_free[sr]))
                psT = bank(0, 128, 256)
                for hh in range(2):
                    h = pair * 2 + hh
                    r = rr[hh]
                    w = [sl_toks[hh], bank_rd.get(0)] if hh == 0 else [sl_toks[hh]]
                    t = mm(psT[:, hh * 128:(hh + 1) * 128],
                           SLOTS[0:2, r * 3072 + kk * Q: r * 3072 + (kk + 1) * Q],
                           SLOTS[0:2, r * 3072 + 1024 + kk * Q: r * 3072 + 1024 + (kk + 1) * Q],
                           start=True, stop=True, waits=tuple(w), inc=(hh == 1))
                bank_wr[0] = t
                er = kk % 2
                clamp_tok = vop(lambda e, psT=psT: e.tensor_scalar(
                    psT, psT, 80.0, None, OP.min), waits=(t,))
                e2_tok = sop(lambda e, er=er, psT=psT: e.activation(
                    E2R[:, er * 256:(er + 1) * 256], psT, AF.Exp),
                    waits=(clamp_tok, e2_free[er]))
                bank_rd[0] = e2_tok
                g_toks = []
                for hh in range(2):
                    g_toks.append(vop(lambda e, er=er, hh=hh, k=k: e.scalar_tensor_tensor(
                        GTR[:, er * 256 + hh * 128: er * 256 + (hh + 1) * 128],
                        E2R[:, er * 256 + hh * 128: er * 256 + (hh + 1) * 128],
                        1e30, P1B[:, k * 128:(k + 1) * 128], OP.min, OP.mult),
                        waits=(e2_tok, p1b_tok, gt_free[er])))
                e2_free[er] = g_toks[1]
                ebv = bank(1, 128, 128)
                w = [sl_toks[0], sl_toks[1], ones_tok, bank_rd.get(1), chr_free[0]]
                t = mm(ebv, onesF[:, :],
                       SLOTS[0:1, rr[0] * 3072 + 2048 + kk * Q: rr[0] * 3072 + 2048 + (kk + 1) * Q],
                       start=True, stop=True, waits=tuple(x for x in w if x), inc=True)
                ch_tokA = vop(lambda e, ebv=ebv, k=k: e.tensor_mul(
                    CHR[:, 0:128], CFM[:, k * 128:(k + 1) * 128], ebv),
                    waits=(t, bc_done))
                w2 = [ones_tok]
                t2 = mm(ebv, onesF[:, :],
                        SLOTS[0:1, rr[1] * 3072 + 2048 + kk * Q: rr[1] * 3072 + 2048 + (kk + 1) * Q],
                        start=True, stop=True, waits=(ch_tokA, chr_free[1]), inc=True)
                ch_tokB = vop(lambda e, ebv=ebv, k=k: e.tensor_mul(
                    CHR[:, 128:256], CFM[:, k * 128:(k + 1) * 128], ebv),
                    waits=(t2, bc_done))
                bank_rd[1] = ch_tokB
                ch_toks = [ch_tokA, ch_tokB]
                scb_tok = vop(lambda e, pair=pair, sr=sr: e.tensor_copy(
                    SCB[:, sr * 128:(sr + 1) * 128], SCARRY[:, pair * 128:(pair + 1) * 128]),
                    waits=(sc_tok2, scb_free[sr]))
                ybi = 2 + (kk // 4)
                yb = bank(ybi, 128, 512)
                for hh in range(2):
                    h = pair * 2 + hh
                    tp = (0, 64 * hh)
                    w = [g_toks[hh], xt_tok]
                    if kk % 4 == 0 and hh == 0:
                        w.append(bank_rd.get(ybi))
                    mm(yb[64 * hh: 64 * hh + 64, (kk % 4) * 128:(kk % 4) * 128 + 128],
                       XTMR[:, xr * 2048 + (k - 0) * 128 + hh * 64: xr * 2048 + k * 128 + hh * 64 + 64]
                       if False else
                       XTMR[:, xr * 2048 + (k * 2 + hh) * 64: xr * 2048 + (k * 2 + hh) * 64 + 64],
                       GTR[:, er * 256 + hh * 128: er * 256 + (hh + 1) * 128],
                       start=True, stop=False, waits=tuple(w), tp=tp)
                    mm(yb[64 * hh: 64 * hh + 64, (kk % 4) * 128:(kk % 4) * 128 + 128],
                       SCB[:, sr * 128 + hh * 64: sr * 128 + hh * 64 + 64],
                       CHR[:, hh * 128:(hh + 1) * 128],
                       start=False, stop=False, waits=(ch_toks[hh], scb_tok), tp=tp)
                    t = mm(yb[64 * hh: 64 * hh + 64, (kk % 4) * 128:(kk % 4) * 128 + 128],
                           XTMR[:, xr * 2048 + (k * 2 + hh) * 64: xr * 2048 + (k * 2 + hh) * 64 + 64],
                           DDG[:, (pair * 2 + hh) * 128:(pair * 2 + hh) * 128 + 128],
                           start=False, stop=True, waits=(ddg_tok,), tp=tp,
                           inc=(hh == 1))
                bank_wr[ybi] = t
                gt_free[er] = t
                chr_free[0] = chr_free[1] = t
                xtmr_free[xr] = t
                scb_free[sr] = t
                for hh in range(2):
                    h = pair * 2 + hh
                    vop(lambda e, pair=pair, hh=hh, h=h, k=k: e.tensor_scalar(
                        SCARRY[:, pair * 128 + hh * 64: pair * 128 + hh * 64 + 64],
                        SCARRY[:, pair * 128 + hh * 64: pair * 128 + hh * 64 + 64],
                        decCols[:, h * 16 + k: h * 16 + k + 1], None, OP.mult),
                        waits=(t, dec_tok))
                vop(lambda e, pair=pair, sr=sr: e.tensor_add(
                    SCARRY[:, pair * 128:(pair + 1) * 128],
                    SCARRY[:, pair * 128:(pair + 1) * 128],
                    SCH[:, sr * 128:(sr + 1) * 128]), waits=(sch_tok,))
                sc_tok2 = ch_v.tok()
                sch_free[sr] = sc_tok2
                if kk % 4 == 3:
                    w = [bank_wr[ybi], sz_tok, mh_guard]
                    gt = vop(lambda e, ybi=ybi, pair=pair, kk=kk, szr=szr: e.tensor_mul(
                        MHALF[:, pair * 1024 + (kk // 4) * 512: pair * 1024 + (kk // 4) * 512 + 512],
                        bank(ybi, 128, 512),
                        SZR[:, szr * 1024 + (kk // 4) * 512: szr * 1024 + (kk // 4) * 512 + 512]),
                        waits=tuple(w))
                    bank_rd[ybi] = gt
                    szr_free[szr] = gt
            for hh in range(2):
                slot_free[rr[hh]] = ch_t.tok()
        m_tok = ch_v.tok()

        if stop_after == "Bg":
            p3s.close()
            pL0.close()
            _finish(em, nc, stack, y_out, dbg, dbg_out, locals(), m_tok, ydma,
                    ch_v, ch_s, ch_t)
            return nc

        for sl2 in range(2):
            so2 = sl2 * 512
            bsum = bank(4, 1, 512)
            for pair in range(16):
                sq = sop(lambda e, pair=pair, sl2=sl2: e.activation(
                    SQH[:, 0:512], MHALF[:, pair * 1024 + sl2 * 512: pair * 1024 + sl2 * 512 + 512],
                    AF.Square), waits=(m_tok,) if pair == 0 else (ch_t.tok(),))
                w = [sq]
                if pair == 0:
                    w.append(bank_rd.get(4))
                t = mm(bsum, onescolb[:, :], SQH[:, 0:512], start=(pair == 0),
                       stop=(pair == 15), waits=tuple(w), inc=True)
            vop(lambda e, bsum=bsum, so2=so2: e.tensor_scalar(
                us_r[0:1, so2:so2 + 512], bsum, 1.0 / DIN, None, OP.mult), waits=(t,))
            bank_rd[4] = ch_v.tok()
            vop(lambda e, so2=so2: e.tensor_scalar(
                us_r[0:1, so2:so2 + 512], us_r[0:1, so2:so2 + 512], 1e-6, None, OP.add))
            sop(lambda e, so2=so2: e.activation(
                us_r[0:1, so2:so2 + 512], us_r[0:1, so2:so2 + 512], AF.Ln),
                waits=(ch_v.tok(),))
            sop(lambda e, so2=so2: e.activation(
                us_r[0:1, so2:so2 + 512], us_r[0:1, so2:so2 + 512], AF.Exp, scale=-0.5))
        yn_tok = ch_s.tok()
        wsc_free = [None, None]
        for e_i in range(8):
            for sl2 in range(2):
                bi = 6 + (sl2 % 2)
                bv = bank(bi, 128, 512)
                for kp in range(16):
                    sl_i = wip_state["idx"] % 8
                    wip_state["idx"] += 1
                    ws = kp % 2
                    wtok0 = gdma(WSC[:, ws * 128:(ws + 1) * 128],
                                 W_op[kp, :, e_i * 128:(e_i + 1) * 128], f"wsc{ws}",
                                 waits=(wsc_free[ws],))
                    wtok = vop(lambda e, sl_i=sl_i, ws=ws, kp=kp: e.tensor_scalar(
                        WIPR[:, sl_i * 128:(sl_i + 1) * 128], WSC[:, ws * 128:(ws + 1) * 128],
                        swcol[:, kp:kp + 1], None, OP.mult),
                        waits=(wtok0, consts, wip_state["free"][sl_i]))
                    wsc_free[ws] = wtok
                    w = [wtok, m_tok]
                    if kp == 0:
                        w.append(bank_rd.get(bi))
                    t = mm(bv, WIPR[:, sl_i * 128:(sl_i + 1) * 128],
                           MHALF[:, kp * 1024 + sl2 * 512: kp * 1024 + sl2 * 512 + 512],
                           start=(kp == 0), stop=(kp == 15), waits=tuple(w),
                           inc=True)
                    wip_state["free"][sl_i] = ch_t.tok()
                bank_wr[bi] = t
                tok0 = half * 1024 + sl2 * 512
                xr2 = sl2 % 2
                x0t = gdma(X0R[:, xr2 * 512:(xr2 + 1) * 512],
                           x_in.rearrange("(c p) t -> p c t", p=128)[:, e_i, HALO + tok0: HALO + tok0 + 512],
                           f"x0r{xr2}", waits=(bank_rd.get(200 + xr2), yn_tok))
                w = [ones_tok, bank_rd.get(1), u_all]
                t1b = mm(bank(1, 128, 512), onesF[:, :], mean_r[0:1, HALO + tok0: HALO + tok0 + 512],
                         start=True, stop=True, waits=tuple(x for x in w if x), inc=True)
                w = [ones_tok, bank_rd.get(0), u_all]
                t2b = mm(bank(0, 128, 512), onesF[:, :], rc_r[0:1, HALO + tok0: HALO + tok0 + 512],
                         start=True, stop=True, waits=tuple(x for x in w if x), inc=True)
                vop(lambda e, xr2=xr2, x0t=x0t, t1b=t1b: e.tensor_sub(
                    X0R[:, xr2 * 512:(xr2 + 1) * 512], X0R[:, xr2 * 512:(xr2 + 1) * 512],
                    bank(1, 128, 512)), waits=(x0t, t1b))
                bank_rd[1] = ch_v.tok()
                vop(lambda e, xr2=xr2: e.tensor_mul(
                    X0R[:, xr2 * 512:(xr2 + 1) * 512], X0R[:, xr2 * 512:(xr2 + 1) * 512],
                    bank(0, 128, 512)), waits=(t2b,))
                bank_rd[0] = ch_v.tok()
                w = [yn_tok, ones_tok, bank_rd.get(2)]
                t3b = mm(bank(2, 128, 512), onesF[:, :], us_r[0:1, sl2 * 512: sl2 * 512 + 512],
                         start=True, stop=True, waits=tuple(x for x in w if x), inc=True)
                cpR = vop(lambda e: e.tensor_copy(
                    XTMR[:, 0:1024].bitcast(F32), bank(2, 128, 512)),
                    waits=(t3b, xtmr_free[0]))
                sc_t = vop(lambda e, bv=bv: e.tensor_mul(
                    bv, bv, XTMR[:, 0:1024].bitcast(F32)), waits=(t, cpR))
                bank_rd[2] = sc_t
                xtmr_free[0] = sc_t
                u1t = vop(lambda e, xr2=xr2, bv=bv: e.tensor_add(
                    X0R[:, xr2 * 512:(xr2 + 1) * 512], X0R[:, xr2 * 512:(xr2 + 1) * 512],
                    bv), waits=())
                bank_rd[bi] = u1t
                ydma(x1u_d.rearrange("(c p) t -> p c t", p=128)[:, e_i, tok0: tok0 + 512],
                     X0R[:, xr2 * 512:(xr2 + 1) * 512], waits=(u1t,))
                bank_rd[200 + xr2] = ldy.tok()
                if stop_after == "Bu" and half == 0 and e_i == 0 and sl2 == 0:
                    p3s.close()
                    pL0.close()
                    _finish(em, nc, stack, y_out, dbg, dbg_out, locals(),
                            bank_rd[200 + xr2], ydma, ch_v, ch_s, ch_t)
                    return nc
        mh_guard = ch_t.tok()
    u1_done = ldy.tok()

    if stop_after == "B":
        p3s.close()
        pL0.close()
        dtok = ydma(y_out[:, :], x1u_d[:, :], waits=(u1_done,))
        _finish(em, nc, stack, y_out, dbg, dbg_out, locals(), dtok, ydma, ch_v, ch_s, ch_t)
        return nc

    p3s.close()
    pL0.close()
    SQ2 = sbuf("SQ2", [128, DC * 512], BF16)
    p45 = contextlib.ExitStack()
    WIP2 = sbuf("WIP2", [128, 16 * 128], BF16, p45)
    wip2 = {"free": [u1_done] * 16, "idx": 0}

    def w2tile(dsrc, waits=()):
        sl_i = wip2["idx"] % 16
        wip2["idx"] += 1
        tok = gdma(WIP2[:, sl_i * 128:(sl_i + 1) * 128], dsrc, f"wip2_{sl_i}",
                   waits=(wip2["free"][sl_i],) + tuple(waits))
        return sl_i, tok

    RES = sbuf("RES", [128, DC * T], F32, p45)
    H2B = sbuf("H2B", [128, DC * T], BF16, p45)
    U1B = sbuf("U1B", [128, 2 * DC * 512], BF16, p45)
    p4u = contextlib.ExitStack()
    U1F = sbuf("U1F", [128, 2 * DC * 512], F32, p4u)
    r1 = gdma(RES[:, :], x_in.rearrange("(c p) t -> p c t", p=128)[:, :, HALO:],
              "res", waits=(u1_done,))
    r2 = gdma(RES[:, :], x1u_d.rearrange("(c p) t -> p c t", p=128),
              "res", waits=(r1,), accum=True)
    u1b_free = [u1_done, u1_done]
    u1f_free = [u1_done, u1_done]
    h2toks = []
    for sl in range(4):
        us = sl % 2
        u1t = gdma(U1F[:, us * DC * 512:(us + 1) * DC * 512],
                   x1u_d.rearrange("(c p) t -> p c t", p=128)[:, :, sl * 512:(sl + 1) * 512],
                   f"u1f{us}", waits=(u1f_free[us],))
        sqt = sop(lambda e, us=us: e.activation(
            SQ2[:, :], U1F[:, us * DC * 512:(us + 1) * DC * 512], AF.Square),
            waits=(u1t,))
        bs = bank(6, 1, 512)
        for c in range(DC):
            w = [sqt, bank_rd.get(6)] if c == 0 else []
            t = mm(bs, onescolb[:, :], SQ2[:, c * 512:(c + 1) * 512],
                   start=(c == 0), stop=(c == DC - 1), waits=tuple(w), inc=(c == DC - 1))
        vop(lambda e, bs=bs, sl=sl: e.tensor_scalar(
            us_r[0:1, sl * 512: sl * 512 + 512], bs, 1.0 / D, None, OP.mult),
            waits=(t,))
        bank_rd[6] = ch_v.tok()
        vop(lambda e, sl=sl: e.tensor_scalar(
            us_r[0:1, sl * 512: sl * 512 + 512], us_r[0:1, sl * 512: sl * 512 + 512],
            1e-6, None, OP.add))
        sop(lambda e, sl=sl: e.activation(
            us_r[0:1, sl * 512: sl * 512 + 512], us_r[0:1, sl * 512: sl * 512 + 512],
            AF.Ln), waits=(ch_v.tok(),))
        rt = sop(lambda e, sl=sl: e.activation(
            us_r[0:1, sl * 512: sl * 512 + 512], us_r[0:1, sl * 512: sl * 512 + 512],
            AF.Exp, scale=-0.5))
        bb = bank(7, 128, 512)
        w = [rt, ones_tok, bank_rd.get(7)]
        t = mm(bb, onesF[:, :], us_r[0:1, sl * 512: sl * 512 + 512], start=True,
               stop=True, waits=tuple(x for x in w if x), inc=True)
        for c in range(DC):
            vop(lambda e, c=c, sl=sl, us=us, bb=bb: e.scalar_tensor_tensor(
                H2B[:, c * T + sl * 512: c * T + sl * 512 + 512],
                U1F[:, us * DC * 512 + c * 512: us * DC * 512 + (c + 1) * 512],
                pwcol[:, c:c + 1], bb, OP.mult, OP.mult),
                waits=(t,) if c == 0 else ())
        bank_rd[7] = ch_v.tok()
        u1f_free[us] = ch_v.tok()
        h2toks.append(ch_v.tok())
    h2_tok = ch_v.tok()
    p4u.close()
    p4h = contextlib.ExitStack()
    SGT = sbuf("SGT", [128, 2 * 512], F32, p4h)
    HH = sbuf("HH", [128, 22 * 1024], BF16, p4h)

    for half in range(2):
        for mt in range(22):
            toks = []
            for k in range(8):
                sg, gtk = w2tile(W_g[k, :, mt * 128:(mt + 1) * 128])
                su, utk = w2tile(W_u[k, :, mt * 128:(mt + 1) * 128])
                toks.append((sg, gtk, su, utk))
            for k, (sg, gtk, su, utk) in enumerate(toks):
                for ss in range(2):
                    no = half * 1024 + ss * 512
                    w = [gtk, h2_tok] if k == 0 else [gtk]
                    if k == 0:
                        w.append(bank_rd.get(ss))
                    mm(bank(ss, 128, 512), WIP2[:, sg * 128:(sg + 1) * 128],
                       _h2mv(H2B, k, no), start=(k == 0), stop=(k == 7),
                       waits=tuple(x for x in w if x), inc=(k == 7))
                    w = [utk] if k else [utk, bank_rd.get(2 + ss)]
                    t = mm(bank(2 + ss, 128, 512), WIP2[:, su * 128:(su + 1) * 128],
                           _h2mv(H2B, k, no), start=(k == 0), stop=(k == 7),
                           waits=tuple(x for x in w if x), inc=True)
                wip2["free"][sg] = ch_t.tok()
                wip2["free"][su] = ch_t.tok()
            gu_done = ch_t.tok()
            for ss in range(2):
                st = sop(lambda e, ss=ss: e.activation(
                    SGT[:, ss * 512:(ss + 1) * 512], bank(ss, 128, 512), AF.Silu),
                    waits=(gu_done,))
                bank_rd[ss] = st
                ht = vop(lambda e, mt=mt, half=half, ss=ss: e.tensor_mul(
                    HH[:, mt * 1024 + ss * 512: mt * 1024 + ss * 512 + 512],
                    SGT[:, ss * 512:(ss + 1) * 512], bank(2 + ss, 128, 512)),
                    waits=(st,))
                bank_rd[2 + ss] = ht
        hh_tok = ch_v.tok()
        for e_i in range(8):
            for k2 in range(22):
                sd, dtk2 = w2tile(W_d[k2, :, e_i * 128:(e_i + 1) * 128])
                for ss in range(2):
                    w = [dtk2, hh_tok] if k2 == 0 else [dtk2]
                    if k2 == 0:
                        w.append(bank_rd.get(4 + ss))
                    t = mm(bank(4 + ss, 128, 512), WIP2[:, sd * 128:(sd + 1) * 128],
                           HH[:, k2 * 1024 + ss * 512: k2 * 1024 + ss * 512 + 512],
                           start=(k2 == 0), stop=(k2 == 21),
                           waits=tuple(x for x in w if x), inc=True)
                wip2["free"][sd] = ch_t.tok()
            for ss in range(2):
                no = half * 1024 + ss * 512
                w = [t if ss == 1 else ch_t.tok(), r2]
                rt = vop(lambda e, e_i=e_i, no=no, ss=ss: e.tensor_add(
                    RES[:, e_i * T + no: e_i * T + no + 512],
                    RES[:, e_i * T + no: e_i * T + no + 512], bank(4 + ss, 128, 512)),
                    waits=tuple(w))
                bank_rd[4 + ss] = rt
    x1_tok = ch_v.tok()

    if stop_after == "P4":
        st = ydma(x1p_d[:, :].rearrange("(c p) t -> p c t", p=128),
                  RES.rearrange("p (c t) -> p c t", c=DC), waits=(x1_tok,))
        st = ydma(y_out[:, :], x1p_d[:, :], waits=(st,))
        p4h.close()
        p45.close()
        _finish(em, nc, stack, y_out, dbg, dbg_out, locals(), st, ydma, ch_v, ch_s, ch_t)
        return nc

    p4h.close()
    p5s = contextlib.ExitStack()
    WINS = sbuf("WINS", [128, 8 * 64], BF16, p5s)
    WOS = sbuf("WOS", [64, 1024], BF16, p5s)
    W1S = sbuf("W1S", [64, 16], F32, p5s)
    B1S = sbuf("B1S", [64, 16], F32, p5s)
    W2S = sbuf("W2S", [64, 16], F32, p5s)
    BINS = sbuf("BINS", [64, 1], F32, p5s)
    B2S = sbuf("B2S", [64, 1], F32, p5s)
    BOUTS = sbuf("BOUTS", [128, 8], F32, p5s)
    WSR = sbuf("WSR", [1, 64], F32, p5s)
    S0 = sbuf("S0", [64, T], F32, p5s)
    GAC = sbuf("GAC", [64, T], F32, p5s)
    TMP5 = sbuf("TMP5", [64, T], F32, p5s)
    GB = sbuf("GB", [64, T], BF16, p5s)
    for dst, srcd in ((WINS, Win.rearrange("k p m -> p k m")),
                      (WOS, Wout0), (W1S, W1_in), (B1S, b1_in), (W2S, W2_in),
                      (BINS, bin_in), (B2S, b2_in), (BOUTS, bout0)):
        gdma(dst[:, :], srcd[:, :, :] if dst is WINS else srcd[:, :], "sc5",
             waits=(x1_tok,) if dst is WINS else ())
    p5w = dch("sc5").tok()
    bw = bank(6, 64, 1)
    for k in range(8):
        w = [p5w, ones_tok, bank_rd.get(6)] if k == 0 else []
        t = mm(bw, WINS[:, k * 64:(k + 1) * 64], onescolb[:, :],
               start=(k == 0), stop=(k == 7), waits=tuple(w), inc=(k == 7))
    wst = vop(lambda e: e.tensor_copy(TMP5[:, 0:1], bw), waits=(t,))
    bank_rd[6] = wst
    tt = tr(bank(7, 1, 64), TMP5[0:64, 0:1], identF[:, :], waits=(wst, bank_rd.get(7)))
    wsr_tok = vop(lambda e: e.tensor_copy(WSR[:, :], bank(7, 1, 64)), waits=(tt,))
    bank_rd[7] = wsr_tok
    for sl in range(4):
        us = sl % 2
        xbt = vop(lambda e, sl=sl, us=us: e.tensor_copy(
            U1B[:, us * DC * 512:(us + 1) * DC * 512],
            RES.rearrange("p (c t) -> p c t", c=DC)[:, :, sl * 512:(sl + 1) * 512]),
            waits=(x1_tok, u1b_free[us]))
        sqt = sop(lambda e, us=us: e.activation(
            SQ2[:, :], U1B[:, us * DC * 512:(us + 1) * DC * 512], AF.Square),
            waits=(xbt,))
        b0, b1v = bank(0, 1, 512), bank(1, 1, 512)
        for c in range(DC):
            w = [xbt, ones_tok, bank_rd.get(0)] if c == 0 else []
            mm(b0, onescolb[:, :], U1B[:, us * DC * 512 + c * 512: us * DC * 512 + (c + 1) * 512],
               start=(c == 0), stop=(c == DC - 1), waits=tuple(w))
        for c in range(DC):
            w = [sqt, bank_rd.get(1)] if c == 0 else []
            t = mm(b1v, onescolb[:, :], SQ2[:, c * 512:(c + 1) * 512],
                   start=(c == 0), stop=(c == DC - 1), waits=tuple(w), inc=(c == DC - 1))
        so = sl * 512
        vop(lambda e, so=so, b0=b0: e.tensor_scalar(
            mean_r[0:1, so:so + 512], b0, 1.0 / D, None, OP.mult), waits=(t,))
        vop(lambda e, so=so, b1v=b1v: e.tensor_scalar(
            us_r[0:1, so:so + 512], b1v, 1.0 / D, None, OP.mult))
        bank_rd[0] = bank_rd[1] = ch_v.tok()
        vop(lambda e, so=so: e.tensor_mul(
            rc_r[0:1, so:so + 512], mean_r[0:1, so:so + 512], mean_r[0:1, so:so + 512]))
        vop(lambda e, so=so: e.tensor_sub(
            rc_r[0:1, so:so + 512], us_r[0:1, so:so + 512], rc_r[0:1, so:so + 512]))
        vop(lambda e, so=so: e.tensor_scalar(
            rc_r[0:1, so:so + 512], rc_r[0:1, so:so + 512], 0.0, 1e-5, OP.max, OP.add))
        lt = ch_v.tok()
        sop(lambda e, so=so: e.activation(
            rc_r[0:1, so:so + 512], rc_r[0:1, so:so + 512], AF.Ln), waits=(lt,))
        rct = sop(lambda e, so=so: e.activation(
            rc_r[0:1, so:so + 512], rc_r[0:1, so:so + 512], AF.Exp, scale=-0.5))
        nmt = vop(lambda e, so=so: e.tensor_scalar(
            us_r[0:1, so:so + 512], mean_r[0:1, so:so + 512], -1.0, None, OP.mult))
        bs = bank(2, 64, 512)
        for k in range(8):
            w = [p5w, xbt, bank_rd.get(2)] if k == 0 else []
            mm(bs, WINS[:, k * 64:(k + 1) * 64],
               U1B[:, us * DC * 512 + k * 512: us * DC * 512 + (k + 1) * 512],
               start=(k == 0), stop=False, waits=tuple(w))
        t = mm(bs, WSR[:, :], us_r[0:1, so:so + 512], start=False, stop=True,
               waits=(wsr_tok, nmt), inc=True)
        bbc = bank(3, 64, 512)
        w = [rct, ones_tok, bank_rd.get(3)]
        t2 = mm(bbc, onesF[:, 0:64], rc_r[0:1, so:so + 512], start=True, stop=True,
                waits=tuple(x for x in w if x), inc=True)
        vop(lambda e, bbc=bbc: e.tensor_copy(
            SQ2[:, 0:1024].bitcast(F32)[0:64, :], bbc), waits=(t2,))
        vop(lambda e, so=so, bs=bs: e.tensor_mul(
            S0[:, so:so + 512], bs, SQ2[:, 0:1024].bitcast(F32)[0:64, :]),
            waits=(t,))
        bank_rd[2] = bank_rd[3] = ch_v.tok()
        vop(lambda e, so=so: e.tensor_scalar(
            S0[:, so:so + 512], S0[:, so:so + 512], BINS[:, :], None, OP.add),
            waits=(p5w,))
        u1b_free[us] = ch_v.tok()
    s_tok = ch_v.tok()
    for i in range(16):
        rt = sop(lambda e, i=i: e.activation(
            TMP5[:, :], S0[0:64, :], AF.Relu, bias=B1S[:, i:i + 1],
            scale=W1S[:, i:i + 1]), waits=(s_tok, p5w) if i == 0 else (ch_v.tok(),))
        if i == 0:
            vop(lambda e: e.tensor_scalar(GAC[0:64, :], TMP5[0:64, :], W2S[:, 0:1],
                                          None, OP.mult), waits=(rt,))
        else:
            vop(lambda e, i=i: e.scalar_tensor_tensor(
                GAC[0:64, :], TMP5[0:64, :], W2S[:, i:i + 1], GAC[0:64, :],
                OP.mult, OP.add), waits=(rt,))
    vop(lambda e: e.tensor_scalar(GAC[0:64, :], GAC[0:64, :], B2S[:, :], None, OP.add))
    gb_tok = vop(lambda e: e.tensor_copy(GB[0:64, :], GAC[0:64, :]))
    for e_i in range(8):
        for sl in range(4):
            bo = bank(4 + sl % 2, 128, 512)
            w = [gb_tok, p5w, bank_rd.get(4 + sl % 2)]
            t = mm(bo, WOS[:, e_i * 128:(e_i + 1) * 128], GB[0:64, sl * 512:(sl + 1) * 512],
                   start=True, stop=True, waits=tuple(x for x in w if x), inc=True)
            rt = vop(lambda e, e_i=e_i, sl=sl, bo=bo: e.scalar_tensor_tensor(
                RES[:, e_i * T + sl * 512: e_i * T + sl * 512 + 512],
                bo, BOUTS[:, e_i:e_i + 1],
                RES[:, e_i * T + sl * 512: e_i * T + sl * 512 + 512], OP.add, OP.add),
                waits=(t,))
            bank_rd[4 + sl % 2] = rt
    x1p_tok = ch_v.tok()
    st = ydma(x1p_d.rearrange("(c p) t -> p c t", p=128),
              RES.rearrange("p (c t) -> p c t", c=DC), waits=(x1p_tok,))
    for sl in range(4):
        us = sl % 2
        xbt = vop(lambda e, sl=sl, us=us: e.tensor_copy(
            U1B[:, us * DC * 512:(us + 1) * DC * 512],
            RES.rearrange("p (c t) -> p c t", c=DC)[:, :, sl * 512:(sl + 1) * 512]),
            waits=(x1p_tok, u1b_free[us]))
        sqt = sop(lambda e, us=us: e.activation(
            SQ2[:, :], U1B[:, us * DC * 512:(us + 1) * DC * 512], AF.Square),
            waits=(xbt,))
        b0, b1v = bank(0, 1, 512), bank(1, 1, 512)
        for c in range(DC):
            w = [xbt, ones_tok, bank_rd.get(0)] if c == 0 else []
            mm(b0, onescolb[:, :], U1B[:, us * DC * 512 + c * 512: us * DC * 512 + (c + 1) * 512],
               start=(c == 0), stop=(c == DC - 1), waits=tuple(w))
        for c in range(DC):
            w = [sqt, bank_rd.get(1)] if c == 0 else []
            t = mm(b1v, onescolb[:, :], SQ2[:, c * 512:(c + 1) * 512],
                   start=(c == 0), stop=(c == DC - 1), waits=tuple(w), inc=(c == DC - 1))
        so = sl * 512
        vop(lambda e, so=so, b0=b0: e.tensor_scalar(
            mean_r[0:1, so:so + 512], b0, 1.0 / D, None, OP.mult), waits=(t,))
        vop(lambda e, so=so, b1v=b1v: e.tensor_scalar(
            us_r[0:1, so:so + 512], b1v, 1.0 / D, None, OP.mult))
        bank_rd[0] = bank_rd[1] = ch_v.tok()
        vop(lambda e, so=so: e.tensor_mul(
            rc_r[0:1, so:so + 512], mean_r[0:1, so:so + 512], mean_r[0:1, so:so + 512]))
        vop(lambda e, so=so: e.tensor_sub(
            rc_r[0:1, so:so + 512], us_r[0:1, so:so + 512], rc_r[0:1, so:so + 512]))
        vop(lambda e, so=so: e.tensor_scalar(
            rc_r[0:1, so:so + 512], rc_r[0:1, so:so + 512], 0.0, 1e-5, OP.max, OP.add))
        lt = ch_v.tok()
        sop(lambda e, so=so: e.activation(
            rc_r[0:1, so:so + 512], rc_r[0:1, so:so + 512], AF.Ln), waits=(lt,))
        rct = sop(lambda e, so=so: e.activation(
            rc_r[0:1, so:so + 512], rc_r[0:1, so:so + 512], AF.Exp, scale=-0.5))
        bm = bank(2, 128, 512)
        w = [ones_tok, bank_rd.get(2), ch_v.tok()]
        mm(bm, onesF[:, :], mean_r[0:1, so:so + 512], start=True, stop=True,
           waits=tuple(x for x in w if x))
        br = bank(3, 128, 512)
        w = [rct, bank_rd.get(3)]
        t2 = mm(br, onesF[:, :], rc_r[0:1, so:so + 512], start=True, stop=True,
                waits=tuple(x for x in w if x), inc=True)
        for c in range(DC):
            vop(lambda e, c=c, us=us, bm=bm: e.tensor_sub(
                SQ2[:, c * 512:(c + 1) * 512],
                U1B[:, us * DC * 512 + c * 512: us * DC * 512 + (c + 1) * 512], bm),
                waits=(t2,) if c == 0 else ())
            vop(lambda e, c=c, us=us, br=br: e.tensor_mul(
                U1B[:, us * DC * 512 + c * 512: us * DC * 512 + (c + 1) * 512],
                SQ2[:, c * 512:(c + 1) * 512], br))
        t1t = ch_v.tok()
        bank_rd[2] = bank_rd[3] = t1t
        gdma(t1AG_in.rearrange("(c p) t -> p c t", p=128)[:, :, sl * 512:(sl + 1) * 512],
             U1B[:, us * DC * 512:(us + 1) * DC * 512], "t1st", waits=(t1t,))
        u1b_free[us] = dch("t1st").tok()
    t1_in_done = dch("t1st").tok()
    em.emit("g", lambda e: e.collective_compute(
        "AllGather", OP.bypass, replica_groups=GROUPS,
        ins=[t1AG_in[:, :]], outs=[t1AG_out[:, :]]),
        (t1_in_done,), ((cc, 1),))
    ag2 = cc.tok()
    p5s.close()
    p45.close()

    if stop_after == "P5":
        st = ydma(y_out[:, :], x1p_d[:, :], waits=(st,))
        _finish(em, nc, stack, y_out, dbg, dbg_out, locals(), st, ydma, ch_v, ch_s, ch_t)
        return nc

    p6s = contextlib.ExitStack()
    QA = sbuf("QA", [128, 4 * S], BF16, p6s)
    KA = sbuf("KA", [128, 4 * S], BF16, p6s)
    V65 = sbuf("V65", [128, 32 * 8 * 65], BF16, p6s)
    PTB = sbuf("PTB", [128, 2 * 1024], BF16, p6s)
    T1T = sbuf("T1T", [128, 2 * DC * 512], BF16, p6s)
    OTB = sbuf("OTB", [64, 2 * 512], BF16, p6s)
    MHC = sbuf("MHC", [128, 8], F32, p6s)
    SRW = sbuf("SRW", [1, 2 * 512], F32, p6s)
    QNR = sbuf("QNR", [1, 160], F32, p6s)
    amaskb = sbuf("amaskb", [128, 4 * 512], BF16, p6s)
    amask_tok = gdma(amaskb[:, :], amask_in.rearrange("d p t -> p d t"), "amask", waits=(ag2,))
    p6w = contextlib.ExitStack()
    QKW = sbuf("QKW", [128, 8 * 1024], BF16, p6w)
    VW = sbuf("VW", [128, 8 * 512], BF16, p6w)
    for k in range(8):
        gdma(QKW[:, k * 1024:(k + 1) * 1024], W_qkv_qk[k, :, :], "qkw",
             waits=(ag2,) if k == 0 else ())
        gdma(VW[:, k * 512:(k + 1) * 512], W_qkv_v[k, :, :], "qkw")
    qkw_tok = dch("qkw").tok()
    vo_tok = vop(lambda e: e.memset(V65[:, :], 1.0))
    t1t_free = [ag2, ag2]
    qk_ev = None
    for ts in range(8):
        slot = ts // 4
        us = ts % 2
        t1tok = gdma(T1T[:, us * DC * 512:(us + 1) * DC * 512],
                     t1AG_out.rearrange("(s c p) t -> s p c t", s=2, p=128)[
                         slot, :, :, (ts % 4) * 512:(ts % 4) * 512 + 512],
                     f"t1t{us}", waits=(ag2, t1t_free[us]))
        for m2 in range(8):
            bi = m2 % 2
            bv = bank(bi, 128, 512)
            for k in range(8):
                w = [t1tok, qkw_tok] if k == 0 else []
                if k == 0:
                    w.append(bank_rd.get(bi))
                t = mm(bv, QKW[:, k * 1024 + m2 * 128: k * 1024 + (m2 + 1) * 128],
                       T1T[:, us * DC * 512 + k * 512: us * DC * 512 + (k + 1) * 512],
                       start=(k == 0), stop=(k == 7), waits=tuple(x for x in w if x),
                       inc=(k == 7))
            dst = QA if m2 % 2 == 0 else KA
            j = m2 // 2
            ev = vop(lambda e, dst=dst, j=j, ts=ts, bv=bv: e.tensor_copy(
                dst[:, j * S + ts * 512: j * S + ts * 512 + 512], bv), waits=(t,))
            bank_rd[bi] = ev
        for tt in range(4):
            bi = 2 + tt % 2
            bv = bank(bi, 128, 512)
            for k in range(8):
                w = [t1tok, qkw_tok] if k == 0 else []
                if k == 0:
                    w.append(bank_rd.get(bi))
                t = mm(bv, T1T[:, us * DC * 512 + k * 512 + tt * 128:
                               us * DC * 512 + k * 512 + tt * 128 + 128],
                       VW[:, k * 512:(k + 1) * 512],
                       start=(k == 0), stop=(k == 7), waits=tuple(x for x in w if x),
                       inc=(k == 7))
            gt = ts * 4 + tt
            for h in range(8):
                w = [t, vo_tok] if h == 0 else []
                ev = vop(lambda e, gt=gt, h=h, bv=bv: e.tensor_copy(
                    V65[:, (gt * 8 + h) * 65: (gt * 8 + h) * 65 + 64], bv[:, h * 64:(h + 1) * 64]),
                    waits=tuple(w))
            bank_rd[bi] = ev
        t1t_free[us] = ch_v.tok()
    qkv_tok = ch_v.tok()
    p6w.close()
    if stop_after == "P6v":
        p6s.close()
        _finish(em, nc, stack, y_out, dbg, dbg_out, locals(), qkv_tok, ydma, ch_v, ch_s, ch_t)
        return nc
    sq_last = {0: None, 64: None}
    for h in range(8):
        j, hb = h // 2, 64 * (h % 2)
        for src_a, qn in ((QA, 0), (KA, 1)):
            sqt = sop(lambda e, src_a=src_a, j=j, hb=hb: e.activation(
                SQ2[hb:hb + 64, :], src_a[hb:hb + 64, j * S:(j + 1) * S], AF.Square),
                waits=(qkv_tok, sq_last[hb]))
            for g in range(8):
                slot = 4 + g % 2
                w = [sqt, bank_rd.get(slot)] if g < 2 else [bank_rd.get(slot)]
                t = mm(bank(slot, 1, 512), onescolb[hb:hb + 64, :],
                       SQ2[hb:hb + 64, g * 512:(g + 1) * 512],
                       start=True, stop=True, waits=tuple(x for x in w if x),
                       inc=True)
                bank_rd[slot] = vop(lambda e, h=h, qn=qn, g=g, slot=slot: e.reduce_max(
                    QNR[0:1, h * 16 + qn * 8 + g: h * 16 + qn * 8 + g + 1],
                    bank(slot, 1, 512), axis=AX.X), waits=(t,))
            sq_last[hb] = t
    for h in range(8):
        vop(lambda e, h=h: e.reduce_max(
            QNR[0:1, 128 + h:129 + h], QNR[0:1, h * 16:h * 16 + 8], axis=AX.X))
        vop(lambda e, h=h: e.reduce_max(
            QNR[0:1, 136 + h:137 + h], QNR[0:1, h * 16 + 8:h * 16 + 16], axis=AX.X))
    for h in range(8):
        vop(lambda e, h=h: e.tensor_mul(
            QNR[0:1, 144 + h:145 + h], QNR[0:1, 128 + h:129 + h],
            QNR[0:1, 136 + h:137 + h]))
    vop(lambda e: e.memset(SRW[0:1, 0:512], 0.0))
    lt = ch_v.tok()
    sop(lambda e: e.activation(QNR[0:1, 144:152], QNR[0:1, 144:152], AF.Ln),
        waits=(lt,))
    et = sop(lambda e: e.activation(QNR[0:1, 144:152], QNR[0:1, 144:152],
                                    AF.Exp, scale=0.5))
    nt = vop(lambda e: e.tensor_scalar(QNR[0:1, 144:152], QNR[0:1, 144:152],
                                       -0.125, None, OP.mult), waits=(et,))
    for h in range(8):
        bb = bank(6, 128, 1)
        w = [nt, ones_tok, bank_rd.get(6)]
        t = mm(bb, onesF[:, :], QNR[0:1, 144 + h:145 + h], start=True, stop=True,
               waits=tuple(x for x in w if x), inc=True)
        mv = vop(lambda e, h=h, bb=bb: e.tensor_copy(MHC[:, h:h + 1], bb), waits=(t,))
        bank_rd[6] = mv
    mhc_tok = ch_v.tok()
    if stop_after == "P6q":
        p6s.close()
        _finish(em, nc, stack, y_out, dbg, dbg_out, locals(), mhc_tok, ydma, ch_v, ch_s, ch_t)
        return nc
    ptb_free = {(st, us): mhc_tok for st in (0, 1) for us in (0, 1)}
    stb_rd = {}
    ot_free = [None, None]
    for j in range(4):
        for qs in range(8):
            nkv = qs * 4 + 4
            po = {st: bank(4 + st, 65, 512) for st in (0, 1)}
            ets = {}
            pv_last = {}

            def emit_pv(st, tk, j=j, qs=qs):
                h = 2 * j + st
                us2 = tk % 2
                w = [ets[(st, tk)]]
                if tk == 0:
                    w.append(bank_rd.get(4 + st))
                pv = mm(po[st], V65[:, (tk * 8 + h) * 65: (tk * 8 + h) * 65 + 65],
                        PTB[:, (st * 2 + us2) * 512: (st * 2 + us2) * 512 + 512],
                        start=(tk == 0), stop=(tk == nkv - 1),
                        waits=tuple(x for x in w if x), inc=(tk == nkv - 1))
                ptb_free[(st, us2)] = pv
                pv_last[st] = pv

            for t_kv in range(nkv):
                us = t_kv % 2
                d = t_kv - 4 * qs
                for st in (0, 1):
                    h = 2 * j + st
                    hb = 64 * st
                    slot = st * 2 + us
                    stb = bank(slot, 128, 512)
                    w = [qkv_tok, stb_rd.get(slot, bank_rd.get(slot))]
                    t = mm(stb,
                           KA[hb:hb + 64, j * S + t_kv * 128: j * S + t_kv * 128 + 128],
                           QA[hb:hb + 64, j * S + qs * 512: j * S + qs * 512 + 512],
                           start=True, stop=(d < 0),
                           waits=tuple(x for x in w if x), inc=(d < 0))
                    if d >= 0:
                        t = mm(stb, identb[:, :], amaskb[:, d * 512:(d + 1) * 512],
                               start=False, stop=True, waits=(amask_tok,),
                               inc=True)
                    ets[(st, t_kv)] = sop(lambda e, slot=slot, stb=stb, h=h:
                        e.activation(PTB[:, slot * 512: slot * 512 + 512], stb,
                                     AF.Exp, bias=MHC[:, h:h + 1], scale=0.125),
                        waits=(t, ptb_free[(st, us)], mhc_tok))
                    stb_rd[slot] = ets[(st, t_kv)]
                if t_kv > 0:
                    for st in (0, 1):
                        emit_pv(st, t_kv - 1)
            for st in (0, 1):
                emit_pv(st, nkv - 1)
            for st in (0, 1):
                h = 2 * j + st
                sv = vop(lambda e, st=st, po=po: e.tensor_copy(
                    SRW[0:1, st * 512:(st + 1) * 512], po[st][64:65, :]),
                    waits=(pv_last[st],))
                rv = vop(lambda e, st=st: e.reciprocal(
                    SRW[0:1, st * 512:(st + 1) * 512],
                    SRW[0:1, st * 512:(st + 1) * 512]))
                bb = bank(6 + st, 64, 512)
                w = [rv, ones_tok, bank_rd.get(6 + st)]
                t = mm(bb, onesF[:, 0:64], SRW[0:1, st * 512:(st + 1) * 512],
                       start=True, stop=True, waits=tuple(x for x in w if x),
                       inc=True)
                scr = SQ2[:, st * 1024:(st + 1) * 1024].bitcast(F32)[0:64, :]
                cpb = vop(lambda e, bb=bb, scr=scr: e.tensor_copy(scr, bb),
                          waits=(t,))
                ov = vop(lambda e, st=st, po=po, scr=scr: e.tensor_mul(
                    OTB[:, st * 512:(st + 1) * 512], po[st][0:64, :], scr),
                    waits=(cpb, ot_free[st]))
                bank_rd[4 + st] = ov
                bank_rd[6 + st] = cpb
                gdma(oAG_in[h * 64:(h + 1) * 64, qs * 512:(qs + 1) * 512],
                     OTB[:, st * 512:(st + 1) * 512], f"ost{st}", waits=(ov,))
                ot_free[st] = dch(f"ost{st}").tok()
    o_done = (dch("ost0").tok(), dch("ost1").tok())
    em.emit("g", lambda e: e.collective_compute(
        "AllGather", OP.bypass, replica_groups=GROUPS,
        ins=[oAG_in[:, :]], outs=[oAG_out[:, :]]),
        o_done, ((cc, 1),))
    ag3 = cc.tok()
    if stop_after == "P6a":
        p6s.close()
        _finish(em, nc, stack, y_out, dbg, dbg_out, locals(), ag3, ydma, ch_v, ch_s, ch_t)
        return nc
    OSEL = sbuf("OSEL", [128, 8 * 512], BF16, p6s)
    OAB = sbuf("OAB", [128, 2 * 512], BF16, p6s)
    X2T = sbuf("X2T", [128, 2 * 512], F32, p6s)
    X2B = sbuf("X2B", [128, 512], BF16, p6s)
    oab_free = [ag3, ag3]
    x2t_free = [None, None]
    osel_free = None
    for g in range(4):
        osel_toks = []
        for k in range(8):
            ua = k % 2
            ta = gdma(OAB[:, ua * 512:(ua + 1) * 512],
                      oAG_out[k * 128:(k + 1) * 128, g * 512:(g + 1) * 512], f"oab{ua}",
                      waits=(ag3, oab_free[ua]))
            sel1 = vop(lambda e, k=k, ua=ua: e.tensor_scalar(
                OSEL[:, k * 512:(k + 1) * 512], OAB[:, ua * 512:(ua + 1) * 512],
                uinvcol[:, :], None, OP.mult), waits=(ta, consts, osel_free))
            tb2 = gdma(OAB[:, ua * 512:(ua + 1) * 512],
                       oAG_out[k * 128:(k + 1) * 128, 2048 + g * 512: 2048 + (g + 1) * 512],
                       f"oab{ua}", waits=(sel1,))
            sel2 = vop(lambda e, k=k, ua=ua: e.scalar_tensor_tensor(
                OSEL[:, k * 512:(k + 1) * 512], OAB[:, ua * 512:(ua + 1) * 512],
                uinitcol[:, :], OSEL[:, k * 512:(k + 1) * 512], OP.mult, OP.add),
                waits=(tb2,))
            oab_free[ua] = sel2
            osel_toks.append(sel2)
        osel_tok = ch_v.tok()
        for e_i in range(8):
            bi = 4 + e_i % 2
            bv = bank(bi, 128, 512)
            toks = []
            for k in range(8):
                swo, wtk = w2tile(W_wo[k, :, e_i * 128:(e_i + 1) * 128])
                toks.append((swo, wtk))
            for k, (swo, wtk) in enumerate(toks):
                w = [wtk, osel_tok] if k == 0 else [wtk]
                if k == 0:
                    w.append(bank_rd.get(bi))
                t = mm(bv, WIP2[:, swo * 128:(swo + 1) * 128],
                       OSEL[:, k * 512:(k + 1) * 512],
                       start=(k == 0), stop=(k == 7), waits=tuple(w), inc=True)
                wip2["free"][swo] = ch_t.tok()
            ux = e_i % 2
            x1pt = gdma(X2T[:, ux * 512:(ux + 1) * 512],
                        x1p_d.rearrange("(c p) t -> p c t", p=128)[:, e_i, g * 512:(g + 1) * 512],
                        f"x2t{ux}", waits=(x2t_free[ux],))
            xt = vop(lambda e, ux=ux, bv=bv: e.tensor_add(
                X2T[:, ux * 512:(ux + 1) * 512], X2T[:, ux * 512:(ux + 1) * 512], bv),
                waits=(x1pt, t))
            bank_rd[bi] = xt
            ydma(x2_d.rearrange("(c p) t -> p c t", p=128)[:, e_i, g * 512:(g + 1) * 512],
                 X2T[:, ux * 512:(ux + 1) * 512], waits=(xt,))
            xbt = vop(lambda e, ux=ux: e.tensor_copy(
                X2B[:, :], X2T[:, ux * 512:(ux + 1) * 512]), waits=(xt,))
            sqt = sop(lambda e: e.activation(SQ2[:, 0:512], X2B[:, :], AF.Square),
                      waits=(xbt,))
            w = [xbt, ones_tok, bank_rd.get(0)] if e_i == 0 else [xbt]
            mm(bank(0, 1, 512), onescolb[:, :], X2B[:, :],
               start=(e_i == 0), stop=(e_i == 7), waits=tuple(x for x in w if x))
            w = [sqt] if e_i else [sqt, bank_rd.get(1)]
            t = mm(bank(1, 1, 512), onescolb[:, :], SQ2[:, 0:512],
                   start=(e_i == 0), stop=(e_i == 7), waits=tuple(x for x in w if x),
                   inc=(e_i == 7))
            x2t_free[ux] = ch_v.tok()
        osel_free = ch_t.tok()
        so = g * 512
        vop(lambda e, so=so: e.tensor_scalar(
            mean_r[0:1, so:so + 512], bank(0, 1, 512), 1.0 / D, None, OP.mult),
            waits=(t,))
        vop(lambda e, so=so: e.tensor_scalar(
            us_r[0:1, so:so + 512], bank(1, 1, 512), 1.0 / D, None, OP.mult))
        bank_rd[0] = bank_rd[1] = ch_v.tok()
        vop(lambda e, so=so: e.tensor_mul(
            rc_r[0:1, so:so + 512], mean_r[0:1, so:so + 512], mean_r[0:1, so:so + 512]))
        vop(lambda e, so=so: e.tensor_sub(
            rc_r[0:1, so:so + 512], us_r[0:1, so:so + 512], rc_r[0:1, so:so + 512]))
        vop(lambda e, so=so: e.tensor_scalar(
            rc_r[0:1, so:so + 512], rc_r[0:1, so:so + 512], 0.0, 1e-5, OP.max, OP.add))
        lt = ch_v.tok()
        sop(lambda e, so=so: e.activation(
            rc_r[0:1, so:so + 512], rc_r[0:1, so:so + 512], AF.Ln), waits=(lt,))
        sop(lambda e, so=so: e.activation(
            rc_r[0:1, so:so + 512], rc_r[0:1, so:so + 512], AF.Exp, scale=-0.5))
    x2_done = ldy.tok()
    x2s_tok = ch_s.tok()
    p6s.close()

    if stop_after == "P6":
        dtok = ydma(y_out[:, :], x2_d[:, :], waits=(x2_done,))
        _finish(em, nc, stack, y_out, dbg, dbg_out, locals(), dtok, ydma, ch_v, ch_s, ch_t)
        return nc

    p7s = contextlib.ExitStack()
    VNB = sbuf("VNB", [128, DC * 1024], BF16, p7s)
    HP = sbuf("HP", [128, 32 * 1024], BF16, p7s)
    GP = sbuf("GP", [128, 32 * 1024], BF16, p7s)
    OSL = sbuf("OSL", [128, 2 * 32 * 128], BF16, p7s)
    X2R = sbuf("X2R", [128, DC * 512], BF16, p7s)
    X3T = sbuf("X3T", [128, 512], F32, p7s)
    SGT2 = sbuf("SGT2", [128, 512], F32, p7s)
    SC1 = sbuf("SC1", [128, 32], F32, p7s)
    SC2 = sbuf("SC2", [128, 32], F32, p7s)
    SCH1 = sbuf("SCH1", [128, 32], F32, p7s)
    SCH2 = sbuf("SCH2", [128, 32], F32, p7s)
    SC3 = sbuf("SC3", [128, 8], F32, p7s)
    for dst, srcd in ((SC1, s_t1), (SC2, s_t2), (SCH1, s_h1), (SCH2, s_h2), (SC3, s_t3)):
        gdma(dst[:, :], srcd[:, :], "sc7", waits=(x2_done,) if dst is SC1 else ())
    p7w = dch("sc7").tok()
    x2r_free = [None, None]

    def emit_v(ps_i):
        vtoks = []
        for sl in (2 * ps_i, 2 * ps_i + 1):
            x2t = gdma(X2R[:, 0: DC * 512],
                       x2_d.rearrange("(c p) t -> p c t", p=128)[:, :, sl * 512:(sl + 1) * 512],
                       "x2rv", waits=(x2_done, x2r_free[0], x2r_free[1]))
            so = sl * 512
            lo = (sl - 2 * ps_i) * 512
            bm = bank(2, 128, 512)
            w = [ones_tok, bank_rd.get(2), x2s_tok]
            mm(bm, onesF[:, :], mean_r[0:1, so:so + 512], start=True, stop=True,
               waits=tuple(x for x in w if x))
            br = bank(3, 128, 512)
            w = [bank_rd.get(3), x2s_tok]
            t2 = mm(br, onesF[:, :], rc_r[0:1, so:so + 512], start=True, stop=True,
                    waits=tuple(x for x in w if x), inc=True)
            for c in range(DC):
                vop(lambda e, c=c, bm=bm: e.tensor_sub(
                    SQ2[:, c * 512:(c + 1) * 512],
                    X2R[:, c * 512:(c + 1) * 512], bm),
                    waits=(t2, x2t, vnb_guard) if c == 0 else ())
                vop(lambda e, c=c, lo=lo, br=br: e.tensor_mul(
                    VNB[:, c * 1024 + lo: c * 1024 + lo + 512],
                    SQ2[:, c * 512:(c + 1) * 512], br))
            bank_rd[2] = bank_rd[3] = ch_v.tok()
            x2r_free[0] = ch_v.tok()
        return ch_v.tok()

    osl_free = [p7w, p7w]
    x3_out_tok = None
    hp_guard = None
    vnb_guard = None
    for ps_i in range(2):
        tok0 = ps_i * 1024
        v_tok = emit_v(ps_i)
        x2r_free[0] = x2r_free[1] = v_tok
        for o in range(32):
            uo = o % 2
            t1w = gdma(OSL[:, uo * 4096: uo * 4096 + 1024], W_t1[o, :, :], f"osl{uo}",
                       waits=(osl_free[uo],))
            t2w = gdma(OSL[:, uo * 4096 + 1024: uo * 4096 + 2048], W_t2[o, :, :], f"osl{uo}")
            wt = dch(f"osl{uo}").tok()
            for ss in range(2):
                no = ss * 512
                for k in range(8):
                    w = [wt, v_tok] if k == 0 else []
                    if k == 0:
                        w.append(bank_rd.get(ss))
                    mm(bank(ss, 128, 512), OSL[:, uo * 4096 + k * 128: uo * 4096 + (k + 1) * 128],
                       VNB[:, k * 1024 + no: k * 1024 + no + 512],
                       start=(k == 0), stop=(k == 7), waits=tuple(x for x in w if x))
                    w = [wt] if k else [wt, bank_rd.get(2 + ss)]
                    t = mm(bank(2 + ss, 128, 512),
                           OSL[:, uo * 4096 + 1024 + k * 128: uo * 4096 + 1024 + (k + 1) * 128],
                           VNB[:, k * 1024 + no: k * 1024 + no + 512],
                           start=(k == 0), stop=(k == 7), waits=tuple(x for x in w if x),
                           inc=(k == 7))
            gu = ch_t.tok()
            osl_free[uo] = gu
            vnb_guard = gu
            ht = None
            for ss in range(2):
                st2 = sop(lambda e, ss=ss, o=o: e.activation(
                    SGT2[:, :], bank(ss, 128, 512), AF.Silu,
                    scale=SC1[:, o:o + 1]), waits=(gu, p7w, ht))
                bank_rd[ss] = st2
                ht = vop(lambda e, o=o, ss=ss: e.scalar_tensor_tensor(
                    HP[:, o * 1024 + ss * 512: o * 1024 + ss * 512 + 512],
                    bank(2 + ss, 128, 512), SC2[:, o:o + 1],
                    SGT2[:, :], OP.mult, OP.mult),
                    waits=(st2, hp_guard))
                bank_rd[2 + ss] = ht
        hp_tok = ch_v.tok()
        for o in range(32):
            uo = o % 2
            hw = gdma(OSL[:, uo * 4096: uo * 4096 + 4096], W_h1[o, :, :], f"osl{uo}",
                      waits=(osl_free[uo],))
            for ss in range(2):
                bi = 4 + ss
                for k in range(32):
                    w = [hw, hp_tok] if k == 0 else []
                    if k == 0:
                        w.append(bank_rd.get(bi))
                    t = mm(bank(bi, 128, 512), OSL[:, uo * 4096 + k * 128: uo * 4096 + (k + 1) * 128],
                           HP[:, k * 1024 + ss * 512: k * 1024 + ss * 512 + 512],
                           start=(k == 0), stop=(k == 31), waits=tuple(x for x in w if x),
                           inc=(k == 31))
                ge = sop(lambda e, o=o, ss=ss, bi=bi: e.activation(
                    GP[:, o * 1024 + ss * 512: o * 1024 + ss * 512 + 512],
                    bank(bi, 128, 512), AF.Gelu, scale=SCH1[:, o:o + 1]),
                    waits=(t, p7w))
                bank_rd[bi] = ge
            osl_free[uo] = ch_t.tok()
        gp_tok = ch_s.tok()
        for o in range(32):
            uo = o % 2
            hw = gdma(OSL[:, uo * 4096: uo * 4096 + 4096], W_h2[o, :, :], f"osl{uo}",
                      waits=(osl_free[uo],))
            for ss in range(2):
                bi = 4 + ss
                for k in range(32):
                    w = [hw, gp_tok] if k == 0 else []
                    if k == 0:
                        w.append(bank_rd.get(bi))
                    t = mm(bank(bi, 128, 512), OSL[:, uo * 4096 + k * 128: uo * 4096 + (k + 1) * 128],
                           GP[:, k * 1024 + ss * 512: k * 1024 + ss * 512 + 512],
                           start=(k == 0), stop=(k == 31), waits=tuple(x for x in w if x),
                           inc=(k == 31))
                ge = sop(lambda e, o=o, ss=ss, bi=bi: e.activation(
                    HP[:, o * 1024 + ss * 512: o * 1024 + ss * 512 + 512],
                    bank(bi, 128, 512), AF.Gelu, scale=SCH2[:, o:o + 1]),
                    waits=(t, p7w))
                bank_rd[bi] = ge
            osl_free[uo] = ch_t.tok()
        h2p_tok = ch_s.tok()
        hp_guard = None
        for e_i in range(8):
            uo = e_i % 2
            hw = gdma(OSL[:, uo * 4096: uo * 4096 + 4096], W_t3[e_i, :, :], f"osl{uo}",
                      waits=(osl_free[uo],))
            for ss in range(2):
                bi = 6 + ss
                no = tok0 + ss * 512
                for k in range(32):
                    w = [hw, h2p_tok] if k == 0 else []
                    if k == 0:
                        w.append(bank_rd.get(bi))
                    t = mm(bank(bi, 128, 512), OSL[:, uo * 4096 + k * 128: uo * 4096 + (k + 1) * 128],
                           HP[:, k * 1024 + ss * 512: k * 1024 + ss * 512 + 512],
                           start=(k == 0), stop=(k == 31), waits=tuple(x for x in w if x),
                           inc=(k == 31))
                ux = ss
                x2t = gdma(X2R[:, ux * 512: ux * 512 + 512],
                           x2_d.rearrange("(c p) t -> p c t", p=128)[:, e_i, no:no + 512],
                           f"x2w{ux}", waits=(x2r_free[ux],))
                xo = vop(lambda e, bi=bi, e_i=e_i, ux=ux: e.scalar_tensor_tensor(
                    X3T[:, :],
                    bank(bi, 128, 512), SC3[:, e_i:e_i + 1],
                    X2R[:, ux * 512: ux * 512 + 512], OP.mult, OP.add),
                    waits=(t, x2t, p7w, x3_out_tok))
                bank_rd[bi] = xo
                x3_out_tok = ydma(
                    y_out.rearrange("(c p) t -> p c t", p=128)[:, e_i, no:no + 512],
                    X3T[:, :], waits=(xo,))
                x2r_free[ux] = x3_out_tok
                if stop_after is not None and stop_after.startswith("P7w") and (
                        (ps_i, e_i, ss) == tuple(int(x) for x in stop_after[3:].split("_"))):
                    p7s.close()
                    _finish(em, nc, stack, y_out, dbg, dbg_out, locals(),
                            x3_out_tok, ydma, ch_v, ch_s, ch_t)
                    return nc
            osl_free[uo] = ch_t.tok()
        hp_guard = ch_t.tok()
    p7s.close()
    _finish(em, nc, stack, y_out, dbg, dbg_out, locals(), x3_out_tok, ydma, ch_v, ch_s, ch_t)
    return nc


def _emit_z(em, mm, bank, sop, vop, gdma, W_ip, WIPR, wip_state, UALL, u_all,
            bank_rd, bank_wr, SZR, szr, pair, zoff, szfree, ch_t, ch_s):
    toks = []
    for k in range(8):
        sl_i = wip_state["idx"] % 8
        wip_state["idx"] += 1
        wtok = gdma(WIPR[:, sl_i * 128:(sl_i + 1) * 128], W_ip[pair, k, :, :],
                    f"wipr{sl_i}", waits=(wip_state["free"][sl_i],))
        for ss in range(2):
            bi = 6 + ss
            w = [wtok]
            if k == 0:
                w += [u_all, bank_rd.get(bi)]
            t = mm(bank(bi, 128, 512), WIPR[:, sl_i * 128:(sl_i + 1) * 128],
                   UALL[:, k * 2051 + zoff + ss * 512: k * 2051 + zoff + ss * 512 + 512],
                   start=(k == 0), stop=(k == 7), waits=tuple(w),
                   inc=(k == 7 or ss == 1))
            if k == 7:
                bank_wr[bi] = t
        wip_state["free"][sl_i] = ch_t.tok()
    for ss in range(2):
        bi = 6 + ss
        tok = sop(lambda e, bi=bi, ss=ss, szr=szr: e.activation(
            SZR[:, szr * 1024 + ss * 512: szr * 1024 + ss * 512 + 512],
            bank(bi, 128, 512), AF.Silu), waits=(bank_wr[bi], szfree))
        bank_rd[bi] = tok
    return ch_s.tok()


def _finish(em, nc, stack, y_out, dbg, dbg_out, lcl, final_tok, ydma, ch_v, ch_s, ch_t):
    ldy = lcl["ldy"]
    if dbg is not None:
        name, rrows, rcols = dbg
        tsrc = lcl[name]
        em.emit("g", lambda e: e.dma_start(out=dbg_out[:, :], in_=tsrc[0:rrows, 0:rcols]),
                (final_tok, ch_v.tok(), ch_s.tok(), ch_t.tok()),
                ((ldy, 16),))
    em.emit("y", lambda e: e.wait_ge(ldy.sem, ldy.n), (final_tok,), ())
    with nc.Block() as block:
        em.realize(block)
    stack.close()


def kernel(**inputs):
    in_maps = host_prepare(inputs)
    nc = build_program()
    res = run_bass_kernel_spmd(nc, in_maps, core_ids=list(range(NCORE)))
    out = np.zeros((B, S, D), np.float32)
    for c in range(NCORE):
        b, hf = c // 2, c % 2
        out[b, hf * T:(hf + 1) * T, :] = res.results[c]["y"].T
    return out


# version 09
# speedup vs baseline: 1.1385x; 1.0145x over previous
import contextlib
import numpy as np
import concourse.bass as bass
import concourse.mybir as mybir
from concourse.bass_utils import run_bass_kernel_spmd

F32 = mybir.dt.float32
BF16 = mybir.dt.bfloat16
OP = mybir.AluOpType
AF = mybir.ActivationFunctionType
AX = mybir.AxisListType

B, S, D = 4, 4096, 1024
T = 2048
HALO = 3
TH = T + HALO
NCORE = 8
DC = 8
DIN = 2048
NH = 32
HP = 64
DST = 128
Q = 128
NCH = 16
INTER0P = 2816
DHID = 4096
NEG = -1e30
GROUPS = [[0, 1], [2, 3], [4, 5], [6, 7]]

NS_OFF = [0, 512, 1024, 1536, 1539]
NS_W = [512, 512, 512, 512, 512]


class Chan:
    def __init__(self, sem):
        self.sem = sem
        self.n = 0

    def tok(self):
        return (self, self.n)


class Emitter:
    ENGS = ("t", "v", "s", "g", "y")

    def __init__(self):
        self.ops = {e: [] for e in self.ENGS}

    def emit(self, eng, fn, waits=(), incs=()):
        ws = []
        for w in waits:
            if w is None:
                continue
            ch, v = w
            if v > 0:
                ws.append((ch, v))
        for ch, v in incs:
            ch.n += v
        self.ops[eng].append((fn, tuple(ws), tuple(incs)))

    def realize(self, block):
        def mk(key):
            def body(eng):
                last = {}
                for fn, waits, incs in self.ops[key]:
                    for ch, v in waits:
                        if last.get(id(ch), -1) >= v:
                            continue
                        eng.wait_ge(ch.sem, v)
                        last[id(ch)] = v
                    ins = fn(eng)
                    for ch, v in incs:
                        ins = ins.then_inc(ch.sem, v)
            return body

        block.sync(mk("y"))
        block.tensor(mk("t"))
        block.vector(mk("v"))
        block.scalar(mk("s"))
        block.gpsimd(mk("g"))


def _tile_k(w, kc):
    K, M = w.shape
    assert K == kc * 128
    return np.ascontiguousarray(w.reshape(kc, 128, M))


def _pad_cols(w, m_to):
    out = np.zeros((w.shape[0], m_to), dtype=w.dtype)
    out[:, : w.shape[1]] = w
    return out


def _pad_rows(w, k_to):
    out = np.zeros((k_to, w.shape[1]), dtype=w.dtype)
    out[: w.shape[0], :] = w
    return out


def _oslab(wT, kc, oc):
    K, M = wT.shape
    assert K == kc * 128 and M == oc * 128
    a = wT.reshape(kc, 128, oc, 128).transpose(2, 1, 0, 3)
    return np.ascontiguousarray(a.reshape(oc, 128, kc * 128))


def _h2mv(H2B, k, no):
    return H2B[:, k * T + no: k * T + no + 512]


def host_prepare(inputs):
    f32 = np.float32
    x0 = np.asarray(inputs["L0_attn_hidden_states"], f32)

    ipT = np.ascontiguousarray(np.asarray(inputs["L0_attn_in_proj_w"], f32).T)
    ipTp = _pad_cols(ipT, 35 * 128)
    W_ip = np.ascontiguousarray(ipTp.reshape(8, 128, 35, 128).transpose(2, 0, 1, 3))

    convw = np.asarray(inputs["L0_attn_conv_w"], f32)
    conv_diag = np.zeros((18, 4, 128, 128), f32)
    for c in range(18):
        for t in range(4):
            np.fill_diagonal(conv_diag[c, t], convw[c * 128:(c + 1) * 128, t])
    conv_b = np.ascontiguousarray(np.asarray(inputs["L0_attn_conv_b"], f32).reshape(18, 128).T)

    dt_bias = np.tile(np.asarray(inputs["L0_attn_dt_bias"], f32).reshape(NH, 1), (4, 1))
    A_log = np.tile(np.asarray(inputs["L0_attn_A_log"], f32).reshape(NH, 1), (4, 1))
    Dvec = np.asarray(inputs["L0_attn_D"], f32)
    D_diag = np.zeros((NH, 128, 128), f32)
    for h in range(NH):
        np.fill_diagonal(D_diag[h], Dvec[h])

    ssm_w = np.ascontiguousarray(np.asarray(inputs["L0_attn_ssm_norm_w"], f32).reshape(16, 128).T)
    norm_w = np.ascontiguousarray(np.asarray(inputs["L0_attn_norm_w"], f32).reshape(8, 128).T)
    post_w = np.ascontiguousarray(np.asarray(inputs["L0_attn_post_norm_w"], f32).reshape(8, 128).T)

    W_op = _tile_k(np.ascontiguousarray(np.asarray(inputs["L0_attn_out_proj_w"], f32).T), 16)
    W_g = _tile_k(_pad_cols(np.ascontiguousarray(np.asarray(inputs["L0_attn_gate_w"], f32).T), INTER0P), 8)
    W_u = _tile_k(_pad_cols(np.ascontiguousarray(np.asarray(inputs["L0_attn_up_w"], f32).T), INTER0P), 8)
    W_d = _tile_k(_pad_rows(np.ascontiguousarray(np.asarray(inputs["L0_attn_down_w"], f32).T), INTER0P), 22)

    Win = _tile_k(np.asarray(inputs["L0_ffn_Win"], f32), 8)
    bin_ = np.asarray(inputs["L0_ffn_bin_"], f32).reshape(64, 1)
    W1 = np.asarray(inputs["L0_ffn_W1"], f32)
    b1 = np.asarray(inputs["L0_ffn_b1"], f32)
    W2 = np.asarray(inputs["L0_ffn_W2"], f32)
    b2 = np.asarray(inputs["L0_ffn_b2"], f32).reshape(64, 1)
    Wout0 = np.ascontiguousarray(np.asarray(inputs["L0_ffn_Wout"], f32))
    bout0 = np.ascontiguousarray(np.asarray(inputs["L0_ffn_bout"], f32).reshape(8, 128).T)

    wqkv = np.asarray(inputs["L1_attn_w_qkv"], f32)
    W_qk, W_v = {}, {}
    for hg in range(2):
        blocks = []
        for j in range(4):
            h0, h1 = hg * 8 + 2 * j, hg * 8 + 2 * j + 1
            blocks.append(np.concatenate([wqkv[:, h0, 0:64], wqkv[:, h1, 0:64]], axis=1))
            blocks.append(np.concatenate([wqkv[:, h0, 64:128], wqkv[:, h1, 64:128]], axis=1))
        W_qk[hg] = _tile_k(np.concatenate(blocks, axis=1), 8)
        v = np.concatenate([wqkv[:, h, 128:192] for h in range(hg * 8, hg * 8 + 8)], axis=1)
        W_v[hg] = _tile_k(v, 8)
    W_wo = _tile_k(np.asarray(inputs["L1_attn_w_out"], f32).reshape(1024, 1024), 8)

    W_t1 = _oslab(np.ascontiguousarray(np.asarray(inputs["L1_ffn_w1_w"], f32).T), 8, 32)
    W_t2 = _oslab(np.ascontiguousarray(np.asarray(inputs["L1_ffn_w2_w"], f32).T), 8, 32)
    W_h1 = _oslab(np.ascontiguousarray(np.asarray(inputs["L1_ffn_h1_w"], f32).T), 32, 32)
    W_h2 = _oslab(np.ascontiguousarray(np.asarray(inputs["L1_ffn_h2_w"], f32).T), 32, 32)
    W_t3 = _oslab(np.ascontiguousarray(np.asarray(inputs["L1_ffn_w3_w"], f32).T), 32, 8)
    s_t1 = np.ascontiguousarray(np.asarray(inputs["L1_ffn_w1_s"], f32).reshape(32, 128).T)
    s_t2 = np.ascontiguousarray(np.asarray(inputs["L1_ffn_w2_s"], f32).reshape(32, 128).T)
    s_h1 = np.ascontiguousarray(np.asarray(inputs["L1_ffn_h1_s"], f32).reshape(32, 128).T)
    s_h2 = np.ascontiguousarray(np.asarray(inputs["L1_ffn_h2_s"], f32).reshape(32, 128).T)
    s_t3 = np.ascontiguousarray(np.asarray(inputs["L1_ffn_w3_s"], f32).reshape(8, 128).T)

    ident = np.eye(128, dtype=f32)
    ltri01 = np.triu(np.ones((128, 128), f32))
    attn_mask = np.zeros((4, 128, 512), f32)
    for d in range(4):
        for k in range(128):
            attn_mask[d, k, : min(d * 128 + k, 512)] = NEG

    shared = dict(
        W_ip=W_ip, conv_diag=conv_diag, conv_b=conv_b, dt_bias=dt_bias,
        A_log=A_log, D_diag=D_diag, ssm_w=ssm_w, norm_w=norm_w, post_w=post_w,
        W_op=W_op, W_g=W_g, W_u=W_u, W_d=W_d,
        Win=Win, bin_=bin_, W1=W1, b1=b1, W2=W2, b2=b2, Wout0=Wout0, bout0=bout0,
        W_wo=W_wo, W_t1=W_t1, W_t2=W_t2, W_h1=W_h1, W_h2=W_h2, W_t3=W_t3,
        s_t1=s_t1, s_t2=s_t2, s_h1=s_h1, s_h2=s_h2, s_t3=s_t3,
        ident=ident, ltri01=ltri01, attn_mask=attn_mask,
    )

    in_maps = []
    for c in range(NCORE):
        b, hf = c // 2, c % 2
        if hf == 0:
            xs = np.zeros((TH, D), np.float32)
            xs[HALO:] = x0[b, 0:T]
        else:
            xs = x0[b, T - HALO: T - HALO + TH]
        m = dict(shared)
        m["x"] = np.ascontiguousarray(xs.T)
        m["W_qkv_qk"] = W_qk[hf]
        m["W_qkv_v"] = W_v[hf]
        m["uinit"] = np.full((128, 1), float(hf), np.float32)
        m["uinv"] = np.full((128, 1), float(1 - hf), np.float32)
        in_maps.append(m)
    return in_maps


def build_program(dbg=None, stop_after=None):
    nc = bass.Bass("TRN2", target_bir_lowering=False, debug=False, detect_race_conditions=False)

    def din(name, shape):
        return nc.dram_tensor(name, list(shape), F32, kind="ExternalInput").ap()

    x_in = din("x", (D, TH))
    W_ip = din("W_ip", (35, 8, 128, 128))
    conv_diag = din("conv_diag", (18, 4, 128, 128))
    conv_b = din("conv_b", (128, 18))
    dt_bias = din("dt_bias", (128, 1))
    A_log = din("A_log", (128, 1))
    D_diag_in = din("D_diag", (NH, 128, 128))
    ssm_w = din("ssm_w", (128, 16))
    norm_w = din("norm_w", (128, 8))
    post_w = din("post_w", (128, 8))
    W_op = din("W_op", (16, 128, 1024))
    W_g = din("W_g", (8, 128, INTER0P))
    W_u = din("W_u", (8, 128, INTER0P))
    W_d = din("W_d", (22, 128, 1024))
    Win = din("Win", (8, 128, 64))
    bin_in = din("bin_", (64, 1))
    W1_in = din("W1", (64, 16))
    b1_in = din("b1", (64, 16))
    W2_in = din("W2", (64, 16))
    b2_in = din("b2", (64, 1))
    Wout0 = din("Wout0", (64, 1024))
    bout0 = din("bout0", (128, 8))
    W_qkv_qk = din("W_qkv_qk", (8, 128, 1024))
    W_qkv_v = din("W_qkv_v", (8, 128, 512))
    W_wo = din("W_wo", (8, 128, 1024))
    W_t1 = din("W_t1", (32, 128, 1024))
    W_t2 = din("W_t2", (32, 128, 1024))
    W_h1 = din("W_h1", (32, 128, 4096))
    W_h2 = din("W_h2", (32, 128, 4096))
    W_t3 = din("W_t3", (8, 128, 4096))
    s_t1 = din("s_t1", (128, 32))
    s_t2 = din("s_t2", (128, 32))
    s_h1 = din("s_h1", (128, 32))
    s_h2 = din("s_h2", (128, 32))
    s_t3 = din("s_t3", (128, 8))
    ident_in = din("ident", (128, 128))
    ltri_in = din("ltri01", (128, 128))
    amask_in = din("attn_mask", (4, 128, 512))
    uinit_in = din("uinit", (128, 1))
    uinv_in = din("uinv", (128, 1))

    y_out = nc.dram_tensor("y", [D, T], F32, kind="ExternalOutput").ap()
    dbg_out = None
    if dbg is not None:
        dbg_out = nc.dram_tensor("dbg", [dbg[1], dbg[2]], F32, kind="ExternalOutput").ap()

    xtm_d = nc.dram_tensor("xtm_d", [16, 128, 2048], BF16).ap()
    schunk_d = nc.dram_tensor("schunk_d", [NCH, 128, DIN], F32).ap()
    sAG_in = nc.dram_tensor("sAG_in", [128, DIN], F32).ap()
    sAG_out = nc.dram_tensor("sAG_out", [256, DIN], F32).ap()
    x1u_d = nc.dram_tensor("x1u_d", [D, T], F32).ap()
    t1AG_in = nc.dram_tensor("t1AG_in", [D, T], BF16).ap()
    t1AG_out = nc.dram_tensor("t1AG_out", [2 * D, T], BF16).ap()
    oAG_in = nc.dram_tensor("oAG_in", [8, 512, 512], BF16).ap()
    oAG_out = nc.dram_tensor("oAG_out", [8, 1024, 512], BF16).ap()
    x1p_d = nc.dram_tensor("x1p_d", [D, T], F32).ap()
    x2_d = nc.dram_tensor("x2_d", [D, T], F32).ap()

    em = Emitter()
    stack = contextlib.ExitStack()

    def sbuf(name, shape, dtype, st=None):
        return (st or stack).enter_context(nc.sbuf_tensor(name, list(shape), dtype))

    def chan(name):
        return Chan(stack.enter_context(nc.semaphore(name)))

    ch_t = chan("c_t")
    ch_v = chan("c_v")
    ch_s = chan("c_s")
    ldy = chan("ldy")
    cc = chan("cc")
    _dch = {}

    def dch(key):
        if key not in _dch:
            _dch[key] = chan(f"d_{key}")
        return _dch[key]

    def gdma(out, in_, key, waits=(), accum=False):
        ch = dch(key) if isinstance(key, str) else key
        if accum:
            em.emit("g", lambda e, o=out, i=in_: e.dma_start(out=o, in_=i, accum_op=OP.add),
                    waits, ((ch, 16),))
        else:
            em.emit("g", lambda e, o=out, i=in_: e.dma_start(out=o, in_=i),
                    waits, ((ch, 16),))
        return ch.tok()

    def ydma(out, in_, waits=()):
        em.emit("y", lambda e, o=out, i=in_: e.dma_start(out=o, in_=i),
                waits, ((ldy, 16),))
        return ldy.tok()

    def mm(out, lhsT, rhs, start, stop, waits=(), inc=False, tp=None):
        em.emit("t", lambda e, o=out, l=lhsT, r=rhs, a=start, b=stop, t=tp:
                e.matmul(o, l, r, start=a, stop=b, tile_position=t),
                waits, ((ch_t, 1),) if inc else ())
        return ch_t.tok() if inc else None

    def tr(out, in_, idn, waits=(), inc=True):
        em.emit("t", lambda e, o=out, i=in_, d=idn: e.transpose(o, i, d),
                waits, ((ch_t, 1),) if inc else ())
        return ch_t.tok() if inc else None

    def vop(fn, waits=(), inc=True):
        em.emit("v", fn, waits, ((ch_v, 1),) if inc else ())
        return ch_v.tok() if inc else None

    def sop(fn, waits=(), inc=True):
        em.emit("s", fn, waits, ((ch_s, 1),) if inc else ())
        return ch_s.tok() if inc else None

    identb = sbuf("identb", [128, 128], BF16)
    identF = sbuf("identF", [64, 64], F32)
    ltrib = sbuf("ltrib", [128, 128], F32)

    onesF = sbuf("onesF", [1, 128], F32)
    onescolb = sbuf("onescolb", [128, 1], BF16)
    onescolF = sbuf("onescolF", [128, 1], F32)
    nwcol = sbuf("nwcol", [128, 8], F32)
    pwcol = sbuf("pwcol", [128, 8], F32)
    swcol = sbuf("swcol", [128, 16], F32)
    cbcol = sbuf("cbcol", [128, 18], F32)
    dtbcol = sbuf("dtbcol", [128, 1], F32)
    nAcol = sbuf("nAcol", [128, 1], F32)
    uinitcol = sbuf("uinitcol", [128, 1], F32)
    uinvcol = sbuf("uinvcol", [128, 1], F32)
    mean_r = sbuf("mean_r", [1, TH], F32)
    rc_r = sbuf("rc_r", [1, TH], F32)
    us_r = sbuf("us_r", [1, TH], F32)
    pL0 = contextlib.ExitStack()
    UALL = sbuf("UALL", [128, DC * TH], BF16, pL0)
    CFM = sbuf("CFM", [128, T], BF16, pL0)
    hrA = sbuf("hrA", [32, T], F32, pL0)
    hrB = sbuf("hrB", [32, T], F32, pL0)
    hrD = sbuf("hrD", [32, T], F32, pL0)
    cgT = sbuf("cgT", [32, 16], F32, pL0)
    cgrep = sbuf("cgrep", [1, 512], F32, pL0)
    decCols = sbuf("decCols", [128, 512], F32, pL0)
    dT = sbuf("dT", [128, NCH * 32], F32, pL0)
    wAll = sbuf("wAll", [128, NCH * 32], F32, pL0)
    P1B = sbuf("P1B", [128, NCH * 128], F32, pL0)
    Scur = sbuf("Scur", [128, 128], F32, pL0)
    Sinit = sbuf("Sinit", [128, DIN], F32, pL0)
    WIPR = sbuf("WIPR", [128, 8 * 128], BF16, pL0)
    p1s2 = contextlib.ExitStack()
    var_r = sbuf("var_r", [1, TH], F32, p1s2)
    scr_r = sbuf("scr_r", [1, TH], F32, p1s2)
    BFM = sbuf("BFM", [128, T], BF16, p1s2)
    BTM = sbuf("BTM", [128, NCH * 128], BF16, p1s2)
    hrC = sbuf("hrC", [32, T], F32, p1s2)
    XBCR = sbuf("XBCR", [128, 2 * TH], BF16, p1s2)
    CDG = sbuf("CDG", [128, 2 * 512], BF16, p1s2)
    decrow = sbuf("decrow", [1, 512], F32, p1s2)
    cgeCols = sbuf("cgeCols", [128, 512], F32, p1s2)
    XFR = sbuf("XFR", [128, TH], BF16, p1s2)
    XCV = sbuf("XCV", [128, T], BF16, p1s2)
    XTMp = sbuf("XTMp", [128, 2048], BF16, p1s2)
    BDR = sbuf("BDR", [128, 2 * 128], BF16, p1s2)
    SEV = sbuf("SEV", [128, 2 * 128], F32, p1s2)

    psA = stack.enter_context(nc.psum_tensor("psA", [128, 2048], F32))
    psB = stack.enter_context(nc.psum_tensor("psB", [128, 2048], F32))

    def bank(i, rows_=128, cols=512, roff=0):
        src = psA if i < 4 else psB
        return src[roff:roff + rows_, (i % 4) * 512:(i % 4) * 512 + cols]

    def bankb(i, rows_=128, cols=128):
        src = psA if i < 4 else psB
        return src[0:rows_, (i % 4) * 512:(i % 4) * 512 + 256].bitcast(BF16)[:, 0:cols]

    bank_rd = {}
    bank_wr = {}

    gdma(identF[:, :], ident_in[0:64, 0:64], "consts")
    for dst, src in ((identb, ident_in), (ltrib, ltri_in), (nwcol, norm_w),
                     (pwcol, post_w), (swcol, ssm_w), (cbcol, conv_b),
                     (dtbcol, dt_bias), (nAcol, A_log), (uinitcol, uinit_in),
                     (uinvcol, uinv_in)):
        gdma(dst[:, :], src[:, :], "consts")
    consts = dch("consts").tok()
    vop(lambda e: e.memset(onesF[:, :], 1.0))
    vop(lambda e: e.memset(onescolb[:, :], 1.0))
    vop(lambda e: e.memset(onescolF[:, :], 1.0))
    ones_tok = ch_v.tok()
    sop(lambda e: e.activation(nAcol[:, :], nAcol[:, :], AF.Exp), waits=(consts,))
    nA_tok = vop(lambda e: e.tensor_scalar(nAcol[:, :], nAcol[:, :], -1.0, None, OP.mult),
                 waits=(ch_s.tok(),))

    p0s = contextlib.ExitStack()
    X0F = sbuf("X0F", [128, DC * 512], F32, p0s)
    X0B = sbuf("X0B", [128, DC * 512], BF16, p0s)
    SQB = sbuf("SQB", [128, DC * 512], BF16, p0s)
    UTMP = sbuf("UTMP", [128, 512], F32, p0s)
    x0_free = [None, None]
    u_tok = [None] * 5
    prev_stats = None
    for g in range(5):
        go, gw = NS_OFF[g], NS_W[g]
        sl = 0
        x0tok = gdma(X0F[:, sl:sl + DC * gw],
                     x_in.rearrange("(c p) t -> p c t", p=128)[:, :, go:go + gw],
                     "x0f", waits=(x0_free[0], x0_free[1]))
        xbtok = vop(lambda e, gw=gw, sl=sl: e.tensor_copy(
            X0B[:, :DC * gw], X0F[:, sl:sl + DC * gw]), waits=(x0tok, prev_stats))
        sqtok = sop(lambda e, gw=gw, sl=sl: e.activation(
            SQB[:, :DC * gw], X0F[:, sl:sl + DC * gw], AF.Square),
            waits=(x0tok, prev_stats))
        b0, b1v = bank(0, 1, gw), bank(1, 1, gw)
        for c in range(DC):
            w = [xbtok, ones_tok, bank_rd.get(0)] if c == 0 else []
            mm(b0, onescolb[:, :], X0B[:, c * gw:(c + 1) * gw],
               start=(c == 0), stop=(c == DC - 1), waits=tuple(w))
        for c in range(DC):
            w = [sqtok, bank_rd.get(1)] if c == 0 else []
            stats_tok = mm(b1v, onescolb[:, :], SQB[:, c * gw:(c + 1) * gw],
                           start=(c == 0), stop=(c == DC - 1), waits=tuple(w),
                           inc=(c == DC - 1))
        vop(lambda e, go=go, gw=gw, b0=b0: e.tensor_scalar(
            mean_r[0:1, go:go + gw], b0, 1.0 / D, None, OP.mult), waits=(stats_tok,))
        vop(lambda e, go=go, gw=gw, b1v=b1v: e.tensor_scalar(
            scr_r[0:1, go:go + gw], b1v, 1.0 / D, None, OP.mult))
        bank_rd[0] = bank_rd[1] = ch_v.tok()
        stats_rows_tok = ch_v.tok()
        vop(lambda e, go=go, gw=gw: e.tensor_mul(
            var_r[0:1, go:go + gw], mean_r[0:1, go:go + gw], mean_r[0:1, go:go + gw]))
        vop(lambda e, go=go, gw=gw: e.tensor_sub(
            var_r[0:1, go:go + gw], scr_r[0:1, go:go + gw], var_r[0:1, go:go + gw]))
        var_tok = vop(lambda e, go=go, gw=gw: e.tensor_scalar(
            var_r[0:1, go:go + gw], var_r[0:1, go:go + gw], 0.0, None, OP.max))
        vop(lambda e, go=go, gw=gw: e.tensor_scalar(
            rc_r[0:1, go:go + gw], var_r[0:1, go:go + gw], 0.0, 1e-5, OP.max, OP.add),
            waits=(var_tok,))
        sop(lambda e, go=go, gw=gw: e.activation(
            rc_r[0:1, go:go + gw], rc_r[0:1, go:go + gw], AF.Ln),
            waits=(ch_v.tok(),))
        rc_tok = sop(lambda e, go=go, gw=gw: e.activation(
            rc_r[0:1, go:go + gw], rc_r[0:1, go:go + gw], AF.Exp, scale=-0.5))
        vop(lambda e, go=go, gw=gw: e.tensor_mul(
            scr_r[0:1, go:go + gw], var_r[0:1, go:go + gw], rc_r[0:1, go:go + gw]),
            waits=(rc_tok,))
        q_tok = vop(lambda e, go=go, gw=gw: e.tensor_mul(
            scr_r[0:1, go:go + gw], scr_r[0:1, go:go + gw], rc_r[0:1, go:go + gw]))
        vop(lambda e, go=go, gw=gw: e.tensor_scalar(
            scr_r[0:1, go:go + gw], scr_r[0:1, go:go + gw], 1e-6, None, OP.add),
            waits=(q_tok,))
        sop(lambda e, go=go, gw=gw: e.activation(
            scr_r[0:1, go:go + gw], scr_r[0:1, go:go + gw], AF.Ln),
            waits=(ch_v.tok(),))
        e_tok = sop(lambda e, go=go, gw=gw: e.activation(
            scr_r[0:1, go:go + gw], scr_r[0:1, go:go + gw], AF.Exp, scale=-0.5))
        us_tok = vop(lambda e, go=go, gw=gw: e.tensor_mul(
            us_r[0:1, go:go + gw], scr_r[0:1, go:go + gw], rc_r[0:1, go:go + gw]),
            waits=(e_tok,))
        b2v, b3v = bank(2, 128, gw), bank(3, 128, gw)
        w = [ones_tok, bank_rd.get(2), stats_rows_tok]
        mm(b2v, onesF[0:1, :], mean_r[0:1, go:go + gw], start=True, stop=True,
           waits=tuple(w))
        w = [us_tok, bank_rd.get(3)]
        bc_tok = mm(b3v, onesF[0:1, :], us_r[0:1, go:go + gw], start=True, stop=True,
                    waits=tuple(w), inc=True)
        for c in range(DC):
            vop(lambda e, c=c, gw=gw, sl=sl, b2v=b2v: e.tensor_sub(
                UTMP[:, 0:gw], X0F[:, sl + c * gw: sl + (c + 1) * gw], b2v),
                waits=(bc_tok,) if c == 0 else ())
            vop(lambda e, c=c, go=go, gw=gw, b3v=b3v: e.scalar_tensor_tensor(
                UALL[:, c * TH + go: c * TH + go + gw],
                UTMP[:, 0:gw], nwcol[:, c:c + 1], b3v, OP.mult, OP.mult),
                waits=(consts,) if c == 0 else ())
        u_tok[g] = ch_v.tok()
        prev_stats = stats_tok
        bank_rd[2] = bank_rd[3] = ch_v.tok()
        x0_free[0] = ch_v.tok()
        x0_free[1] = ch_s.tok()
    u_all = ch_v.tok()
    p0s.close()
    for i in range(8):
        bank_rd[i] = None
    bank_rd[0] = bank_rd[1] = bank_rd[2] = bank_rd[3] = u_all

    if stop_after == "P0":
        DUMPS = sbuf("DUMPS", [128, 64], F32, p1s2)
        vop(lambda e: e.tensor_copy(DUMPS[0:1, 0:3], psA[0:1, 0:3]), waits=(u_all,))
        vop(lambda e: e.tensor_copy(DUMPS[0:1, 4:7], psA[0:1, 512:515]))
        vop(lambda e: e.tensor_copy(DUMPS[:, 8:32], X0B[:, 0:24]))
        vop(lambda e: e.tensor_copy(DUMPS[:, 32:56], SQB[:, 0:24]))
        dtk = vop(lambda e: e.tensor_copy(DUMPS[:, 56:57], onescolb[:, :]))
        gdma(y_out[0:1, :], mean_r[0:1, 3:2051], "dump", waits=(u_all,))
        gdma(y_out[1:2, :], us_r[0:1, 3:2051], "dump")
        gdma(y_out[2:3, :], rc_r[0:1, 3:2051], "dump")
        gdma(y_out[3:4, :], var_r[0:1, 3:2051], "dump")
        gdma(y_out[4:5, :], scr_r[0:1, 3:2051], "dump")
        gdma(y_out[576:704, :], UALL[:, 3:2051], "dump")
        gdma(y_out[704:832, :], UALL[:, 7 * TH + 3:7 * TH + 2051], "dump")
        d9 = gdma(y_out[320:448, 0:64], DUMPS[:, :], "dump", waits=(dtk,))
        p1s2.close()
        pL0.close()
        _finish(em, nc, stack, y_out, dbg, dbg_out, locals(), d9, ydma, ch_v, ch_s, ch_t)
        return nc

    wip_state = {"free": [u_all] * 8, "idx": 0}

    def inproj_mtile(mt, evac, slices=range(5), bksel=(4, 5, 6, 7, 0), dmaslot=1):
        toks = []
        for k in range(8):
            sl_i = wip_state["idx"] % 8
            wip_state["idx"] += 1
            wtok = gdma(WIPR[:, sl_i * 128:(sl_i + 1) * 128], W_ip[mt, k, :, :],
                        f"wipr{sl_i}", waits=(wip_state["free"][sl_i],))
            toks.append((sl_i, wtok))
        for ki, (sl_i, wtok) in enumerate(toks):
            for ng in slices:
                no, nw = NS_OFF[ng], NS_W[ng]
                bi = bksel[ng if len(bksel) == 5 else list(slices).index(ng)]
                w = [wtok]
                if ki == 0:
                    w.append(u_all)
                    w.append(bank_rd.get(bi))
                t = mm(bank(bi, 128, nw), WIPR[:, sl_i * 128:(sl_i + 1) * 128],
                       UALL[:, ki * TH + no: ki * TH + no + nw],
                       start=(ki == 0), stop=(ki == 7),
                       waits=tuple(w), inc=(ki == 7 or ng == list(slices)[-1]))
                if ki == 7:
                    bank_wr[bi] = t
            wip_state["free"][sl_i] = ch_t.tok()
        for ng in slices:
            bi = bksel[ng if len(bksel) == 5 else list(slices).index(ng)]
            bank_rd[bi] = evac(ng, bank(bi, 128, NS_W[ng]), bank_wr[bi])

    cdg_free = [u_all, u_all]
    xbc_free = [None, None]

    def evac_to_ring(ring_slot):
        def evac(ng, bv, wr):
            no, nw = NS_OFF[ng], NS_W[ng]
            return vop(lambda e, no=no, nw=nw, bv=bv, rs=ring_slot: e.tensor_copy(
                XBCR[:, rs * TH + no: rs * TH + no + nw], bv),
                waits=(wr, xbc_free[ring_slot]) if ng == 0 else (wr,))
        return evac

    def emit_conv(cchunk, ring_slot, out_fn, bks=(1, 2)):
        dslot = cchunk % 2
        dtok = gdma(CDG[:, dslot * 512:(dslot + 1) * 512],
                    conv_diag.rearrange("c t p m -> c p t m")[cchunk, :, :, :],
                    f"cdg{dslot}", waits=(cdg_free[dslot],))
        xtok = ch_v.tok()
        last = None
        for ng in range(4):
            no = ng * 512
            bi = bks[ng % 2]
            cb = bank(bi, 128, 512)
            for tap in range(4):
                w = [dtok, xtok, bank_rd.get(bi)] if tap == 0 else []
                t = mm(cb, CDG[:, dslot * 512 + tap * 128: dslot * 512 + (tap + 1) * 128],
                       XBCR[:, ring_slot * TH + no + tap: ring_slot * TH + no + tap + 512],
                       start=(tap == 0), stop=(tap == 3), waits=tuple(w), inc=(tap == 3))
            bank_wr[bi] = t
            last = out_fn(ng, cb, no, 512, t)
            bank_rd[bi] = last
        cdg_free[dslot] = last
        xbc_free[ring_slot] = last
        return last

    for bc in range(2):
        mt = 32 + bc
        inproj_mtile(mt, evac_to_ring(bc % 2))
        xbc_tok = ch_v.tok()

        def bc_out(ng, cb, no, nw, wr, bc=bc):
            dst = BFM if bc == 0 else CFM
            return sop(lambda e, cb=cb, no=no, nw=nw, dst=dst, bc=bc: e.activation(
                dst[:, no: no + nw], cb, AF.Silu,
                bias=cbcol[:, 16 + bc:17 + bc]), waits=(wr,))
        emit_conv(16 + bc, bc % 2, bc_out)
    bc_done = ch_s.tok()
    def evac_dt(ng, bv, wr):
        no, nw = NS_OFF[ng], NS_W[ng]
        if ng == 0:
            return vop(lambda e, bv=bv: e.tensor_copy(hrA[0:32, 0:509], bv[0:32, 3:512]),
                       waits=(wr,))
        return vop(lambda e, bv=bv, no=no, nw=nw: e.tensor_copy(
            hrA[0:32, no - 3: no - 3 + nw], bv[0:32, 0:nw]), waits=(wr,))
    inproj_mtile(34, evac_dt)
    dtraw_tok = ch_v.tok()
    for tc in range(NCH):
        tb = bankb(7, 128, 128)
        w = [bc_done, consts, bank_rd.get(7)]
        t = tr(tb, BFM[:, tc * 128:(tc + 1) * 128], identb[:, :], waits=tuple(w))
        bank_rd[7] = vop(lambda e, tc=tc, tb=tb: e.tensor_copy(
            BTM[:, tc * 128:(tc + 1) * 128], tb), waits=(t,))
    btm_tok = ch_v.tok()

    vop(lambda e: e.tensor_scalar(hrA[:, :], hrA[:, :], dtbcol[0:32, :], None, OP.add),
        waits=(dtraw_tok, consts))
    dtb_tok = ch_v.tok()
    sop(lambda e: e.activation(hrC[:, :], hrA[:, :], AF.Abs), waits=(dtb_tok,))
    sop(lambda e: e.activation(hrC[:, :], hrC[:, :], AF.Exp, scale=-1.0))
    vop(lambda e: e.tensor_scalar(hrC[:, :], hrC[:, :], 1.0, None, OP.add),
        waits=(ch_s.tok(),))
    ln_tok = sop(lambda e: e.activation(hrC[:, :], hrC[:, :], AF.Ln),
                 waits=(ch_v.tok(),))
    vop(lambda e: e.tensor_relu(hrB[:, :], hrA[:, :]))
    dt_tok = vop(lambda e: e.tensor_add(hrB[:, :], hrB[:, :], hrC[:, :]), waits=(ln_tok,))
    dtA_tok = vop(lambda e: e.tensor_scalar(hrC[:, :], hrB[:, :], nAcol[0:32, :],
                                            None, OP.mult), waits=(dt_tok, nA_tok))
    lndt_tok = sop(lambda e: e.activation(hrB[:, :], hrB[:, :], AF.Ln), waits=(dtA_tok,))
    zr_tok = vop(lambda e: e.memset(hrA[:, :], 0.0))
    cg_tok = vop(lambda e: e.tensor_tensor_scan(
        hrD[:, :], hrC[:, :], hrA[:, :], 0.0, OP.add, OP.add), waits=(zr_tok,))
    d_tok = vop(lambda e: e.tensor_sub(hrB[:, :], hrB[:, :], hrD[:, :]),
                waits=(cg_tok, lndt_tok))

    cgT_tok = vop(lambda e: e.tensor_copy(
        cgT[:, :], hrD.rearrange("h (k q) -> h k q", q=Q)[:, :, Q - 1]), waits=(cg_tok,))
    cgrep_tok = gdma(cgrep[:, :], cgT[:, :], "rows0", waits=(cgT_tok,))
    vop(lambda e: e.memset(decrow[:, :], 0.0))
    z_tok = ch_v.tok()
    vop(lambda e: e.tensor_copy(
        decrow.rearrange("o (h k) -> o h k", k=16)[:, :, 1:16],
        cgrep.rearrange("o (h k) -> o h k", k=16)[:, :, 0:15]),
        waits=(cgrep_tok, z_tok))
    ddiff_tok = vop(lambda e: e.tensor_sub(decrow[:, :], cgrep[:, :], decrow[:, :]))
    decrow_tok = sop(lambda e: e.activation(decrow[:, :], decrow[:, :], AF.Exp),
                     waits=(ddiff_tok,))
    w = [decrow_tok, ones_tok, bank_rd.get(0)]
    t = mm(bank(0, 128, 512), onesF[:, :], decrow[:, :], start=True, stop=True,
           waits=tuple(w), inc=True)
    dec_tok = vop(lambda e: e.tensor_copy(decCols[:, :], bank(0, 128, 512)), waits=(t,))
    bank_rd[0] = dec_tok
    w = [cgrep_tok, ones_tok, bank_rd.get(1)]
    t = mm(bank(1, 128, 512), onesF[:, :], cgrep[:, :], start=True, stop=True,
           waits=tuple(w), inc=True)
    cge_tok = vop(lambda e: e.tensor_copy(cgeCols[:, :], bank(1, 128, 512)), waits=(t,))
    bank_rd[1] = cge_tok
    for k in range(NCH):
        tb = bank(2, 128, 32)
        w = [d_tok, consts, bank_rd.get(2)]
        t = tr(tb, hrB[0:32, k * Q:(k + 1) * Q], identF[0:32, 0:32], waits=tuple(w))
        bank_rd[2] = vop(lambda e, k=k, tb=tb: e.tensor_copy(
            dT[:, k * 32:(k + 1) * 32], tb), waits=(t,))
    dT_tok = ch_v.tok()
    for k in range(NCH):
        vop(lambda e, k=k: e.tensor_add(
            wAll[:, k * 32:(k + 1) * 32], dT[:, k * 32:(k + 1) * 32],
            cgeCols[:, k::16]), waits=(dT_tok, cge_tok) if k == 0 else ())
        sop(lambda e, k=k: e.activation(
            wAll[:, k * 32:(k + 1) * 32], wAll[:, k * 32:(k + 1) * 32], AF.Exp),
            waits=(ch_v.tok(),))
    wAll_tok = ch_s.tok()
    for k in range(NCH):
        if k == 0:
            vop(lambda e: e.tensor_copy(hrA[:, 0:Q], hrD[:, 0:Q]), waits=(cg_tok,))
        else:
            vop(lambda e, k=k: e.tensor_scalar(
                hrA[:, k * Q:(k + 1) * Q], hrD[:, k * Q:(k + 1) * Q],
                cgT[:, k - 1:k], None, OP.subtract),
                waits=(cgT_tok,) if k == 1 else ())
    edec_tok = sop(lambda e: e.activation(hrA[:, :], hrA[:, :], AF.Exp),
                   waits=(ch_v.tok(),))
    for k in range(NCH):
        bv = bank(3, 128, 128)
        w = [bc_done, bank_rd.get(3)]
        t = mm(bv, BFM[:, k * 128:(k + 1) * 128], CFM[:, k * 128:(k + 1) * 128],
               start=True, stop=True, waits=tuple(w), inc=True)
        bank_rd[3] = vop(lambda e, k=k, bv=bv: e.tensor_mul(
            P1B[:, k * 128:(k + 1) * 128], bv, ltrib[:, :]), waits=(t, consts))
    p1b_tok = ch_v.tok()

    if stop_after == "P2":
        p1s2.close()
        pL0.close()
        _finish(em, nc, stack, y_out, dbg, dbg_out, locals(),
                p1b_tok, ydma, ch_v, ch_s, ch_t)
        return nc

    xfr_free = None
    xcv_free = None
    xtmp_free = None
    sc_zero = None
    for pair in range(16):
        def evac_x(ng, bv, wr, _ff=None):
            no, nw = NS_OFF[ng], NS_W[ng]
            w = [wr] if ng else [wr, xfr_free]
            return sop(lambda e, no=no, nw=nw, bv=bv: e.activation(
                XFR[:, no:no + nw], bv, AF.Identity), waits=tuple(x for x in w if x))
        inproj_mtile(16 + pair, evac_x)
        xf_tok = ch_s.tok()
        def x_out(ng, cb, no, nw, wr, pair=pair):
            w = [wr] if ng else [wr, xcv_free]
            return sop(lambda e, cb=cb, no=no, nw=nw, pair=pair: e.activation(
                XCV[:, no:no + nw], cb, AF.Silu, bias=cbcol[:, pair:pair + 1]),
                waits=tuple(x for x in w if x))
        dslot = pair % 2
        dtok = gdma(CDG[:, dslot * 512:(dslot + 1) * 512],
                    conv_diag.rearrange("c t p m -> c p t m")[pair, :, :, :],
                    f"cdg{dslot}", waits=(cdg_free[dslot],))
        last = None
        for ng in range(4):
            no = ng * 512
            bi = (1, 2)[ng % 2]
            cb = bank(bi, 128, 512)
            for tap in range(4):
                w = [dtok, xf_tok, bank_rd.get(bi)] if tap == 0 else []
                t = mm(cb, CDG[:, dslot * 512 + tap * 128: dslot * 512 + (tap + 1) * 128],
                       XFR[:, no + tap: no + tap + 512],
                       start=(tap == 0), stop=(tap == 3), waits=tuple(w), inc=(tap == 3))
            last = x_out(ng, cb, no, 512, t)
            bank_rd[bi] = last
        cdg_free[dslot] = last
        xfr_free = last
        silu_tok = ch_s.tok()
        for tc in range(NCH):
            for hh in range(2):
                tb = bankb(3, 128, 64)
                w = [silu_tok, consts, bank_rd.get(3)]
                if tc == 0 and hh == 0 and xtmp_free is not None:
                    w.extend(xtmp_free)
                t = tr(tb, XCV[hh * 64:(hh + 1) * 64, tc * 128:(tc + 1) * 128],
                       identb[hh * 64:(hh + 1) * 64, hh * 64:(hh + 1) * 64],
                       waits=tuple(x for x in w if x))
                bank_rd[3] = vop(lambda e, tc=tc, hh=hh, tb=tb: e.tensor_copy(
                    XTMp[:, (tc * 2 + hh) * 64:(tc * 2 + hh) * 64 + 64], tb), waits=(t,))
        xtm_tok = ch_v.tok()
        xcv_free = xtm_tok
        ydma(xtm_d[pair, :, :], XTMp[:, :], waits=(xtm_tok,))
        xtm_st = ldy.tok()
        vop(lambda e: e.memset(Scur[:, :], 0.0), waits=(sc_zero,))
        sc_tok = ch_v.tok()
        sev_free = [None, None]
        for k in range(NCH):
            bd_toks = []
            for hh in range(2):
                h = pair * 2 + hh
                w = [btm_tok, wAll_tok]
                if bank_rd.get(90 + hh):
                    w.append(bank_rd[90 + hh])
                bd_toks.append(vop(lambda e, k=k, hh=hh, h=h: e.tensor_scalar(
                    BDR[:, hh * 128:(hh + 1) * 128], BTM[:, k * 128:(k + 1) * 128],
                    wAll[:, k * 32 + h: k * 32 + h + 1], None, OP.mult),
                    waits=tuple(w)))
            bi = 4 + (k % 2)
            bv = bank(bi, 128, 128)
            for hh in range(2):
                w = [bd_toks[hh], xtm_tok]
                if hh == 0 and bank_rd.get(bi):
                    w.append(bank_rd[bi])
                t = mm(bv[:, hh * 64:(hh + 1) * 64], BDR[:, hh * 128:(hh + 1) * 128],
                       XTMp[:, (k * 2 + hh) * 64:(k * 2 + hh) * 64 + 64],
                       start=True, stop=True, waits=tuple(w), inc=(hh == 1))
            bank_wr[bi] = t
            bank_rd[90] = bank_rd[91] = t
            es = k % 2
            ev = vop(lambda e, bv=bv, es=es: e.tensor_copy(
                SEV[:, es * 128:(es + 1) * 128], bv), waits=(t, sev_free[es]))
            bank_rd[bi] = ev
            ydma(schunk_d[k, :, pair * 128:(pair + 1) * 128],
                 SEV[:, es * 128:(es + 1) * 128], waits=(ev,))
            sev_free[es] = ldy.tok()
            for hh in range(2):
                h = pair * 2 + hh
                vop(lambda e, hh=hh, h=h, k=k: e.tensor_scalar(
                    Scur[:, hh * 64:(hh + 1) * 64], Scur[:, hh * 64:(hh + 1) * 64],
                    decCols[:, h * 16 + k: h * 16 + k + 1], None, OP.mult),
                    waits=(sc_tok, dec_tok) if k == 0 and hh == 0 else ())
            vop(lambda e, es=es: e.tensor_add(
                Scur[:, :], Scur[:, :], SEV[:, es * 128:(es + 1) * 128]), waits=(ev,))
            sc_tok = ch_v.tok()
        ydma(sAG_in[:, pair * 128:(pair + 1) * 128], Scur[:, :], waits=(sc_tok,))
        sc_zero = ldy.tok()
        xtmp_free = (xtm_st, ch_t.tok())
    sA_done = ldy.tok()

    em.emit("g", lambda e: e.collective_compute(
        "AllGather", OP.bypass, replica_groups=GROUPS,
        ins=[sAG_in[:, :]], outs=[sAG_out[:, :]]),
        (sA_done,), ((cc, 1),))
    ag1 = cc.tok()
    si_ld = gdma(Sinit[:, :], sAG_out[0:128, :], "sinit", waits=(ag1,))
    si_tok = vop(lambda e: e.tensor_scalar(Sinit[:, :], Sinit[:, :], uinitcol[:, :],
                                           None, OP.mult), waits=(si_ld, consts))

    if stop_after == "A":
        d1 = ydma(y_out[0:128, :], Sinit[:, :], waits=(si_tok,))
        d2 = ydma(y_out[128:160, :], hrD[:, :], waits=(cg_tok,))
        d3 = ydma(y_out[160:192, :], hrB[:, :], waits=(d_tok,))
        d4 = ydma(y_out[192:320, :], schunk_d[5, :, :], waits=(sA_done,))
        d5 = gdma(y_out[320:448, :], xtm_d[3, :, :], "dump", waits=(sA_done,))
        d6 = ydma(y_out[448:576, :], sAG_in[:, :], waits=(sA_done,))
        d7 = gdma(y_out[576:704, :], UALL[:, 3:2051], "dump", waits=(u_all,))
        d8 = gdma(y_out[704:832, :], BFM[:, :], "dump", waits=(bc_done,))
        d9 = ydma(y_out[832:864, :], hrA[:, :], waits=(dtraw_tok,))
        p1s2.close()
        pL0.close()
        _finish(em, nc, stack, y_out, dbg, dbg_out, locals(), d6, ydma, ch_v, ch_s, ch_t)
        return nc

    p1s2.close()
    p3s = contextlib.ExitStack()
    SLOTS = sbuf("SLOTS", [2, 6 * 1024], F32, p3s)
    MHALF = sbuf("MHALF", [128, 16 * 1024], BF16, p3s)
    SZR = sbuf("SZR", [128, 2 * 1024], BF16, p3s)
    XTMR = sbuf("XTMR", [128, 2 * 2048], BF16, p3s)
    SCH = sbuf("SCH", [128, 2 * 128], F32, p3s)
    SCARRY = sbuf("SCARRY", [128, 16 * 128], F32, p3s)
    E2R = sbuf("E2R", [128, 2 * 256], F32, p3s)
    SCB = sbuf("SCB", [128, 2 * 128], BF16, p3s)
    GTR = sbuf("GTR", [128, 2 * 256], BF16, p3s)
    CHR = sbuf("CHR", [128, 2 * 128], BF16, p3s)
    DDG = sbuf("DDG", [128, NH * 128], BF16, p3s)
    ddg_toks = []
    for h in range(NH):
        ddg_toks.append(gdma(DDG[:, h * 128:(h + 1) * 128], D_diag_in[h, :, :], "ddg",
                             waits=(sA_done,) if h == 0 else ()))
    ddg_tok = dch("ddg").tok()
    for r in range(2):
        vop(lambda e, r=r: e.memset(SLOTS[0:2, r * 3072: r * 3072 + 2048], 1.0))
    slots_ones = ch_v.tok()
    slot_free = [sA_done] * 2
    szr_free = [None, None]
    e2_free = [None, None]
    xtmr_free = [None, None]
    sch_free = [None, None]
    scb_free = [None, None]
    gt_free = [None, None]
    chr_free = [None, None]
    mh_guard = None

    SQH = sbuf("SQH", [128, 512], BF16, p3s)
    X0R = sbuf("X0R", [128, 2 * 512], F32, p3s)
    WSC = sbuf("WSC", [128, 2 * 128], F32, p3s)
    for half in range(2):
        k0 = half * 8
        for pair in range(16):
            sl_toks = []
            rr = [0, 0]
            for hh in range(2):
                h = pair * 2 + hh
                r = hh
                rr[hh] = r
                hc = slice(k0 * Q, k0 * Q + 1024)
                w = (slot_free[r], slots_ones, d_tok, edec_tok)
                gdma(SLOTS[1:2, r * 3072: r * 3072 + 1024], hrB[h:h + 1, hc], f"slr{r}", waits=w)
                gdma(SLOTS[0:1, r * 3072 + 1024: r * 3072 + 2048], hrD[h:h + 1, hc], f"slr{r}")
                sl_toks.append(gdma(SLOTS[0:1, r * 3072 + 2048: r * 3072 + 3072],
                                    hrA[h:h + 1, hc], f"slr{r}"))
            xr = pair % 2
            xt_tok = gdma(XTMR[:, xr * 2048:(xr + 1) * 2048], xtm_d[pair, :, :], f"xtmr{xr}",
                          waits=(sA_done, xtmr_free[xr]))
            zoff = k0 * Q + HALO
            szr = pair % 2
            sz_tok = _emit_z(em, mm, bank, sop, vop, gdma, W_ip, WIPR, wip_state, UALL,
                             u_all, bank_rd, bank_wr, SZR, szr, pair, zoff,
                             szr_free[szr], ch_t, ch_s)
            scur_init = None
            if half == 0:
                scur_init = vop(lambda e, pair=pair: e.tensor_copy(
                    SCARRY[:, pair * 128:(pair + 1) * 128], Sinit[:, pair * 128:(pair + 1) * 128]),
                    waits=(si_tok,))
            sc_tok2 = scur_init or ch_v.tok()
            for kk in range(8):
                k = k0 + kk
                sr = kk % 2
                sch_tok = gdma(SCH[:, sr * 128:(sr + 1) * 128],
                               schunk_d[k, :, pair * 128:(pair + 1) * 128], f"sch{sr}",
                               waits=(sA_done, sch_free[sr]))
                psT = bank(0, 128, 256)
                for hh in range(2):
                    h = pair * 2 + hh
                    r = rr[hh]
                    w = [sl_toks[hh], bank_rd.get(0)] if hh == 0 else [sl_toks[hh]]
                    t = mm(psT[:, hh * 128:(hh + 1) * 128],
                           SLOTS[0:2, r * 3072 + kk * Q: r * 3072 + (kk + 1) * Q],
                           SLOTS[0:2, r * 3072 + 1024 + kk * Q: r * 3072 + 1024 + (kk + 1) * Q],
                           start=True, stop=True, waits=tuple(w), inc=(hh == 1))
                bank_wr[0] = t
                er = kk % 2
                clamp_tok = vop(lambda e, psT=psT: e.tensor_scalar(
                    psT, psT, 80.0, None, OP.min), waits=(t,))
                e2_tok = sop(lambda e, er=er, psT=psT: e.activation(
                    E2R[:, er * 256:(er + 1) * 256], psT, AF.Exp),
                    waits=(clamp_tok, e2_free[er]))
                bank_rd[0] = e2_tok
                g_toks = []
                for hh in range(2):
                    g_toks.append(vop(lambda e, er=er, hh=hh, k=k: e.scalar_tensor_tensor(
                        GTR[:, er * 256 + hh * 128: er * 256 + (hh + 1) * 128],
                        E2R[:, er * 256 + hh * 128: er * 256 + (hh + 1) * 128],
                        1e30, P1B[:, k * 128:(k + 1) * 128], OP.min, OP.mult),
                        waits=(e2_tok, p1b_tok, gt_free[er])))
                e2_free[er] = g_toks[1]
                ebv = bank(1, 128, 128)
                w = [sl_toks[0], sl_toks[1], ones_tok, bank_rd.get(1), chr_free[0]]
                t = mm(ebv, onesF[:, :],
                       SLOTS[0:1, rr[0] * 3072 + 2048 + kk * Q: rr[0] * 3072 + 2048 + (kk + 1) * Q],
                       start=True, stop=True, waits=tuple(x for x in w if x), inc=True)
                ch_tokA = vop(lambda e, ebv=ebv, k=k: e.tensor_mul(
                    CHR[:, 0:128], CFM[:, k * 128:(k + 1) * 128], ebv),
                    waits=(t, bc_done))
                w2 = [ones_tok]
                t2 = mm(ebv, onesF[:, :],
                        SLOTS[0:1, rr[1] * 3072 + 2048 + kk * Q: rr[1] * 3072 + 2048 + (kk + 1) * Q],
                        start=True, stop=True, waits=(ch_tokA, chr_free[1]), inc=True)
                ch_tokB = vop(lambda e, ebv=ebv, k=k: e.tensor_mul(
                    CHR[:, 128:256], CFM[:, k * 128:(k + 1) * 128], ebv),
                    waits=(t2, bc_done))
                bank_rd[1] = ch_tokB
                ch_toks = [ch_tokA, ch_tokB]
                scb_tok = vop(lambda e, pair=pair, sr=sr: e.tensor_copy(
                    SCB[:, sr * 128:(sr + 1) * 128], SCARRY[:, pair * 128:(pair + 1) * 128]),
                    waits=(sc_tok2, scb_free[sr]))
                ybi = 2 + (kk // 4)
                yb = bank(ybi, 128, 512)
                for hh in range(2):
                    h = pair * 2 + hh
                    tp = (0, 64 * hh)
                    w = [g_toks[hh], xt_tok]
                    if kk % 4 == 0 and hh == 0:
                        w.append(bank_rd.get(ybi))
                    mm(yb[64 * hh: 64 * hh + 64, (kk % 4) * 128:(kk % 4) * 128 + 128],
                       XTMR[:, xr * 2048 + (k - 0) * 128 + hh * 64: xr * 2048 + k * 128 + hh * 64 + 64]
                       if False else
                       XTMR[:, xr * 2048 + (k * 2 + hh) * 64: xr * 2048 + (k * 2 + hh) * 64 + 64],
                       GTR[:, er * 256 + hh * 128: er * 256 + (hh + 1) * 128],
                       start=True, stop=False, waits=tuple(w), tp=tp)
                    mm(yb[64 * hh: 64 * hh + 64, (kk % 4) * 128:(kk % 4) * 128 + 128],
                       SCB[:, sr * 128 + hh * 64: sr * 128 + hh * 64 + 64],
                       CHR[:, hh * 128:(hh + 1) * 128],
                       start=False, stop=False, waits=(ch_toks[hh], scb_tok), tp=tp)
                    t = mm(yb[64 * hh: 64 * hh + 64, (kk % 4) * 128:(kk % 4) * 128 + 128],
                           XTMR[:, xr * 2048 + (k * 2 + hh) * 64: xr * 2048 + (k * 2 + hh) * 64 + 64],
                           DDG[:, (pair * 2 + hh) * 128:(pair * 2 + hh) * 128 + 128],
                           start=False, stop=True, waits=(ddg_tok,), tp=tp,
                           inc=(hh == 1))
                bank_wr[ybi] = t
                gt_free[er] = t
                chr_free[0] = chr_free[1] = t
                xtmr_free[xr] = t
                scb_free[sr] = t
                for hh in range(2):
                    h = pair * 2 + hh
                    vop(lambda e, pair=pair, hh=hh, h=h, k=k: e.tensor_scalar(
                        SCARRY[:, pair * 128 + hh * 64: pair * 128 + hh * 64 + 64],
                        SCARRY[:, pair * 128 + hh * 64: pair * 128 + hh * 64 + 64],
                        decCols[:, h * 16 + k: h * 16 + k + 1], None, OP.mult),
                        waits=(t, dec_tok))
                vop(lambda e, pair=pair, sr=sr: e.tensor_add(
                    SCARRY[:, pair * 128:(pair + 1) * 128],
                    SCARRY[:, pair * 128:(pair + 1) * 128],
                    SCH[:, sr * 128:(sr + 1) * 128]), waits=(sch_tok,))
                sc_tok2 = ch_v.tok()
                sch_free[sr] = sc_tok2
                if kk % 4 == 3:
                    w = [bank_wr[ybi], sz_tok, mh_guard]
                    gt = vop(lambda e, ybi=ybi, pair=pair, kk=kk, szr=szr: e.tensor_mul(
                        MHALF[:, pair * 1024 + (kk // 4) * 512: pair * 1024 + (kk // 4) * 512 + 512],
                        bank(ybi, 128, 512),
                        SZR[:, szr * 1024 + (kk // 4) * 512: szr * 1024 + (kk // 4) * 512 + 512]),
                        waits=tuple(w))
                    bank_rd[ybi] = gt
                    szr_free[szr] = gt
            for hh in range(2):
                slot_free[rr[hh]] = ch_t.tok()
        m_tok = ch_v.tok()

        if stop_after == "Bg":
            p3s.close()
            pL0.close()
            _finish(em, nc, stack, y_out, dbg, dbg_out, locals(), m_tok, ydma,
                    ch_v, ch_s, ch_t)
            return nc

        for sl2 in range(2):
            so2 = sl2 * 512
            bsum = bank(4, 1, 512)
            for pair in range(16):
                sq = sop(lambda e, pair=pair, sl2=sl2: e.activation(
                    SQH[:, 0:512], MHALF[:, pair * 1024 + sl2 * 512: pair * 1024 + sl2 * 512 + 512],
                    AF.Square), waits=(m_tok,) if pair == 0 else (ch_t.tok(),))
                w = [sq]
                if pair == 0:
                    w.append(bank_rd.get(4))
                t = mm(bsum, onescolb[:, :], SQH[:, 0:512], start=(pair == 0),
                       stop=(pair == 15), waits=tuple(w), inc=True)
            vop(lambda e, bsum=bsum, so2=so2: e.tensor_scalar(
                us_r[0:1, so2:so2 + 512], bsum, 1.0 / DIN, None, OP.mult), waits=(t,))
            bank_rd[4] = ch_v.tok()
            vop(lambda e, so2=so2: e.tensor_scalar(
                us_r[0:1, so2:so2 + 512], us_r[0:1, so2:so2 + 512], 1e-6, None, OP.add))
            sop(lambda e, so2=so2: e.activation(
                us_r[0:1, so2:so2 + 512], us_r[0:1, so2:so2 + 512], AF.Ln),
                waits=(ch_v.tok(),))
            sop(lambda e, so2=so2: e.activation(
                us_r[0:1, so2:so2 + 512], us_r[0:1, so2:so2 + 512], AF.Exp, scale=-0.5))
        yn_tok = ch_s.tok()
        wsc_free = [None, None]
        for e_i in range(8):
            for sl2 in range(2):
                bi = 6 + (sl2 % 2)
                bv = bank(bi, 128, 512)
                for kp in range(16):
                    sl_i = wip_state["idx"] % 8
                    wip_state["idx"] += 1
                    ws = kp % 2
                    wtok0 = gdma(WSC[:, ws * 128:(ws + 1) * 128],
                                 W_op[kp, :, e_i * 128:(e_i + 1) * 128], f"wsc{ws}",
                                 waits=(wsc_free[ws],))
                    wtok = vop(lambda e, sl_i=sl_i, ws=ws, kp=kp: e.tensor_scalar(
                        WIPR[:, sl_i * 128:(sl_i + 1) * 128], WSC[:, ws * 128:(ws + 1) * 128],
                        swcol[:, kp:kp + 1], None, OP.mult),
                        waits=(wtok0, consts, wip_state["free"][sl_i]))
                    wsc_free[ws] = wtok
                    w = [wtok, m_tok]
                    if kp == 0:
                        w.append(bank_rd.get(bi))
                    t = mm(bv, WIPR[:, sl_i * 128:(sl_i + 1) * 128],
                           MHALF[:, kp * 1024 + sl2 * 512: kp * 1024 + sl2 * 512 + 512],
                           start=(kp == 0), stop=(kp == 15), waits=tuple(w),
                           inc=True)
                    wip_state["free"][sl_i] = ch_t.tok()
                bank_wr[bi] = t
                tok0 = half * 1024 + sl2 * 512
                xr2 = sl2 % 2
                x0t = gdma(X0R[:, xr2 * 512:(xr2 + 1) * 512],
                           x_in.rearrange("(c p) t -> p c t", p=128)[:, e_i, HALO + tok0: HALO + tok0 + 512],
                           f"x0r{xr2}", waits=(bank_rd.get(200 + xr2), yn_tok))
                w = [ones_tok, bank_rd.get(1), u_all]
                t1b = mm(bank(1, 128, 512), onesF[:, :], mean_r[0:1, HALO + tok0: HALO + tok0 + 512],
                         start=True, stop=True, waits=tuple(x for x in w if x), inc=True)
                w = [ones_tok, bank_rd.get(0), u_all]
                t2b = mm(bank(0, 128, 512), onesF[:, :], rc_r[0:1, HALO + tok0: HALO + tok0 + 512],
                         start=True, stop=True, waits=tuple(x for x in w if x), inc=True)
                vop(lambda e, xr2=xr2, x0t=x0t, t1b=t1b: e.tensor_sub(
                    X0R[:, xr2 * 512:(xr2 + 1) * 512], X0R[:, xr2 * 512:(xr2 + 1) * 512],
                    bank(1, 128, 512)), waits=(x0t, t1b))
                bank_rd[1] = ch_v.tok()
                vop(lambda e, xr2=xr2: e.tensor_mul(
                    X0R[:, xr2 * 512:(xr2 + 1) * 512], X0R[:, xr2 * 512:(xr2 + 1) * 512],
                    bank(0, 128, 512)), waits=(t2b,))
                bank_rd[0] = ch_v.tok()
                w = [yn_tok, ones_tok, bank_rd.get(2)]
                t3b = mm(bank(2, 128, 512), onesF[:, :], us_r[0:1, sl2 * 512: sl2 * 512 + 512],
                         start=True, stop=True, waits=tuple(x for x in w if x), inc=True)
                cpR = vop(lambda e: e.tensor_copy(
                    XTMR[:, 0:1024].bitcast(F32), bank(2, 128, 512)),
                    waits=(t3b, xtmr_free[0]))
                sc_t = vop(lambda e, bv=bv: e.tensor_mul(
                    bv, bv, XTMR[:, 0:1024].bitcast(F32)), waits=(t, cpR))
                bank_rd[2] = sc_t
                xtmr_free[0] = sc_t
                u1t = vop(lambda e, xr2=xr2, bv=bv: e.tensor_add(
                    X0R[:, xr2 * 512:(xr2 + 1) * 512], X0R[:, xr2 * 512:(xr2 + 1) * 512],
                    bv), waits=())
                bank_rd[bi] = u1t
                ydma(x1u_d.rearrange("(c p) t -> p c t", p=128)[:, e_i, tok0: tok0 + 512],
                     X0R[:, xr2 * 512:(xr2 + 1) * 512], waits=(u1t,))
                bank_rd[200 + xr2] = ldy.tok()
                if stop_after == "Bu" and half == 0 and e_i == 0 and sl2 == 0:
                    p3s.close()
                    pL0.close()
                    _finish(em, nc, stack, y_out, dbg, dbg_out, locals(),
                            bank_rd[200 + xr2], ydma, ch_v, ch_s, ch_t)
                    return nc
        mh_guard = ch_t.tok()
    u1_done = ldy.tok()

    if stop_after == "B":
        p3s.close()
        pL0.close()
        dtok = ydma(y_out[:, :], x1u_d[:, :], waits=(u1_done,))
        _finish(em, nc, stack, y_out, dbg, dbg_out, locals(), dtok, ydma, ch_v, ch_s, ch_t)
        return nc

    p3s.close()
    pL0.close()
    SQ2 = sbuf("SQ2", [128, DC * 512], BF16)
    p45 = contextlib.ExitStack()
    WIP2 = sbuf("WIP2", [128, 16 * 128], BF16, p45)
    wip2 = {"free": [u1_done] * 16, "idx": 0}

    def w2tile(dsrc, waits=()):
        sl_i = wip2["idx"] % 16
        wip2["idx"] += 1
        tok = gdma(WIP2[:, sl_i * 128:(sl_i + 1) * 128], dsrc, f"wip2_{sl_i}",
                   waits=(wip2["free"][sl_i],) + tuple(waits))
        return sl_i, tok

    RES = sbuf("RES", [128, DC * T], F32, p45)
    H2B = sbuf("H2B", [128, DC * T], BF16, p45)
    U1B = sbuf("U1B", [128, 2 * DC * 512], BF16, p45)
    p4u = contextlib.ExitStack()
    U1F = sbuf("U1F", [128, 2 * DC * 512], F32, p4u)
    r1 = gdma(RES[:, :], x_in.rearrange("(c p) t -> p c t", p=128)[:, :, HALO:],
              "res", waits=(u1_done,))
    r2 = gdma(RES[:, :], x1u_d.rearrange("(c p) t -> p c t", p=128),
              "res", waits=(r1,), accum=True)
    u1b_free = [u1_done, u1_done]
    u1f_free = [u1_done, u1_done]
    h2toks = []
    for sl in range(4):
        us = sl % 2
        u1t = gdma(U1F[:, us * DC * 512:(us + 1) * DC * 512],
                   x1u_d.rearrange("(c p) t -> p c t", p=128)[:, :, sl * 512:(sl + 1) * 512],
                   f"u1f{us}", waits=(u1f_free[us],))
        sqt = sop(lambda e, us=us: e.activation(
            SQ2[:, :], U1F[:, us * DC * 512:(us + 1) * DC * 512], AF.Square),
            waits=(u1t,))
        bs = bank(6, 1, 512)
        for c in range(DC):
            w = [sqt, bank_rd.get(6)] if c == 0 else []
            t = mm(bs, onescolb[:, :], SQ2[:, c * 512:(c + 1) * 512],
                   start=(c == 0), stop=(c == DC - 1), waits=tuple(w), inc=(c == DC - 1))
        vop(lambda e, bs=bs, sl=sl: e.tensor_scalar(
            us_r[0:1, sl * 512: sl * 512 + 512], bs, 1.0 / D, None, OP.mult),
            waits=(t,))
        bank_rd[6] = ch_v.tok()
        vop(lambda e, sl=sl: e.tensor_scalar(
            us_r[0:1, sl * 512: sl * 512 + 512], us_r[0:1, sl * 512: sl * 512 + 512],
            1e-6, None, OP.add))
        sop(lambda e, sl=sl: e.activation(
            us_r[0:1, sl * 512: sl * 512 + 512], us_r[0:1, sl * 512: sl * 512 + 512],
            AF.Ln), waits=(ch_v.tok(),))
        rt = sop(lambda e, sl=sl: e.activation(
            us_r[0:1, sl * 512: sl * 512 + 512], us_r[0:1, sl * 512: sl * 512 + 512],
            AF.Exp, scale=-0.5))
        bb = bank(7, 128, 512)
        w = [rt, ones_tok, bank_rd.get(7)]
        t = mm(bb, onesF[:, :], us_r[0:1, sl * 512: sl * 512 + 512], start=True,
               stop=True, waits=tuple(x for x in w if x), inc=True)
        for c in range(DC):
            vop(lambda e, c=c, sl=sl, us=us, bb=bb: e.scalar_tensor_tensor(
                H2B[:, c * T + sl * 512: c * T + sl * 512 + 512],
                U1F[:, us * DC * 512 + c * 512: us * DC * 512 + (c + 1) * 512],
                pwcol[:, c:c + 1], bb, OP.mult, OP.mult),
                waits=(t,) if c == 0 else ())
        bank_rd[7] = ch_v.tok()
        u1f_free[us] = ch_v.tok()
        h2toks.append(ch_v.tok())
    h2_tok = ch_v.tok()
    p4u.close()
    p4h = contextlib.ExitStack()
    SGT = sbuf("SGT", [128, 2 * 512], F32, p4h)
    HH = sbuf("HH", [128, 22 * 1024], BF16, p4h)

    for half in range(2):
        for mt in range(22):
            toks = []
            for k in range(8):
                sg, gtk = w2tile(W_g[k, :, mt * 128:(mt + 1) * 128])
                su, utk = w2tile(W_u[k, :, mt * 128:(mt + 1) * 128])
                toks.append((sg, gtk, su, utk))
            for k, (sg, gtk, su, utk) in enumerate(toks):
                for ss in range(2):
                    no = half * 1024 + ss * 512
                    w = [gtk, h2_tok] if k == 0 else [gtk]
                    if k == 0:
                        w.append(bank_rd.get(ss))
                    mm(bank(ss, 128, 512), WIP2[:, sg * 128:(sg + 1) * 128],
                       _h2mv(H2B, k, no), start=(k == 0), stop=(k == 7),
                       waits=tuple(x for x in w if x), inc=(k == 7))
                    w = [utk] if k else [utk, bank_rd.get(2 + ss)]
                    t = mm(bank(2 + ss, 128, 512), WIP2[:, su * 128:(su + 1) * 128],
                           _h2mv(H2B, k, no), start=(k == 0), stop=(k == 7),
                           waits=tuple(x for x in w if x), inc=True)
                wip2["free"][sg] = ch_t.tok()
                wip2["free"][su] = ch_t.tok()
            gu_done = ch_t.tok()
            for ss in range(2):
                st = sop(lambda e, ss=ss: e.activation(
                    SGT[:, ss * 512:(ss + 1) * 512], bank(ss, 128, 512), AF.Silu),
                    waits=(gu_done,))
                bank_rd[ss] = st
                ht = vop(lambda e, mt=mt, half=half, ss=ss: e.tensor_mul(
                    HH[:, mt * 1024 + ss * 512: mt * 1024 + ss * 512 + 512],
                    SGT[:, ss * 512:(ss + 1) * 512], bank(2 + ss, 128, 512)),
                    waits=(st,))
                bank_rd[2 + ss] = ht
        hh_tok = ch_v.tok()
        for e_i in range(8):
            for k2 in range(22):
                sd, dtk2 = w2tile(W_d[k2, :, e_i * 128:(e_i + 1) * 128])
                for ss in range(2):
                    w = [dtk2, hh_tok] if k2 == 0 else [dtk2]
                    if k2 == 0:
                        w.append(bank_rd.get(4 + ss))
                    t = mm(bank(4 + ss, 128, 512), WIP2[:, sd * 128:(sd + 1) * 128],
                           HH[:, k2 * 1024 + ss * 512: k2 * 1024 + ss * 512 + 512],
                           start=(k2 == 0), stop=(k2 == 21),
                           waits=tuple(x for x in w if x), inc=True)
                wip2["free"][sd] = ch_t.tok()
            for ss in range(2):
                no = half * 1024 + ss * 512
                w = [t if ss == 1 else ch_t.tok(), r2]
                rt = vop(lambda e, e_i=e_i, no=no, ss=ss: e.tensor_add(
                    RES[:, e_i * T + no: e_i * T + no + 512],
                    RES[:, e_i * T + no: e_i * T + no + 512], bank(4 + ss, 128, 512)),
                    waits=tuple(w))
                bank_rd[4 + ss] = rt
    x1_tok = ch_v.tok()

    if stop_after == "P4":
        st = ydma(x1p_d[:, :].rearrange("(c p) t -> p c t", p=128),
                  RES.rearrange("p (c t) -> p c t", c=DC), waits=(x1_tok,))
        st = ydma(y_out[:, :], x1p_d[:, :], waits=(st,))
        p4h.close()
        p45.close()
        _finish(em, nc, stack, y_out, dbg, dbg_out, locals(), st, ydma, ch_v, ch_s, ch_t)
        return nc

    p4h.close()
    p5s = contextlib.ExitStack()
    WINS = sbuf("WINS", [128, 8 * 64], BF16, p5s)
    WOS = sbuf("WOS", [64, 1024], BF16, p5s)
    W1S = sbuf("W1S", [64, 16], F32, p5s)
    B1S = sbuf("B1S", [64, 16], F32, p5s)
    W2S = sbuf("W2S", [64, 16], F32, p5s)
    BINS = sbuf("BINS", [64, 1], F32, p5s)
    B2S = sbuf("B2S", [64, 1], F32, p5s)
    BOUTS = sbuf("BOUTS", [128, 8], F32, p5s)
    WSR = sbuf("WSR", [1, 64], F32, p5s)
    S0 = sbuf("S0", [64, T], F32, p5s)
    GAC = sbuf("GAC", [64, T], F32, p5s)
    TMP5 = sbuf("TMP5", [64, T], F32, p5s)
    GB = sbuf("GB", [64, T], BF16, p5s)
    for dst, srcd in ((WINS, Win.rearrange("k p m -> p k m")),
                      (WOS, Wout0), (W1S, W1_in), (B1S, b1_in), (W2S, W2_in),
                      (BINS, bin_in), (B2S, b2_in), (BOUTS, bout0)):
        gdma(dst[:, :], srcd[:, :, :] if dst is WINS else srcd[:, :], "sc5",
             waits=(x1_tok,) if dst is WINS else ())
    p5w = dch("sc5").tok()
    bw = bank(6, 64, 1)
    for k in range(8):
        w = [p5w, ones_tok, bank_rd.get(6)] if k == 0 else []
        t = mm(bw, WINS[:, k * 64:(k + 1) * 64], onescolb[:, :],
               start=(k == 0), stop=(k == 7), waits=tuple(w), inc=(k == 7))
    wst = vop(lambda e: e.tensor_copy(TMP5[:, 0:1], bw), waits=(t,))
    bank_rd[6] = wst
    tt = tr(bank(7, 1, 64), TMP5[0:64, 0:1], identF[:, :], waits=(wst, bank_rd.get(7)))
    wsr_tok = vop(lambda e: e.tensor_copy(WSR[:, :], bank(7, 1, 64)), waits=(tt,))
    bank_rd[7] = wsr_tok
    for sl in range(4):
        us = sl % 2
        xbt = vop(lambda e, sl=sl, us=us: e.tensor_copy(
            U1B[:, us * DC * 512:(us + 1) * DC * 512],
            RES.rearrange("p (c t) -> p c t", c=DC)[:, :, sl * 512:(sl + 1) * 512]),
            waits=(x1_tok, u1b_free[us]))
        sqt = sop(lambda e, us=us: e.activation(
            SQ2[:, :], U1B[:, us * DC * 512:(us + 1) * DC * 512], AF.Square),
            waits=(xbt,))
        b0, b1v = bank(0, 1, 512), bank(1, 1, 512)
        for c in range(DC):
            w = [xbt, ones_tok, bank_rd.get(0)] if c == 0 else []
            mm(b0, onescolb[:, :], U1B[:, us * DC * 512 + c * 512: us * DC * 512 + (c + 1) * 512],
               start=(c == 0), stop=(c == DC - 1), waits=tuple(w))
        for c in range(DC):
            w = [sqt, bank_rd.get(1)] if c == 0 else []
            t = mm(b1v, onescolb[:, :], SQ2[:, c * 512:(c + 1) * 512],
                   start=(c == 0), stop=(c == DC - 1), waits=tuple(w), inc=(c == DC - 1))
        so = sl * 512
        vop(lambda e, so=so, b0=b0: e.tensor_scalar(
            mean_r[0:1, so:so + 512], b0, 1.0 / D, None, OP.mult), waits=(t,))
        vop(lambda e, so=so, b1v=b1v: e.tensor_scalar(
            us_r[0:1, so:so + 512], b1v, 1.0 / D, None, OP.mult))
        bank_rd[0] = bank_rd[1] = ch_v.tok()
        vop(lambda e, so=so: e.tensor_mul(
            rc_r[0:1, so:so + 512], mean_r[0:1, so:so + 512], mean_r[0:1, so:so + 512]))
        vop(lambda e, so=so: e.tensor_sub(
            rc_r[0:1, so:so + 512], us_r[0:1, so:so + 512], rc_r[0:1, so:so + 512]))
        vop(lambda e, so=so: e.tensor_scalar(
            rc_r[0:1, so:so + 512], rc_r[0:1, so:so + 512], 0.0, 1e-5, OP.max, OP.add))
        lt = ch_v.tok()
        sop(lambda e, so=so: e.activation(
            rc_r[0:1, so:so + 512], rc_r[0:1, so:so + 512], AF.Ln), waits=(lt,))
        rct = sop(lambda e, so=so: e.activation(
            rc_r[0:1, so:so + 512], rc_r[0:1, so:so + 512], AF.Exp, scale=-0.5))
        nmt = vop(lambda e, so=so: e.tensor_scalar(
            us_r[0:1, so:so + 512], mean_r[0:1, so:so + 512], -1.0, None, OP.mult))
        bs = bank(2, 64, 512)
        for k in range(8):
            w = [p5w, xbt, bank_rd.get(2)] if k == 0 else []
            mm(bs, WINS[:, k * 64:(k + 1) * 64],
               U1B[:, us * DC * 512 + k * 512: us * DC * 512 + (k + 1) * 512],
               start=(k == 0), stop=False, waits=tuple(w))
        t = mm(bs, WSR[:, :], us_r[0:1, so:so + 512], start=False, stop=True,
               waits=(wsr_tok, nmt), inc=True)
        bbc = bank(3, 64, 512)
        w = [rct, ones_tok, bank_rd.get(3)]
        t2 = mm(bbc, onesF[:, 0:64], rc_r[0:1, so:so + 512], start=True, stop=True,
                waits=tuple(x for x in w if x), inc=True)
        vop(lambda e, bbc=bbc: e.tensor_copy(
            SQ2[:, 0:1024].bitcast(F32)[0:64, :], bbc), waits=(t2,))
        vop(lambda e, so=so, bs=bs: e.tensor_mul(
            S0[:, so:so + 512], bs, SQ2[:, 0:1024].bitcast(F32)[0:64, :]),
            waits=(t,))
        bank_rd[2] = bank_rd[3] = ch_v.tok()
        vop(lambda e, so=so: e.tensor_scalar(
            S0[:, so:so + 512], S0[:, so:so + 512], BINS[:, :], None, OP.add),
            waits=(p5w,))
        u1b_free[us] = ch_v.tok()
    s_tok = ch_v.tok()
    for i in range(16):
        rt = sop(lambda e, i=i: e.activation(
            TMP5[:, :], S0[0:64, :], AF.Relu, bias=B1S[:, i:i + 1],
            scale=W1S[:, i:i + 1]), waits=(s_tok, p5w) if i == 0 else (ch_v.tok(),))
        if i == 0:
            vop(lambda e: e.tensor_scalar(GAC[0:64, :], TMP5[0:64, :], W2S[:, 0:1],
                                          None, OP.mult), waits=(rt,))
        else:
            vop(lambda e, i=i: e.scalar_tensor_tensor(
                GAC[0:64, :], TMP5[0:64, :], W2S[:, i:i + 1], GAC[0:64, :],
                OP.mult, OP.add), waits=(rt,))
    vop(lambda e: e.tensor_scalar(GAC[0:64, :], GAC[0:64, :], B2S[:, :], None, OP.add))
    gb_tok = vop(lambda e: e.tensor_copy(GB[0:64, :], GAC[0:64, :]))
    for e_i in range(8):
        for sl in range(4):
            bo = bank(4 + sl % 2, 128, 512)
            w = [gb_tok, p5w, bank_rd.get(4 + sl % 2)]
            t = mm(bo, WOS[:, e_i * 128:(e_i + 1) * 128], GB[0:64, sl * 512:(sl + 1) * 512],
                   start=True, stop=True, waits=tuple(x for x in w if x), inc=True)
            rt = vop(lambda e, e_i=e_i, sl=sl, bo=bo: e.scalar_tensor_tensor(
                RES[:, e_i * T + sl * 512: e_i * T + sl * 512 + 512],
                bo, BOUTS[:, e_i:e_i + 1],
                RES[:, e_i * T + sl * 512: e_i * T + sl * 512 + 512], OP.add, OP.add),
                waits=(t,))
            bank_rd[4 + sl % 2] = rt
    x1p_tok = ch_v.tok()
    st = ydma(x1p_d.rearrange("(c p) t -> p c t", p=128),
              RES.rearrange("p (c t) -> p c t", c=DC), waits=(x1p_tok,))
    for sl in range(4):
        us = sl % 2
        xbt = vop(lambda e, sl=sl, us=us: e.tensor_copy(
            U1B[:, us * DC * 512:(us + 1) * DC * 512],
            RES.rearrange("p (c t) -> p c t", c=DC)[:, :, sl * 512:(sl + 1) * 512]),
            waits=(x1p_tok, u1b_free[us]))
        sqt = sop(lambda e, us=us: e.activation(
            SQ2[:, :], U1B[:, us * DC * 512:(us + 1) * DC * 512], AF.Square),
            waits=(xbt,))
        b0, b1v = bank(0, 1, 512), bank(1, 1, 512)
        for c in range(DC):
            w = [xbt, ones_tok, bank_rd.get(0)] if c == 0 else []
            mm(b0, onescolb[:, :], U1B[:, us * DC * 512 + c * 512: us * DC * 512 + (c + 1) * 512],
               start=(c == 0), stop=(c == DC - 1), waits=tuple(w))
        for c in range(DC):
            w = [sqt, bank_rd.get(1)] if c == 0 else []
            t = mm(b1v, onescolb[:, :], SQ2[:, c * 512:(c + 1) * 512],
                   start=(c == 0), stop=(c == DC - 1), waits=tuple(w), inc=(c == DC - 1))
        so = sl * 512
        vop(lambda e, so=so, b0=b0: e.tensor_scalar(
            mean_r[0:1, so:so + 512], b0, 1.0 / D, None, OP.mult), waits=(t,))
        vop(lambda e, so=so, b1v=b1v: e.tensor_scalar(
            us_r[0:1, so:so + 512], b1v, 1.0 / D, None, OP.mult))
        bank_rd[0] = bank_rd[1] = ch_v.tok()
        vop(lambda e, so=so: e.tensor_mul(
            rc_r[0:1, so:so + 512], mean_r[0:1, so:so + 512], mean_r[0:1, so:so + 512]))
        vop(lambda e, so=so: e.tensor_sub(
            rc_r[0:1, so:so + 512], us_r[0:1, so:so + 512], rc_r[0:1, so:so + 512]))
        vop(lambda e, so=so: e.tensor_scalar(
            rc_r[0:1, so:so + 512], rc_r[0:1, so:so + 512], 0.0, 1e-5, OP.max, OP.add))
        lt = ch_v.tok()
        sop(lambda e, so=so: e.activation(
            rc_r[0:1, so:so + 512], rc_r[0:1, so:so + 512], AF.Ln), waits=(lt,))
        rct = sop(lambda e, so=so: e.activation(
            rc_r[0:1, so:so + 512], rc_r[0:1, so:so + 512], AF.Exp, scale=-0.5))
        bm = bank(2, 128, 512)
        w = [ones_tok, bank_rd.get(2), ch_v.tok()]
        mm(bm, onesF[:, :], mean_r[0:1, so:so + 512], start=True, stop=True,
           waits=tuple(x for x in w if x))
        br = bank(3, 128, 512)
        w = [rct, bank_rd.get(3)]
        t2 = mm(br, onesF[:, :], rc_r[0:1, so:so + 512], start=True, stop=True,
                waits=tuple(x for x in w if x), inc=True)
        for c in range(DC):
            vop(lambda e, c=c, us=us, bm=bm: e.tensor_sub(
                SQ2[:, c * 512:(c + 1) * 512],
                U1B[:, us * DC * 512 + c * 512: us * DC * 512 + (c + 1) * 512], bm),
                waits=(t2,) if c == 0 else ())
            vop(lambda e, c=c, us=us, br=br: e.tensor_mul(
                U1B[:, us * DC * 512 + c * 512: us * DC * 512 + (c + 1) * 512],
                SQ2[:, c * 512:(c + 1) * 512], br))
        t1t = ch_v.tok()
        bank_rd[2] = bank_rd[3] = t1t
        gdma(t1AG_in.rearrange("(c p) t -> p c t", p=128)[:, :, sl * 512:(sl + 1) * 512],
             U1B[:, us * DC * 512:(us + 1) * DC * 512], "t1st", waits=(t1t,))
        u1b_free[us] = dch("t1st").tok()
    t1_in_done = dch("t1st").tok()
    em.emit("g", lambda e: e.collective_compute(
        "AllGather", OP.bypass, replica_groups=GROUPS,
        ins=[t1AG_in[:, :]], outs=[t1AG_out[:, :]]),
        (t1_in_done,), ((cc, 1),))
    ag2 = cc.tok()
    p5s.close()
    p45.close()

    if stop_after == "P5":
        st = ydma(y_out[:, :], x1p_d[:, :], waits=(st,))
        _finish(em, nc, stack, y_out, dbg, dbg_out, locals(), st, ydma, ch_v, ch_s, ch_t)
        return nc

    p6s = contextlib.ExitStack()
    QA = sbuf("QA", [128, 4 * S], BF16, p6s)
    KA = sbuf("KA", [128, 4 * S], BF16, p6s)
    V65 = sbuf("V65", [128, 32 * 8 * 65], BF16, p6s)
    PTB = sbuf("PTB", [128, 2 * 1024], BF16, p6s)
    T1T = sbuf("T1T", [128, 2 * DC * 512], BF16, p6s)
    OTB = sbuf("OTB", [64, 2 * 512], BF16, p6s)
    MHC = sbuf("MHC", [128, 8], F32, p6s)
    SRW = sbuf("SRW", [1, 2 * 512], F32, p6s)
    QNR = sbuf("QNR", [1, 160], F32, p6s)
    amaskb = sbuf("amaskb", [128, 4 * 512], BF16, p6s)
    amask_tok = gdma(amaskb[:, :], amask_in.rearrange("d p t -> p d t"), "amask", waits=(ag2,))
    p6w = contextlib.ExitStack()
    QKW = sbuf("QKW", [128, 8 * 1024], BF16, p6w)
    VW = sbuf("VW", [128, 8 * 512], BF16, p6w)
    for k in range(8):
        gdma(QKW[:, k * 1024:(k + 1) * 1024], W_qkv_qk[k, :, :], "qkw",
             waits=(ag2,) if k == 0 else ())
        gdma(VW[:, k * 512:(k + 1) * 512], W_qkv_v[k, :, :], "qkw")
    qkw_tok = dch("qkw").tok()
    vo_tok = vop(lambda e: e.memset(V65[:, :], 1.0))
    t1t_free = [ag2, ag2]
    qk_ev = None
    for ts in range(8):
        slot = ts // 4
        us = ts % 2
        t1tok = gdma(T1T[:, us * DC * 512:(us + 1) * DC * 512],
                     t1AG_out.rearrange("(s c p) t -> s p c t", s=2, p=128)[
                         slot, :, :, (ts % 4) * 512:(ts % 4) * 512 + 512],
                     f"t1t{us}", waits=(ag2, t1t_free[us]))
        for m2 in range(8):
            bi = m2 % 2
            bv = bank(bi, 128, 512)
            for k in range(8):
                w = [t1tok, qkw_tok] if k == 0 else []
                if k == 0:
                    w.append(bank_rd.get(bi))
                t = mm(bv, QKW[:, k * 1024 + m2 * 128: k * 1024 + (m2 + 1) * 128],
                       T1T[:, us * DC * 512 + k * 512: us * DC * 512 + (k + 1) * 512],
                       start=(k == 0), stop=(k == 7), waits=tuple(x for x in w if x),
                       inc=(k == 7))
            dst = QA if m2 % 2 == 0 else KA
            j = m2 // 2
            ev = vop(lambda e, dst=dst, j=j, ts=ts, bv=bv: e.tensor_copy(
                dst[:, j * S + ts * 512: j * S + ts * 512 + 512], bv), waits=(t,))
            bank_rd[bi] = ev
        for tt in range(4):
            bi = 2 + tt % 2
            bv = bank(bi, 128, 512)
            for k in range(8):
                w = [t1tok, qkw_tok] if k == 0 else []
                if k == 0:
                    w.append(bank_rd.get(bi))
                t = mm(bv, T1T[:, us * DC * 512 + k * 512 + tt * 128:
                               us * DC * 512 + k * 512 + tt * 128 + 128],
                       VW[:, k * 512:(k + 1) * 512],
                       start=(k == 0), stop=(k == 7), waits=tuple(x for x in w if x),
                       inc=(k == 7))
            gt = ts * 4 + tt
            for h in range(8):
                w = [t, vo_tok] if h == 0 else []
                ev = vop(lambda e, gt=gt, h=h, bv=bv: e.tensor_copy(
                    V65[:, (gt * 8 + h) * 65: (gt * 8 + h) * 65 + 64], bv[:, h * 64:(h + 1) * 64]),
                    waits=tuple(w))
            bank_rd[bi] = ev
        t1t_free[us] = ch_v.tok()
    qkv_tok = ch_v.tok()
    p6w.close()
    if stop_after == "P6v":
        p6s.close()
        _finish(em, nc, stack, y_out, dbg, dbg_out, locals(), qkv_tok, ydma, ch_v, ch_s, ch_t)
        return nc
    sq_last = {0: None, 64: None}
    for h in range(8):
        j, hb = h // 2, 64 * (h % 2)
        for src_a, qn in ((QA, 0), (KA, 1)):
            sqt = sop(lambda e, src_a=src_a, j=j, hb=hb: e.activation(
                SQ2[hb:hb + 64, :], src_a[hb:hb + 64, j * S:(j + 1) * S], AF.Square),
                waits=(qkv_tok, sq_last[hb]))
            for g in range(8):
                slot = 4 + g % 2
                w = [sqt, bank_rd.get(slot)] if g < 2 else [bank_rd.get(slot)]
                t = mm(bank(slot, 1, 512), onescolb[hb:hb + 64, :],
                       SQ2[hb:hb + 64, g * 512:(g + 1) * 512],
                       start=True, stop=True, waits=tuple(x for x in w if x),
                       inc=True)
                bank_rd[slot] = vop(lambda e, h=h, qn=qn, g=g, slot=slot: e.reduce_max(
                    QNR[0:1, h * 16 + qn * 8 + g: h * 16 + qn * 8 + g + 1],
                    bank(slot, 1, 512), axis=AX.X), waits=(t,))
            sq_last[hb] = t
    for h in range(8):
        vop(lambda e, h=h: e.reduce_max(
            QNR[0:1, 128 + h:129 + h], QNR[0:1, h * 16:h * 16 + 8], axis=AX.X))
        vop(lambda e, h=h: e.reduce_max(
            QNR[0:1, 136 + h:137 + h], QNR[0:1, h * 16 + 8:h * 16 + 16], axis=AX.X))
    for h in range(8):
        vop(lambda e, h=h: e.tensor_mul(
            QNR[0:1, 144 + h:145 + h], QNR[0:1, 128 + h:129 + h],
            QNR[0:1, 136 + h:137 + h]))
    vop(lambda e: e.memset(SRW[0:1, 0:512], 0.0))
    lt = ch_v.tok()
    sop(lambda e: e.activation(QNR[0:1, 144:152], QNR[0:1, 144:152], AF.Ln),
        waits=(lt,))
    et = sop(lambda e: e.activation(QNR[0:1, 144:152], QNR[0:1, 144:152],
                                    AF.Exp, scale=0.5))
    nt = vop(lambda e: e.tensor_scalar(QNR[0:1, 144:152], QNR[0:1, 144:152],
                                       -0.125, None, OP.mult), waits=(et,))
    for h in range(8):
        bb = bank(6, 128, 1)
        w = [nt, ones_tok, bank_rd.get(6)]
        t = mm(bb, onesF[:, :], QNR[0:1, 144 + h:145 + h], start=True, stop=True,
               waits=tuple(x for x in w if x), inc=True)
        mv = vop(lambda e, h=h, bb=bb: e.tensor_copy(MHC[:, h:h + 1], bb), waits=(t,))
        bank_rd[6] = mv
    mhc_tok = ch_v.tok()
    if stop_after == "P6q":
        p6s.close()
        _finish(em, nc, stack, y_out, dbg, dbg_out, locals(), mhc_tok, ydma, ch_v, ch_s, ch_t)
        return nc
    ptb_free = {(st, us): mhc_tok for st in (0, 1) for us in (0, 1)}
    stb_rd = {}
    ot_free = [None, None]
    ag_list = []
    for qs_pair in ((0, 4), (1, 5), (2, 6), (3, 7)):
      for qs in qs_pair:
        for j in range(4):
            nkv = qs * 4 + 4
            po = {st: bank(4 + st, 65, 512) for st in (0, 1)}
            ets = {}
            pv_last = {}

            def emit_pv(st, tk, j=j, qs=qs):
                h = 2 * j + st
                us2 = tk % 2
                w = [ets[(st, tk)]]
                if tk == 0:
                    w.append(bank_rd.get(4 + st))
                pv = mm(po[st], V65[:, (tk * 8 + h) * 65: (tk * 8 + h) * 65 + 65],
                        PTB[:, (st * 2 + us2) * 512: (st * 2 + us2) * 512 + 512],
                        start=(tk == 0), stop=(tk == nkv - 1),
                        waits=tuple(x for x in w if x), inc=(tk == nkv - 1))
                ptb_free[(st, us2)] = pv
                pv_last[st] = pv

            for t_kv in range(nkv):
                us = t_kv % 2
                d = t_kv - 4 * qs
                for st in (0, 1):
                    h = 2 * j + st
                    hb = 64 * st
                    slot = st * 2 + us
                    stb = bank(slot, 128, 512)
                    w = [qkv_tok, stb_rd.get(slot, bank_rd.get(slot))]
                    t = mm(stb,
                           KA[hb:hb + 64, j * S + t_kv * 128: j * S + t_kv * 128 + 128],
                           QA[hb:hb + 64, j * S + qs * 512: j * S + qs * 512 + 512],
                           start=True, stop=(d < 0),
                           waits=tuple(x for x in w if x), inc=(d < 0))
                    if d >= 0:
                        t = mm(stb, identb[:, :], amaskb[:, d * 512:(d + 1) * 512],
                               start=False, stop=True, waits=(amask_tok,),
                               inc=True)
                    ets[(st, t_kv)] = sop(lambda e, slot=slot, stb=stb, h=h:
                        e.activation(PTB[:, slot * 512: slot * 512 + 512], stb,
                                     AF.Exp, bias=MHC[:, h:h + 1], scale=0.125),
                        waits=(t, ptb_free[(st, us)], mhc_tok))
                    stb_rd[slot] = ets[(st, t_kv)]
                if t_kv > 0:
                    for st in (0, 1):
                        emit_pv(st, t_kv - 1)
            for st in (0, 1):
                emit_pv(st, nkv - 1)
            for st in (0, 1):
                h = 2 * j + st
                sv = vop(lambda e, st=st, po=po: e.tensor_copy(
                    SRW[0:1, st * 512:(st + 1) * 512], po[st][64:65, :]),
                    waits=(pv_last[st],))
                rv = vop(lambda e, st=st: e.reciprocal(
                    SRW[0:1, st * 512:(st + 1) * 512],
                    SRW[0:1, st * 512:(st + 1) * 512]))
                bb = bank(6 + st, 64, 512)
                w = [rv, ones_tok, bank_rd.get(6 + st)]
                t = mm(bb, onesF[:, 0:64], SRW[0:1, st * 512:(st + 1) * 512],
                       start=True, stop=True, waits=tuple(x for x in w if x),
                       inc=True)
                scr = SQ2[:, st * 1024:(st + 1) * 1024].bitcast(F32)[0:64, :]
                cpb = vop(lambda e, bb=bb, scr=scr: e.tensor_copy(scr, bb),
                          waits=(t,))
                ov = vop(lambda e, st=st, po=po, scr=scr: e.tensor_mul(
                    OTB[:, st * 512:(st + 1) * 512], po[st][0:64, :], scr),
                    waits=(cpb, ot_free[st]))
                bank_rd[4 + st] = ov
                bank_rd[6 + st] = cpb
                gdma(oAG_in[qs, h * 64:(h + 1) * 64, :],
                     OTB[:, st * 512:(st + 1) * 512], f"ost{st}", waits=(ov,))
                ot_free[st] = dch(f"ost{st}").tok()
      o_done = (dch("ost0").tok(), dch("ost1").tok())
      for qs in qs_pair:
          em.emit("g", lambda e, qs=qs: e.collective_compute(
              "AllGather", OP.bypass, replica_groups=GROUPS,
              ins=[oAG_in[qs, :, :]], outs=[oAG_out[qs, :, :]]),
              o_done, ((cc, 1),))
      ag_list.append(cc.tok())
    ag3 = ag_list[-1]
    if stop_after == "P6a":
        p6s.close()
        _finish(em, nc, stack, y_out, dbg, dbg_out, locals(), ag3, ydma, ch_v, ch_s, ch_t)
        return nc
    OSEL = sbuf("OSEL", [128, 8 * 512], BF16, p6s)
    OAB = sbuf("OAB", [128, 2 * 512], BF16, p6s)
    X2T = sbuf("X2T", [128, 2 * 512], F32, p6s)
    X2B = sbuf("X2B", [128, 512], BF16, p6s)
    oab_free = [None, None]
    x2t_free = [None, None]
    osel_free = None
    for g in range(4):
        osel_toks = []
        for k in range(8):
            ua = k % 2
            ta = gdma(OAB[:, ua * 512:(ua + 1) * 512],
                      oAG_out[g, k * 128:(k + 1) * 128, :], f"oab{ua}",
                      waits=(ag_list[g], oab_free[ua]))
            sel1 = vop(lambda e, k=k, ua=ua: e.tensor_scalar(
                OSEL[:, k * 512:(k + 1) * 512], OAB[:, ua * 512:(ua + 1) * 512],
                uinvcol[:, :], None, OP.mult), waits=(ta, consts, osel_free))
            tb2 = gdma(OAB[:, ua * 512:(ua + 1) * 512],
                       oAG_out[4 + g, k * 128:(k + 1) * 128, :],
                       f"oab{ua}", waits=(sel1,))
            sel2 = vop(lambda e, k=k, ua=ua: e.scalar_tensor_tensor(
                OSEL[:, k * 512:(k + 1) * 512], OAB[:, ua * 512:(ua + 1) * 512],
                uinitcol[:, :], OSEL[:, k * 512:(k + 1) * 512], OP.mult, OP.add),
                waits=(tb2,))
            oab_free[ua] = sel2
            osel_toks.append(sel2)
        osel_tok = ch_v.tok()
        for e_i in range(8):
            bi = 4 + e_i % 2
            bv = bank(bi, 128, 512)
            toks = []
            for k in range(8):
                swo, wtk = w2tile(W_wo[k, :, e_i * 128:(e_i + 1) * 128])
                toks.append((swo, wtk))
            for k, (swo, wtk) in enumerate(toks):
                w = [wtk, osel_tok] if k == 0 else [wtk]
                if k == 0:
                    w.append(bank_rd.get(bi))
                t = mm(bv, WIP2[:, swo * 128:(swo + 1) * 128],
                       OSEL[:, k * 512:(k + 1) * 512],
                       start=(k == 0), stop=(k == 7), waits=tuple(w), inc=True)
                wip2["free"][swo] = ch_t.tok()
            ux = e_i % 2
            x1pt = gdma(X2T[:, ux * 512:(ux + 1) * 512],
                        x1p_d.rearrange("(c p) t -> p c t", p=128)[:, e_i, g * 512:(g + 1) * 512],
                        f"x2t{ux}", waits=(x2t_free[ux],))
            xt = vop(lambda e, ux=ux, bv=bv: e.tensor_add(
                X2T[:, ux * 512:(ux + 1) * 512], X2T[:, ux * 512:(ux + 1) * 512], bv),
                waits=(x1pt, t))
            bank_rd[bi] = xt
            ydma(x2_d.rearrange("(c p) t -> p c t", p=128)[:, e_i, g * 512:(g + 1) * 512],
                 X2T[:, ux * 512:(ux + 1) * 512], waits=(xt,))
            xbt = vop(lambda e, ux=ux: e.tensor_copy(
                X2B[:, :], X2T[:, ux * 512:(ux + 1) * 512]), waits=(xt,))
            sqt = sop(lambda e: e.activation(SQ2[:, 0:512], X2B[:, :], AF.Square),
                      waits=(xbt,))
            w = [xbt, ones_tok, bank_rd.get(0)] if e_i == 0 else [xbt]
            mm(bank(0, 1, 512), onescolb[:, :], X2B[:, :],
               start=(e_i == 0), stop=(e_i == 7), waits=tuple(x for x in w if x))
            w = [sqt] if e_i else [sqt, bank_rd.get(1)]
            t = mm(bank(1, 1, 512), onescolb[:, :], SQ2[:, 0:512],
                   start=(e_i == 0), stop=(e_i == 7), waits=tuple(x for x in w if x),
                   inc=(e_i == 7))
            x2t_free[ux] = ch_v.tok()
        osel_free = ch_t.tok()
        so = g * 512
        vop(lambda e, so=so: e.tensor_scalar(
            mean_r[0:1, so:so + 512], bank(0, 1, 512), 1.0 / D, None, OP.mult),
            waits=(t,))
        vop(lambda e, so=so: e.tensor_scalar(
            us_r[0:1, so:so + 512], bank(1, 1, 512), 1.0 / D, None, OP.mult))
        bank_rd[0] = bank_rd[1] = ch_v.tok()
        vop(lambda e, so=so: e.tensor_mul(
            rc_r[0:1, so:so + 512], mean_r[0:1, so:so + 512], mean_r[0:1, so:so + 512]))
        vop(lambda e, so=so: e.tensor_sub(
            rc_r[0:1, so:so + 512], us_r[0:1, so:so + 512], rc_r[0:1, so:so + 512]))
        vop(lambda e, so=so: e.tensor_scalar(
            rc_r[0:1, so:so + 512], rc_r[0:1, so:so + 512], 0.0, 1e-5, OP.max, OP.add))
        lt = ch_v.tok()
        sop(lambda e, so=so: e.activation(
            rc_r[0:1, so:so + 512], rc_r[0:1, so:so + 512], AF.Ln), waits=(lt,))
        sop(lambda e, so=so: e.activation(
            rc_r[0:1, so:so + 512], rc_r[0:1, so:so + 512], AF.Exp, scale=-0.5))
    x2_done = ldy.tok()
    x2s_tok = ch_s.tok()
    p6s.close()

    if stop_after == "P6":
        dtok = ydma(y_out[:, :], x2_d[:, :], waits=(x2_done,))
        _finish(em, nc, stack, y_out, dbg, dbg_out, locals(), dtok, ydma, ch_v, ch_s, ch_t)
        return nc

    p7s = contextlib.ExitStack()
    VNB = sbuf("VNB", [128, DC * 1024], BF16, p7s)
    HP = sbuf("HP", [128, 32 * 1024], BF16, p7s)
    GP = sbuf("GP", [128, 32 * 1024], BF16, p7s)
    OSL = sbuf("OSL", [128, 2 * 32 * 128], BF16, p7s)
    X2R = sbuf("X2R", [128, DC * 512], BF16, p7s)
    X3T = sbuf("X3T", [128, 512], F32, p7s)
    SGT2 = sbuf("SGT2", [128, 512], F32, p7s)
    SC1 = sbuf("SC1", [128, 32], F32, p7s)
    SC2 = sbuf("SC2", [128, 32], F32, p7s)
    SCH1 = sbuf("SCH1", [128, 32], F32, p7s)
    SCH2 = sbuf("SCH2", [128, 32], F32, p7s)
    SC3 = sbuf("SC3", [128, 8], F32, p7s)
    for dst, srcd in ((SC1, s_t1), (SC2, s_t2), (SCH1, s_h1), (SCH2, s_h2), (SC3, s_t3)):
        gdma(dst[:, :], srcd[:, :], "sc7", waits=(x2_done,) if dst is SC1 else ())
    p7w = dch("sc7").tok()
    x2r_free = [None, None]

    def emit_v(ps_i):
        vtoks = []
        for sl in (2 * ps_i, 2 * ps_i + 1):
            x2t = gdma(X2R[:, 0: DC * 512],
                       x2_d.rearrange("(c p) t -> p c t", p=128)[:, :, sl * 512:(sl + 1) * 512],
                       "x2rv", waits=(x2_done, x2r_free[0], x2r_free[1]))
            so = sl * 512
            lo = (sl - 2 * ps_i) * 512
            bm = bank(2, 128, 512)
            w = [ones_tok, bank_rd.get(2), x2s_tok]
            mm(bm, onesF[:, :], mean_r[0:1, so:so + 512], start=True, stop=True,
               waits=tuple(x for x in w if x))
            br = bank(3, 128, 512)
            w = [bank_rd.get(3), x2s_tok]
            t2 = mm(br, onesF[:, :], rc_r[0:1, so:so + 512], start=True, stop=True,
                    waits=tuple(x for x in w if x), inc=True)
            for c in range(DC):
                vop(lambda e, c=c, bm=bm: e.tensor_sub(
                    SQ2[:, c * 512:(c + 1) * 512],
                    X2R[:, c * 512:(c + 1) * 512], bm),
                    waits=(t2, x2t, vnb_guard) if c == 0 else ())
                vop(lambda e, c=c, lo=lo, br=br: e.tensor_mul(
                    VNB[:, c * 1024 + lo: c * 1024 + lo + 512],
                    SQ2[:, c * 512:(c + 1) * 512], br))
            bank_rd[2] = bank_rd[3] = ch_v.tok()
            x2r_free[0] = ch_v.tok()
        return ch_v.tok()

    osl_free = [p7w, p7w]
    x3_out_tok = None
    hp_guard = None
    vnb_guard = None
    for ps_i in range(2):
        tok0 = ps_i * 1024
        v_tok = emit_v(ps_i)
        x2r_free[0] = x2r_free[1] = v_tok
        for o in range(32):
            uo = o % 2
            t1w = gdma(OSL[:, uo * 4096: uo * 4096 + 1024], W_t1[o, :, :], f"osl{uo}",
                       waits=(osl_free[uo],))
            t2w = gdma(OSL[:, uo * 4096 + 1024: uo * 4096 + 2048], W_t2[o, :, :], f"osl{uo}")
            wt = dch(f"osl{uo}").tok()
            for ss in range(2):
                no = ss * 512
                for k in range(8):
                    w = [wt, v_tok] if k == 0 else []
                    if k == 0:
                        w.append(bank_rd.get(ss))
                    mm(bank(ss, 128, 512), OSL[:, uo * 4096 + k * 128: uo * 4096 + (k + 1) * 128],
                       VNB[:, k * 1024 + no: k * 1024 + no + 512],
                       start=(k == 0), stop=(k == 7), waits=tuple(x for x in w if x))
                    w = [wt] if k else [wt, bank_rd.get(2 + ss)]
                    t = mm(bank(2 + ss, 128, 512),
                           OSL[:, uo * 4096 + 1024 + k * 128: uo * 4096 + 1024 + (k + 1) * 128],
                           VNB[:, k * 1024 + no: k * 1024 + no + 512],
                           start=(k == 0), stop=(k == 7), waits=tuple(x for x in w if x),
                           inc=(k == 7))
            gu = ch_t.tok()
            osl_free[uo] = gu
            vnb_guard = gu
            ht = None
            for ss in range(2):
                st2 = sop(lambda e, ss=ss, o=o: e.activation(
                    SGT2[:, :], bank(ss, 128, 512), AF.Silu,
                    scale=SC1[:, o:o + 1]), waits=(gu, p7w, ht))
                bank_rd[ss] = st2
                ht = vop(lambda e, o=o, ss=ss: e.scalar_tensor_tensor(
                    HP[:, o * 1024 + ss * 512: o * 1024 + ss * 512 + 512],
                    bank(2 + ss, 128, 512), SC2[:, o:o + 1],
                    SGT2[:, :], OP.mult, OP.mult),
                    waits=(st2, hp_guard))
                bank_rd[2 + ss] = ht
        hp_tok = ch_v.tok()
        for o in range(32):
            uo = o % 2
            hw = gdma(OSL[:, uo * 4096: uo * 4096 + 4096], W_h1[o, :, :], f"osl{uo}",
                      waits=(osl_free[uo],))
            for ss in range(2):
                bi = 4 + ss
                for k in range(32):
                    w = [hw, hp_tok] if k == 0 else []
                    if k == 0:
                        w.append(bank_rd.get(bi))
                    t = mm(bank(bi, 128, 512), OSL[:, uo * 4096 + k * 128: uo * 4096 + (k + 1) * 128],
                           HP[:, k * 1024 + ss * 512: k * 1024 + ss * 512 + 512],
                           start=(k == 0), stop=(k == 31), waits=tuple(x for x in w if x),
                           inc=(k == 31))
                ge = sop(lambda e, o=o, ss=ss, bi=bi: e.activation(
                    GP[:, o * 1024 + ss * 512: o * 1024 + ss * 512 + 512],
                    bank(bi, 128, 512), AF.Gelu, scale=SCH1[:, o:o + 1]),
                    waits=(t, p7w))
                bank_rd[bi] = ge
            osl_free[uo] = ch_t.tok()
        gp_tok = ch_s.tok()
        for o in range(32):
            uo = o % 2
            hw = gdma(OSL[:, uo * 4096: uo * 4096 + 4096], W_h2[o, :, :], f"osl{uo}",
                      waits=(osl_free[uo],))
            for ss in range(2):
                bi = 4 + ss
                for k in range(32):
                    w = [hw, gp_tok] if k == 0 else []
                    if k == 0:
                        w.append(bank_rd.get(bi))
                    t = mm(bank(bi, 128, 512), OSL[:, uo * 4096 + k * 128: uo * 4096 + (k + 1) * 128],
                           GP[:, k * 1024 + ss * 512: k * 1024 + ss * 512 + 512],
                           start=(k == 0), stop=(k == 31), waits=tuple(x for x in w if x),
                           inc=(k == 31))
                ge = sop(lambda e, o=o, ss=ss, bi=bi: e.activation(
                    HP[:, o * 1024 + ss * 512: o * 1024 + ss * 512 + 512],
                    bank(bi, 128, 512), AF.Gelu, scale=SCH2[:, o:o + 1]),
                    waits=(t, p7w))
                bank_rd[bi] = ge
            osl_free[uo] = ch_t.tok()
        h2p_tok = ch_s.tok()
        hp_guard = None
        for e_i in range(8):
            uo = e_i % 2
            hw = gdma(OSL[:, uo * 4096: uo * 4096 + 4096], W_t3[e_i, :, :], f"osl{uo}",
                      waits=(osl_free[uo],))
            for ss in range(2):
                bi = 6 + ss
                no = tok0 + ss * 512
                for k in range(32):
                    w = [hw, h2p_tok] if k == 0 else []
                    if k == 0:
                        w.append(bank_rd.get(bi))
                    t = mm(bank(bi, 128, 512), OSL[:, uo * 4096 + k * 128: uo * 4096 + (k + 1) * 128],
                           HP[:, k * 1024 + ss * 512: k * 1024 + ss * 512 + 512],
                           start=(k == 0), stop=(k == 31), waits=tuple(x for x in w if x),
                           inc=(k == 31))
                ux = ss
                x2t = gdma(X2R[:, ux * 512: ux * 512 + 512],
                           x2_d.rearrange("(c p) t -> p c t", p=128)[:, e_i, no:no + 512],
                           f"x2w{ux}", waits=(x2r_free[ux],))
                xo = vop(lambda e, bi=bi, e_i=e_i, ux=ux: e.scalar_tensor_tensor(
                    X3T[:, :],
                    bank(bi, 128, 512), SC3[:, e_i:e_i + 1],
                    X2R[:, ux * 512: ux * 512 + 512], OP.mult, OP.add),
                    waits=(t, x2t, p7w, x3_out_tok))
                bank_rd[bi] = xo
                x3_out_tok = ydma(
                    y_out.rearrange("(c p) t -> p c t", p=128)[:, e_i, no:no + 512],
                    X3T[:, :], waits=(xo,))
                x2r_free[ux] = x3_out_tok
                if stop_after is not None and stop_after.startswith("P7w") and (
                        (ps_i, e_i, ss) == tuple(int(x) for x in stop_after[3:].split("_"))):
                    p7s.close()
                    _finish(em, nc, stack, y_out, dbg, dbg_out, locals(),
                            x3_out_tok, ydma, ch_v, ch_s, ch_t)
                    return nc
            osl_free[uo] = ch_t.tok()
        hp_guard = ch_t.tok()
    p7s.close()
    _finish(em, nc, stack, y_out, dbg, dbg_out, locals(), x3_out_tok, ydma, ch_v, ch_s, ch_t)
    return nc


def _emit_z(em, mm, bank, sop, vop, gdma, W_ip, WIPR, wip_state, UALL, u_all,
            bank_rd, bank_wr, SZR, szr, pair, zoff, szfree, ch_t, ch_s):
    toks = []
    for k in range(8):
        sl_i = wip_state["idx"] % 8
        wip_state["idx"] += 1
        wtok = gdma(WIPR[:, sl_i * 128:(sl_i + 1) * 128], W_ip[pair, k, :, :],
                    f"wipr{sl_i}", waits=(wip_state["free"][sl_i],))
        for ss in range(2):
            bi = 6 + ss
            w = [wtok]
            if k == 0:
                w += [u_all, bank_rd.get(bi)]
            t = mm(bank(bi, 128, 512), WIPR[:, sl_i * 128:(sl_i + 1) * 128],
                   UALL[:, k * 2051 + zoff + ss * 512: k * 2051 + zoff + ss * 512 + 512],
                   start=(k == 0), stop=(k == 7), waits=tuple(w),
                   inc=(k == 7 or ss == 1))
            if k == 7:
                bank_wr[bi] = t
        wip_state["free"][sl_i] = ch_t.tok()
    for ss in range(2):
        bi = 6 + ss
        tok = sop(lambda e, bi=bi, ss=ss, szr=szr: e.activation(
            SZR[:, szr * 1024 + ss * 512: szr * 1024 + ss * 512 + 512],
            bank(bi, 128, 512), AF.Silu), waits=(bank_wr[bi], szfree))
        bank_rd[bi] = tok
    return ch_s.tok()


def _finish(em, nc, stack, y_out, dbg, dbg_out, lcl, final_tok, ydma, ch_v, ch_s, ch_t):
    ldy = lcl["ldy"]
    if dbg is not None:
        name, rrows, rcols = dbg
        tsrc = lcl[name]
        em.emit("g", lambda e: e.dma_start(out=dbg_out[:, :], in_=tsrc[0:rrows, 0:rcols]),
                (final_tok, ch_v.tok(), ch_s.tok(), ch_t.tok()),
                ((ldy, 16),))
    em.emit("y", lambda e: e.wait_ge(ldy.sem, ldy.n), (final_tok,), ())
    with nc.Block() as block:
        em.realize(block)
    stack.close()


def kernel(**inputs):
    in_maps = host_prepare(inputs)
    nc = build_program()
    res = run_bass_kernel_spmd(nc, in_maps, core_ids=list(range(NCORE)))
    out = np.zeros((B, S, D), np.float32)
    for c in range(NCORE):
        b, hf = c // 2, c % 2
        out[b, hf * T:(hf + 1) * T, :] = res.results[c]["y"].T
    return out
